# version 08
# baseline (speedup 1.0000x reference)
import math
import numpy as np
import ml_dtypes

import concourse.bass as bass
import concourse.mybir as mybir
from concourse.bass_utils import run_bass_kernel_spmd

F32 = mybir.dt.float32
BF16 = mybir.dt.bfloat16
AX = mybir.AxisListType.X
ALU = mybir.AluOpType
ACT = mybir.ActivationFunctionType

T = 2048
C = 1024
H = 16
HD = 64
LAT = 32
G = 4
NT = 16
NB = 4
NCK = 8
E = 16
S = 64
EL = 1024
FF = 2048
NCORES = 8
SCALE = 1.0 / math.sqrt(LAT)
EPS = 1e-5

_BUILD_OPTS = {"taps": (), "phases": 99}


class _Sem:
    def __init__(self, nc, name):
        self.h = nc.alloc_semaphore(name)
        self.n = 0


class MK:
    def __init__(self, nc):
        self.nc = nc
        self.ops = {e: [] for e in "tvsgy"}
        self.sem = {k: _Sem(nc, "sem_" + k) for k in
                    ("t", "v", "s", "g", "gd", "yd", "cc")}
        self.seen = {e: {} for e in "tvsgy"}
        self.sb = {}
        self.regions = []
        self.specs = []
        self.aliases = {}
        self.plan = None
        self.arena_base = None
        self.arena_cap = None

    def on(self, e, fn):
        self.ops[e].append(fn)

    def w(self, e, tok):
        if tok is None:
            return
        sem, n = tok
        if self.seen[e].get(id(sem), -1) >= n:
            return
        self.seen[e][id(sem)] = n
        self.on(e, lambda E, s=sem, v=n: E.wait_ge(s.h, v))

    def do(self, e, fn, sem=None, inc=1):
        if e in ("v", "s") and sem is not None:
            self.w(e, (sem, sem.n))
        if sem is None:
            self.on(e, fn)
            return None
        sem.n += inc
        n = sem.n
        self.on(e, lambda E, f=fn, s=sem, i=inc: f(E).then_inc(s.h, i))
        return (sem, n)

    def tok(self, k):
        s = self.sem[k]
        return (s, s.n)

    def dsem(self, name):
        if name not in self.sem:
            self.sem[name] = _Sem(self.nc, "sem_" + name)
        return self.sem[name]

    def salloc(self, name, shape, dtype, p0, p1, alias=None):
        size = shape[1] * (2 if dtype == BF16 else 4)
        size = (size + 63) & ~63
        if alias is None:
            self.specs.append((name, size, p0, p1))
        else:
            self.aliases[name] = alias
        if self.plan is not None:
            off = self.plan[name] if alias is None else self.plan[alias]
        else:
            off = 0
        self.regions.append((off, size, p0, p1))
        h = self.nc.alloc_sbuf_tensor_at(name, list(shape), dtype,
                                         offset=self.arena_base + off)
        self.sb[name] = h
        return h


def _rope_tables():
    inv = 1.0 / (10000.0 ** (np.arange(0, LAT, 2, dtype=np.float64) / LAT))
    t = np.arange(T, dtype=np.float64)
    fr = np.outer(t, inv)
    cos = np.repeat(np.cos(fr), 2, axis=-1)[:, :LAT]
    sin = np.repeat(np.sin(fr), 2, axis=-1)[:, :LAT]
    return (np.tile(cos.T, (G, 1)).astype(np.float32),
            np.tile(sin.T, (G, 1)).astype(np.float32))


def _const_inputs():
    bf = ml_dtypes.bfloat16
    cosT, sinT = _rope_tables()
    prot = np.zeros((128, 128), dtype=np.float32)
    for blk in range(G):
        for i in range(LAT // 2):
            b = blk * 32
            prot[b + 2 * i, b + 2 * i + 1] = -1.0
            prot[b + 2 * i + 1, b + 2 * i] = 1.0
    tri = (np.arange(128)[None, :] >= np.arange(128)[:, None]).astype(np.float32)
    blk4T = np.zeros((4, 128), dtype=np.float32)
    for j in range(4):
        blk4T[j, 32 * j:32 * (j + 1)] = 1.0
    return {
        "cosT": cosT, "sinT": sinT,
        "protT": prot.T.astype(bf).copy(),
        "trimask": tri.astype(bf),
        "identf": np.eye(128, dtype=np.float32),
        "identb": np.eye(128, dtype=np.float32).astype(ml_dtypes.bfloat16),
        "ones1": np.ones((1, 512), dtype=bf),
        "ones128": np.ones((128, 1), dtype=bf),
        "blk4T": blk4T.astype(bf),
        "ones65": np.ones((65, 512), dtype=bf),
        "blkones4": blk4T.T.astype(bf).copy(),
        "eps128": np.full((128, 1), EPS, dtype=np.float32),
    }


_CONSTS = _const_inputs()


def build(plan=None):
    nc = bass.Bass("TRN2", target_bir_lowering=False)
    DI = lambda n, s, dt=F32: nc.dram_tensor(n, list(s), dt, kind="ExternalInput")

    x_tm_in = DI("x_tm_in", [T, C])
    x_fm_in = DI("x_fm_in", [C, T])
    caw_in = DI("caw", [3 * C, C])
    cprojT_in = DI("cprojT_in", [C, C])
    q2lT_in = DI("q2lT_in", [HD, LAT])
    kv2lT_in = DI("kv2lT_in", [HD, LAT])
    l2o_in = DI("l2o_in", [HD, LAT])
    wqkT_in = DI("wqkT_in", [LAT, LAT])
    bqr_in = DI("bqr_in", [HD, H])
    bkr_in = DI("bkr_in", [HD, H])
    bvr_in = DI("bvr_in", [HD, H])
    q2lb16_in = DI("q2lb16_in", [H, LAT])
    kvlb16_in = DI("kvlb16_in", [H, LAT])
    wqkb512_in = DI("wqkb512_in", [1, 512])
    l2ob_in = DI("l2ob_in", [128, 8])
    cprojb_in = DI("cprojb_in", [1, C])
    cosT_in = DI("cosT", [128, T])
    sinT_in = DI("sinT", [128, T])
    protT_in = DI("protT", [128, 128], BF16)
    tri_in = DI("trimask", [128, 128], BF16)
    identf_in = DI("identf", [128, 128], F32)
    identb_in = DI("identb", [128, 128], BF16)
    ones1_in = DI("ones1", [1, 512], BF16)
    ones128_in = DI("ones128", [128, 1], BF16)
    ones65_in = DI("ones65", [65, 512], BF16)
    blk4T_in = DI("blk4T", [4, 128], BF16)
    blkones4_in = DI("blkones4", [128, 4], BF16)
    blkdQ_in = DI("blkdQ_in", [128, 64])
    blkdKV_in = DI("blkdKV_in", [128, 64])
    wqkT4_in = DI("wqkT4_in", [128, LAT])
    l2o2_in = DI("l2o2_in", [128, LAT])
    eps128_in = DI("eps128", [128, 1], F32)
    normg_in = DI("normg128", [128, C])
    normb_in = DI("normb128", [128, C])
    slotg_in = DI("slotg128", [128, C])
    slotb_in = DI("slotb128", [128, C])
    scale128_in = DI("scale128", [128, 1])
    semb_in = DI("semb", [EL, C])
    wg2_in = DI("wg2", [2, C, FF])
    wu2_in = DI("wu2", [2, C, FF])
    wd2_in = DI("wd2", [2, FF, C])
    bg2_in = DI("bg2", [2, FF])
    bu2_in = DI("bu2", [2, FF])
    bd2_in = DI("bd2", [2, C])

    out_d = nc.dram_tensor("out", [T, C], F32, kind="ExternalOutput")

    ag1_in = nc.dram_tensor("ag1_in", [C, EL], BF16)
    ag1_out = nc.dram_tensor("ag1_out", [NCORES * C, EL], BF16, addr_space="Shared")
    ag2_in = nc.dram_tensor("ag2_in", [EL, C], BF16)
    ag2_out = nc.dram_tensor("ag2_out", [NCORES * EL, C], BF16, addr_space="Shared")
    x2_spill = nc.dram_tensor("x2_spill", [T, C], BF16)

    ARENA = 206336
    arena = nc.alloc_sbuf_tensor("arena", [128, ARENA], mybir.dt.uint8)
    mk = MK(nc)
    mk.plan = plan
    mk.arena_base = nc.lookup_mloc(arena).addr
    mk.arena_cap = ARENA

    A = mk.salloc
    scrA = A("scrA", [128, T], F32, 0, 9)
    scrB = A("scrB", [128, T], F32, 0, 9)
    scbA = A("scbA", [128, T], BF16, 0, 9)
    dbgA = A("dbgA", [128, 1024], F32, 0, 2)
    scbB = A("scbB", [128, T], BF16, 0, 9)
    stat = A("stat", [128, 64], F32, 0, 9)
    identf_s = A("identf_s", [128, 128], F32, 0, 9)
    identb_s = A("identb_s", [128, 128], BF16, 0, 9)
    ones1_s = A("ones1_s", [1, 512], BF16, 0, 9)
    ones128_s = A("ones128_s", [128, 1], BF16, 0, 9)
    ones65_s = A("ones65_s", [65, 512], BF16, 0, 10)
    eps_s = A("eps_s", [128, 1], F32, 0, 9)
    x_tm = A("x_tm", [128, NT * C], BF16, 0, 4)
    wpeT = A("wpeT", [128, G * C], BF16, 0, 4)
    bprj_b = A("bprj_b", [1, C], BF16, 0, 4)
    tri_s = A("tri_s", [128, 128], BF16, 0, 3)
    blk4T_s = A("blk4T_s", [4, 128], BF16, 0, 3)
    combineT = A("combineT", [128, NCK * T], BF16, 7, 10)
    recip_tm = A("recip_tm", [128, 16], F32, 7, 10)
    x_fm = A("x_fm", [128, NCK * T], BF16, 0, 2)
    weffq = A("weffq", [128, NCK * 512], BF16, 0, 2)
    weffk = A("weffk", [128, NCK * 512], BF16, 0, 2)
    weffv = A("weffv", [128, NCK * 512], BF16, 0, 2)
    cosbf = A("cosbf", [128, T], BF16, 0, 2)
    sinbf = A("sinbf", [128, T], BF16, 0, 2)
    prot_s = A("prot_s", [128, 128], BF16, 0, 2)
    blkones4_s = A("blkones4_s", [128, 4], BF16, 0, 2)
    wqkT_b = A("wqkT_b", [32, 32], BF16, 0, 2)
    wqkT4 = A("wqkT4", [128, 32], BF16, 0, 2)
    biasq_row = A("biasq_row", [1, 512], BF16, 0, 2)
    biask_row = A("biask_row", [1, 512], BF16, 0, 2)
    biasv_row = A("biasv_row", [1, 512], BF16, 0, 2)
    wqkb_b = A("wqkb_b", [1, 512], BF16, 0, 2)
    QK = A("QK", [128, G * T], BF16, 1, 3)
    KR = A("KR", [128, G * T], BF16, 1, 3)
    vt = A("vt", [128, NT * 512], BF16, 1, 3)
    xn_tm = A("xn_tm", [128, NT * C], BF16, 5, 7)
    rowsB2 = A("rowsB2", [65, 2 * T], BF16, 7, 9)
    l2o_b = A("l2o_b", [64, 32], BF16, 0, 1)
    l2o2 = A("l2o2", [128, 32], BF16, 0, 1)
    nbias_pp = A("nbias_pp", [128, 1], F32, 2, 3)
    ylat = A("ylat", [128, G * T], BF16, 3, 4)
    x2 = A("x2", [128, NT * C], BF16, 4, 5)
    normg_s = A("normg_s", [128, C], F32, 4, 5)
    normb_s = A("normb_s", [128, C], F32, 4, 5)
    slotg_s = A("slotg_s", [128, C], F32, 4, 5)
    slotb_s = A("slotb_s", [128, C], F32, 4, 5)
    scale_s = A("scale_s", [128, 1], F32, 4, 5)
    xn_fm = A("xn_fm", [128, NCK * T], BF16, 5, 6)
    seT = A("seT", [128, NCK * C], BF16, 5, 6)
    logits = A("logits", [128, NT * C], F32, 6, 7)
    rowmax_tm = A("rowmax_tm", [128, 16], F32, 6, 7)
    XeT = A("XeT", [128, NCK * C], BF16, 8, 9)
    xstg = A("xstg", [128, 2 * C], BF16, 8, 8)
    eout_f = A("eout_f", [128, NCK * C], BF16, 9, 10)
    x2b = A("x2b", [128, NT * C], BF16, 9, 10)
    cawb0 = A("cawb0", [128, C], BF16, 0, 0)
    cawb1 = A("cawb1", [128, C], BF16, 0, 0)
    cprojT2 = A("cprojT2", [128, 2 * C], BF16, 0, 0)
    q2lT_f = A("q2lT_f", [64, 32], F32, 0, 0)
    kv2lT_f = A("kv2lT_f", [64, 32], F32, 0, 0)
    q2lT_b = A("q2lT_b", [64, 32], BF16, 0, 0)
    kv2lT_b = A("kv2lT_b", [64, 32], BF16, 0, 0)
    l2o_f = A("l2o_f", [64, 32], F32, 0, 0)
    wqkT_f = A("wqkT_f", [32, 32], F32, 0, 0)
    blkdQ = A("blkdQ", [128, 64], BF16, 0, 2)
    blkdKV = A("blkdKV", [128, 64], BF16, 0, 2)
    blkdQf = A("blkdQf", [128, 64], F32, 0, 0)
    blkdKVf = A("blkdKVf", [128, 64], F32, 0, 0)
    wqkT4f = A("wqkT4f", [128, LAT], F32, 0, 0)
    l2o2f = A("l2o2f", [128, LAT], F32, 0, 0)
    bxr_f = A("bxr_f", [64, 16], F32, 0, 0)
    bxr_b = A("bxr_b", [64, 16], BF16, 0, 0)
    bx16c = A("bx16c", [16, 32], F32, 0, 0)
    bx16_f = A("bx16_f", [16, 32], F32, 0, 0)
    bx16b = A("bx16b", [16, 32], BF16, 0, 0)
    wqkb_f = A("wqkb_f", [1, 512], F32, 0, 0)
    l2ob_f = A("l2ob_f", [128, 8], F32, 0, 0)
    l2ob_b = A("l2ob_b", [128, 8], BF16, 0, 0)
    cprojb_f = A("cprojb_f", [1, C], F32, 0, 0)
    bprj_f = A("bprj_f", [1, C], F32, 0, 0)
    nb44 = A("nb44", [4, 4], F32, 2, 2)
    nbrow = A("nbrow", [1, 16], F32, 2, 2)
    nbrow2 = A("nbrow2", [1, 1], BF16, 2, 2)
    PT = A("PT", [128, 2 * T], BF16, 3, 3)
    sums4 = A("sums4", [4, 512], F32, 3, 3)
    rec4 = A("rec4", [4, 512], F32, 3, 3)
    rec4b = A("rec4b", [4, 512], BF16, 3, 3)
    rbbf = A("rbbf", [128, 512], BF16, 3, 3)
    dispatchN = A("dispatchN", [128, NT * C], BF16, 7, 7)
    lgb = A("lgb", [128, 2 * NT * C], BF16, 7, 7, alias="logits")
    scrBb = A("scrBb", [128, T], BF16, 7, 7, alias="scrB")
    nrm_s = A("nrm_s", [128, 16], F32, 7, 7)
    rsum_tm = A("rsum_tm", [128, 16], F32, 7, 7)
    rsL_s = A("rsL_s", [128, 16], F32, 7, 7)
    cmax_s = A("cmax_s", [128, 8], F32, 7, 7)
    ncmax_s = A("ncmax_s", [128, 8], F32, 7, 7)
    cs2_s = A("cs2_s", [128, 8], F32, 7, 7)
    rcl_s = A("rcl_s", [128, 8], F32, 7, 7)
    rcT_s = A("rcT_s", [128, 8], F32, 7, 7)
    wgA = A("wgA", [128, NCK * C], BF16, 8, 8)
    wgB = A("wgB", [128, NCK * C], BF16, 8, 8)
    wuA = A("wuA", [128, NCK * C], BF16, 8, 8)
    wuB = A("wuB", [128, NCK * C], BF16, 8, 8)
    wdS = A("wdS", [128, 16 * 512], BF16, 8, 8)
    hfm = A("hfm", [128, 16 * 512], BF16, 8, 8)
    outbuf = A("outbuf", [128, NT * C], F32, 10, 10)

    bg_row = rowsB2[0:1, 0:2 * FF]
    bu_row = rowsB2[32:33, 0:2 * FF]
    bd_row = rowsB2[64:65, 0:2 * C]

    p4 = nc.alloc_psum_tensor("p4", [128, 2048], F32)
    pA = nc.alloc_psum_tensor("pA", [128, 512], F32)
    pB = nc.alloc_psum_tensor("pB", [128, 512], F32)
    pC = nc.alloc_psum_tensor("pC", [128, 512], F32)
    pD = nc.alloc_psum_tensor("pD", [128, 512], F32)

    ST, SV, SS = mk.sem["t"], mk.sem["v"], mk.sem["s"]
    GD, YD = mk.sem["gd"], mk.sem["yd"]

    x_fm_v = x_fm.rearrange("p (c t) -> p c t", c=NCK)
    x_tm_v = x_tm.rearrange("p (i c) -> p i c", i=NT)
    weffq_v = weffq.rearrange("p (c x) -> p c x", c=NCK)
    weffk_v = weffk.rearrange("p (c x) -> p c x", c=NCK)
    weffv_v = weffv.rearrange("p (c x) -> p c x", c=NCK)
    wpeT_v = wpeT.rearrange("p (g o) -> p g o", g=G)
    QK_v = QK.rearrange("p (g t) -> p g t", g=G)
    KR_v = KR.rearrange("p (g t) -> p g t", g=G)
    vt_v = vt.rearrange("p (i x) -> p i x", i=NT)
    ylat_v = ylat.rearrange("p (g t) -> p g t", g=G)
    PT_v = PT.rearrange("p (u t) -> p u t", u=2)
    x2_v = x2.rearrange("p (i c) -> p i c", i=NT)
    xn_tm_v = xn_tm.rearrange("p (i c) -> p i c", i=NT)
    xn_fm_v = xn_fm.rearrange("p (c t) -> p c t", c=NCK)
    seT_v = seT.rearrange("p (c e) -> p c e", c=NCK)
    logits_v = logits.rearrange("p (i c) -> p i c", i=NT)
    combineT_v = combineT.rearrange("p (c t) -> p c t", c=NCK)
    dnT_v = dispatchN.rearrange("p (e t) -> p e t", e=NCK)
    lgb_v = lgb.rearrange("p (i c) -> p i c", i=2 * NT)
    XeT_v = XeT.rearrange("p (c e) -> p c e", c=NCK)
    wgA_v = wgA.rearrange("p (c f) -> p c f", c=NCK)
    wgB_v = wgB.rearrange("p (c f) -> p c f", c=NCK)
    wuA_v = wuA.rearrange("p (c f) -> p c f", c=NCK)
    wuB_v = wuB.rearrange("p (c f) -> p c f", c=NCK)
    wdS_v = wdS.rearrange("p (c o) -> p c o", c=16)
    hfm_v = hfm.rearrange("p (c x) -> p c x", c=16)
    eout_f_v = eout_f.rearrange("p (c o) -> p c o", c=NCK)
    x2b_v = x2b.rearrange("p (i c) -> p i c", i=NT)
    outbuf_v = outbuf.rearrange("p (i c) -> p i c", i=NT)
    p4q = p4.rearrange("p (q x) -> p q x", q=16)
    p4j = p4.rearrange("p (j x) -> p j x", j=4)

    ldtoks = {}

    def gload(name, dst_ap, src_ap, sem=None):
        s_ = mk.dsem(sem or ("g_" + name))
        ldtoks[name] = mk.do("g", lambda E, d=dst_ap, s=src_ap: E.dma_start(d, s), s_, 16)

    def yload(name, dst_ap, src_ap, sem=None):
        s_ = mk.dsem(sem or ("y_" + name))
        ldtoks[name] = mk.do("y", lambda E, d=dst_ap, s=src_ap: E.dma_start(d, s), s_, 16)
        return ldtoks[name]

    def yload_Y0(name, dst_ap, src_ap):
        return yload(name, dst_ap, src_ap, sem="Y0")

    gload("x_fm", x_fm_v[:, :, :], x_fm_in.rearrange("(c p) t -> p c t", p=128), sem="G0")
    gload("x_tm", x_tm_v[:, :, :], x_tm_in.rearrange("(i p) c -> p i c", p=128), sem="G0")
    gload("cos", cosbf[:, :], cosT_in[:, :], sem="G0")
    gload("sin", sinbf[:, :], sinT_in[:, :], sem="G0")
    yload_Y0("identf", identf_s[:, :], identf_in[:, :])
    yload_Y0("identb", identb_s[:, :], identb_in[:, :])
    yload_Y0("tri", tri_s[:, :], tri_in[:, :])
    yload_Y0("prot", prot_s[:, :], protT_in[:, :])
    yload_Y0("ones1", ones1_s[:, :], ones1_in[:, :])
    yload_Y0("ones128", ones128_s[:, :], ones128_in[:, :])
    yload_Y0("ones65", ones65_s[:, :], ones65_in[:, :])
    yload_Y0("blk4T", blk4T_s[:, :], blk4T_in[:, :])
    yload_Y0("blkones4", blkones4_s[:, :], blkones4_in[:, :])
    yload_Y0("eps", eps_s[:, :], eps128_in[:, :])
    yload_Y0("q2lT", q2lT_f[:, :], q2lT_in[:, :])
    yload_Y0("kv2lT", kv2lT_f[:, :], kv2lT_in[:, :])
    yload_Y0("wqkb", wqkb_f[:, :], wqkb512_in[:, :])
    yload_Y0("l2ob", l2ob_f[:, :], l2ob_in[:, :])
    yload_Y0("cprojb", cprojb_f[:, :], cprojb_in[:, :])
    yload_Y0("blkdQ", blkdQf[:, :], blkdQ_in[:, :])
    yload_Y0("blkdKV", blkdKVf[:, :], blkdKV_in[:, :])
    yload_Y0("wqkT4", wqkT4f[:, :], wqkT4_in[:, :])
    yload_Y0("l2o2", l2o2f[:, :], l2o2_in[:, :])

    TOK_G0 = mk.tok("G0")
    TOK_Y0 = mk.tok("Y0")
    for _n in list(ldtoks):
        if mk.sem.get("G0") and ldtoks[_n] and ldtoks[_n][0] is mk.sem["G0"]:
            ldtoks[_n] = TOK_G0
        if mk.sem.get("Y0") and ldtoks[_n] and ldtoks[_n][0] is mk.sem["Y0"]:
            ldtoks[_n] = TOK_Y0
    tok_smalls = None
    for nm, (dst, src) in {
        "q2lT": (q2lT_b, q2lT_f), "kv2lT": (kv2lT_b, kv2lT_f),
        "wqkb": (wqkb_b, wqkb_f), "l2ob": (l2ob_b, l2ob_f),
        "blkdQ": (blkdQ, blkdQf), "blkdKV": (blkdKV, blkdKVf),
        "wqkT4": (wqkT4, wqkT4f), "l2o2": (l2o2, l2o2f),
    }.items():
        mk.w("v", ldtoks[nm])
        tok_smalls = mk.do("v", lambda E, d=dst, s=src:
                           E.tensor_copy(d[:, :], s[:, :]), SV)
    tok_blkd = tok_smalls
    tok_l2o2 = tok_smalls
    tok_wqk4 = tok_smalls

    parts = [(0, blkdQ, weffq_v), (C, blkdKV, weffk_v), (2 * C, blkdKV, weffv_v)]
    caw_bufs = [cawb0, cawb1]
    caw_tok = [None, None]
    psQ = [pA, pB]
    ps_tok = [None, None]
    for pi, (roff, blkd, wv) in enumerate(parts):
        for p in range(8):
            ii = pi * 8 + p
            par = ii % 2
            buf = caw_bufs[par]
            mk.w("g", caw_tok[par])
            ct = mk.do("g", lambda E, b=buf, r=roff + p * 128:
                       E.dma_start(b[:, :], caw_in[r:r + 128, :]),
                       mk.dsem(f"CAW{par}"), 16)
            ps = psQ[par]
            mk.w("t", ct)
            mk.w("t", tok_blkd)
            mk.w("t", ps_tok[par])
            tk = None
            for j in range(NCK):
                tk = mk.do("t", lambda E, ps_=ps, b=buf, bd=blkd, j_=j:
                           E.matmul(ps_[:, j_ * 64:(j_ + 1) * 64],
                                    b[:, j_ * 128:(j_ + 1) * 128], bd[:, :],
                                    start=True, stop=True), ST)
            caw_tok[par] = tk
            mk.w("v", tk)
            dst = wv[:, :, p * 64:(p + 1) * 64]
            src = ps.rearrange("p (j x) -> p j x", j=8)
            ps_tok[par] = mk.do("v", lambda E, d=dst, s=src:
                                E.tensor_copy(d, s[:, :, :]), SV)

    prev_cast = None
    for brow, bin_, b16in, rhsw in ((biasq_row, bqr_in, q2lb16_in, q2lT_b),
                                    (biask_row, bkr_in, kvlb16_in, kv2lT_b),
                                    (biasv_row, bvr_in, kvlb16_in, kv2lT_b)):
        mk.w("y", prev_cast)
        mk.do("y", lambda E, b=bin_: E.dma_start(bxr_f[:, :], b[:, :]),
              mk.dsem("YB"), 16)
        t2 = mk.do("y", lambda E, b=b16in: E.dma_start(bx16c[:, :], b[:, :]),
                   mk.dsem("YB"), 16)
        mk.w("v", t2)
        t3 = mk.do("v", lambda E: E.tensor_copy(bxr_b[:, :], bxr_f[:, :]), SV)
        mk.w("t", t3)
        mk.w("t", ps_tok[0])
        t4 = mk.do("t", lambda E, r=rhsw: E.matmul(pA[0:16, 0:32], bxr_b[:, :],
                                                   r[:, :], start=True, stop=True), ST)
        mk.w("v", t4)
        t5 = mk.do("v", lambda E: E.tensor_tensor(bx16_f[:, :], pA[0:16, 0:32],
                                                  bx16c[:, :], ALU.add), SV)
        ps_tok[0] = t5
        t6 = mk.do("v", lambda E: E.tensor_copy(bx16b[:, :], bx16_f[:, :]), SV)
        mk.w("g", t6)
        t7 = mk.do("g", lambda E, d=brow: E.dma_start(d[0:1, :], bx16b[:, :]),
                   mk.dsem("BRW"), 16)
        mk.w("v", t7)
        prev_cast = t5
        ldtoks["brow"] = t7

    cp2 = cprojT2.rearrange("p (u o) -> p u o", u=2)
    wpe_prev_mm = None
    wpe_done = None
    for g in range(G):
        mk.w("g", wpe_prev_mm)
        ct = mk.do("g", lambda E, g_=g: E.dma_start(
            cp2[:, :, :],
            cprojT_in.rearrange("(u p) o -> p u o", p=128)[:, 2 * g_:2 * g_ + 2, :]),
            mk.dsem("CPJ"), 16)
        mk.w("t", ct)
        mk.w("t", tok_l2o2)
        mk.w("t", wpe_done)
        tk = None
        for j in range(G):
            for hb in range(2):
                tk = mk.do("t", lambda E, j_=j, hb_=hb: E.matmul(
                    p4[32 * j_:32 * (j_ + 1), hb_ * 512:(hb_ + 1) * 512],
                    l2o2[(j_ % 2) * 64:(j_ % 2) * 64 + 64, :],
                    cp2[(j_ % 2) * 64:(j_ % 2) * 64 + 64, j_ // 2,
                        hb_ * 512:(hb_ + 1) * 512],
                    start=True, stop=True,
                    tile_position=((j_ % 2) * 64, 32 * j_)), ST)
        for u in range(2):
            for hb in range(2):
                tk2 = mk.do("t", lambda E, u_=u, g_=g, hb_=hb: E.matmul(
                    p4[0:1, 1024 + hb_ * 512:1024 + (hb_ + 1) * 512],
                    l2ob_b[:, 2 * g_ + u_:2 * g_ + u_ + 1],
                    cp2[:, u_, hb_ * 512:(hb_ + 1) * 512],
                    start=(2 * g_ + u_ == 0), stop=(2 * g_ + u_ == 7)), ST)
        wpe_prev_mm = tk2
        mk.w("v", tk)
        wpe_done = mk.do("v", lambda E, g_=g: E.tensor_copy(
            wpeT_v[:, g_, :], p4[:, 0:1024]), SV)
    mk.w("v", wpe_prev_mm)
    mk.w("v", ldtoks["cprojb"])
    mk.do("v", lambda E: E.tensor_tensor(bprj_f[0:1, :], p4[0:1, 1024:2048],
                                         cprojb_f[0:1, :], ALU.add), SV)
    tok_bprj = mk.do("v", lambda E: E.tensor_copy(bprj_b[0:1, :], bprj_f[0:1, :]), SV)

    if _BUILD_OPTS["phases"] < 1:
        return _finish(mk, nc, out_d, outbuf_v)

    mk.w("t", ldtoks["x_fm"])
    mk.w("t", ldtoks["ones1"])
    mk.w("t", ldtoks["brow"])
    mk.w("t", tok_wqk4)
    mk.w("t", tok_bprj)
    mk.w("v", ldtoks["cos"])
    mk.w("v", ldtoks["sin"])

    tA = tB = tC = None
    prev_rot = None
    prev_wqk = None
    for g in range(G):
        for side in ("q", "k"):
            wv = weffq_v if side == "q" else weffk_v
            brow = biasq_row if side == "q" else biask_row
            for b in range(NB):
                mk.w("t", tA)
                for j in range(NCK):
                    mk.do("t", lambda E, w_=wv, j_=j, g_=g, b_=b: E.matmul(
                        pA[:, :], w_[:, j_, g_ * 128:(g_ + 1) * 128],
                        x_fm_v[:, j_, b_ * 512:(b_ + 1) * 512],
                        start=(j_ == 0), stop=False), ST)
                tk = mk.do("t", lambda E, br=brow, g_=g: E.matmul(
                    pA[:, :], br[0:1, g_ * 128:(g_ + 1) * 128],
                    ones1_s[0:1, 0:512], start=False, stop=True), ST)
                mk.w("v", tk)
                mk.w("v", prev_rot)
                t_qs = mk.do("v", lambda E, b_=b: E.tensor_tensor(
                    scbA[:, 0:512], pA[:, :],
                    sinbf[:, b_ * 512:(b_ + 1) * 512], ALU.mult), SV)
                mk.w("t", t_qs)
                mk.w("t", tB)
                t_rot = mk.do("t", lambda E: E.matmul(
                    pB[:, :], prot_s[:, :], scbA[:, 0:512],
                    start=True, stop=True), ST)
                prev_rot = t_rot
                mk.w("v", t_rot)
                if _BUILD_OPTS.get("dbg") and g == 0 and side == "k" and b == 0:
                    mk.do("v", lambda E: E.tensor_copy(dbgA[:, 0:512], scbA[:, 0:512]), SV)
                    mk.do("v", lambda E: E.tensor_copy(dbgA[:, 512:1024], pB[:, :]), SV)
                t_qc = mk.do("v", lambda E, b_=b: E.tensor_tensor(
                    scbB[:, 0:512], pA[:, :],
                    cosbf[:, b_ * 512:(b_ + 1) * 512], ALU.mult), SV)
                tA = t_qc
                if side == "q":
                    mk.w("v", prev_wqk)
                    t_qr = mk.do("v", lambda E: E.tensor_tensor(
                        scbA[:, 512:1024], pB[:, :], scbB[:, 0:512], ALU.add), SV)
                    tB = t_qr
                    mk.w("t", t_qr)
                    mk.w("t", tC)
                    for j in range(4):
                        mk.do("t", lambda E, j_=j: E.matmul(
                            pC[32 * j_:32 * (j_ + 1), :], wqkT4[32 * j_:32 * (j_ + 1), :],
                            scbA[32 * j_:32 * (j_ + 1), 512:1024],
                            start=True, stop=False,
                            tile_position=(32 * j_, 32 * j_)), ST)
                    tk2 = mk.do("t", lambda E, g_=g: E.matmul(
                        pC[:, :], wqkb_b[0:1, g_ * 128:(g_ + 1) * 128],
                        ones1_s[0:1, 0:512], start=False, stop=True), ST)
                    prev_wqk = tk2
                    mk.w("v", tk2)
                    tC = mk.do("v", lambda E, g_=g, b_=b: E.tensor_copy(
                        QK_v[:, g_, b_ * 512:(b_ + 1) * 512], pC[:, :]), SV)
                else:
                    tB = mk.do("v", lambda E, g_=g, b_=b: E.tensor_tensor(
                        KR_v[:, g_, b_ * 512:(b_ + 1) * 512], pB[:, :],
                        scbB[:, 0:512], ALU.add), SV)
    for tc in range(NT):
        par = tc % 2
        ps = pA if par == 0 else pB
        mk.w("t", tA if par == 0 else tB)
        for j in range(NCK):
            mk.do("t", lambda E, ps_=ps, j_=j, tc_=tc: E.matmul(
                ps_[:, :], x_fm_v[:, j_, tc_ * 128:(tc_ + 1) * 128],
                weffv_v[:, j_, :], start=(j_ == 0), stop=False), ST)
        tk = mk.do("t", lambda E, ps_=ps: E.matmul(
            ps_[:, :], ones1_s[0:1, 0:128], biasv_row[0:1, :],
            start=False, stop=True), ST)
        mk.w("v", tk)
        ev = mk.do("v", lambda E, ps_=ps, tc_=tc: E.tensor_copy(
            vt_v[:, tc_, :], ps_[:, :]), SV)
        if par == 0:
            tA = ev
        else:
            tB = ev

    if _BUILD_OPTS["phases"] < 2:
        return _finish(mk, nc, out_d, outbuf_v)

    sq_tok = None
    for g in range(G):
        for side in ("q", "k"):
            src = QK_v if side == "q" else KR_v
            col = 56 if side == "q" else 57
            mk.w("v", mk.tok("v"))
            sq_tok = mk.do("v", lambda E, g_=g, s_=src: E.scalar_tensor_tensor(
                scbA[:, :], s_[:, g_, :], 1.0, s_[:, g_, :],
                ALU.mult, ALU.mult), SV)
            mk.w("t", sq_tok)
            tk = None
            for b in range(NB):
                tk = mk.do("t", lambda E, b_=b: E.matmul(
                    p4[0:4, b_ * 512:(b_ + 1) * 512], blkones4_s[:, :],
                    scbA[:, b_ * 512:(b_ + 1) * 512], start=True, stop=True), ST)
            mk.w("v", tk)
            mk.do("v", lambda E, c_=col: E.reduce_max(
                stat[0:4, c_:c_ + 1], p4[0:4, :], axis=AX), SV)
        mk.do("v", lambda E, g_=g: E.tensor_tensor(
            nb44[0:4, g_:g_ + 1], stat[0:4, 56:57], stat[0:4, 57:58], ALU.add), SV)
    mk.w("g", mk.tok("v"))
    gt = None
    for g in range(G):
        gt = mk.do("g", lambda E, g_=g: E.dma_start(
            nbrow[0:1, 4 * g_:4 * g_ + 4], nb44[0:4, g_:g_ + 1]),
            mk.dsem("NB4"), 16)
    mk.w("v", gt)
    mk.do("v", lambda E: E.reduce_max(stat[0:1, 58:59], nbrow[0:1, :], axis=AX), SV)
    mk.do("v", lambda E: E.tensor_scalar_mul(stat[0:1, 59:60], stat[0:1, 58:59],
                                             -0.5 * SCALE), SV)
    tkc = mk.do("v", lambda E: E.tensor_copy(nbrow2[0:1, 0:1], stat[0:1, 59:60]), SV)
    mk.w("t", tkc)
    tkm = mk.do("t", lambda E: E.matmul(pA[:, 0:1], ones1_s[0:1, 0:128],
                                        nbrow2[0:1, 0:1], start=True, stop=True), ST)
    mk.w("v", tkm)
    tk_nb = mk.do("v", lambda E: E.tensor_copy(nbias_pp[:, :], pA[:, 0:1]), SV)
    tA = tk_nb

    if _BUILD_OPTS["phases"] < 3:
        return _finish(mk, nc, out_d, outbuf_v)

    mk.w("s", tk_nb)
    prev_tkb = None
    exp_toks = {}
    sums_toks = {}
    PT_v4 = PT.rearrange("p (s x) -> p s x", s=4)
    att_banks = [pA, pD]
    tile_y = {-2: tA, -1: None}
    st3 = {"p": 0, "tk_rb": None, "tk_s4": None, "prev_tkb": None,
           "prev_gsm": None, "evac": None, "defer": None, "y_last": None}

    def _att_consume(pr):
        (pi, g_, ci_, hp_, off_, w_, sl_, n_, diag_, etok_, mtok_, bank_,
         first_, yt_, se_) = pr
        mk.w("t", mtok_ if diag_ else etok_)
        if first_:
            mk.w("t", yt_)
        for j in (2 * hp_, 2 * hp_ + 1):
            mk.do("t", lambda E, g2=g_, j_=j, jj_=j - 2 * hp_, c2=ci_, o2=off_,
                  w2=w_, s2=sl_, n2=n_, bk=bank_:
                  E.matmul(bk[32 * j_:32 * (j_ + 1), o2:o2 + w2],
                           vt_v[:, c2, (4 * g2 + j_) * 32:(4 * g2 + j_ + 1) * 32],
                           PT_v4[:, s2, jj_ * 512 + o2:jj_ * 512 + o2 + w2],
                           start=(c2 == 0), stop=(c2 == n2 - 1),
                           tile_position=(0, 32 * j_)), ST)
        if first_:
            mk.w("t", se_)
        pu = None
        for j in (2 * hp_, 2 * hp_ + 1):
            pu = mk.do("t", lambda E, j_=j, jj_=j - 2 * hp_, c2=ci_, o2=off_,
                       w2=w_, s2=sl_, n2=n_:
                       E.matmul(pB[32 * j_:32 * j_ + 1, o2:o2 + w2],
                                ones128_s[:, 0:1],
                                PT_v4[:, s2, jj_ * 512 + o2:jj_ * 512 + o2 + w2],
                                start=(c2 == 0), stop=(c2 == n2 - 1),
                                tile_position=(0, 32 * j_)), ST)
        sums_toks[pi] = pu

    pend = None
    ti = 0
    for g in range(G):
        for b in range(NB):
            nch = 4 * (b + 1)
            first_pidx = st3["p"]
            bank = att_banks[ti % 2]
            for ci in range(nch):
                off = (ci - 4 * b) * 128 if ci >= 4 * b else 0
                w = 512 - off
                diag = (ci >= 4 * b)
                for hp in range(2):
                    p_idx = st3["p"]
                    par2 = p_idx % 2
                    slot = p_idx % 4
                    mk.w("t", exp_toks.get(p_idx - 2))
                    stok = None
                    for j in (2 * hp, 2 * hp + 1):
                        stok = mk.do("t", lambda E, g_=g, j_=j, jj_=j - 2 * hp, ci_=ci,
                                     off_=off, w_=w, b_=b, pr_=par2:
                                     E.matmul(p4j[:, 2 * pr_ + jj_, off_:off_ + w_],
                                              KR_v[32 * j_:32 * (j_ + 1), g_,
                                                   ci_ * 128:(ci_ + 1) * 128],
                                              QK_v[32 * j_:32 * (j_ + 1), g_,
                                                   b_ * 512 + off_:b_ * 512 + off_ + w_],
                                              start=True, stop=True,
                                              tile_position=(32 * j_, 0)), ST)
                    if p_idx == first_pidx + 2 and st3["defer"] is not None:
                        st3["defer"]()
                        st3["defer"] = None
                    mk.w("s", stok)
                    mk.w("s", sums_toks.get(p_idx - 4))
                    pin = p4j[:, 2 * par2:2 * par2 + 2, off:off + w]
                    pout = PT_v4[:, slot, :].rearrange(
                        "p (j x) -> p j x", j=2)[:, :, off:off + w]
                    etok = mk.do("s", lambda E, pi=pin, po=pout: E.activation(
                        po, pi, ACT.Exp, bias=nbias_pp[:, 0:1], scale=SCALE), SS)
                    exp_toks[p_idx] = etok
                    mtok = None
                    if diag:
                        mk.w("v", etok)
                        pm = PT_v4[:, slot, :].rearrange(
                            "p (j x) -> p j x", j=2)[:, :, off:off + 128]
                        mtok = mk.do("v", lambda E, pm_=pm: E.tensor_tensor(
                            pm_, pm_, tri_s[:, None, :].broadcast_to([128, 2, 128]),
                            ALU.mult), SV)
                    if pend is not None:
                        _att_consume(pend)
                    pend = (p_idx, g, ci, hp, off, w, slot, nch, diag, etok, mtok,
                            bank, p_idx == first_pidx, tile_y.get(ti - 2),
                            st3["evac"])
                    st3["p"] = p_idx + 1
            _att_consume(pend)
            pend = None
            ptail = mk.tok("t")
            mk.w("v", ptail)
            mk.w("v", st3["prev_gsm"])
            ev = mk.do("v", lambda E: E.tensor_copy(
                scrB[0:97, 0:512], pB[0:97, :]), SV)
            st3["evac"] = ev
            mk.w("g", ev)
            mk.w("g", st3["tk_s4"])
            gsm = mk.do("g", lambda E: E.dma_start(
                sums4[:, :], bass.AP(scrB, 0, [[32 * T, 4], [1, 512]])),
                mk.dsem("SM4"), 16)
            st3["prev_gsm"] = gsm

            def _defer(g_=g, b_=b, ti_=ti, gsm_=gsm, bank_=bank):
                mk.w("s", gsm_)
                mk.w("s", st3["prev_tkb"])
                mk.do("s", lambda E: E.activation(rec4[:, :], sums4[:, :],
                                                  ACT.Ln), SS)
                tk_s4 = mk.do("s", lambda E: E.activation(
                    rec4b[:, :], rec4[:, :], ACT.Exp, scale=-1.0), SS)
                st3["tk_s4"] = tk_s4
                mk.w("t", tk_s4)
                mk.w("t", st3["tk_rb"])
                tkb = mk.do("t", lambda E: E.matmul(
                    pC[:, :], blk4T_s[:, :], rec4b[:, :],
                    start=True, stop=True), ST)
                st3["prev_tkb"] = tkb
                mk.w("v", tkb)
                st3["tk_rb"] = mk.do("v", lambda E: E.tensor_copy(
                    rbbf[:, :], pC[:, :]), SV)
                yt = mk.do("v", lambda E, g2=g_, b2=b_, bk=bank_: E.tensor_tensor(
                    ylat_v[:, g2, b2 * 512:(b2 + 1) * 512], bk[:, :], rbbf[:, :],
                    ALU.mult), SV)
                tile_y[ti_] = yt
                st3["y_last"] = yt
            st3["defer"] = _defer
            ti += 1
    if st3["defer"] is not None:
        st3["defer"]()
        st3["defer"] = None
    tk_y = st3["y_last"]

    if _BUILD_OPTS["phases"] < 4:
        return _finish(mk, nc, out_d, outbuf_v)

    mk.w("y", mk.tok("t"))
    yload("normg", normg_s[:, :], normg_in[:, :], sem="YN")
    yload("normb", normb_s[:, :], normb_in[:, :], sem="YN")
    yload("slotg", slotg_s[:, :], slotg_in[:, :], sem="YN")
    yload("slotb", slotb_s[:, :], slotb_in[:, :], sem="YN")
    yload("scl", scale_s[:, :], scale128_in[:, :], sem="YN")
    TOK_YN = mk.tok("YN")
    for _n in ("normg", "normb", "slotg", "slotb", "scl"):
        ldtoks[_n] = TOK_YN
    mk.w("t", tk_y)
    mk.w("v", ldtoks["x_tm"])
    ln_toks = [None, None]
    h1s = [scrA, scrB]
    h1_fin = [None, None]
    x2_done = []

    def _postln(i, par, psum_off, h1, x_src_v, x_dst_v, base, extra_tok=None):
        mk.w("v", extra_tok)
        atok = mk.do("v", lambda E: E.scalar_tensor_tensor(
            h1[:, 0:1024], p4[:, psum_off:psum_off + 1024], 1.0, x_src_v,
            ALU.mult, ALU.add, accum_out=stat[:, base:base + 1]), SV)
        mk.w("s", atok)
        sqt = mk.do("s", lambda E: E.activation(
            h1[:, 1024:2048], h1[:, 0:1024], ACT.Square,
            accum_out=stat[:, base + 1:base + 2]), SS)
        mk.do("v", lambda E: E.tensor_tensor(
            stat[:, base + 2:base + 3], stat[:, base:base + 1],
            stat[:, base:base + 1], ALU.mult), SV)
        mk.w("v", sqt)
        vtok = mk.do("v", lambda E: E.scalar_tensor_tensor(
            stat[:, base + 3:base + 4], stat[:, base + 2:base + 3], -1.0 / C,
            stat[:, base + 1:base + 2], ALU.mult, ALU.add), SV)
        mk.w("s", vtok)
        mk.do("s", lambda E: E.activation(stat[:, base + 4:base + 5],
                                          stat[:, base + 3:base + 4], ACT.Ln,
                                          bias=eps_s[:, 0:1], scale=1.0 / C), SS)
        stok = mk.do("s", lambda E: E.activation(stat[:, base + 5:base + 6],
                                                 stat[:, base + 4:base + 5],
                                                 ACT.Exp, scale=-0.5), SS)
        mk.do("v", lambda E: E.tensor_scalar_mul(
            stat[:, base + 6:base + 7], stat[:, base:base + 1], 1.0 / C), SV)
        mk.w("v", stok)
        fin = mk.do("v", lambda E: E.tensor_scalar(
            x_dst_v, h1[:, 0:1024], stat[:, base + 6:base + 7],
            stat[:, base + 5:base + 6], ALU.subtract, ALU.mult), SV)
        return atok, fin

    for i in range(NT):
        par = i % 2
        off0 = 0 if par == 0 else 1024
        mk.w("t", ln_toks[par])
        tk = None
        for g in range(G):
            for hb in range(2):
                tk = mk.do("t", lambda E, g_=g, hb_=hb, i_=i, o=off0: E.matmul(
                    p4[:, o + hb_ * 512:o + (hb_ + 1) * 512],
                    ylat_v[:, g_, i_ * 128:(i_ + 1) * 128],
                    wpeT_v[:, g_, hb_ * 512:(hb_ + 1) * 512],
                    start=(g_ == 0), stop=False), ST)
        for hb in range(2):
            tk = mk.do("t", lambda E, hb_=hb, o=off0: E.matmul(
                p4[:, o + hb_ * 512:o + (hb_ + 1) * 512],
                ones1_s[0:1, 0:128], bprj_b[0:1, hb_ * 512:(hb_ + 1) * 512],
                start=False, stop=True), ST)
        mk.w("v", tk)
        ln_toks[par], fin = _postln(i, par, off0, h1s[par], x_tm_v[:, i, :],
                                    x2_v[:, i, :], base=0 if par == 0 else 8,
                                    extra_tok=h1_fin[par])
        h1_fin[par] = fin
        x2_done.append(fin)
    mk.w("g", x2_done[-1])
    mk.w("g", x2_done[-2])
    spill_tok = mk.do("g", lambda E: E.dma_start(
        x2_spill.rearrange("(i p) c -> p i c", p=128), x2_v[:, :, :]),
        mk.dsem("SPL"), 16)

    if _BUILD_OPTS["phases"] < 5:
        return _finish(mk, nc, out_d, outbuf_v)

    tr_evac = [None] * 4
    xf_free = [None, None]
    xn_f32s = [scrA, scrB]
    for i in range(NT):
        par = i % 2
        xf = xn_f32s[par]
        base = 16 if par == 0 else 20
        mk.w("s", x2_done[i])
        sqt = mk.do("s", lambda E, i_=i, x_=xf, b_=base: E.activation(
            x_[:, 1024:2048], x2_v[:, i_, :], ACT.Square,
            accum_out=stat[:, b_:b_ + 1]), SS)
        mk.do("s", lambda E, b_=base: E.activation(
            stat[:, b_ + 2:b_ + 3], stat[:, b_:b_ + 1], ACT.Ln,
            bias=eps_s[:, 0:1], scale=1.0 / C), SS)
        stok = mk.do("s", lambda E, b_=base: E.activation(
            stat[:, b_ + 3:b_ + 4], stat[:, b_ + 2:b_ + 3], ACT.Exp, scale=-0.5), SS)
        mk.w("v", stok)
        mk.w("v", ldtoks["normg"])
        mk.w("v", ldtoks["normb"])
        mk.do("v", lambda E, i_=i, b_=base: E.scalar_tensor_tensor(
            xn_tm_v[:, i_, :], x2_v[:, i_, :], stat[:, b_ + 3:b_ + 4],
            normg_s[:, :], ALU.mult, ALU.mult), SV)
        ftok = mk.do("v", lambda E, i_=i: E.tensor_tensor(
            xn_tm_v[:, i_, :], xn_tm_v[:, i_, :], normb_s[:, :], ALU.add), SV)
        ttok = None
        for half in range(2):
            bank = (i * 2 + half) % 4
            mk.w("t", ftok)
            mk.w("t", tr_evac[bank])
            for q in range(4):
                dc = half * 4 + q
                ttok = mk.do("t", lambda E, i_=i, dc_=dc, bank_=bank, q_=q:
                             E.matmul(p4q[:, bank_ * 4 + q_, :],
                                      xn_tm_v[:, i_, dc_ * 128:(dc_ + 1) * 128],
                                      identb_s[:, :], start=True, stop=True), ST)
            mk.w("v", ttok)
            tr_evac[bank] = mk.do("v", lambda E, i_=i, h_=half, bank_=bank:
                                  E.tensor_copy(
                xn_fm_v[:, h_ * 4:(h_ + 1) * 4, i_ * 128:(i_ + 1) * 128],
                p4q[:, bank_ * 4:(bank_ + 1) * 4, :]), SV)
        xf_free[par] = sqt
    for ec in range(NCK):
        par = ec % 2
        sf = xn_f32s[par]
        base = 24 if par == 0 else 32
        mk.w("y", xf_free[par])
        mk.w("y", mk.tok("v"))
        mk.w("y", mk.tok("s"))
        lt = mk.do("y", lambda E, ec_=ec, s_=sf: E.dma_start(
            s_[:, 0:1024], semb_in[ec_ * 128:(ec_ + 1) * 128, :]),
            mk.dsem(f"SE{par}"), 16)
        mk.w("v", lt)
        mk.w("v", ldtoks["slotg"])
        mk.w("v", ldtoks["slotb"])
        mk.w("v", ldtoks["scl"])
        mk.do("v", lambda E, s_=sf, b_=base: E.reduce_sum(
            stat[:, b_:b_ + 1], s_[:, 0:1024], axis=AX), SV)
        mk.do("v", lambda E, s_=sf, b_=base: E.scalar_tensor_tensor(
            s_[:, 1024:2048], s_[:, 0:1024], 1.0, s_[:, 0:1024], ALU.mult, ALU.mult,
            accum_out=stat[:, b_ + 1:b_ + 2]), SV)
        mk.do("v", lambda E, b_=base: E.tensor_scalar_mul(
            stat[:, b_ + 2:b_ + 3], stat[:, b_:b_ + 1], 1.0 / C), SV)
        mk.do("v", lambda E, b_=base: E.tensor_scalar_mul(
            stat[:, b_ + 3:b_ + 4], stat[:, b_ + 1:b_ + 2], 1.0 / C), SV)
        mk.do("v", lambda E, b_=base: E.tensor_tensor(
            stat[:, b_ + 4:b_ + 5], stat[:, b_ + 2:b_ + 3],
            stat[:, b_ + 2:b_ + 3], ALU.mult), SV)
        v2 = mk.do("v", lambda E, b_=base: E.tensor_tensor(
            stat[:, b_ + 5:b_ + 6], stat[:, b_ + 3:b_ + 4],
            stat[:, b_ + 4:b_ + 5], ALU.subtract), SV)
        mk.w("s", v2)
        mk.do("s", lambda E, b_=base: E.activation(
            stat[:, b_ + 6:b_ + 7], stat[:, b_ + 5:b_ + 6], ACT.Ln,
            bias=eps_s[:, 0:1]), SS)
        stok = mk.do("s", lambda E, b_=base: E.activation(
            stat[:, b_ + 7:b_ + 8], stat[:, b_ + 6:b_ + 7], ACT.Exp, scale=-0.5), SS)
        mk.w("v", stok)
        mk.do("v", lambda E, s_=sf, b_=base: E.tensor_scalar(
            s_[:, 1024:2048], s_[:, 0:1024], stat[:, b_ + 2:b_ + 3],
            stat[:, b_ + 7:b_ + 8], ALU.subtract, ALU.mult), SV)
        mk.do("v", lambda E, s_=sf: E.tensor_tensor(
            s_[:, 1024:2048], s_[:, 1024:2048], slotg_s[:, :], ALU.mult), SV)
        mk.do("v", lambda E, s_=sf: E.tensor_tensor(
            s_[:, 1024:2048], s_[:, 1024:2048], slotb_s[:, :], ALU.add), SV)
        ftok = mk.do("v", lambda E, s_=sf: E.tensor_scalar_mul(
            s_[:, 1024:2048], s_[:, 1024:2048], scale_s[:, 0:1]), SV)
        ttok = None
        for half in range(2):
            bank = (ec * 2 + half) % 4
            mk.w("t", ftok)
            mk.w("t", tr_evac[bank])
            for q in range(4):
                dc = half * 4 + q
                ttok = mk.do("t", lambda E, s_=sf, dc_=dc, bank_=bank, q_=q:
                             E.transpose(p4q[:, bank_ * 4 + q_, :],
                                         s_[:, 1024 + dc_ * 128:1024 + (dc_ + 1) * 128],
                                         identf_s[:, :]), ST)
            mk.w("v", ttok)
            tr_evac[bank] = mk.do("v", lambda E, ec_=ec, h_=half, bank_=bank:
                                  E.tensor_copy(
                seT_v[:, h_ * 4:(h_ + 1) * 4, ec_ * 128:(ec_ + 1) * 128],
                p4q[:, bank_ * 4:(bank_ + 1) * 4, :]), SV)
        xf_free[par] = ttok

    if _BUILD_OPTS["phases"] < 6:
        return _finish(mk, nc, out_d, outbuf_v)

    for bank in range(4):
        mk.w("t", tr_evac[bank])
    lg_tok = [None, None]
    mk.w("v", spill_tok)
    mk.w("s", spill_tok)
    for i in range(NT):
        par = i % 2
        off0 = 0 if par == 0 else 1024
        mk.w("t", lg_tok[par])
        tk = None
        for hb in range(2):
            for dc in range(NCK):
                tk = mk.do("t", lambda E, i_=i, hb_=hb, dc_=dc, o=off0: E.matmul(
                    p4[:, o + hb_ * 512:o + (hb_ + 1) * 512],
                    xn_fm_v[:, dc_, i_ * 128:(i_ + 1) * 128],
                    seT_v[:, dc_, hb_ * 512:(hb_ + 1) * 512],
                    start=(dc_ == 0), stop=(dc_ == NCK - 1)), ST)
        mk.w("s", tk)
        cpt = mk.do("s", lambda E, i_=i, o=off0: E.activation(
            logits_v[:, i_, :], p4[:, o:o + 1024], ACT.Copy), SS)
        lg_tok[par] = cpt
        mk.w("v", cpt)
        mk.do("v", lambda E, i_=i: E.reduce_max(
            rowmax_tm[:, i_:i_ + 1], logits_v[:, i_, :], axis=AX), SV)

    if _BUILD_OPTS["phases"] < 7:
        return _finish(mk, nc, out_d, outbuf_v)

    nrm_tok = mk.do("v", lambda E: E.tensor_scalar_mul(
        nrm_s[:, :], rowmax_tm[:, :], -1.0), SV)
    tr_tok = None
    exp_tok3 = None
    for ec in range(NCK):
        mk.w("t", tr_tok)
        ttok = None
        for i in range(NT):
            ttok = mk.do("t", lambda E, i_=i, ec_=ec: E.transpose(
                p4q[:, i_, :], logits_v[:, i_, ec_ * 128:(ec_ + 1) * 128],
                identf_s[:, :]), ST)
        mk.w("v", ttok)
        mk.do("v", lambda E, ec_=ec: E.reduce_max(
            cmax_s[:, ec_:ec_ + 1], p4[:, :], axis=AX), SV)
        nm2 = mk.do("v", lambda E, ec_=ec: E.tensor_scalar_mul(
            ncmax_s[:, ec_:ec_ + 1], cmax_s[:, ec_:ec_ + 1], -1.0), SV)
        mk.w("s", nm2)
        tr_tok = mk.do("s", lambda E, ec_=ec: E.activation(
            dnT_v[:, ec_, :], p4[:, :], ACT.Exp, bias=ncmax_s[:, ec_:ec_ + 1],
            accum_out=cs2_s[:, ec_:ec_ + 1]), SS)
        exp_tok3 = tr_tok
    mk.do("s", lambda E: E.activation(rcl_s[:, :], cs2_s[:, :], ACT.Ln), SS)
    rck = mk.do("s", lambda E: E.activation(rcT_s[:, :], rcl_s[:, :], ACT.Exp,
                                            scale=-1.0), SS)
    mk.w("v", rck)
    nlast = None
    for ec in range(NCK):
        nlast = mk.do("v", lambda E, ec_=ec: E.tensor_scalar_mul(
            dnT_v[:, ec_, :], dnT_v[:, ec_, :], rcT_s[:, ec_:ec_ + 1]), SV)
    expc_toks = []
    for i in range(NT - 1, -1, -1):
        dst = lgb_v[:, NT + i, :] if i < 15 else scrBb[:, 0:1024]
        expc_toks.append(mk.do("s", lambda E, i_=i, d_=dst: E.activation(
            d_, logits_v[:, i_, :], ACT.Exp, bias=nrm_s[:, i_:i_ + 1],
            accum_out=rsum_tm[:, i_:i_ + 1]), SS))
    expc_tok = expc_toks[-1]
    mk.do("s", lambda E: E.activation(rsL_s[:, :], rsum_tm[:, :], ACT.Ln), SS)
    rct = mk.do("s", lambda E: E.activation(recip_tm[:, :], rsL_s[:, :], ACT.Exp,
                                            scale=-1.0), SS)
    dp_tok = [None, None]
    mk.w("t", nlast)
    mk.w("t", exp_tok3)
    for i in range(NT):
        par = i % 2
        o = par * 1024
        mk.w("t", dp_tok[par])
        tk = None
        for ec in range(NCK):
            tk = mk.do("t", lambda E, i_=i, ec_=ec, o_=o: E.matmul(
                p4[:, o_ + ec_ * 128:o_ + (ec_ + 1) * 128],
                dnT_v[:, ec_, i_ * 128:(i_ + 1) * 128], identb_s[:, :],
                start=True, stop=True), ST)
        mk.w("v", tk)
        if i == 0:
            mk.w("v", expc_tok)
        dp_tok[par] = mk.do("v", lambda E, i_=i, o_=o: E.tensor_copy(
            lgb_v[:, i_, :], p4[:, o_:o_ + 1024]), SV)
    sl_tok = [None, None]
    sl_dma = [None, None]
    mk.w("t", dp_tok[0])
    mk.w("t", dp_tok[1])
    for dc in range(NCK):
        par = dc % 2
        off0 = 0 if par == 0 else 1024
        mk.w("t", sl_tok[par])
        tk = None
        for hb in range(2):
            for i in range(NT):
                tk = mk.do("t", lambda E, dc_=dc, hb_=hb, i_=i, o=off0: E.matmul(
                    p4[:, o + hb_ * 512:o + (hb_ + 1) * 512],
                    xn_tm_v[:, i_, dc_ * 128:(dc_ + 1) * 128],
                    lgb_v[:, i_, hb_ * 512:(hb_ + 1) * 512],
                    start=(i_ == 0), stop=(i_ == NT - 1)), ST)
        mk.w("v", tk)
        mk.w("v", sl_dma[par])
        sl_tok[par] = mk.do("v", lambda E, dc_=dc, o=off0, par_=par: E.tensor_copy(
            scbA[:, par_ * 1024:(par_ + 1) * 1024], p4[:, o:o + 1024]), SV)
        mk.w("g", sl_tok[par])
        sl_dma[par] = mk.do("g", lambda E, dc_=dc, par_=par: E.dma_start(
            ag1_in.rearrange("(c p) e -> p c e", p=128)[:, dc_, :],
            scbA[:, par_ * 1024:(par_ + 1) * 1024]), mk.dsem(f"SL{par}"), 16)
    sl_ev = sl_dma[(NCK - 1) % 2]
    mk.w("v", sl_dma[0])
    mk.w("v", sl_dma[1])
    mk.w("t", sl_tok[0])
    mk.w("t", sl_tok[1])
    tr2_tok = None
    for ec in range(NCK):
        mk.w("t", tr2_tok)
        ttok = None
        for i in range(NT):
            lhs = (lgb_v[:, NT + i, :] if i < 15 else scrBb[:, 0:1024])
            ttok = mk.do("t", lambda E, i_=i, ec_=ec, l_=lhs: E.matmul(
                p4q[:, i_, :], l_[:, ec_ * 128:(ec_ + 1) * 128],
                identb_s[:, :], start=True, stop=True), ST)
        mk.w("v", ttok)
        tr2_tok = mk.do("v", lambda E, ec_=ec: E.tensor_copy(
            combineT_v[:, ec_, :], p4[:, :]), SV)

    mk.w("y", mk.tok("t"))
    mk.w("y", mk.tok("v"))
    mk.w("y", mk.tok("s"))
    brow_tok = None
    for nm, bin2, row, wdt in (("bg2", bg2_in, bg_row, FF), ("bu2", bu2_in, bu_row, FF),
                               ("bd2", bd2_in, bd_row, C)):
        mk.w("y", brow_tok)
        lt = mk.do("y", lambda E, b=bin2, w_=wdt: E.dma_start(
            scrB[0:2, 0:w_], b[:, :]), mk.dsem("YG"), 16)
        mk.w("v", lt)
        mk.w("v", brow_tok)
        ctk = mk.do("v", lambda E, w_=wdt: E.tensor_copy(
            scbB[0:2, 0:w_], scrB[0:2, 0:w_]), SV)
        mk.w("g", ctk)
        brow_tok = mk.do("g", lambda E, r=row, w_=wdt: E.dma_start(
            r[0:1, 0:2 * w_], scbB[0:2, 0:w_]), mk.dsem("BRS"), 16)
    mk.w("v", brow_tok)

    if _BUILD_OPTS["phases"] < 8:
        return _finish(mk, nc, out_d, outbuf_v)

    mk.w("g", sl_ev)
    mk.w("g", mk.tok("t"))
    mk.w("g", mk.tok("v"))
    mk.w("g", mk.tok("s"))
    wbufs = {0: (wgA_v, wuA_v), 1: (wgB_v, wuB_v)}
    wld = {}
    wfree = {0: None, 1: None}

    def load_wh(e, Hh, par, eng="g"):
        gv, uv = wbufs[par]
        mk.w(eng, wfree[par])
        mk.do(eng, lambda E, e_=e, H_=Hh, g_=gv: E.dma_start(
            g_[:, :, :],
            wg2_in[e_].rearrange("(c p) f -> p c f", p=128)[:, :, H_ * 1024:(H_ + 1) * 1024]),
            mk.dsem(f"WH{par}"), 16)
        wld[(e, Hh)] = mk.do(eng, lambda E, e_=e, H_=Hh, u_=uv: E.dma_start(
            u_[:, :, :],
            wu2_in[e_].rearrange("(c p) f -> p c f", p=128)[:, :, H_ * 1024:(H_ + 1) * 1024]),
            mk.dsem(f"WH{par}"), 16)

    def load_wd(e, oh, eng="g"):
        return mk.do(eng, lambda E, e_=e, o_=oh: E.dma_start(
            wdS_v[:, :, :],
            wd2_in[e_].rearrange("(c p) o -> p c o", p=128)[:, :, o_ * 512:(o_ + 1) * 512]),
            mk.dsem("WD"), 16)

    agt = mk.do("g", lambda E: E.collective_compute(
        "AllGather", ALU.bypass, replica_groups=[list(range(NCORES))],
        ins=[ag1_in.ap().opt()], outs=[ag1_out.ap().opt()]), mk.sem["cc"], 1)
    mk.w("g", brow_tok)
    load_wh(0, 0, 0)
    wd_tok = load_wd(0, 0)
    load_wh(0, 1, 1)
    mk.w("g", agt)

    regcell = {}

    def _dynoff(E, shift, delta):
        if "pid" not in regcell:
            regcell["pid"] = E.to_reg(E.partition_id())
            regcell["off"] = E.alloc_register("dynoff")
        if shift is not None:
            E.reg_alu(regcell["off"], regcell["pid"], shift, ALU.arith_shift_left)
        if delta:
            E.reg_add(regcell["off"], regcell["off"], delta)
        return regcell["off"]

    XeT_x = XeT.rearrange("p (dc s) -> p dc s", dc=8)
    xstg_v = xstg.rearrange("p (u dc s) -> p u dc s", u=2, dc=8)

    def _xet_dyn(E, b_, par):
        first = (b_ == 0)
        off = _dynoff(E, 7 if first else None, 0 if first else (1 << 20))
        src = bass.AP(ag1_out, off, [[EL, 128], [128 * EL, 8], [1, 128]])
        return E.dma_start(xstg_v[:, par, :, :], src)

    xet_tok = None
    xcp_tok = [None, None]
    for b_ in range(NCORES):
        par = b_ % 2
        mk.w("g", xcp_tok[par])
        ld = mk.do("g", lambda E, b__=b_, p_=par: _xet_dyn(E, b__, p_),
                   mk.dsem(f"XS{par}"), 16)
        mk.w("g", ld)
        for el_ in range(2):
            xcp_tok[par] = mk.do("g", lambda E, b__=b_, el__=el_, p_=par: E.dma_start(
                XeT_x[:, :, el__ * 512 + b__ * 64:el__ * 512 + b__ * 64 + 64],
                xstg_v[:, p_, :, el__ * 64:el__ * 64 + 64]),
                mk.dsem(f"XC{par}"), 16)
        xet_tok = xcp_tok[par]
    xet_tok2 = xcp_tok[1 - (NCORES - 1) % 2]

    mk.w("t", xet_tok)
    mk.w("t", xet_tok2)
    mk.w("t", brow_tok)
    mk.w("s", brow_tok)
    eo_dma = [None, None]
    gelu_free = None
    up_free = None
    gate_free = None
    h_cons = None
    for e in range(2):
        for Hh in range(2):
            par = Hh
            gv, uv = wbufs[par]
            mk.w("t", wld[(e, Hh)])
            for fs in range(8):
                fc = Hh * 8 + fs
                if fc == 0:
                    mk.w("t", h_cons)
                mk.w("t", gate_free)
                tkg = None
                for dc in range(NCK):
                    tkg = mk.do("t", lambda E, g_=gv, dc_=dc, fs_=fs, e_=e: E.matmul(
                        pA[:, :], g_[:, dc_, fs_ * 128:(fs_ + 1) * 128],
                        XeT_v[:, dc_, e_ * 512:(e_ + 1) * 512],
                        start=(dc_ == 0), stop=False), ST)
                tkg = mk.do("t", lambda E, e_=e, fc_=fc: E.matmul(
                    pA[:, :], bg_row[0:1, e_ * FF + fc_ * 128:e_ * FF + (fc_ + 1) * 128],
                    ones1_s[0:1, 0:512], start=False, stop=True), ST)
                mk.w("t", up_free)
                tku = None
                for dc in range(NCK):
                    tku = mk.do("t", lambda E, u_=uv, dc_=dc, fs_=fs, e_=e: E.matmul(
                        pB[:, :], u_[:, dc_, fs_ * 128:(fs_ + 1) * 128],
                        XeT_v[:, dc_, e_ * 512:(e_ + 1) * 512],
                        start=(dc_ == 0), stop=False), ST)
                tku = mk.do("t", lambda E, e_=e, fc_=fc: E.matmul(
                    pB[:, :], bu_row[0:1, e_ * FF + fc_ * 128:e_ * FF + (fc_ + 1) * 128],
                    ones65_s[32:33, 0:512], start=False, stop=True), ST)
                mk.w("s", tkg)
                mk.w("s", gelu_free)
                gt = mk.do("s", lambda E: E.activation(scbA[:, 0:512], pA[:, :],
                                                       ACT.Gelu), SS)
                gate_free = gt
                mk.w("v", tku)
                ut = mk.do("v", lambda E: E.tensor_copy(scbB[:, 0:512], pB[:, :]), SV)
                up_free = ut
                mk.w("v", gt)
                htok = mk.do("v", lambda E, fc_=fc: E.tensor_tensor(
                    hfm_v[:, fc_, :], scbA[:, 0:512], scbB[:, 0:512], ALU.mult), SV)
                gelu_free = htok
            if e == 0:
                wfree[par] = mk.tok("t")
                load_wh(1, Hh, par)
        mk.w("t", mk.tok("v"))
        for oh in range(2):
            mk.w("t", wd_tok)
            eo_par = [None, None]
            for bsl in range(4):
                par = bsl % 2
                mk.w("t", eo_par[par])
                tk = None
                for fc in range(16):
                    tk = mk.do("t", lambda E, fc_=fc, bsl_=bsl, par_=par: E.matmul(
                        p4[:, par_ * 1024:par_ * 1024 + 512],
                        hfm_v[:, fc_, bsl_ * 128:(bsl_ + 1) * 128],
                        wdS_v[:, fc_, :],
                        start=(fc_ == 0), stop=(fc_ == 15)), ST)
                tk = mk.do("t", lambda E, e_=e, oh_=oh, par_=par: E.matmul(
                    p4[:, par_ * 1024:par_ * 1024 + 512], ones65_s[64:65, 0:128],
                    bd_row[0:1, e_ * 1024 + oh_ * 512:e_ * 1024 + (oh_ + 1) * 512],
                    start=False, stop=True), ST)
                mk.w("v", tk)
                mk.w("v", eo_dma[par])
                eo_par[par] = mk.do("v", lambda E, par_=par: E.tensor_copy(
                    scbA[:, 1024 + par_ * 512:1024 + (par_ + 1) * 512],
                    p4[:, par_ * 1024:par_ * 1024 + 512]), SV)
                mk.w("g", eo_par[par])
                eo_dma[par] = mk.do("g", lambda E, e_=e, bsl_=bsl, oh_=oh, par_=par:
                                    E.dma_start(
                    ag2_in.rearrange("(c p) o -> p c o", p=128)[
                        :, e_ * 4 + bsl_, oh_ * 512:(oh_ + 1) * 512],
                    scbA[:, 1024 + par_ * 512:1024 + (par_ + 1) * 512]),
                    mk.dsem(f"EO{par}"), 16)
            mk.w("g", mk.tok("t"))
            if oh == 0:
                wd_tok = load_wd(e, 1)
            elif e == 0:
                wd_tok = load_wd(1, 0)
        h_cons = mk.tok("t")
    mk.w("g", eo_dma[0])
    mk.w("g", eo_dma[1])
    mk.w("g", mk.tok("t"))
    x2r_tok = mk.do("g", lambda E: E.dma_start(
        x2b_v[:, :, :], x2_spill.rearrange("(i p) c -> p i c", p=128)),
        mk.dsem("REL"), 16)
    agt2 = mk.do("g", lambda E: E.collective_compute(
        "AllGather", ALU.bypass, replica_groups=[list(range(NCORES))],
        ins=[ag2_in.ap().opt()], outs=[ag2_out.ap().opt()]), mk.sem["cc"], 1)
    mk.w("g", agt2)
    mk.w("g", mk.tok("t"))

    def _ef_dma(E, c_):
        first = (c_ == 0)
        off = _dynoff(E, 16 if first else None, 0 if first else (1 << 20))
        src = bass.AP(ag2_out, off, [[512 * C, 2], [C, 64], [1, C]])
        return E.dma_start(eout_f_v[:, c_, :], src)

    ef_tok = None
    for c_ in range(NCORES):
        ef_tok = mk.do("g", lambda E, c__=c_: _ef_dma(E, c__),
                       mk.dsem("EFG"), 16)

    if _BUILD_OPTS["phases"] < 10:
        return _finish(mk, nc, out_d, outbuf_v)

    mk.w("t", ef_tok)
    mk.w("t", x2r_tok)
    of_tok = [None, None]
    of_fin = [None, None]
    for i in range(NT):
        par = i % 2
        off0 = 0 if par == 0 else 1024
        base = 40 if par == 0 else 48
        mk.w("t", of_tok[par])
        tk = None
        for hb in range(2):
            for ec in range(NCK):
                tk = mk.do("t", lambda E, i_=i, hb_=hb, ec_=ec, o=off0: E.matmul(
                    p4[:, o + hb_ * 512:o + (hb_ + 1) * 512],
                    combineT_v[:, ec_, i_ * 128:(i_ + 1) * 128],
                    eout_f_v[:, ec_, hb_ * 512:(hb_ + 1) * 512],
                    start=(ec_ == 0), stop=(ec_ == NCK - 1)), ST)
        mk.w("v", tk)
        mk.w("v", x2r_tok)
        mk.w("v", rct)
        h1 = scrA if par == 0 else scrB
        mk.w("v", of_fin[par])
        atok = mk.do("v", lambda E, h=h1, i_=i, o=off0, b_=base:
                     E.scalar_tensor_tensor(
                         h[:, 0:1024], p4[:, o:o + 1024], recip_tm[:, i_:i_ + 1],
                         x2b_v[:, i_, :], ALU.mult, ALU.add,
                         accum_out=stat[:, b_:b_ + 1]), SV)
        mk.w("s", atok)
        sqt = mk.do("s", lambda E, h=h1, b_=base: E.activation(
            h[:, 1024:2048], h[:, 0:1024], ACT.Square,
            accum_out=stat[:, b_ + 1:b_ + 2]), SS)
        mk.do("v", lambda E, b_=base: E.tensor_tensor(
            stat[:, b_ + 2:b_ + 3], stat[:, b_:b_ + 1],
            stat[:, b_:b_ + 1], ALU.mult), SV)
        mk.w("v", sqt)
        vtok = mk.do("v", lambda E, b_=base: E.scalar_tensor_tensor(
            stat[:, b_ + 3:b_ + 4], stat[:, b_ + 2:b_ + 3], -1.0 / C,
            stat[:, b_ + 1:b_ + 2], ALU.mult, ALU.add), SV)
        mk.w("s", vtok)
        mk.do("s", lambda E, b_=base: E.activation(
            stat[:, b_ + 4:b_ + 5], stat[:, b_ + 3:b_ + 4], ACT.Ln,
            bias=eps_s[:, 0:1], scale=1.0 / C), SS)
        stok = mk.do("s", lambda E, b_=base: E.activation(
            stat[:, b_ + 5:b_ + 6], stat[:, b_ + 4:b_ + 5], ACT.Exp,
            scale=-0.5), SS)
        mk.do("v", lambda E, b_=base: E.tensor_scalar_mul(
            stat[:, b_ + 6:b_ + 7], stat[:, b_:b_ + 1], 1.0 / C), SV)
        mk.w("v", stok)
        of_fin[par] = mk.do("v", lambda E, h=h1, i_=i, b_=base: E.tensor_scalar(
            outbuf_v[:, i_, :], h[:, 0:1024], stat[:, b_ + 6:b_ + 7],
            stat[:, b_ + 5:b_ + 6], ALU.subtract, ALU.mult), SV)
        of_tok[par] = atok

    return _finish(mk, nc, out_d, outbuf_v)


def _finish(mk, nc, out_d, outbuf_v):
    mk.w("g", mk.tok("v"))
    mk.w("g", mk.tok("s"))
    st = mk.do("g", lambda E: E.dma_start(
        out_d.rearrange("(i p) c -> p i c", p=128), outbuf_v[:, :, :]),
        mk.dsem("OUT"), 16)
    for name in _BUILD_OPTS["taps"]:
        h = mk.sb[name]
        shp = list(h.shape)
        o = nc.dram_tensor("tap_" + name, shp, h.ap().dtype, kind="ExternalOutput")
        st = mk.do("g", lambda E, o_=o, h_=h: E.dma_start(o_[:, :], h_[:, :]),
                   mk.dsem("OUT"), 16)
    mk.w("g", st)
    nc._mk_specs = list(mk.specs)
    with nc.Block() as block:
        block.tensor(lambda E: [f(E) for f in mk.ops["t"]])
        block.vector(lambda E: [f(E) for f in mk.ops["v"]])
        block.scalar(lambda E: [f(E) for f in mk.ops["s"]])
        block.gpsimd(lambda E: [f(E) for f in mk.ops["g"]])
        block.sync(lambda E: [f(E) for f in mk.ops["y"]])
    return nc


def _solve_plan(specs, cap):
    plan = {}
    placed = []
    for name, size, p0, p1 in sorted(specs, key=lambda x: -x[1]):
        taken = sorted((plan[n], plan[n] + s) for (n, s, a0, a1) in placed
                       if not (a1 < p0 or a0 > p1))
        off = 0
        for (a, bnd) in taken:
            if off + size <= a:
                break
            off = max(off, bnd)
        assert off + size <= cap, f"SBUF plan overflow: {name} {size}B at {off}"
        plan[name] = off
        placed.append((name, size, p0, p1))
    return plan


_PLAN_CACHE = {}


def build_planned():
    key = _BUILD_OPTS["phases"]
    if key not in _PLAN_CACHE:
        nc0 = build(None)
        _PLAN_CACHE[key] = _solve_plan(nc0._mk_specs, 206336)
    return build(_PLAN_CACHE[key])


def _blkd(p2lT):
    z = np.zeros((128, 64), np.float32)
    z[0:64, 0:32] = p2lT
    z[64:128, 32:64] = p2lT
    return z


def _stage_inputs(inputs, core):
    f32 = np.float32
    x = np.asarray(inputs["L0_attn_x"][core], dtype=f32)
    caw = np.asarray(inputs["L0_attn_c_attn_w"], dtype=f32)
    cab = np.asarray(inputs["L0_attn_c_attn_b"], dtype=f32)
    q2l = np.asarray(inputs["L0_attn_q2l_w"], dtype=f32)
    q2lb = np.asarray(inputs["L0_attn_q2l_b"], dtype=f32)
    kv2l = np.asarray(inputs["L0_attn_kv2l_w"], dtype=f32)
    kv2lb = np.asarray(inputs["L0_attn_kv2l_b"], dtype=f32)
    l2o = np.asarray(inputs["L0_attn_l2o_w"], dtype=f32)
    l2ob = np.asarray(inputs["L0_attn_l2o_b"], dtype=f32)
    wqk = np.asarray(inputs["L0_attn_wqk_w"], dtype=f32)
    wqkb = np.asarray(inputs["L0_attn_wqk_b"], dtype=f32)
    cpw = np.asarray(inputs["L0_attn_cproj_w"], dtype=f32)
    cpb = np.asarray(inputs["L0_attn_cproj_b"], dtype=f32)
    d = dict(_CONSTS)
    d.update({
        "x_tm_in": np.ascontiguousarray(x),
        "x_fm_in": np.ascontiguousarray(x.T),
        "caw": caw,
        "cprojT_in": np.ascontiguousarray(cpw.T),
        "q2lT_in": np.ascontiguousarray(q2l.T),
        "kv2lT_in": np.ascontiguousarray(kv2l.T),
        "l2o_in": l2o,
        "wqkT_in": np.ascontiguousarray(wqk.T),
        "blkdQ_in": _blkd(q2l.T),
        "blkdKV_in": _blkd(kv2l.T),
        "wqkT4_in": np.tile(np.ascontiguousarray(wqk.T), (4, 1)),
        "l2o2_in": np.tile(l2o, (2, 1)),
        "bqr_in": np.ascontiguousarray(cab[0:C].reshape(H, HD).T),
        "bkr_in": np.ascontiguousarray(cab[C:2 * C].reshape(H, HD).T),
        "bvr_in": np.ascontiguousarray(cab[2 * C:3 * C].reshape(H, HD).T),
        "q2lb16_in": np.tile(q2lb[None, :], (H, 1)),
        "kvlb16_in": np.tile(kv2lb[None, :], (H, 1)),
        "wqkb512_in": np.tile(wqkb, H)[None, :].copy(),
        "l2ob_in": np.ascontiguousarray(np.tile(l2ob, H).reshape(8, 128).T),
        "cprojb_in": cpb[None, :].copy(),
        "normg128": np.tile(np.asarray(inputs["L0_ffn_norm_g"], f32)[None, :], (128, 1)),
        "normb128": np.tile(np.asarray(inputs["L0_ffn_norm_b"], f32)[None, :], (128, 1)),
        "slotg128": np.tile(np.asarray(inputs["L0_ffn_slot_g"], f32)[None, :], (128, 1)),
        "slotb128": np.tile(np.asarray(inputs["L0_ffn_slot_b"], f32)[None, :], (128, 1)),
        "scale128": np.full((128, 1), np.asarray(inputs["L0_ffn_scale"], f32)[0], f32),
        "semb": np.asarray(inputs["L0_ffn_slot_embeds"], f32).reshape(EL, C),
        "wg2": np.asarray(inputs["L0_ffn_Wg"], f32)[2 * core:2 * core + 2],
        "wu2": np.asarray(inputs["L0_ffn_Wu"], f32)[2 * core:2 * core + 2],
        "wd2": np.asarray(inputs["L0_ffn_Wd"], f32)[2 * core:2 * core + 2],
        "bg2": np.asarray(inputs["L0_ffn_bg"], f32)[2 * core:2 * core + 2],
        "bu2": np.asarray(inputs["L0_ffn_bu"], f32)[2 * core:2 * core + 2],
        "bd2": np.asarray(inputs["L0_ffn_bd"], f32)[2 * core:2 * core + 2],
    })
    return d


def kernel(**inputs):
    nc = build_planned()
    in_maps = [_stage_inputs(inputs, c) for c in range(NCORES)]
    res = run_bass_kernel_spmd(nc, in_maps, core_ids=list(range(NCORES)))
    out = np.stack([np.asarray(res.results[c]["out"]) for c in range(NCORES)], axis=0)
    if _BUILD_OPTS["taps"]:
        kernel._taps = [{k: np.asarray(v) for k, v in r.items()} for r in res.results]
    return out


# version 09
# speedup vs baseline: 1.0133x; 1.0133x over previous
import math
import numpy as np
import ml_dtypes

import concourse.bass as bass
import concourse.mybir as mybir
from concourse.bass_utils import run_bass_kernel_spmd

F32 = mybir.dt.float32
BF16 = mybir.dt.bfloat16
AX = mybir.AxisListType.X
ALU = mybir.AluOpType
ACT = mybir.ActivationFunctionType

T = 2048
C = 1024
H = 16
HD = 64
LAT = 32
G = 4
NT = 16
NB = 4
NCK = 8
E = 16
S = 64
EL = 1024
FF = 2048
NCORES = 8
SCALE = 1.0 / math.sqrt(LAT)
EPS = 1e-5

_BUILD_OPTS = {"taps": (), "phases": 99}


class _Sem:
    def __init__(self, nc, name):
        self.h = nc.alloc_semaphore(name)
        self.n = 0


class MK:
    def __init__(self, nc):
        self.nc = nc
        self.ops = {e: [] for e in "tvsgy"}
        self.sem = {k: _Sem(nc, "sem_" + k) for k in
                    ("t", "v", "s", "g", "gd", "yd", "cc")}
        self.seen = {e: {} for e in "tvsgy"}
        self.sb = {}
        self.regions = []
        self.specs = []
        self.aliases = {}
        self.plan = None
        self.arena_base = None
        self.arena_cap = None

    def on(self, e, fn):
        self.ops[e].append(fn)

    def w(self, e, tok):
        if tok is None:
            return
        sem, n = tok
        if self.seen[e].get(id(sem), -1) >= n:
            return
        self.seen[e][id(sem)] = n
        self.on(e, lambda E, s=sem, v=n: E.wait_ge(s.h, v))

    def do(self, e, fn, sem=None, inc=1):
        if e in ("v", "s") and sem is not None:
            self.w(e, (sem, sem.n))
        if sem is None:
            self.on(e, fn)
            return None
        sem.n += inc
        n = sem.n
        self.on(e, lambda E, f=fn, s=sem, i=inc: f(E).then_inc(s.h, i))
        return (sem, n)

    def tok(self, k):
        s = self.sem[k]
        return (s, s.n)

    def dsem(self, name):
        if name not in self.sem:
            self.sem[name] = _Sem(self.nc, "sem_" + name)
        return self.sem[name]

    def salloc(self, name, shape, dtype, p0, p1, alias=None):
        size = shape[1] * (2 if dtype == BF16 else 4)
        size = (size + 63) & ~63
        if alias is None:
            self.specs.append((name, size, p0, p1))
        else:
            self.aliases[name] = alias
        if self.plan is not None:
            off = self.plan[name] if alias is None else self.plan[alias]
        else:
            off = 0
        self.regions.append((off, size, p0, p1))
        h = self.nc.alloc_sbuf_tensor_at(name, list(shape), dtype,
                                         offset=self.arena_base + off)
        self.sb[name] = h
        return h


def _rope_tables():
    inv = 1.0 / (10000.0 ** (np.arange(0, LAT, 2, dtype=np.float64) / LAT))
    t = np.arange(T, dtype=np.float64)
    fr = np.outer(t, inv)
    cos = np.repeat(np.cos(fr), 2, axis=-1)[:, :LAT]
    sin = np.repeat(np.sin(fr), 2, axis=-1)[:, :LAT]
    return (np.tile(cos.T, (G, 1)).astype(np.float32),
            np.tile(sin.T, (G, 1)).astype(np.float32))


def _const_inputs():
    bf = ml_dtypes.bfloat16
    cosT, sinT = _rope_tables()
    prot = np.zeros((128, 128), dtype=np.float32)
    for blk in range(G):
        for i in range(LAT // 2):
            b = blk * 32
            prot[b + 2 * i, b + 2 * i + 1] = -1.0
            prot[b + 2 * i + 1, b + 2 * i] = 1.0
    tri = (np.arange(128)[None, :] >= np.arange(128)[:, None]).astype(np.float32)
    blk4T = np.zeros((4, 128), dtype=np.float32)
    for j in range(4):
        blk4T[j, 32 * j:32 * (j + 1)] = 1.0
    return {
        "cosT": cosT, "sinT": sinT,
        "protT": prot.T.astype(bf).copy(),
        "trimask": tri.astype(bf),
        "identf": np.eye(128, dtype=np.float32),
        "identb": np.eye(128, dtype=np.float32).astype(ml_dtypes.bfloat16),
        "ones1": np.ones((1, 512), dtype=bf),
        "ones128": np.ones((128, 1), dtype=bf),
        "blk4T": blk4T.astype(bf),
        "ones65": np.ones((65, 512), dtype=bf),
        "blkones4": blk4T.T.astype(bf).copy(),
        "eps128": np.full((128, 1), EPS, dtype=np.float32),
    }


_CONSTS = _const_inputs()


def build(plan=None):
    nc = bass.Bass("TRN2", target_bir_lowering=False)
    DI = lambda n, s, dt=F32: nc.dram_tensor(n, list(s), dt, kind="ExternalInput")

    x_tm_in = DI("x_tm_in", [T, C])
    x_fm_in = DI("x_fm_in", [C, T])
    caw_in = DI("caw", [3 * C, C])
    cprojT_in = DI("cprojT_in", [C, C])
    q2lT_in = DI("q2lT_in", [HD, LAT])
    kv2lT_in = DI("kv2lT_in", [HD, LAT])
    l2o_in = DI("l2o_in", [HD, LAT])
    wqkT_in = DI("wqkT_in", [LAT, LAT])
    bqr_in = DI("bqr_in", [HD, H])
    bkr_in = DI("bkr_in", [HD, H])
    bvr_in = DI("bvr_in", [HD, H])
    q2lb16_in = DI("q2lb16_in", [H, LAT])
    kvlb16_in = DI("kvlb16_in", [H, LAT])
    wqkb512_in = DI("wqkb512_in", [1, 512])
    l2ob_in = DI("l2ob_in", [128, 8])
    cprojb_in = DI("cprojb_in", [1, C])
    cosT_in = DI("cosT", [128, T])
    sinT_in = DI("sinT", [128, T])
    protT_in = DI("protT", [128, 128], BF16)
    tri_in = DI("trimask", [128, 128], BF16)
    identf_in = DI("identf", [128, 128], F32)
    identb_in = DI("identb", [128, 128], BF16)
    ones1_in = DI("ones1", [1, 512], BF16)
    ones128_in = DI("ones128", [128, 1], BF16)
    ones65_in = DI("ones65", [65, 512], BF16)
    blk4T_in = DI("blk4T", [4, 128], BF16)
    blkones4_in = DI("blkones4", [128, 4], BF16)
    blkdQ_in = DI("blkdQ_in", [128, 64])
    blkdKV_in = DI("blkdKV_in", [128, 64])
    wqkT4_in = DI("wqkT4_in", [128, LAT])
    l2o2_in = DI("l2o2_in", [128, LAT])
    eps128_in = DI("eps128", [128, 1], F32)
    normg_in = DI("normg128", [128, C])
    normb_in = DI("normb128", [128, C])
    slotg_in = DI("slotg128", [128, C])
    slotb_in = DI("slotb128", [128, C])
    scale128_in = DI("scale128", [128, 1])
    semb_in = DI("semb", [EL, C])
    wg2_in = DI("wg2", [2, C, FF])
    wu2_in = DI("wu2", [2, C, FF])
    wd2_in = DI("wd2", [2, FF, C])
    bg2_in = DI("bg2", [2, FF])
    bu2_in = DI("bu2", [2, FF])
    bd2_in = DI("bd2", [2, C])

    out_d = nc.dram_tensor("out", [T, C], F32, kind="ExternalOutput")

    ag1_in = nc.dram_tensor("ag1_in", [C, EL], BF16)
    ag1_out = nc.dram_tensor("ag1_out", [NCORES * C, EL], BF16, addr_space="Shared")
    ag2_in = nc.dram_tensor("ag2_in", [EL, C], BF16)
    ag2_out = nc.dram_tensor("ag2_out", [NCORES * EL, C], BF16, addr_space="Shared")
    x2_spill = nc.dram_tensor("x2_spill", [T, C], BF16)

    ARENA = 206336
    arena = nc.alloc_sbuf_tensor("arena", [128, ARENA], mybir.dt.uint8)
    mk = MK(nc)
    mk.plan = plan
    mk.arena_base = nc.lookup_mloc(arena).addr
    mk.arena_cap = ARENA

    A = mk.salloc
    scrA = A("scrA", [128, T], F32, 0, 9)
    scrB = A("scrB", [128, T], F32, 0, 9)
    scbA = A("scbA", [128, T], BF16, 0, 9)
    dbgA = A("dbgA", [128, 1024], F32, 0, 2)
    scbB = A("scbB", [128, T], BF16, 0, 9)
    stat = A("stat", [128, 64], F32, 0, 9)
    identf_s = A("identf_s", [128, 128], F32, 0, 9)
    identb_s = A("identb_s", [128, 128], BF16, 0, 9)
    ones1_s = A("ones1_s", [1, 512], BF16, 0, 9)
    ones128_s = A("ones128_s", [128, 1], BF16, 0, 9)
    ones65_s = A("ones65_s", [65, 512], BF16, 0, 10)
    eps_s = A("eps_s", [128, 1], F32, 0, 9)
    x_tm = A("x_tm", [128, NT * C], BF16, 0, 4)
    wpeT = A("wpeT", [128, G * C], BF16, 0, 4)
    bprj_b = A("bprj_b", [1, C], BF16, 0, 4)
    tri_s = A("tri_s", [128, 128], BF16, 0, 3)
    blk4T_s = A("blk4T_s", [4, 128], BF16, 0, 3)
    combineT = A("combineT", [128, NCK * T], BF16, 7, 10)
    recip_tm = A("recip_tm", [128, 16], F32, 7, 10)
    x_fm = A("x_fm", [128, NCK * T], BF16, 0, 2)
    weffq = A("weffq", [128, NCK * 512], BF16, 0, 2)
    weffk = A("weffk", [128, NCK * 512], BF16, 0, 2)
    weffv = A("weffv", [128, NCK * 512], BF16, 0, 2)
    cosbf = A("cosbf", [128, T], BF16, 0, 2)
    sinbf = A("sinbf", [128, T], BF16, 0, 2)
    prot_s = A("prot_s", [128, 128], BF16, 0, 2)
    blkones4_s = A("blkones4_s", [128, 4], BF16, 0, 2)
    wqkT_b = A("wqkT_b", [32, 32], BF16, 0, 2)
    wqkT4 = A("wqkT4", [128, 32], BF16, 0, 2)
    biasq_row = A("biasq_row", [1, 512], BF16, 0, 2)
    biask_row = A("biask_row", [1, 512], BF16, 0, 2)
    biasv_row = A("biasv_row", [1, 512], BF16, 0, 2)
    wqkb_b = A("wqkb_b", [1, 512], BF16, 0, 2)
    QK = A("QK", [128, G * T], BF16, 1, 3)
    KR = A("KR", [128, G * T], BF16, 1, 3)
    vt = A("vt", [128, NT * 512], BF16, 1, 3)
    xn_tm = A("xn_tm", [128, NT * C], BF16, 5, 7)
    rowsB2 = A("rowsB2", [65, 2 * T], BF16, 7, 9)
    l2o_b = A("l2o_b", [64, 32], BF16, 0, 1)
    l2o2 = A("l2o2", [128, 32], BF16, 0, 1)
    nbias_pp = A("nbias_pp", [128, 1], F32, 2, 3)
    ylat = A("ylat", [128, G * T], BF16, 3, 4)
    x2 = A("x2", [128, NT * C], BF16, 4, 5)
    normg_s = A("normg_s", [128, C], F32, 4, 5)
    normb_s = A("normb_s", [128, C], F32, 4, 5)
    slotg_s = A("slotg_s", [128, C], F32, 4, 5)
    slotb_s = A("slotb_s", [128, C], F32, 4, 5)
    scale_s = A("scale_s", [128, 1], F32, 4, 5)
    xn_fm = A("xn_fm", [128, NCK * T], BF16, 5, 6)
    seT = A("seT", [128, NCK * C], BF16, 5, 6)
    logits = A("logits", [128, NT * C], F32, 6, 7)
    rowmax_tm = A("rowmax_tm", [128, 16], F32, 6, 7)
    XeT = A("XeT", [128, NCK * C], BF16, 8, 9)
    xstg = A("xstg", [128, 2 * C], BF16, 8, 8)
    eout_f = A("eout_f", [128, NCK * C], BF16, 9, 10)
    x2b = A("x2b", [128, NT * C], BF16, 9, 10)
    cawb0 = A("cawb0", [128, C], BF16, 0, 0)
    cawb1 = A("cawb1", [128, C], BF16, 0, 0)
    cprojT2 = A("cprojT2", [128, 2 * C], BF16, 0, 0)
    q2lT_f = A("q2lT_f", [64, 32], F32, 0, 0)
    kv2lT_f = A("kv2lT_f", [64, 32], F32, 0, 0)
    q2lT_b = A("q2lT_b", [64, 32], BF16, 0, 0)
    kv2lT_b = A("kv2lT_b", [64, 32], BF16, 0, 0)
    l2o_f = A("l2o_f", [64, 32], F32, 0, 0)
    wqkT_f = A("wqkT_f", [32, 32], F32, 0, 0)
    blkdQ = A("blkdQ", [128, 64], BF16, 0, 2)
    blkdKV = A("blkdKV", [128, 64], BF16, 0, 2)
    blkdQf = A("blkdQf", [128, 64], F32, 0, 0)
    blkdKVf = A("blkdKVf", [128, 64], F32, 0, 0)
    wqkT4f = A("wqkT4f", [128, LAT], F32, 0, 0)
    l2o2f = A("l2o2f", [128, LAT], F32, 0, 0)
    bxr_f = A("bxr_f", [64, 16], F32, 0, 0)
    bxr_b = A("bxr_b", [64, 16], BF16, 0, 0)
    bx16c = A("bx16c", [16, 32], F32, 0, 0)
    bx16_f = A("bx16_f", [16, 32], F32, 0, 0)
    bx16b = A("bx16b", [16, 32], BF16, 0, 0)
    wqkb_f = A("wqkb_f", [1, 512], F32, 0, 0)
    l2ob_f = A("l2ob_f", [128, 8], F32, 0, 0)
    l2ob_b = A("l2ob_b", [128, 8], BF16, 0, 0)
    cprojb_f = A("cprojb_f", [1, C], F32, 0, 0)
    bprj_f = A("bprj_f", [1, C], F32, 0, 0)
    nb44 = A("nb44", [4, 4], F32, 2, 2)
    nbrow = A("nbrow", [1, 16], F32, 2, 2)
    nbrow2 = A("nbrow2", [1, 1], BF16, 2, 2)
    PT = A("PT", [128, 2 * T], BF16, 3, 3)
    sums4 = A("sums4", [4, 512], F32, 3, 3)
    rec4 = A("rec4", [4, 512], F32, 3, 3)
    rec4b = A("rec4b", [4, 512], BF16, 3, 3)
    rbbf = A("rbbf", [128, 512], BF16, 3, 3)
    dispatchN = A("dispatchN", [128, NT * C], BF16, 7, 7)
    lgb = A("lgb", [128, 2 * NT * C], BF16, 7, 7, alias="logits")
    scrBb = A("scrBb", [128, T], BF16, 7, 7, alias="scrB")
    nrm_s = A("nrm_s", [128, 16], F32, 7, 7)
    rsum_tm = A("rsum_tm", [128, 16], F32, 7, 7)
    rsL_s = A("rsL_s", [128, 16], F32, 7, 7)
    cmax_s = A("cmax_s", [128, 8], F32, 7, 7)
    ncmax_s = A("ncmax_s", [128, 8], F32, 7, 7)
    cs2_s = A("cs2_s", [128, 8], F32, 7, 7)
    rcl_s = A("rcl_s", [128, 8], F32, 7, 7)
    rcT_s = A("rcT_s", [128, 8], F32, 7, 7)
    wgA = A("wgA", [128, NCK * C], BF16, 8, 8)
    wgB = A("wgB", [128, NCK * C], BF16, 8, 8)
    wuA = A("wuA", [128, NCK * C], BF16, 8, 8)
    wuB = A("wuB", [128, NCK * C], BF16, 8, 8)
    wdS = A("wdS", [128, 16 * 512], BF16, 8, 8)
    hfm = A("hfm", [128, 16 * 512], BF16, 8, 8)
    outbuf = A("outbuf", [128, NT * C], F32, 10, 10)

    bg_row = rowsB2[0:1, 0:2 * FF]
    bu_row = rowsB2[32:33, 0:2 * FF]
    bd_row = rowsB2[64:65, 0:2 * C]

    p4 = nc.alloc_psum_tensor("p4", [128, 2048], F32)
    pA = nc.alloc_psum_tensor("pA", [128, 512], F32)
    pB = nc.alloc_psum_tensor("pB", [128, 512], F32)
    pC = nc.alloc_psum_tensor("pC", [128, 512], F32)
    pD = nc.alloc_psum_tensor("pD", [128, 512], F32)

    ST, SV, SS = mk.sem["t"], mk.sem["v"], mk.sem["s"]
    GD, YD = mk.sem["gd"], mk.sem["yd"]

    x_fm_v = x_fm.rearrange("p (c t) -> p c t", c=NCK)
    x_tm_v = x_tm.rearrange("p (i c) -> p i c", i=NT)
    weffq_v = weffq.rearrange("p (c x) -> p c x", c=NCK)
    weffk_v = weffk.rearrange("p (c x) -> p c x", c=NCK)
    weffv_v = weffv.rearrange("p (c x) -> p c x", c=NCK)
    wpeT_v = wpeT.rearrange("p (g o) -> p g o", g=G)
    QK_v = QK.rearrange("p (g t) -> p g t", g=G)
    KR_v = KR.rearrange("p (g t) -> p g t", g=G)
    vt_v = vt.rearrange("p (i x) -> p i x", i=NT)
    ylat_v = ylat.rearrange("p (g t) -> p g t", g=G)
    PT_v = PT.rearrange("p (u t) -> p u t", u=2)
    x2_v = x2.rearrange("p (i c) -> p i c", i=NT)
    xn_tm_v = xn_tm.rearrange("p (i c) -> p i c", i=NT)
    xn_fm_v = xn_fm.rearrange("p (c t) -> p c t", c=NCK)
    seT_v = seT.rearrange("p (c e) -> p c e", c=NCK)
    logits_v = logits.rearrange("p (i c) -> p i c", i=NT)
    combineT_v = combineT.rearrange("p (c t) -> p c t", c=NCK)
    dnT_v = dispatchN.rearrange("p (e t) -> p e t", e=NCK)
    lgb_v = lgb.rearrange("p (i c) -> p i c", i=2 * NT)
    XeT_v = XeT.rearrange("p (c e) -> p c e", c=NCK)
    wgA_v = wgA.rearrange("p (c f) -> p c f", c=NCK)
    wgB_v = wgB.rearrange("p (c f) -> p c f", c=NCK)
    wuA_v = wuA.rearrange("p (c f) -> p c f", c=NCK)
    wuB_v = wuB.rearrange("p (c f) -> p c f", c=NCK)
    wdS_v = wdS.rearrange("p (c o) -> p c o", c=16)
    hfm_v = hfm.rearrange("p (c x) -> p c x", c=16)
    eout_f_v = eout_f.rearrange("p (c o) -> p c o", c=NCK)
    x2b_v = x2b.rearrange("p (i c) -> p i c", i=NT)
    outbuf_v = outbuf.rearrange("p (i c) -> p i c", i=NT)
    p4q = p4.rearrange("p (q x) -> p q x", q=16)
    p4j = p4.rearrange("p (j x) -> p j x", j=4)

    ldtoks = {}

    def gload(name, dst_ap, src_ap, sem=None):
        s_ = mk.dsem(sem or ("g_" + name))
        ldtoks[name] = mk.do("g", lambda E, d=dst_ap, s=src_ap: E.dma_start(d, s), s_, 16)

    def yload(name, dst_ap, src_ap, sem=None):
        s_ = mk.dsem(sem or ("y_" + name))
        ldtoks[name] = mk.do("y", lambda E, d=dst_ap, s=src_ap: E.dma_start(d, s), s_, 16)
        return ldtoks[name]

    def yload_Y0(name, dst_ap, src_ap):
        return yload(name, dst_ap, src_ap, sem="Y0")

    gload("x_fm", x_fm_v[:, :, :], x_fm_in.rearrange("(c p) t -> p c t", p=128), sem="G0")
    gload("x_tm", x_tm_v[:, :, :], x_tm_in.rearrange("(i p) c -> p i c", p=128), sem="G0")
    gload("cos", cosbf[:, :], cosT_in[:, :], sem="G0")
    gload("sin", sinbf[:, :], sinT_in[:, :], sem="G0")
    yload_Y0("identf", identf_s[:, :], identf_in[:, :])
    yload_Y0("identb", identb_s[:, :], identb_in[:, :])
    yload_Y0("tri", tri_s[:, :], tri_in[:, :])
    yload_Y0("prot", prot_s[:, :], protT_in[:, :])
    yload_Y0("ones1", ones1_s[:, :], ones1_in[:, :])
    yload_Y0("ones128", ones128_s[:, :], ones128_in[:, :])
    yload_Y0("ones65", ones65_s[:, :], ones65_in[:, :])
    yload_Y0("blk4T", blk4T_s[:, :], blk4T_in[:, :])
    yload_Y0("blkones4", blkones4_s[:, :], blkones4_in[:, :])
    yload_Y0("eps", eps_s[:, :], eps128_in[:, :])
    yload_Y0("q2lT", q2lT_f[:, :], q2lT_in[:, :])
    yload_Y0("kv2lT", kv2lT_f[:, :], kv2lT_in[:, :])
    yload_Y0("wqkb", wqkb_f[:, :], wqkb512_in[:, :])
    yload_Y0("l2ob", l2ob_f[:, :], l2ob_in[:, :])
    yload_Y0("cprojb", cprojb_f[:, :], cprojb_in[:, :])
    yload_Y0("blkdQ", blkdQf[:, :], blkdQ_in[:, :])
    yload_Y0("blkdKV", blkdKVf[:, :], blkdKV_in[:, :])
    yload_Y0("wqkT4", wqkT4f[:, :], wqkT4_in[:, :])
    yload_Y0("l2o2", l2o2f[:, :], l2o2_in[:, :])

    TOK_G0 = mk.tok("G0")
    TOK_Y0 = mk.tok("Y0")
    for _n in list(ldtoks):
        if mk.sem.get("G0") and ldtoks[_n] and ldtoks[_n][0] is mk.sem["G0"]:
            ldtoks[_n] = TOK_G0
        if mk.sem.get("Y0") and ldtoks[_n] and ldtoks[_n][0] is mk.sem["Y0"]:
            ldtoks[_n] = TOK_Y0
    tok_smalls = None
    for nm, (dst, src) in {
        "q2lT": (q2lT_b, q2lT_f), "kv2lT": (kv2lT_b, kv2lT_f),
        "wqkb": (wqkb_b, wqkb_f), "l2ob": (l2ob_b, l2ob_f),
        "blkdQ": (blkdQ, blkdQf), "blkdKV": (blkdKV, blkdKVf),
        "wqkT4": (wqkT4, wqkT4f), "l2o2": (l2o2, l2o2f),
    }.items():
        mk.w("v", ldtoks[nm])
        tok_smalls = mk.do("v", lambda E, d=dst, s=src:
                           E.tensor_copy(d[:, :], s[:, :]), SV)
    tok_blkd = tok_smalls
    tok_l2o2 = tok_smalls
    tok_wqk4 = tok_smalls

    parts = [(0, blkdQ, weffq_v), (C, blkdKV, weffk_v), (2 * C, blkdKV, weffv_v)]
    caw_bufs = [cawb0, cawb1]
    caw_tok = [None, None]
    psQ = [pA, pB]
    ps_tok = [None, None]
    for pi, (roff, blkd, wv) in enumerate(parts):
        for p in range(8):
            ii = pi * 8 + p
            par = ii % 2
            buf = caw_bufs[par]
            mk.w("g", caw_tok[par])
            ct = mk.do("g", lambda E, b=buf, r=roff + p * 128:
                       E.dma_start(b[:, :], caw_in[r:r + 128, :]),
                       mk.dsem(f"CAW{par}"), 16)
            ps = psQ[par]
            mk.w("t", ct)
            mk.w("t", tok_blkd)
            mk.w("t", ps_tok[par])
            tk = None
            for j in range(NCK):
                tk = mk.do("t", lambda E, ps_=ps, b=buf, bd=blkd, j_=j:
                           E.matmul(ps_[:, j_ * 64:(j_ + 1) * 64],
                                    b[:, j_ * 128:(j_ + 1) * 128], bd[:, :],
                                    start=True, stop=True), ST)
            caw_tok[par] = tk
            mk.w("v", tk)
            dst = wv[:, :, p * 64:(p + 1) * 64]
            src = ps.rearrange("p (j x) -> p j x", j=8)
            ps_tok[par] = mk.do("v", lambda E, d=dst, s=src:
                                E.tensor_copy(d, s[:, :, :]), SV)

    prev_cast = None
    for brow, bin_, b16in, rhsw in ((biasq_row, bqr_in, q2lb16_in, q2lT_b),
                                    (biask_row, bkr_in, kvlb16_in, kv2lT_b),
                                    (biasv_row, bvr_in, kvlb16_in, kv2lT_b)):
        mk.w("y", prev_cast)
        mk.do("y", lambda E, b=bin_: E.dma_start(bxr_f[:, :], b[:, :]),
              mk.dsem("YB"), 16)
        t2 = mk.do("y", lambda E, b=b16in: E.dma_start(bx16c[:, :], b[:, :]),
                   mk.dsem("YB"), 16)
        mk.w("v", t2)
        t3 = mk.do("v", lambda E: E.tensor_copy(bxr_b[:, :], bxr_f[:, :]), SV)
        mk.w("t", t3)
        mk.w("t", ps_tok[0])
        t4 = mk.do("t", lambda E, r=rhsw: E.matmul(pA[0:16, 0:32], bxr_b[:, :],
                                                   r[:, :], start=True, stop=True), ST)
        mk.w("v", t4)
        t5 = mk.do("v", lambda E: E.tensor_tensor(bx16_f[:, :], pA[0:16, 0:32],
                                                  bx16c[:, :], ALU.add), SV)
        ps_tok[0] = t5
        t6 = mk.do("v", lambda E: E.tensor_copy(bx16b[:, :], bx16_f[:, :]), SV)
        mk.w("g", t6)
        t7 = mk.do("g", lambda E, d=brow: E.dma_start(d[0:1, :], bx16b[:, :]),
                   mk.dsem("BRW"), 16)
        mk.w("v", t7)
        prev_cast = t5
        ldtoks["brow"] = t7

    cp2 = cprojT2.rearrange("p (u o) -> p u o", u=2)
    wpe_prev_mm = None
    wpe_done = None
    for g in range(G):
        mk.w("g", wpe_prev_mm)
        ct = mk.do("g", lambda E, g_=g: E.dma_start(
            cp2[:, :, :],
            cprojT_in.rearrange("(u p) o -> p u o", p=128)[:, 2 * g_:2 * g_ + 2, :]),
            mk.dsem("CPJ"), 16)
        mk.w("t", ct)
        mk.w("t", tok_l2o2)
        mk.w("t", wpe_done)
        tk = None
        for j in range(G):
            for hb in range(2):
                tk = mk.do("t", lambda E, j_=j, hb_=hb: E.matmul(
                    p4[32 * j_:32 * (j_ + 1), hb_ * 512:(hb_ + 1) * 512],
                    l2o2[(j_ % 2) * 64:(j_ % 2) * 64 + 64, :],
                    cp2[(j_ % 2) * 64:(j_ % 2) * 64 + 64, j_ // 2,
                        hb_ * 512:(hb_ + 1) * 512],
                    start=True, stop=True,
                    tile_position=((j_ % 2) * 64, 32 * j_)), ST)
        for u in range(2):
            for hb in range(2):
                tk2 = mk.do("t", lambda E, u_=u, g_=g, hb_=hb: E.matmul(
                    p4[0:1, 1024 + hb_ * 512:1024 + (hb_ + 1) * 512],
                    l2ob_b[:, 2 * g_ + u_:2 * g_ + u_ + 1],
                    cp2[:, u_, hb_ * 512:(hb_ + 1) * 512],
                    start=(2 * g_ + u_ == 0), stop=(2 * g_ + u_ == 7)), ST)
        wpe_prev_mm = tk2
        mk.w("v", tk)
        wpe_done = mk.do("v", lambda E, g_=g: E.tensor_copy(
            wpeT_v[:, g_, :], p4[:, 0:1024]), SV)
    mk.w("v", wpe_prev_mm)
    mk.w("v", ldtoks["cprojb"])
    mk.do("v", lambda E: E.tensor_tensor(bprj_f[0:1, :], p4[0:1, 1024:2048],
                                         cprojb_f[0:1, :], ALU.add), SV)
    tok_bprj = mk.do("v", lambda E: E.tensor_copy(bprj_b[0:1, :], bprj_f[0:1, :]), SV)

    if _BUILD_OPTS["phases"] < 1:
        return _finish(mk, nc, out_d, outbuf_v)

    mk.w("t", ldtoks["x_fm"])
    mk.w("t", ldtoks["ones1"])
    mk.w("t", ldtoks["brow"])
    mk.w("t", tok_wqk4)
    mk.w("t", tok_bprj)
    mk.w("v", ldtoks["cos"])
    mk.w("v", ldtoks["sin"])

    tA = tB = tC = None
    prev_rot = None
    prev_wqk = None
    for g in range(G):
        for side in ("q", "k"):
            wv = weffq_v if side == "q" else weffk_v
            brow = biasq_row if side == "q" else biask_row
            for b in range(NB):
                mk.w("t", tA)
                for j in range(NCK):
                    mk.do("t", lambda E, w_=wv, j_=j, g_=g, b_=b: E.matmul(
                        pA[:, :], w_[:, j_, g_ * 128:(g_ + 1) * 128],
                        x_fm_v[:, j_, b_ * 512:(b_ + 1) * 512],
                        start=(j_ == 0), stop=False), ST)
                tk = mk.do("t", lambda E, br=brow, g_=g: E.matmul(
                    pA[:, :], br[0:1, g_ * 128:(g_ + 1) * 128],
                    ones1_s[0:1, 0:512], start=False, stop=True), ST)
                mk.w("v", tk)
                mk.w("v", prev_rot)
                t_qs = mk.do("v", lambda E, b_=b: E.tensor_tensor(
                    scbA[:, 0:512], pA[:, :],
                    sinbf[:, b_ * 512:(b_ + 1) * 512], ALU.mult), SV)
                mk.w("t", t_qs)
                mk.w("t", tB)
                t_rot = mk.do("t", lambda E: E.matmul(
                    pB[:, :], prot_s[:, :], scbA[:, 0:512],
                    start=True, stop=True), ST)
                prev_rot = t_rot
                mk.w("v", t_rot)
                if _BUILD_OPTS.get("dbg") and g == 0 and side == "k" and b == 0:
                    mk.do("v", lambda E: E.tensor_copy(dbgA[:, 0:512], scbA[:, 0:512]), SV)
                    mk.do("v", lambda E: E.tensor_copy(dbgA[:, 512:1024], pB[:, :]), SV)
                t_qc = mk.do("v", lambda E, b_=b: E.tensor_tensor(
                    scbB[:, 0:512], pA[:, :],
                    cosbf[:, b_ * 512:(b_ + 1) * 512], ALU.mult), SV)
                tA = t_qc
                if side == "q":
                    mk.w("v", prev_wqk)
                    t_qr = mk.do("v", lambda E: E.tensor_tensor(
                        scbA[:, 512:1024], pB[:, :], scbB[:, 0:512], ALU.add), SV)
                    tB = t_qr
                    mk.w("t", t_qr)
                    mk.w("t", tC)
                    for j in range(4):
                        mk.do("t", lambda E, j_=j: E.matmul(
                            pC[32 * j_:32 * (j_ + 1), :], wqkT4[32 * j_:32 * (j_ + 1), :],
                            scbA[32 * j_:32 * (j_ + 1), 512:1024],
                            start=True, stop=False,
                            tile_position=(32 * j_, 32 * j_)), ST)
                    tk2 = mk.do("t", lambda E, g_=g: E.matmul(
                        pC[:, :], wqkb_b[0:1, g_ * 128:(g_ + 1) * 128],
                        ones1_s[0:1, 0:512], start=False, stop=True), ST)
                    prev_wqk = tk2
                    mk.w("v", tk2)
                    tC = mk.do("v", lambda E, g_=g, b_=b: E.tensor_copy(
                        QK_v[:, g_, b_ * 512:(b_ + 1) * 512], pC[:, :]), SV)
                else:
                    tB = mk.do("v", lambda E, g_=g, b_=b: E.tensor_tensor(
                        KR_v[:, g_, b_ * 512:(b_ + 1) * 512], pB[:, :],
                        scbB[:, 0:512], ALU.add), SV)
    for tc in range(NT):
        par = tc % 2
        ps = pA if par == 0 else pB
        mk.w("t", tA if par == 0 else tB)
        for j in range(NCK):
            mk.do("t", lambda E, ps_=ps, j_=j, tc_=tc: E.matmul(
                ps_[:, :], x_fm_v[:, j_, tc_ * 128:(tc_ + 1) * 128],
                weffv_v[:, j_, :], start=(j_ == 0), stop=False), ST)
        tk = mk.do("t", lambda E, ps_=ps: E.matmul(
            ps_[:, :], ones1_s[0:1, 0:128], biasv_row[0:1, :],
            start=False, stop=True), ST)
        mk.w("v", tk)
        ev = mk.do("v", lambda E, ps_=ps, tc_=tc: E.tensor_copy(
            vt_v[:, tc_, :], ps_[:, :]), SV)
        if par == 0:
            tA = ev
        else:
            tB = ev

    if _BUILD_OPTS["phases"] < 2:
        return _finish(mk, nc, out_d, outbuf_v)

    sq_tok = None
    for g in range(G):
        for side in ("q", "k"):
            src = QK_v if side == "q" else KR_v
            col = 56 if side == "q" else 57
            mk.w("v", mk.tok("v"))
            sq_tok = mk.do("v", lambda E, g_=g, s_=src: E.scalar_tensor_tensor(
                scbA[:, :], s_[:, g_, :], 1.0, s_[:, g_, :],
                ALU.mult, ALU.mult), SV)
            mk.w("t", sq_tok)
            tk = None
            for b in range(NB):
                tk = mk.do("t", lambda E, b_=b: E.matmul(
                    p4[0:4, b_ * 512:(b_ + 1) * 512], blkones4_s[:, :],
                    scbA[:, b_ * 512:(b_ + 1) * 512], start=True, stop=True), ST)
            mk.w("v", tk)
            mk.do("v", lambda E, c_=col: E.reduce_max(
                stat[0:4, c_:c_ + 1], p4[0:4, :], axis=AX), SV)
        mk.do("v", lambda E, g_=g: E.tensor_tensor(
            nb44[0:4, g_:g_ + 1], stat[0:4, 56:57], stat[0:4, 57:58], ALU.add), SV)
    mk.w("g", mk.tok("v"))
    gt = None
    for g in range(G):
        gt = mk.do("g", lambda E, g_=g: E.dma_start(
            nbrow[0:1, 4 * g_:4 * g_ + 4], nb44[0:4, g_:g_ + 1]),
            mk.dsem("NB4"), 16)
    mk.w("v", gt)
    mk.do("v", lambda E: E.reduce_max(stat[0:1, 58:59], nbrow[0:1, :], axis=AX), SV)
    mk.do("v", lambda E: E.tensor_scalar_mul(stat[0:1, 59:60], stat[0:1, 58:59],
                                             -0.5 * SCALE), SV)
    tkc = mk.do("v", lambda E: E.tensor_copy(nbrow2[0:1, 0:1], stat[0:1, 59:60]), SV)
    mk.w("t", tkc)
    tkm = mk.do("t", lambda E: E.matmul(pA[:, 0:1], ones1_s[0:1, 0:128],
                                        nbrow2[0:1, 0:1], start=True, stop=True), ST)
    mk.w("v", tkm)
    tk_nb = mk.do("v", lambda E: E.tensor_copy(nbias_pp[:, :], pA[:, 0:1]), SV)
    tA = tk_nb

    if _BUILD_OPTS["phases"] < 3:
        return _finish(mk, nc, out_d, outbuf_v)

    mk.w("s", tk_nb)
    prev_tkb = None
    exp_toks = {}
    sums_toks = {}
    PT_v4 = PT.rearrange("p (s x) -> p s x", s=4)
    att_banks = [pA, pD]
    tile_y = {-2: tA, -1: None}
    st3 = {"p": 0, "tk_rb": None, "tk_s4": None, "prev_tkb": None,
           "prev_gsm": None, "evac": None, "defer": None, "y_last": None}

    def _att_consume(pr):
        (pi, g_, ci_, hp_, off_, w_, sl_, n_, diag_, etok_, mtok_, bank_,
         first_, yt_, se_) = pr
        mk.w("t", mtok_ if diag_ else etok_)
        if first_:
            mk.w("t", yt_)
        for j in (2 * hp_, 2 * hp_ + 1):
            mk.do("t", lambda E, g2=g_, j_=j, jj_=j - 2 * hp_, c2=ci_, o2=off_,
                  w2=w_, s2=sl_, n2=n_, bk=bank_:
                  E.matmul(bk[32 * j_:32 * (j_ + 1), o2:o2 + w2],
                           vt_v[:, c2, (4 * g2 + j_) * 32:(4 * g2 + j_ + 1) * 32],
                           PT_v4[:, s2, jj_ * 512 + o2:jj_ * 512 + o2 + w2],
                           start=(c2 == 0), stop=(c2 == n2 - 1),
                           tile_position=(0, 32 * j_)), ST)
        if first_:
            mk.w("t", se_)
        pu = None
        for j in (2 * hp_, 2 * hp_ + 1):
            pu = mk.do("t", lambda E, j_=j, jj_=j - 2 * hp_, c2=ci_, o2=off_,
                       w2=w_, s2=sl_, n2=n_:
                       E.matmul(pB[32 * j_:32 * j_ + 1, o2:o2 + w2],
                                ones128_s[:, 0:1],
                                PT_v4[:, s2, jj_ * 512 + o2:jj_ * 512 + o2 + w2],
                                start=(c2 == 0), stop=(c2 == n2 - 1),
                                tile_position=(0, 32 * j_)), ST)
        sums_toks[pi] = pu

    pend = None
    ti = 0
    for g in range(G):
        for b in range(NB):
            nch = 4 * (b + 1)
            first_pidx = st3["p"]
            bank = att_banks[ti % 2]
            for ci in range(nch):
                off = (ci - 4 * b) * 128 if ci >= 4 * b else 0
                w = 512 - off
                diag = (ci >= 4 * b)
                for hp in range(2):
                    p_idx = st3["p"]
                    par2 = p_idx % 2
                    slot = p_idx % 4
                    mk.w("t", exp_toks.get(p_idx - 2))
                    stok = None
                    for j in (2 * hp, 2 * hp + 1):
                        stok = mk.do("t", lambda E, g_=g, j_=j, jj_=j - 2 * hp, ci_=ci,
                                     off_=off, w_=w, b_=b, pr_=par2:
                                     E.matmul(p4j[:, 2 * pr_ + jj_, off_:off_ + w_],
                                              KR_v[32 * j_:32 * (j_ + 1), g_,
                                                   ci_ * 128:(ci_ + 1) * 128],
                                              QK_v[32 * j_:32 * (j_ + 1), g_,
                                                   b_ * 512 + off_:b_ * 512 + off_ + w_],
                                              start=True, stop=True,
                                              tile_position=(32 * j_, 0)), ST)
                    if p_idx == first_pidx + 2 and st3["defer"] is not None:
                        st3["defer"]()
                        st3["defer"] = None
                    mk.w("s", stok)
                    mk.w("s", sums_toks.get(p_idx - 4))
                    pin = p4j[:, 2 * par2:2 * par2 + 2, off:off + w]
                    pout = PT_v4[:, slot, :].rearrange(
                        "p (j x) -> p j x", j=2)[:, :, off:off + w]
                    etok = mk.do("s", lambda E, pi=pin, po=pout: E.activation(
                        po, pi, ACT.Exp, bias=nbias_pp[:, 0:1], scale=SCALE), SS)
                    exp_toks[p_idx] = etok
                    mtok = None
                    if diag:
                        mk.w("v", etok)
                        pm = PT_v4[:, slot, :].rearrange(
                            "p (j x) -> p j x", j=2)[:, :, off:off + 128]
                        mtok = mk.do("v", lambda E, pm_=pm: E.tensor_tensor(
                            pm_, pm_, tri_s[:, None, :].broadcast_to([128, 2, 128]),
                            ALU.mult), SV)
                    if pend is not None:
                        _att_consume(pend)
                    pend = (p_idx, g, ci, hp, off, w, slot, nch, diag, etok, mtok,
                            bank, p_idx == first_pidx, tile_y.get(ti - 2),
                            st3["evac"])
                    st3["p"] = p_idx + 1
            _att_consume(pend)
            pend = None
            ptail = mk.tok("t")
            mk.w("v", ptail)
            mk.w("v", st3["prev_gsm"])
            ev = mk.do("v", lambda E: E.tensor_copy(
                scrB[0:97, 0:512], pB[0:97, :]), SV)
            st3["evac"] = ev
            mk.w("g", ev)
            mk.w("g", st3["tk_s4"])
            gsm = mk.do("g", lambda E: E.dma_start(
                sums4[:, :], bass.AP(scrB, 0, [[32 * T, 4], [1, 512]])),
                mk.dsem("SM4"), 16)
            st3["prev_gsm"] = gsm

            def _defer(g_=g, b_=b, ti_=ti, gsm_=gsm, bank_=bank):
                mk.w("s", gsm_)
                mk.w("s", st3["prev_tkb"])
                mk.do("s", lambda E: E.activation(rec4[:, :], sums4[:, :],
                                                  ACT.Ln), SS)
                tk_s4 = mk.do("s", lambda E: E.activation(
                    rec4b[:, :], rec4[:, :], ACT.Exp, scale=-1.0), SS)
                st3["tk_s4"] = tk_s4
                mk.w("t", tk_s4)
                mk.w("t", st3["tk_rb"])
                tkb = mk.do("t", lambda E: E.matmul(
                    pC[:, :], blk4T_s[:, :], rec4b[:, :],
                    start=True, stop=True), ST)
                st3["prev_tkb"] = tkb
                mk.w("v", tkb)
                st3["tk_rb"] = mk.do("v", lambda E: E.tensor_copy(
                    rbbf[:, :], pC[:, :]), SV)
                yt = mk.do("v", lambda E, g2=g_, b2=b_, bk=bank_: E.tensor_tensor(
                    ylat_v[:, g2, b2 * 512:(b2 + 1) * 512], bk[:, :], rbbf[:, :],
                    ALU.mult), SV)
                tile_y[ti_] = yt
                st3["y_last"] = yt
            st3["defer"] = _defer
            ti += 1
    if st3["defer"] is not None:
        st3["defer"]()
        st3["defer"] = None
    tk_y = st3["y_last"]

    if _BUILD_OPTS["phases"] < 4:
        return _finish(mk, nc, out_d, outbuf_v)

    mk.w("y", mk.tok("t"))
    yload("normg", normg_s[:, :], normg_in[:, :], sem="YN")
    yload("normb", normb_s[:, :], normb_in[:, :], sem="YN")
    yload("slotg", slotg_s[:, :], slotg_in[:, :], sem="YN")
    yload("slotb", slotb_s[:, :], slotb_in[:, :], sem="YN")
    yload("scl", scale_s[:, :], scale128_in[:, :], sem="YN")
    TOK_YN = mk.tok("YN")
    for _n in ("normg", "normb", "slotg", "slotb", "scl"):
        ldtoks[_n] = TOK_YN
    mk.w("t", tk_y)
    mk.w("v", ldtoks["x_tm"])
    ln_toks = [None, None]
    h1s = [scrA, scrB]
    h1_fin = [None, None]
    x2_done = []

    def _postln(i, par, psum_off, h1, x_src_v, x_dst_v, base, extra_tok=None):
        mk.w("v", extra_tok)
        atok = mk.do("v", lambda E: E.scalar_tensor_tensor(
            h1[:, 0:1024], p4[:, psum_off:psum_off + 1024], 1.0, x_src_v,
            ALU.mult, ALU.add, accum_out=stat[:, base:base + 1]), SV)
        mk.w("s", atok)
        sqt = mk.do("s", lambda E: E.activation(
            h1[:, 1024:2048], h1[:, 0:1024], ACT.Square,
            accum_out=stat[:, base + 1:base + 2]), SS)
        mk.do("v", lambda E: E.tensor_tensor(
            stat[:, base + 2:base + 3], stat[:, base:base + 1],
            stat[:, base:base + 1], ALU.mult), SV)
        mk.w("v", sqt)
        vtok = mk.do("v", lambda E: E.scalar_tensor_tensor(
            stat[:, base + 3:base + 4], stat[:, base + 2:base + 3], -1.0 / C,
            stat[:, base + 1:base + 2], ALU.mult, ALU.add), SV)
        mk.w("s", vtok)
        mk.do("s", lambda E: E.activation(stat[:, base + 4:base + 5],
                                          stat[:, base + 3:base + 4], ACT.Ln,
                                          bias=eps_s[:, 0:1], scale=1.0 / C), SS)
        stok = mk.do("s", lambda E: E.activation(stat[:, base + 5:base + 6],
                                                 stat[:, base + 4:base + 5],
                                                 ACT.Exp, scale=-0.5), SS)
        mk.do("v", lambda E: E.tensor_scalar_mul(
            stat[:, base + 6:base + 7], stat[:, base:base + 1], 1.0 / C), SV)
        mk.w("v", stok)
        fin = mk.do("v", lambda E: E.tensor_scalar(
            x_dst_v, h1[:, 0:1024], stat[:, base + 6:base + 7],
            stat[:, base + 5:base + 6], ALU.subtract, ALU.mult), SV)
        return atok, fin

    for i in range(NT):
        par = i % 2
        off0 = 0 if par == 0 else 1024
        mk.w("t", ln_toks[par])
        tk = None
        for g in range(G):
            for hb in range(2):
                tk = mk.do("t", lambda E, g_=g, hb_=hb, i_=i, o=off0: E.matmul(
                    p4[:, o + hb_ * 512:o + (hb_ + 1) * 512],
                    ylat_v[:, g_, i_ * 128:(i_ + 1) * 128],
                    wpeT_v[:, g_, hb_ * 512:(hb_ + 1) * 512],
                    start=(g_ == 0), stop=False), ST)
        for hb in range(2):
            tk = mk.do("t", lambda E, hb_=hb, o=off0: E.matmul(
                p4[:, o + hb_ * 512:o + (hb_ + 1) * 512],
                ones1_s[0:1, 0:128], bprj_b[0:1, hb_ * 512:(hb_ + 1) * 512],
                start=False, stop=True), ST)
        mk.w("v", tk)
        ln_toks[par], fin = _postln(i, par, off0, h1s[par], x_tm_v[:, i, :],
                                    x2_v[:, i, :], base=0 if par == 0 else 8,
                                    extra_tok=h1_fin[par])
        h1_fin[par] = fin
        x2_done.append(fin)
    mk.w("g", x2_done[-1])
    mk.w("g", x2_done[-2])
    spill_tok = mk.do("g", lambda E: E.dma_start(
        x2_spill.rearrange("(i p) c -> p i c", p=128), x2_v[:, :, :]),
        mk.dsem("SPL"), 16)

    if _BUILD_OPTS["phases"] < 5:
        return _finish(mk, nc, out_d, outbuf_v)

    tr_evac = [None] * 4
    xf_free = [None, None]
    xn_f32s = [scrA, scrB]
    for i in range(NT):
        par = i % 2
        xf = xn_f32s[par]
        base = 16 if par == 0 else 20
        mk.w("s", x2_done[i])
        sqt = mk.do("s", lambda E, i_=i, x_=xf, b_=base: E.activation(
            x_[:, 1024:2048], x2_v[:, i_, :], ACT.Square,
            accum_out=stat[:, b_:b_ + 1]), SS)
        mk.do("s", lambda E, b_=base: E.activation(
            stat[:, b_ + 2:b_ + 3], stat[:, b_:b_ + 1], ACT.Ln,
            bias=eps_s[:, 0:1], scale=1.0 / C), SS)
        stok = mk.do("s", lambda E, b_=base: E.activation(
            stat[:, b_ + 3:b_ + 4], stat[:, b_ + 2:b_ + 3], ACT.Exp, scale=-0.5), SS)
        mk.w("v", stok)
        mk.w("v", ldtoks["normg"])
        mk.w("v", ldtoks["normb"])
        mk.do("v", lambda E, i_=i, b_=base: E.scalar_tensor_tensor(
            xn_tm_v[:, i_, :], x2_v[:, i_, :], stat[:, b_ + 3:b_ + 4],
            normg_s[:, :], ALU.mult, ALU.mult), SV)
        ftok = mk.do("v", lambda E, i_=i: E.tensor_tensor(
            xn_tm_v[:, i_, :], xn_tm_v[:, i_, :], normb_s[:, :], ALU.add), SV)
        ttok = None
        for half in range(2):
            bank = (i * 2 + half) % 4
            mk.w("t", ftok)
            mk.w("t", tr_evac[bank])
            for q in range(4):
                dc = half * 4 + q
                ttok = mk.do("t", lambda E, i_=i, dc_=dc, bank_=bank, q_=q:
                             E.matmul(p4q[:, bank_ * 4 + q_, :],
                                      xn_tm_v[:, i_, dc_ * 128:(dc_ + 1) * 128],
                                      identb_s[:, :], start=True, stop=True), ST)
            mk.w("v", ttok)
            tr_evac[bank] = mk.do("v", lambda E, i_=i, h_=half, bank_=bank:
                                  E.tensor_copy(
                xn_fm_v[:, h_ * 4:(h_ + 1) * 4, i_ * 128:(i_ + 1) * 128],
                p4q[:, bank_ * 4:(bank_ + 1) * 4, :]), SV)
        xf_free[par] = sqt
    for ec in range(NCK):
        par = ec % 2
        sf = xn_f32s[par]
        base = 24 if par == 0 else 32
        mk.w("y", xf_free[par])
        mk.w("y", mk.tok("v"))
        mk.w("y", mk.tok("s"))
        lt = mk.do("y", lambda E, ec_=ec, s_=sf: E.dma_start(
            s_[:, 0:1024], semb_in[ec_ * 128:(ec_ + 1) * 128, :]),
            mk.dsem(f"SE{par}"), 16)
        mk.w("v", lt)
        mk.w("v", ldtoks["slotg"])
        mk.w("v", ldtoks["slotb"])
        mk.w("v", ldtoks["scl"])
        mk.do("v", lambda E, s_=sf, b_=base: E.reduce_sum(
            stat[:, b_:b_ + 1], s_[:, 0:1024], axis=AX), SV)
        mk.do("v", lambda E, s_=sf, b_=base: E.scalar_tensor_tensor(
            s_[:, 1024:2048], s_[:, 0:1024], 1.0, s_[:, 0:1024], ALU.mult, ALU.mult,
            accum_out=stat[:, b_ + 1:b_ + 2]), SV)
        mk.do("v", lambda E, b_=base: E.tensor_scalar_mul(
            stat[:, b_ + 2:b_ + 3], stat[:, b_:b_ + 1], 1.0 / C), SV)
        mk.do("v", lambda E, b_=base: E.tensor_scalar_mul(
            stat[:, b_ + 3:b_ + 4], stat[:, b_ + 1:b_ + 2], 1.0 / C), SV)
        mk.do("v", lambda E, b_=base: E.tensor_tensor(
            stat[:, b_ + 4:b_ + 5], stat[:, b_ + 2:b_ + 3],
            stat[:, b_ + 2:b_ + 3], ALU.mult), SV)
        v2 = mk.do("v", lambda E, b_=base: E.tensor_tensor(
            stat[:, b_ + 5:b_ + 6], stat[:, b_ + 3:b_ + 4],
            stat[:, b_ + 4:b_ + 5], ALU.subtract), SV)
        mk.w("s", v2)
        mk.do("s", lambda E, b_=base: E.activation(
            stat[:, b_ + 6:b_ + 7], stat[:, b_ + 5:b_ + 6], ACT.Ln,
            bias=eps_s[:, 0:1]), SS)
        stok = mk.do("s", lambda E, b_=base: E.activation(
            stat[:, b_ + 7:b_ + 8], stat[:, b_ + 6:b_ + 7], ACT.Exp, scale=-0.5), SS)
        mk.w("v", stok)
        mk.do("v", lambda E, s_=sf, b_=base: E.tensor_scalar(
            s_[:, 1024:2048], s_[:, 0:1024], stat[:, b_ + 2:b_ + 3],
            stat[:, b_ + 7:b_ + 8], ALU.subtract, ALU.mult), SV)
        mk.do("v", lambda E, s_=sf: E.tensor_tensor(
            s_[:, 1024:2048], s_[:, 1024:2048], slotg_s[:, :], ALU.mult), SV)
        mk.do("v", lambda E, s_=sf: E.tensor_tensor(
            s_[:, 1024:2048], s_[:, 1024:2048], slotb_s[:, :], ALU.add), SV)
        ftok = mk.do("v", lambda E, s_=sf: E.tensor_scalar_mul(
            s_[:, 1024:2048], s_[:, 1024:2048], scale_s[:, 0:1]), SV)
        ttok = None
        for half in range(2):
            bank = (ec * 2 + half) % 4
            mk.w("t", ftok)
            mk.w("t", tr_evac[bank])
            for q in range(4):
                dc = half * 4 + q
                ttok = mk.do("t", lambda E, s_=sf, dc_=dc, bank_=bank, q_=q:
                             E.transpose(p4q[:, bank_ * 4 + q_, :],
                                         s_[:, 1024 + dc_ * 128:1024 + (dc_ + 1) * 128],
                                         identf_s[:, :]), ST)
            mk.w("v", ttok)
            tr_evac[bank] = mk.do("v", lambda E, ec_=ec, h_=half, bank_=bank:
                                  E.tensor_copy(
                seT_v[:, h_ * 4:(h_ + 1) * 4, ec_ * 128:(ec_ + 1) * 128],
                p4q[:, bank_ * 4:(bank_ + 1) * 4, :]), SV)
        xf_free[par] = ttok

    if _BUILD_OPTS["phases"] < 6:
        return _finish(mk, nc, out_d, outbuf_v)

    for bank in range(4):
        mk.w("t", tr_evac[bank])
    lg_tok = [None, None]
    mk.w("v", spill_tok)
    mk.w("s", spill_tok)
    for i in range(NT):
        par = i % 2
        off0 = 0 if par == 0 else 1024
        mk.w("t", lg_tok[par])
        tk = None
        for hb in range(2):
            for dc in range(NCK):
                tk = mk.do("t", lambda E, i_=i, hb_=hb, dc_=dc, o=off0: E.matmul(
                    p4[:, o + hb_ * 512:o + (hb_ + 1) * 512],
                    xn_fm_v[:, dc_, i_ * 128:(i_ + 1) * 128],
                    seT_v[:, dc_, hb_ * 512:(hb_ + 1) * 512],
                    start=(dc_ == 0), stop=(dc_ == NCK - 1)), ST)
        mk.w("s", tk)
        cpt = mk.do("s", lambda E, i_=i, o=off0: E.activation(
            logits_v[:, i_, :], p4[:, o:o + 1024], ACT.Copy), SS)
        lg_tok[par] = cpt
        mk.w("v", cpt)
        mk.do("v", lambda E, i_=i: E.reduce_max(
            rowmax_tm[:, i_:i_ + 1], logits_v[:, i_, :], axis=AX), SV)

    if _BUILD_OPTS["phases"] < 7:
        return _finish(mk, nc, out_d, outbuf_v)

    nrm_tok = mk.do("v", lambda E: E.tensor_scalar_mul(
        nrm_s[:, :], rowmax_tm[:, :], -1.0), SV)
    tr_tok = None
    exp_tok3 = None
    for ec in range(NCK):
        mk.w("t", tr_tok)
        ttok = None
        for i in range(NT):
            ttok = mk.do("t", lambda E, i_=i, ec_=ec: E.transpose(
                p4q[:, i_, :], logits_v[:, i_, ec_ * 128:(ec_ + 1) * 128],
                identf_s[:, :]), ST)
        mk.w("v", ttok)
        mk.do("v", lambda E, ec_=ec: E.reduce_max(
            cmax_s[:, ec_:ec_ + 1], p4[:, :], axis=AX), SV)
        nm2 = mk.do("v", lambda E, ec_=ec: E.tensor_scalar_mul(
            ncmax_s[:, ec_:ec_ + 1], cmax_s[:, ec_:ec_ + 1], -1.0), SV)
        mk.w("s", nm2)
        tr_tok = mk.do("s", lambda E, ec_=ec: E.activation(
            dnT_v[:, ec_, :], p4[:, :], ACT.Exp, bias=ncmax_s[:, ec_:ec_ + 1],
            accum_out=cs2_s[:, ec_:ec_ + 1]), SS)
        exp_tok3 = tr_tok
    mk.do("s", lambda E: E.activation(rcl_s[:, :], cs2_s[:, :], ACT.Ln), SS)
    rck = mk.do("s", lambda E: E.activation(rcT_s[:, :], rcl_s[:, :], ACT.Exp,
                                            scale=-1.0), SS)
    mk.w("v", rck)
    nlast = None
    for ec in range(NCK):
        nlast = mk.do("v", lambda E, ec_=ec: E.tensor_scalar_mul(
            dnT_v[:, ec_, :], dnT_v[:, ec_, :], rcT_s[:, ec_:ec_ + 1]), SV)
    expc_toks = []
    for i in range(NT - 1, -1, -1):
        dst = lgb_v[:, NT + i, :] if i < 15 else scrBb[:, 0:1024]
        expc_toks.append(mk.do("s", lambda E, i_=i, d_=dst: E.activation(
            d_, logits_v[:, i_, :], ACT.Exp, bias=nrm_s[:, i_:i_ + 1],
            accum_out=rsum_tm[:, i_:i_ + 1]), SS))
    expc_tok = expc_toks[-1]
    mk.do("s", lambda E: E.activation(rsL_s[:, :], rsum_tm[:, :], ACT.Ln), SS)
    rct = mk.do("s", lambda E: E.activation(recip_tm[:, :], rsL_s[:, :], ACT.Exp,
                                            scale=-1.0), SS)
    dp_tok = [None, None]
    mk.w("t", nlast)
    mk.w("t", exp_tok3)
    for i in range(NT):
        par = i % 2
        o = par * 1024
        mk.w("t", dp_tok[par])
        tk = None
        for ec in range(NCK):
            tk = mk.do("t", lambda E, i_=i, ec_=ec, o_=o: E.matmul(
                p4[:, o_ + ec_ * 128:o_ + (ec_ + 1) * 128],
                dnT_v[:, ec_, i_ * 128:(i_ + 1) * 128], identb_s[:, :],
                start=True, stop=True), ST)
        mk.w("v", tk)
        if i == 0:
            mk.w("v", expc_tok)
        dp_tok[par] = mk.do("v", lambda E, i_=i, o_=o: E.tensor_copy(
            lgb_v[:, i_, :], p4[:, o_:o_ + 1024]), SV)
    sl_tok = [None, None]
    sl_dma = [None, None]
    mk.w("t", dp_tok[0])
    mk.w("t", dp_tok[1])
    for dc in range(NCK):
        par = dc % 2
        off0 = 0 if par == 0 else 1024
        mk.w("t", sl_tok[par])
        tk = None
        for hb in range(2):
            for i in range(NT):
                tk = mk.do("t", lambda E, dc_=dc, hb_=hb, i_=i, o=off0: E.matmul(
                    p4[:, o + hb_ * 512:o + (hb_ + 1) * 512],
                    xn_tm_v[:, i_, dc_ * 128:(dc_ + 1) * 128],
                    lgb_v[:, i_, hb_ * 512:(hb_ + 1) * 512],
                    start=(i_ == 0), stop=(i_ == NT - 1)), ST)
        mk.w("v", tk)
        mk.w("v", sl_dma[par])
        sl_tok[par] = mk.do("v", lambda E, dc_=dc, o=off0, par_=par: E.tensor_copy(
            scbA[:, par_ * 1024:(par_ + 1) * 1024], p4[:, o:o + 1024]), SV)
        mk.w("g", sl_tok[par])
        sl_dma[par] = mk.do("g", lambda E, dc_=dc, par_=par: E.dma_start(
            ag1_in.rearrange("(c p) e -> p c e", p=128)[:, dc_, :],
            scbA[:, par_ * 1024:(par_ + 1) * 1024]), mk.dsem(f"SL{par}"), 16)
    sl_ev = sl_dma[(NCK - 1) % 2]
    mk.w("v", sl_dma[0])
    mk.w("v", sl_dma[1])
    mk.w("g", sl_ev)
    mk.w("g", mk.tok("t"))
    mk.w("g", mk.tok("v"))
    mk.w("g", mk.tok("s"))
    wbufs = {0: (wgA_v, wuA_v), 1: (wgB_v, wuB_v)}
    wld = {}
    wfree = {0: None, 1: None}

    def load_wh(e, Hh, par, eng="g"):
        gv, uv = wbufs[par]
        mk.w(eng, wfree[par])
        mk.do(eng, lambda E, e_=e, H_=Hh, g_=gv: E.dma_start(
            g_[:, :, :],
            wg2_in[e_].rearrange("(c p) f -> p c f", p=128)[:, :, H_ * 1024:(H_ + 1) * 1024]),
            mk.dsem(f"WH{par}"), 16)
        wld[(e, Hh)] = mk.do(eng, lambda E, e_=e, H_=Hh, u_=uv: E.dma_start(
            u_[:, :, :],
            wu2_in[e_].rearrange("(c p) f -> p c f", p=128)[:, :, H_ * 1024:(H_ + 1) * 1024]),
            mk.dsem(f"WH{par}"), 16)

    def load_wd(e, oh, eng="g"):
        return mk.do(eng, lambda E, e_=e, o_=oh: E.dma_start(
            wdS_v[:, :, :],
            wd2_in[e_].rearrange("(c p) o -> p c o", p=128)[:, :, o_ * 512:(o_ + 1) * 512]),
            mk.dsem("WD"), 16)

    agt = mk.do("g", lambda E: E.collective_compute(
        "AllGather", ALU.bypass, replica_groups=[list(range(NCORES))],
        ins=[ag1_in.ap().opt()], outs=[ag1_out.ap().opt()]), mk.sem["cc"], 1)
    mk.w("t", sl_tok[0])
    mk.w("t", sl_tok[1])
    tr2_tok = None
    for ec in range(NCK):
        mk.w("t", tr2_tok)
        ttok = None
        for i in range(NT):
            lhs = (lgb_v[:, NT + i, :] if i < 15 else scrBb[:, 0:1024])
            ttok = mk.do("t", lambda E, i_=i, ec_=ec, l_=lhs: E.matmul(
                p4q[:, i_, :], l_[:, ec_ * 128:(ec_ + 1) * 128],
                identb_s[:, :], start=True, stop=True), ST)
        mk.w("v", ttok)
        tr2_tok = mk.do("v", lambda E, ec_=ec: E.tensor_copy(
            combineT_v[:, ec_, :], p4[:, :]), SV)

    mk.w("y", mk.tok("t"))
    mk.w("y", mk.tok("v"))
    mk.w("y", mk.tok("s"))
    brow_tok = None
    for nm, bin2, row, wdt in (("bg2", bg2_in, bg_row, FF), ("bu2", bu2_in, bu_row, FF),
                               ("bd2", bd2_in, bd_row, C)):
        mk.w("y", brow_tok)
        lt = mk.do("y", lambda E, b=bin2, w_=wdt: E.dma_start(
            scrB[0:2, 0:w_], b[:, :]), mk.dsem("YG"), 16)
        mk.w("v", lt)
        mk.w("v", brow_tok)
        ctk = mk.do("v", lambda E, w_=wdt: E.tensor_copy(
            scbB[0:2, 0:w_], scrB[0:2, 0:w_]), SV)
        mk.w("g", ctk)
        brow_tok = mk.do("g", lambda E, r=row, w_=wdt: E.dma_start(
            r[0:1, 0:2 * w_], scbB[0:2, 0:w_]), mk.dsem("BRS"), 16)
    mk.w("v", brow_tok)

    if _BUILD_OPTS["phases"] < 8:
        return _finish(mk, nc, out_d, outbuf_v)

    mk.w("g", tr2_tok)
    mk.w("g", brow_tok)
    load_wh(0, 0, 0)
    wd_tok = load_wd(0, 0)
    load_wh(0, 1, 1)
    mk.w("g", agt)

    regcell = {}

    def _dynoff(E, shift, delta):
        if "pid" not in regcell:
            regcell["pid"] = E.to_reg(E.partition_id())
            regcell["off"] = E.alloc_register("dynoff")
        if shift is not None:
            E.reg_alu(regcell["off"], regcell["pid"], shift, ALU.arith_shift_left)
        if delta:
            E.reg_add(regcell["off"], regcell["off"], delta)
        return regcell["off"]

    XeT_x = XeT.rearrange("p (dc s) -> p dc s", dc=8)
    xstg_v = xstg.rearrange("p (u dc s) -> p u dc s", u=2, dc=8)

    def _xet_dyn(E, b_, par):
        first = (b_ == 0)
        off = _dynoff(E, 7 if first else None, 0 if first else (1 << 20))
        src = bass.AP(ag1_out, off, [[EL, 128], [128 * EL, 8], [1, 128]])
        return E.dma_start(xstg_v[:, par, :, :], src)

    xet_tok = None
    xcp_tok = [None, None]
    for b_ in range(NCORES):
        par = b_ % 2
        mk.w("g", xcp_tok[par])
        ld = mk.do("g", lambda E, b__=b_, p_=par: _xet_dyn(E, b__, p_),
                   mk.dsem(f"XS{par}"), 16)
        mk.w("g", ld)
        for el_ in range(2):
            xcp_tok[par] = mk.do("g", lambda E, b__=b_, el__=el_, p_=par: E.dma_start(
                XeT_x[:, :, el__ * 512 + b__ * 64:el__ * 512 + b__ * 64 + 64],
                xstg_v[:, p_, :, el__ * 64:el__ * 64 + 64]),
                mk.dsem(f"XC{par}"), 16)
        xet_tok = xcp_tok[par]
    xet_tok2 = xcp_tok[1 - (NCORES - 1) % 2]

    mk.w("t", xet_tok)
    mk.w("t", xet_tok2)
    mk.w("t", brow_tok)
    mk.w("s", brow_tok)
    eo_dma = [None, None]
    gelu_free = None
    up_free = None
    gate_free = None
    h_cons = None
    for e in range(2):
        for Hh in range(2):
            par = Hh
            gv, uv = wbufs[par]
            mk.w("t", wld[(e, Hh)])
            for fs in range(8):
                fc = Hh * 8 + fs
                if fc == 0:
                    mk.w("t", h_cons)
                mk.w("t", gate_free)
                tkg = None
                for dc in range(NCK):
                    tkg = mk.do("t", lambda E, g_=gv, dc_=dc, fs_=fs, e_=e: E.matmul(
                        pA[:, :], g_[:, dc_, fs_ * 128:(fs_ + 1) * 128],
                        XeT_v[:, dc_, e_ * 512:(e_ + 1) * 512],
                        start=(dc_ == 0), stop=False), ST)
                tkg = mk.do("t", lambda E, e_=e, fc_=fc: E.matmul(
                    pA[:, :], bg_row[0:1, e_ * FF + fc_ * 128:e_ * FF + (fc_ + 1) * 128],
                    ones1_s[0:1, 0:512], start=False, stop=True), ST)
                mk.w("t", up_free)
                tku = None
                for dc in range(NCK):
                    tku = mk.do("t", lambda E, u_=uv, dc_=dc, fs_=fs, e_=e: E.matmul(
                        pB[:, :], u_[:, dc_, fs_ * 128:(fs_ + 1) * 128],
                        XeT_v[:, dc_, e_ * 512:(e_ + 1) * 512],
                        start=(dc_ == 0), stop=False), ST)
                tku = mk.do("t", lambda E, e_=e, fc_=fc: E.matmul(
                    pB[:, :], bu_row[0:1, e_ * FF + fc_ * 128:e_ * FF + (fc_ + 1) * 128],
                    ones65_s[32:33, 0:512], start=False, stop=True), ST)
                mk.w("s", tkg)
                mk.w("s", gelu_free)
                gt = mk.do("s", lambda E: E.activation(scbA[:, 0:512], pA[:, :],
                                                       ACT.Gelu), SS)
                gate_free = gt
                mk.w("v", tku)
                ut = mk.do("v", lambda E: E.tensor_copy(scbB[:, 0:512], pB[:, :]), SV)
                up_free = ut
                mk.w("v", gt)
                htok = mk.do("v", lambda E, fc_=fc: E.tensor_tensor(
                    hfm_v[:, fc_, :], scbA[:, 0:512], scbB[:, 0:512], ALU.mult), SV)
                gelu_free = htok
            if e == 0:
                wfree[par] = mk.tok("t")
                load_wh(1, Hh, par)
        mk.w("t", mk.tok("v"))
        for oh in range(2):
            mk.w("t", wd_tok)
            eo_par = [None, None]
            for bsl in range(4):
                par = bsl % 2
                mk.w("t", eo_par[par])
                tk = None
                for fc in range(16):
                    tk = mk.do("t", lambda E, fc_=fc, bsl_=bsl, par_=par: E.matmul(
                        p4[:, par_ * 1024:par_ * 1024 + 512],
                        hfm_v[:, fc_, bsl_ * 128:(bsl_ + 1) * 128],
                        wdS_v[:, fc_, :],
                        start=(fc_ == 0), stop=(fc_ == 15)), ST)
                tk = mk.do("t", lambda E, e_=e, oh_=oh, par_=par: E.matmul(
                    p4[:, par_ * 1024:par_ * 1024 + 512], ones65_s[64:65, 0:128],
                    bd_row[0:1, e_ * 1024 + oh_ * 512:e_ * 1024 + (oh_ + 1) * 512],
                    start=False, stop=True), ST)
                mk.w("v", tk)
                mk.w("v", eo_dma[par])
                eo_par[par] = mk.do("v", lambda E, par_=par: E.tensor_copy(
                    scbA[:, 1024 + par_ * 512:1024 + (par_ + 1) * 512],
                    p4[:, par_ * 1024:par_ * 1024 + 512]), SV)
                mk.w("g", eo_par[par])
                eo_dma[par] = mk.do("g", lambda E, e_=e, bsl_=bsl, oh_=oh, par_=par:
                                    E.dma_start(
                    ag2_in.rearrange("(c p) o -> p c o", p=128)[
                        :, e_ * 4 + bsl_, oh_ * 512:(oh_ + 1) * 512],
                    scbA[:, 1024 + par_ * 512:1024 + (par_ + 1) * 512]),
                    mk.dsem(f"EO{par}"), 16)
            mk.w("g", mk.tok("t"))
            if oh == 0:
                wd_tok = load_wd(e, 1)
            elif e == 0:
                wd_tok = load_wd(1, 0)
        h_cons = mk.tok("t")
    mk.w("g", eo_dma[0])
    mk.w("g", eo_dma[1])
    mk.w("g", mk.tok("t"))
    x2r_tok = mk.do("g", lambda E: E.dma_start(
        x2b_v[:, :, :], x2_spill.rearrange("(i p) c -> p i c", p=128)),
        mk.dsem("REL"), 16)
    agt2 = mk.do("g", lambda E: E.collective_compute(
        "AllGather", ALU.bypass, replica_groups=[list(range(NCORES))],
        ins=[ag2_in.ap().opt()], outs=[ag2_out.ap().opt()]), mk.sem["cc"], 1)
    mk.w("g", agt2)
    mk.w("g", mk.tok("t"))

    def _ef_dma(E, c_):
        first = (c_ == 0)
        off = _dynoff(E, 16 if first else None, 0 if first else (1 << 20))
        src = bass.AP(ag2_out, off, [[512 * C, 2], [C, 64], [1, C]])
        return E.dma_start(eout_f_v[:, c_, :], src)

    ef_tok = None
    for c_ in range(NCORES):
        ef_tok = mk.do("g", lambda E, c__=c_: _ef_dma(E, c__),
                       mk.dsem("EFG"), 16)

    if _BUILD_OPTS["phases"] < 10:
        return _finish(mk, nc, out_d, outbuf_v)

    mk.w("t", ef_tok)
    mk.w("t", x2r_tok)
    of_tok = [None, None]
    of_fin = [None, None]
    for i in range(NT):
        par = i % 2
        off0 = 0 if par == 0 else 1024
        base = 40 if par == 0 else 48
        mk.w("t", of_tok[par])
        tk = None
        for hb in range(2):
            for ec in range(NCK):
                tk = mk.do("t", lambda E, i_=i, hb_=hb, ec_=ec, o=off0: E.matmul(
                    p4[:, o + hb_ * 512:o + (hb_ + 1) * 512],
                    combineT_v[:, ec_, i_ * 128:(i_ + 1) * 128],
                    eout_f_v[:, ec_, hb_ * 512:(hb_ + 1) * 512],
                    start=(ec_ == 0), stop=(ec_ == NCK - 1)), ST)
        mk.w("v", tk)
        mk.w("v", x2r_tok)
        mk.w("v", rct)
        h1 = scrA if par == 0 else scrB
        mk.w("v", of_fin[par])
        atok = mk.do("v", lambda E, h=h1, i_=i, o=off0, b_=base:
                     E.scalar_tensor_tensor(
                         h[:, 0:1024], p4[:, o:o + 1024], recip_tm[:, i_:i_ + 1],
                         x2b_v[:, i_, :], ALU.mult, ALU.add,
                         accum_out=stat[:, b_:b_ + 1]), SV)
        mk.w("s", atok)
        sqt = mk.do("s", lambda E, h=h1, b_=base: E.activation(
            h[:, 1024:2048], h[:, 0:1024], ACT.Square,
            accum_out=stat[:, b_ + 1:b_ + 2]), SS)
        mk.do("v", lambda E, b_=base: E.tensor_tensor(
            stat[:, b_ + 2:b_ + 3], stat[:, b_:b_ + 1],
            stat[:, b_:b_ + 1], ALU.mult), SV)
        mk.w("v", sqt)
        vtok = mk.do("v", lambda E, b_=base: E.scalar_tensor_tensor(
            stat[:, b_ + 3:b_ + 4], stat[:, b_ + 2:b_ + 3], -1.0 / C,
            stat[:, b_ + 1:b_ + 2], ALU.mult, ALU.add), SV)
        mk.w("s", vtok)
        mk.do("s", lambda E, b_=base: E.activation(
            stat[:, b_ + 4:b_ + 5], stat[:, b_ + 3:b_ + 4], ACT.Ln,
            bias=eps_s[:, 0:1], scale=1.0 / C), SS)
        stok = mk.do("s", lambda E, b_=base: E.activation(
            stat[:, b_ + 5:b_ + 6], stat[:, b_ + 4:b_ + 5], ACT.Exp,
            scale=-0.5), SS)
        mk.do("v", lambda E, b_=base: E.tensor_scalar_mul(
            stat[:, b_ + 6:b_ + 7], stat[:, b_:b_ + 1], 1.0 / C), SV)
        mk.w("v", stok)
        of_fin[par] = mk.do("v", lambda E, h=h1, i_=i, b_=base: E.tensor_scalar(
            outbuf_v[:, i_, :], h[:, 0:1024], stat[:, b_ + 6:b_ + 7],
            stat[:, b_ + 5:b_ + 6], ALU.subtract, ALU.mult), SV)
        of_tok[par] = atok

    return _finish(mk, nc, out_d, outbuf_v)


def _finish(mk, nc, out_d, outbuf_v):
    mk.w("g", mk.tok("v"))
    mk.w("g", mk.tok("s"))
    st = mk.do("g", lambda E: E.dma_start(
        out_d.rearrange("(i p) c -> p i c", p=128), outbuf_v[:, :, :]),
        mk.dsem("OUT"), 16)
    for name in _BUILD_OPTS["taps"]:
        h = mk.sb[name]
        shp = list(h.shape)
        o = nc.dram_tensor("tap_" + name, shp, h.ap().dtype, kind="ExternalOutput")
        st = mk.do("g", lambda E, o_=o, h_=h: E.dma_start(o_[:, :], h_[:, :]),
                   mk.dsem("OUT"), 16)
    mk.w("g", st)
    nc._mk_specs = list(mk.specs)
    with nc.Block() as block:
        block.tensor(lambda E: [f(E) for f in mk.ops["t"]])
        block.vector(lambda E: [f(E) for f in mk.ops["v"]])
        block.scalar(lambda E: [f(E) for f in mk.ops["s"]])
        block.gpsimd(lambda E: [f(E) for f in mk.ops["g"]])
        block.sync(lambda E: [f(E) for f in mk.ops["y"]])
    return nc


def _solve_plan(specs, cap):
    plan = {}
    placed = []
    for name, size, p0, p1 in sorted(specs, key=lambda x: -x[1]):
        taken = sorted((plan[n], plan[n] + s) for (n, s, a0, a1) in placed
                       if not (a1 < p0 or a0 > p1))
        off = 0
        for (a, bnd) in taken:
            if off + size <= a:
                break
            off = max(off, bnd)
        assert off + size <= cap, f"SBUF plan overflow: {name} {size}B at {off}"
        plan[name] = off
        placed.append((name, size, p0, p1))
    return plan


_PLAN_CACHE = {}


def build_planned():
    key = _BUILD_OPTS["phases"]
    if key not in _PLAN_CACHE:
        nc0 = build(None)
        _PLAN_CACHE[key] = _solve_plan(nc0._mk_specs, 206336)
    return build(_PLAN_CACHE[key])


def _blkd(p2lT):
    z = np.zeros((128, 64), np.float32)
    z[0:64, 0:32] = p2lT
    z[64:128, 32:64] = p2lT
    return z


def _stage_inputs(inputs, core):
    f32 = np.float32
    x = np.asarray(inputs["L0_attn_x"][core], dtype=f32)
    caw = np.asarray(inputs["L0_attn_c_attn_w"], dtype=f32)
    cab = np.asarray(inputs["L0_attn_c_attn_b"], dtype=f32)
    q2l = np.asarray(inputs["L0_attn_q2l_w"], dtype=f32)
    q2lb = np.asarray(inputs["L0_attn_q2l_b"], dtype=f32)
    kv2l = np.asarray(inputs["L0_attn_kv2l_w"], dtype=f32)
    kv2lb = np.asarray(inputs["L0_attn_kv2l_b"], dtype=f32)
    l2o = np.asarray(inputs["L0_attn_l2o_w"], dtype=f32)
    l2ob = np.asarray(inputs["L0_attn_l2o_b"], dtype=f32)
    wqk = np.asarray(inputs["L0_attn_wqk_w"], dtype=f32)
    wqkb = np.asarray(inputs["L0_attn_wqk_b"], dtype=f32)
    cpw = np.asarray(inputs["L0_attn_cproj_w"], dtype=f32)
    cpb = np.asarray(inputs["L0_attn_cproj_b"], dtype=f32)
    d = dict(_CONSTS)
    d.update({
        "x_tm_in": np.ascontiguousarray(x),
        "x_fm_in": np.ascontiguousarray(x.T),
        "caw": caw,
        "cprojT_in": np.ascontiguousarray(cpw.T),
        "q2lT_in": np.ascontiguousarray(q2l.T),
        "kv2lT_in": np.ascontiguousarray(kv2l.T),
        "l2o_in": l2o,
        "wqkT_in": np.ascontiguousarray(wqk.T),
        "blkdQ_in": _blkd(q2l.T),
        "blkdKV_in": _blkd(kv2l.T),
        "wqkT4_in": np.tile(np.ascontiguousarray(wqk.T), (4, 1)),
        "l2o2_in": np.tile(l2o, (2, 1)),
        "bqr_in": np.ascontiguousarray(cab[0:C].reshape(H, HD).T),
        "bkr_in": np.ascontiguousarray(cab[C:2 * C].reshape(H, HD).T),
        "bvr_in": np.ascontiguousarray(cab[2 * C:3 * C].reshape(H, HD).T),
        "q2lb16_in": np.tile(q2lb[None, :], (H, 1)),
        "kvlb16_in": np.tile(kv2lb[None, :], (H, 1)),
        "wqkb512_in": np.tile(wqkb, H)[None, :].copy(),
        "l2ob_in": np.ascontiguousarray(np.tile(l2ob, H).reshape(8, 128).T),
        "cprojb_in": cpb[None, :].copy(),
        "normg128": np.tile(np.asarray(inputs["L0_ffn_norm_g"], f32)[None, :], (128, 1)),
        "normb128": np.tile(np.asarray(inputs["L0_ffn_norm_b"], f32)[None, :], (128, 1)),
        "slotg128": np.tile(np.asarray(inputs["L0_ffn_slot_g"], f32)[None, :], (128, 1)),
        "slotb128": np.tile(np.asarray(inputs["L0_ffn_slot_b"], f32)[None, :], (128, 1)),
        "scale128": np.full((128, 1), np.asarray(inputs["L0_ffn_scale"], f32)[0], f32),
        "semb": np.asarray(inputs["L0_ffn_slot_embeds"], f32).reshape(EL, C),
        "wg2": np.asarray(inputs["L0_ffn_Wg"], f32)[2 * core:2 * core + 2],
        "wu2": np.asarray(inputs["L0_ffn_Wu"], f32)[2 * core:2 * core + 2],
        "wd2": np.asarray(inputs["L0_ffn_Wd"], f32)[2 * core:2 * core + 2],
        "bg2": np.asarray(inputs["L0_ffn_bg"], f32)[2 * core:2 * core + 2],
        "bu2": np.asarray(inputs["L0_ffn_bu"], f32)[2 * core:2 * core + 2],
        "bd2": np.asarray(inputs["L0_ffn_bd"], f32)[2 * core:2 * core + 2],
    })
    return d


def kernel(**inputs):
    nc = build_planned()
    in_maps = [_stage_inputs(inputs, c) for c in range(NCORES)]
    res = run_bass_kernel_spmd(nc, in_maps, core_ids=list(range(NCORES)))
    out = np.stack([np.asarray(res.results[c]["out"]) for c in range(NCORES)], axis=0)
    if _BUILD_OPTS["taps"]:
        kernel._taps = [{k: np.asarray(v) for k, v in r.items()} for r in res.results]
    return out


# version 10
# speedup vs baseline: 1.0212x; 1.0078x over previous
import math
import numpy as np
import ml_dtypes

import concourse.bass as bass
import concourse.mybir as mybir
from concourse.bass_utils import run_bass_kernel_spmd

F32 = mybir.dt.float32
BF16 = mybir.dt.bfloat16
AX = mybir.AxisListType.X
ALU = mybir.AluOpType
ACT = mybir.ActivationFunctionType

T = 2048
C = 1024
H = 16
HD = 64
LAT = 32
G = 4
NT = 16
NB = 4
NCK = 8
E = 16
S = 64
EL = 1024
FF = 2048
NCORES = 8
SCALE = 1.0 / math.sqrt(LAT)
EPS = 1e-5

_BUILD_OPTS = {"taps": (), "phases": 99}


class _Sem:
    def __init__(self, nc, name):
        self.h = nc.alloc_semaphore(name)
        self.n = 0


class MK:
    def __init__(self, nc):
        self.nc = nc
        self.ops = {e: [] for e in "tvsgy"}
        self.sem = {k: _Sem(nc, "sem_" + k) for k in
                    ("t", "v", "s", "g", "gd", "yd", "cc")}
        self.seen = {e: {} for e in "tvsgy"}
        self.sb = {}
        self.regions = []
        self.specs = []
        self.aliases = {}
        self.plan = None
        self.arena_base = None
        self.arena_cap = None

    def on(self, e, fn):
        self.ops[e].append(fn)

    def w(self, e, tok):
        if tok is None:
            return
        sem, n = tok
        if self.seen[e].get(id(sem), -1) >= n:
            return
        self.seen[e][id(sem)] = n
        self.on(e, lambda E, s=sem, v=n: E.wait_ge(s.h, v))

    def do(self, e, fn, sem=None, inc=1):
        if e in ("v", "s") and sem is not None:
            self.w(e, (sem, sem.n))
        if sem is None:
            self.on(e, fn)
            return None
        sem.n += inc
        n = sem.n
        self.on(e, lambda E, f=fn, s=sem, i=inc: f(E).then_inc(s.h, i))
        return (sem, n)

    def tok(self, k):
        s = self.sem[k]
        return (s, s.n)

    def dsem(self, name):
        if name not in self.sem:
            self.sem[name] = _Sem(self.nc, "sem_" + name)
        return self.sem[name]

    def salloc(self, name, shape, dtype, p0, p1, alias=None):
        size = shape[1] * (2 if dtype == BF16 else 4)
        size = (size + 63) & ~63
        if alias is None:
            self.specs.append((name, size, p0, p1))
        else:
            self.aliases[name] = alias
        if self.plan is not None:
            off = self.plan[name] if alias is None else self.plan[alias]
        else:
            off = 0
        self.regions.append((off, size, p0, p1))
        h = self.nc.alloc_sbuf_tensor_at(name, list(shape), dtype,
                                         offset=self.arena_base + off)
        self.sb[name] = h
        return h


def _rope_tables():
    inv = 1.0 / (10000.0 ** (np.arange(0, LAT, 2, dtype=np.float64) / LAT))
    t = np.arange(T, dtype=np.float64)
    fr = np.outer(t, inv)
    cos = np.repeat(np.cos(fr), 2, axis=-1)[:, :LAT]
    sin = np.repeat(np.sin(fr), 2, axis=-1)[:, :LAT]
    return (np.tile(cos.T, (G, 1)).astype(np.float32),
            np.tile(sin.T, (G, 1)).astype(np.float32))


def _const_inputs():
    bf = ml_dtypes.bfloat16
    cosT, sinT = _rope_tables()
    prot = np.zeros((128, 128), dtype=np.float32)
    for blk in range(G):
        for i in range(LAT // 2):
            b = blk * 32
            prot[b + 2 * i, b + 2 * i + 1] = -1.0
            prot[b + 2 * i + 1, b + 2 * i] = 1.0
    tri = (np.arange(128)[None, :] >= np.arange(128)[:, None]).astype(np.float32)
    blk4T = np.zeros((4, 128), dtype=np.float32)
    for j in range(4):
        blk4T[j, 32 * j:32 * (j + 1)] = 1.0
    return {
        "cosT": cosT, "sinT": sinT,
        "protT": prot.T.astype(bf).copy(),
        "trimask": tri.astype(bf),
        "identf": np.eye(128, dtype=np.float32),
        "identb": np.eye(128, dtype=np.float32).astype(ml_dtypes.bfloat16),
        "ones1": np.ones((1, 512), dtype=bf),
        "ones128": np.ones((128, 1), dtype=bf),
        "blk4T": blk4T.astype(bf),
        "ones65": np.ones((65, 512), dtype=bf),
        "blkones4": blk4T.T.astype(bf).copy(),
        "eps128": np.full((128, 1), EPS, dtype=np.float32),
    }


_CONSTS = _const_inputs()


def build(plan=None):
    nc = bass.Bass("TRN2", target_bir_lowering=False)
    DI = lambda n, s, dt=F32: nc.dram_tensor(n, list(s), dt, kind="ExternalInput")

    x_tm_in = DI("x_tm_in", [T, C])
    x_fm_in = DI("x_fm_in", [C, T])
    caw_in = DI("caw", [3 * C, C])
    cprojT_in = DI("cprojT_in", [C, C])
    q2lT_in = DI("q2lT_in", [HD, LAT])
    kv2lT_in = DI("kv2lT_in", [HD, LAT])
    l2o_in = DI("l2o_in", [HD, LAT])
    wqkT_in = DI("wqkT_in", [LAT, LAT])
    bqr_in = DI("bqr_in", [HD, H])
    bkr_in = DI("bkr_in", [HD, H])
    bvr_in = DI("bvr_in", [HD, H])
    q2lb16_in = DI("q2lb16_in", [H, LAT])
    kvlb16_in = DI("kvlb16_in", [H, LAT])
    wqkb512_in = DI("wqkb512_in", [1, 512])
    l2ob_in = DI("l2ob_in", [128, 8])
    cprojb_in = DI("cprojb_in", [1, C])
    cosT_in = DI("cosT", [128, T])
    sinT_in = DI("sinT", [128, T])
    protT_in = DI("protT", [128, 128], BF16)
    tri_in = DI("trimask", [128, 128], BF16)
    identf_in = DI("identf", [128, 128], F32)
    identb_in = DI("identb", [128, 128], BF16)
    ones1_in = DI("ones1", [1, 512], BF16)
    ones128_in = DI("ones128", [128, 1], BF16)
    ones65_in = DI("ones65", [65, 512], BF16)
    blk4T_in = DI("blk4T", [4, 128], BF16)
    blkones4_in = DI("blkones4", [128, 4], BF16)
    blkdQ_in = DI("blkdQ_in", [128, 64])
    blkdKV_in = DI("blkdKV_in", [128, 64])
    wqkT4_in = DI("wqkT4_in", [128, LAT])
    l2o2_in = DI("l2o2_in", [128, LAT])
    eps128_in = DI("eps128", [128, 1], F32)
    normg_in = DI("normg128", [128, C])
    normb_in = DI("normb128", [128, C])
    slotg_in = DI("slotg128", [128, C])
    slotb_in = DI("slotb128", [128, C])
    scale128_in = DI("scale128", [128, 1])
    semb_in = DI("semb", [EL, C])
    wg2_in = DI("wg2", [2, C, FF])
    wu2_in = DI("wu2", [2, C, FF])
    wd2_in = DI("wd2", [2, FF, C])
    bg2_in = DI("bg2", [2, FF])
    bu2_in = DI("bu2", [2, FF])
    bd2_in = DI("bd2", [2, C])

    out_d = nc.dram_tensor("out", [T, C], F32, kind="ExternalOutput")

    ag1_in = nc.dram_tensor("ag1_in", [C, EL], BF16)
    ag1_out = nc.dram_tensor("ag1_out", [NCORES * C, EL], BF16, addr_space="Shared")
    ag2_in = nc.dram_tensor("ag2_in", [EL, C], BF16)
    ag2_out = nc.dram_tensor("ag2_out", [NCORES * EL, C], BF16, addr_space="Shared")
    x2_spill = nc.dram_tensor("x2_spill", [T, C], BF16)

    ARENA = 206336
    arena = nc.alloc_sbuf_tensor("arena", [128, ARENA], mybir.dt.uint8)
    mk = MK(nc)
    mk.plan = plan
    mk.arena_base = nc.lookup_mloc(arena).addr
    mk.arena_cap = ARENA

    A = mk.salloc
    scrA = A("scrA", [128, T], F32, 0, 9)
    scrB = A("scrB", [128, T], F32, 0, 9)
    scrC = A("scrC", [128, T], F32, 4, 5)
    scbA = A("scbA", [128, T], BF16, 0, 9)
    dbgA = A("dbgA", [128, 1024], F32, 0, 2)
    scbB = A("scbB", [128, T], BF16, 0, 9)
    stat = A("stat", [128, 64], F32, 0, 9)
    identf_s = A("identf_s", [128, 128], F32, 0, 9)
    identb_s = A("identb_s", [128, 128], BF16, 0, 9)
    ones1_s = A("ones1_s", [1, 512], BF16, 0, 9)
    ones128_s = A("ones128_s", [128, 1], BF16, 0, 9)
    ones65_s = A("ones65_s", [65, 512], BF16, 0, 10)
    eps_s = A("eps_s", [128, 1], F32, 0, 9)
    x_tm = A("x_tm", [128, NT * C], BF16, 0, 4)
    wpeT = A("wpeT", [128, G * C], BF16, 0, 4)
    bprj_b = A("bprj_b", [1, C], BF16, 0, 4)
    tri_s = A("tri_s", [128, 128], BF16, 0, 3)
    blk4T_s = A("blk4T_s", [4, 128], BF16, 0, 3)
    combineT = A("combineT", [128, NCK * T], BF16, 7, 10)
    recip_tm = A("recip_tm", [128, 16], F32, 7, 10)
    x_fm = A("x_fm", [128, NCK * T], BF16, 0, 2)
    weffq = A("weffq", [128, NCK * 512], BF16, 0, 2)
    weffk = A("weffk", [128, NCK * 512], BF16, 0, 2)
    weffv = A("weffv", [128, NCK * 512], BF16, 0, 2)
    cosbf = A("cosbf", [128, T], BF16, 0, 2)
    sinbf = A("sinbf", [128, T], BF16, 0, 2)
    prot_s = A("prot_s", [128, 128], BF16, 0, 2)
    blkones4_s = A("blkones4_s", [128, 4], BF16, 0, 2)
    wqkT_b = A("wqkT_b", [32, 32], BF16, 0, 2)
    wqkT4 = A("wqkT4", [128, 32], BF16, 0, 2)
    biasq_row = A("biasq_row", [1, 512], BF16, 0, 2)
    biask_row = A("biask_row", [1, 512], BF16, 0, 2)
    biasv_row = A("biasv_row", [1, 512], BF16, 0, 2)
    wqkb_b = A("wqkb_b", [1, 512], BF16, 0, 2)
    QK = A("QK", [128, G * T], BF16, 1, 3)
    KR = A("KR", [128, G * T], BF16, 1, 3)
    vt = A("vt", [128, NT * 512], BF16, 1, 3)
    xn_tm = A("xn_tm", [128, NT * C], BF16, 5, 7)
    rowsB2 = A("rowsB2", [65, 2 * T], BF16, 7, 9)
    l2o_b = A("l2o_b", [64, 32], BF16, 0, 1)
    l2o2 = A("l2o2", [128, 32], BF16, 0, 1)
    nbias_pp = A("nbias_pp", [128, 1], F32, 2, 3)
    ylat = A("ylat", [128, G * T], BF16, 3, 4)
    x2 = A("x2", [128, NT * C], BF16, 4, 5)
    normg_s = A("normg_s", [128, C], F32, 4, 5)
    normb_s = A("normb_s", [128, C], F32, 4, 5)
    slotg_s = A("slotg_s", [128, C], F32, 4, 5)
    slotb_s = A("slotb_s", [128, C], F32, 4, 5)
    scale_s = A("scale_s", [128, 1], F32, 4, 5)
    xn_fm = A("xn_fm", [128, NCK * T], BF16, 5, 6)
    seT = A("seT", [128, NCK * C], BF16, 5, 6)
    logits = A("logits", [128, NT * C], F32, 6, 7)
    rowmax_tm = A("rowmax_tm", [128, 16], F32, 6, 7)
    XeT = A("XeT", [128, NCK * C], BF16, 8, 9)
    xstg = A("xstg", [128, 2 * C], BF16, 8, 8)
    eout_f = A("eout_f", [128, NCK * C], BF16, 9, 10)
    x2b = A("x2b", [128, NT * C], BF16, 9, 10)
    cawb0 = A("cawb0", [128, C], BF16, 0, 0)
    cawb1 = A("cawb1", [128, C], BF16, 0, 0)
    cprojT2 = A("cprojT2", [128, 2 * C], BF16, 0, 0)
    q2lT_f = A("q2lT_f", [64, 32], F32, 0, 0)
    kv2lT_f = A("kv2lT_f", [64, 32], F32, 0, 0)
    q2lT_b = A("q2lT_b", [64, 32], BF16, 0, 0)
    kv2lT_b = A("kv2lT_b", [64, 32], BF16, 0, 0)
    l2o_f = A("l2o_f", [64, 32], F32, 0, 0)
    wqkT_f = A("wqkT_f", [32, 32], F32, 0, 0)
    blkdQ = A("blkdQ", [128, 64], BF16, 0, 2)
    blkdKV = A("blkdKV", [128, 64], BF16, 0, 2)
    blkdQf = A("blkdQf", [128, 64], F32, 0, 0)
    blkdKVf = A("blkdKVf", [128, 64], F32, 0, 0)
    wqkT4f = A("wqkT4f", [128, LAT], F32, 0, 0)
    l2o2f = A("l2o2f", [128, LAT], F32, 0, 0)
    bxr_f = A("bxr_f", [64, 16], F32, 0, 0)
    bxr_b = A("bxr_b", [64, 16], BF16, 0, 0)
    bx16c = A("bx16c", [16, 32], F32, 0, 0)
    bx16_f = A("bx16_f", [16, 32], F32, 0, 0)
    bx16b = A("bx16b", [16, 32], BF16, 0, 0)
    wqkb_f = A("wqkb_f", [1, 512], F32, 0, 0)
    l2ob_f = A("l2ob_f", [128, 8], F32, 0, 0)
    l2ob_b = A("l2ob_b", [128, 8], BF16, 0, 0)
    cprojb_f = A("cprojb_f", [1, C], F32, 0, 0)
    bprj_f = A("bprj_f", [1, C], F32, 0, 0)
    nb44 = A("nb44", [4, 4], F32, 2, 2)
    nbrow = A("nbrow", [1, 16], F32, 2, 2)
    nbrow2 = A("nbrow2", [1, 1], BF16, 2, 2)
    PT = A("PT", [128, 2 * T], BF16, 3, 3)
    sums4 = A("sums4", [4, 512], F32, 3, 3)
    rec4 = A("rec4", [4, 512], F32, 3, 3)
    rec4b = A("rec4b", [4, 512], BF16, 3, 3)
    rbbf = A("rbbf", [128, 512], BF16, 3, 3)
    dispatchN = A("dispatchN", [128, NT * C], BF16, 7, 7)
    lgb = A("lgb", [128, 2 * NT * C], BF16, 7, 7, alias="logits")
    scrBb = A("scrBb", [128, T], BF16, 7, 7, alias="scrB")
    nrm_s = A("nrm_s", [128, 16], F32, 7, 7)
    rsum_tm = A("rsum_tm", [128, 16], F32, 7, 7)
    rsL_s = A("rsL_s", [128, 16], F32, 7, 7)
    cmax_s = A("cmax_s", [128, 8], F32, 7, 7)
    ncmax_s = A("ncmax_s", [128, 8], F32, 7, 7)
    cs2_s = A("cs2_s", [128, 8], F32, 7, 7)
    rcl_s = A("rcl_s", [128, 8], F32, 7, 7)
    rcT_s = A("rcT_s", [128, 8], F32, 7, 7)
    wgA = A("wgA", [128, NCK * C], BF16, 8, 8)
    wgB = A("wgB", [128, NCK * C], BF16, 8, 8)
    wuA = A("wuA", [128, NCK * C], BF16, 8, 8)
    wuB = A("wuB", [128, NCK * C], BF16, 8, 8)
    wdS = A("wdS", [128, 16 * 512], BF16, 8, 8)
    hfm = A("hfm", [128, 16 * 512], BF16, 8, 8)
    outbuf = A("outbuf", [128, NT * C], F32, 10, 10)

    bg_row = rowsB2[0:1, 0:2 * FF]
    bu_row = rowsB2[32:33, 0:2 * FF]
    bd_row = rowsB2[64:65, 0:2 * C]

    p4 = nc.alloc_psum_tensor("p4", [128, 2048], F32)
    pA = nc.alloc_psum_tensor("pA", [128, 512], F32)
    pB = nc.alloc_psum_tensor("pB", [128, 512], F32)
    pC = nc.alloc_psum_tensor("pC", [128, 512], F32)
    pD = nc.alloc_psum_tensor("pD", [128, 512], F32)

    ST, SV, SS = mk.sem["t"], mk.sem["v"], mk.sem["s"]
    GD, YD = mk.sem["gd"], mk.sem["yd"]

    x_fm_v = x_fm.rearrange("p (c t) -> p c t", c=NCK)
    x_tm_v = x_tm.rearrange("p (i c) -> p i c", i=NT)
    weffq_v = weffq.rearrange("p (c x) -> p c x", c=NCK)
    weffk_v = weffk.rearrange("p (c x) -> p c x", c=NCK)
    weffv_v = weffv.rearrange("p (c x) -> p c x", c=NCK)
    wpeT_v = wpeT.rearrange("p (g o) -> p g o", g=G)
    QK_v = QK.rearrange("p (g t) -> p g t", g=G)
    KR_v = KR.rearrange("p (g t) -> p g t", g=G)
    vt_v = vt.rearrange("p (i x) -> p i x", i=NT)
    ylat_v = ylat.rearrange("p (g t) -> p g t", g=G)
    PT_v = PT.rearrange("p (u t) -> p u t", u=2)
    x2_v = x2.rearrange("p (i c) -> p i c", i=NT)
    xn_tm_v = xn_tm.rearrange("p (i c) -> p i c", i=NT)
    xn_fm_v = xn_fm.rearrange("p (c t) -> p c t", c=NCK)
    seT_v = seT.rearrange("p (c e) -> p c e", c=NCK)
    logits_v = logits.rearrange("p (i c) -> p i c", i=NT)
    combineT_v = combineT.rearrange("p (c t) -> p c t", c=NCK)
    dnT_v = dispatchN.rearrange("p (e t) -> p e t", e=NCK)
    lgb_v = lgb.rearrange("p (i c) -> p i c", i=2 * NT)
    XeT_v = XeT.rearrange("p (c e) -> p c e", c=NCK)
    wgA_v = wgA.rearrange("p (c f) -> p c f", c=NCK)
    wgB_v = wgB.rearrange("p (c f) -> p c f", c=NCK)
    wuA_v = wuA.rearrange("p (c f) -> p c f", c=NCK)
    wuB_v = wuB.rearrange("p (c f) -> p c f", c=NCK)
    wdS_v = wdS.rearrange("p (c o) -> p c o", c=16)
    hfm_v = hfm.rearrange("p (c x) -> p c x", c=16)
    eout_f_v = eout_f.rearrange("p (c o) -> p c o", c=NCK)
    x2b_v = x2b.rearrange("p (i c) -> p i c", i=NT)
    outbuf_v = outbuf.rearrange("p (i c) -> p i c", i=NT)
    p4q = p4.rearrange("p (q x) -> p q x", q=16)
    p4j = p4.rearrange("p (j x) -> p j x", j=4)

    ldtoks = {}

    def gload(name, dst_ap, src_ap, sem=None):
        s_ = mk.dsem(sem or ("g_" + name))
        ldtoks[name] = mk.do("g", lambda E, d=dst_ap, s=src_ap: E.dma_start(d, s), s_, 16)

    def yload(name, dst_ap, src_ap, sem=None):
        s_ = mk.dsem(sem or ("y_" + name))
        ldtoks[name] = mk.do("y", lambda E, d=dst_ap, s=src_ap: E.dma_start(d, s), s_, 16)
        return ldtoks[name]

    def yload_Y0(name, dst_ap, src_ap):
        return yload(name, dst_ap, src_ap, sem="Y0")

    gload("x_fm", x_fm_v[:, :, :], x_fm_in.rearrange("(c p) t -> p c t", p=128), sem="G0")
    gload("x_tm", x_tm_v[:, :, :], x_tm_in.rearrange("(i p) c -> p i c", p=128), sem="G0")
    gload("cos", cosbf[:, :], cosT_in[:, :], sem="G0")
    gload("sin", sinbf[:, :], sinT_in[:, :], sem="G0")
    yload_Y0("identf", identf_s[:, :], identf_in[:, :])
    yload_Y0("identb", identb_s[:, :], identb_in[:, :])
    yload_Y0("tri", tri_s[:, :], tri_in[:, :])
    yload_Y0("prot", prot_s[:, :], protT_in[:, :])
    yload_Y0("ones1", ones1_s[:, :], ones1_in[:, :])
    yload_Y0("ones128", ones128_s[:, :], ones128_in[:, :])
    yload_Y0("ones65", ones65_s[:, :], ones65_in[:, :])
    yload_Y0("blk4T", blk4T_s[:, :], blk4T_in[:, :])
    yload_Y0("blkones4", blkones4_s[:, :], blkones4_in[:, :])
    yload_Y0("eps", eps_s[:, :], eps128_in[:, :])
    yload_Y0("q2lT", q2lT_f[:, :], q2lT_in[:, :])
    yload_Y0("kv2lT", kv2lT_f[:, :], kv2lT_in[:, :])
    yload_Y0("wqkb", wqkb_f[:, :], wqkb512_in[:, :])
    yload_Y0("l2ob", l2ob_f[:, :], l2ob_in[:, :])
    yload_Y0("cprojb", cprojb_f[:, :], cprojb_in[:, :])
    yload_Y0("blkdQ", blkdQf[:, :], blkdQ_in[:, :])
    yload_Y0("blkdKV", blkdKVf[:, :], blkdKV_in[:, :])
    yload_Y0("wqkT4", wqkT4f[:, :], wqkT4_in[:, :])
    yload_Y0("l2o2", l2o2f[:, :], l2o2_in[:, :])

    TOK_G0 = mk.tok("G0")
    TOK_Y0 = mk.tok("Y0")
    for _n in list(ldtoks):
        if mk.sem.get("G0") and ldtoks[_n] and ldtoks[_n][0] is mk.sem["G0"]:
            ldtoks[_n] = TOK_G0
        if mk.sem.get("Y0") and ldtoks[_n] and ldtoks[_n][0] is mk.sem["Y0"]:
            ldtoks[_n] = TOK_Y0
    tok_smalls = None
    for nm, (dst, src) in {
        "q2lT": (q2lT_b, q2lT_f), "kv2lT": (kv2lT_b, kv2lT_f),
        "wqkb": (wqkb_b, wqkb_f), "l2ob": (l2ob_b, l2ob_f),
        "blkdQ": (blkdQ, blkdQf), "blkdKV": (blkdKV, blkdKVf),
        "wqkT4": (wqkT4, wqkT4f), "l2o2": (l2o2, l2o2f),
    }.items():
        mk.w("v", ldtoks[nm])
        tok_smalls = mk.do("v", lambda E, d=dst, s=src:
                           E.tensor_copy(d[:, :], s[:, :]), SV)
    tok_blkd = tok_smalls
    tok_l2o2 = tok_smalls
    tok_wqk4 = tok_smalls

    parts = [(0, blkdQ, weffq_v), (C, blkdKV, weffk_v), (2 * C, blkdKV, weffv_v)]
    caw_bufs = [cawb0, cawb1]
    caw_tok = [None, None]
    psQ = [pA, pB]
    ps_tok = [None, None]
    for pi, (roff, blkd, wv) in enumerate(parts):
        for p in range(8):
            ii = pi * 8 + p
            par = ii % 2
            buf = caw_bufs[par]
            mk.w("g", caw_tok[par])
            ct = mk.do("g", lambda E, b=buf, r=roff + p * 128:
                       E.dma_start(b[:, :], caw_in[r:r + 128, :]),
                       mk.dsem(f"CAW{par}"), 16)
            ps = psQ[par]
            mk.w("t", ct)
            mk.w("t", tok_blkd)
            mk.w("t", ps_tok[par])
            tk = None
            for j in range(NCK):
                tk = mk.do("t", lambda E, ps_=ps, b=buf, bd=blkd, j_=j:
                           E.matmul(ps_[:, j_ * 64:(j_ + 1) * 64],
                                    b[:, j_ * 128:(j_ + 1) * 128], bd[:, :],
                                    start=True, stop=True), ST)
            caw_tok[par] = tk
            mk.w("v", tk)
            dst = wv[:, :, p * 64:(p + 1) * 64]
            src = ps.rearrange("p (j x) -> p j x", j=8)
            ps_tok[par] = mk.do("v", lambda E, d=dst, s=src:
                                E.tensor_copy(d, s[:, :, :]), SV)

    prev_cast = None
    for brow, bin_, b16in, rhsw in ((biasq_row, bqr_in, q2lb16_in, q2lT_b),
                                    (biask_row, bkr_in, kvlb16_in, kv2lT_b),
                                    (biasv_row, bvr_in, kvlb16_in, kv2lT_b)):
        mk.w("y", prev_cast)
        mk.do("y", lambda E, b=bin_: E.dma_start(bxr_f[:, :], b[:, :]),
              mk.dsem("YB"), 16)
        t2 = mk.do("y", lambda E, b=b16in: E.dma_start(bx16c[:, :], b[:, :]),
                   mk.dsem("YB"), 16)
        mk.w("v", t2)
        t3 = mk.do("v", lambda E: E.tensor_copy(bxr_b[:, :], bxr_f[:, :]), SV)
        mk.w("t", t3)
        mk.w("t", ps_tok[0])
        t4 = mk.do("t", lambda E, r=rhsw: E.matmul(pA[0:16, 0:32], bxr_b[:, :],
                                                   r[:, :], start=True, stop=True), ST)
        mk.w("v", t4)
        t5 = mk.do("v", lambda E: E.tensor_tensor(bx16_f[:, :], pA[0:16, 0:32],
                                                  bx16c[:, :], ALU.add), SV)
        ps_tok[0] = t5
        t6 = mk.do("v", lambda E: E.tensor_copy(bx16b[:, :], bx16_f[:, :]), SV)
        mk.w("g", t6)
        t7 = mk.do("g", lambda E, d=brow: E.dma_start(d[0:1, :], bx16b[:, :]),
                   mk.dsem("BRW"), 16)
        mk.w("v", t7)
        prev_cast = t5
        ldtoks["brow"] = t7

    cp2 = cprojT2.rearrange("p (u o) -> p u o", u=2)
    wpe_prev_mm = None
    wpe_done = None
    for g in range(G):
        mk.w("g", wpe_prev_mm)
        ct = mk.do("g", lambda E, g_=g: E.dma_start(
            cp2[:, :, :],
            cprojT_in.rearrange("(u p) o -> p u o", p=128)[:, 2 * g_:2 * g_ + 2, :]),
            mk.dsem("CPJ"), 16)
        mk.w("t", ct)
        mk.w("t", tok_l2o2)
        mk.w("t", wpe_done)
        tk = None
        for j in range(G):
            for hb in range(2):
                tk = mk.do("t", lambda E, j_=j, hb_=hb: E.matmul(
                    p4[32 * j_:32 * (j_ + 1), hb_ * 512:(hb_ + 1) * 512],
                    l2o2[(j_ % 2) * 64:(j_ % 2) * 64 + 64, :],
                    cp2[(j_ % 2) * 64:(j_ % 2) * 64 + 64, j_ // 2,
                        hb_ * 512:(hb_ + 1) * 512],
                    start=True, stop=True,
                    tile_position=((j_ % 2) * 64, 32 * j_)), ST)
        for u in range(2):
            for hb in range(2):
                tk2 = mk.do("t", lambda E, u_=u, g_=g, hb_=hb: E.matmul(
                    p4[0:1, 1024 + hb_ * 512:1024 + (hb_ + 1) * 512],
                    l2ob_b[:, 2 * g_ + u_:2 * g_ + u_ + 1],
                    cp2[:, u_, hb_ * 512:(hb_ + 1) * 512],
                    start=(2 * g_ + u_ == 0), stop=(2 * g_ + u_ == 7)), ST)
        wpe_prev_mm = tk2
        mk.w("v", tk)
        wpe_done = mk.do("v", lambda E, g_=g: E.tensor_copy(
            wpeT_v[:, g_, :], p4[:, 0:1024]), SV)
    mk.w("v", wpe_prev_mm)
    mk.w("v", ldtoks["cprojb"])
    mk.do("v", lambda E: E.tensor_tensor(bprj_f[0:1, :], p4[0:1, 1024:2048],
                                         cprojb_f[0:1, :], ALU.add), SV)
    tok_bprj = mk.do("v", lambda E: E.tensor_copy(bprj_b[0:1, :], bprj_f[0:1, :]), SV)

    if _BUILD_OPTS["phases"] < 1:
        return _finish(mk, nc, out_d, outbuf_v)

    mk.w("t", ldtoks["x_fm"])
    mk.w("t", ldtoks["ones1"])
    mk.w("t", ldtoks["brow"])
    mk.w("t", tok_wqk4)
    mk.w("t", tok_bprj)
    mk.w("v", ldtoks["cos"])
    mk.w("v", ldtoks["sin"])

    tA = tB = tC = None
    prev_rot = None
    prev_wqk = None
    for g in range(G):
        for side in ("q", "k"):
            wv = weffq_v if side == "q" else weffk_v
            brow = biasq_row if side == "q" else biask_row
            for b in range(NB):
                mk.w("t", tA)
                for j in range(NCK):
                    mk.do("t", lambda E, w_=wv, j_=j, g_=g, b_=b: E.matmul(
                        pA[:, :], w_[:, j_, g_ * 128:(g_ + 1) * 128],
                        x_fm_v[:, j_, b_ * 512:(b_ + 1) * 512],
                        start=(j_ == 0), stop=False), ST)
                tk = mk.do("t", lambda E, br=brow, g_=g: E.matmul(
                    pA[:, :], br[0:1, g_ * 128:(g_ + 1) * 128],
                    ones1_s[0:1, 0:512], start=False, stop=True), ST)
                mk.w("v", tk)
                mk.w("v", prev_rot)
                t_qs = mk.do("v", lambda E, b_=b: E.tensor_tensor(
                    scbA[:, 0:512], pA[:, :],
                    sinbf[:, b_ * 512:(b_ + 1) * 512], ALU.mult), SV)
                mk.w("t", t_qs)
                mk.w("t", tB)
                t_rot = mk.do("t", lambda E: E.matmul(
                    pB[:, :], prot_s[:, :], scbA[:, 0:512],
                    start=True, stop=True), ST)
                prev_rot = t_rot
                mk.w("v", t_rot)
                if _BUILD_OPTS.get("dbg") and g == 0 and side == "k" and b == 0:
                    mk.do("v", lambda E: E.tensor_copy(dbgA[:, 0:512], scbA[:, 0:512]), SV)
                    mk.do("v", lambda E: E.tensor_copy(dbgA[:, 512:1024], pB[:, :]), SV)
                t_qc = mk.do("v", lambda E, b_=b: E.tensor_tensor(
                    scbB[:, 0:512], pA[:, :],
                    cosbf[:, b_ * 512:(b_ + 1) * 512], ALU.mult), SV)
                tA = t_qc
                if side == "q":
                    mk.w("v", prev_wqk)
                    t_qr = mk.do("v", lambda E: E.tensor_tensor(
                        scbA[:, 512:1024], pB[:, :], scbB[:, 0:512], ALU.add), SV)
                    tB = t_qr
                    mk.w("t", t_qr)
                    mk.w("t", tC)
                    for j in range(4):
                        mk.do("t", lambda E, j_=j: E.matmul(
                            pC[32 * j_:32 * (j_ + 1), :], wqkT4[32 * j_:32 * (j_ + 1), :],
                            scbA[32 * j_:32 * (j_ + 1), 512:1024],
                            start=True, stop=False,
                            tile_position=(32 * j_, 32 * j_)), ST)
                    tk2 = mk.do("t", lambda E, g_=g: E.matmul(
                        pC[:, :], wqkb_b[0:1, g_ * 128:(g_ + 1) * 128],
                        ones1_s[0:1, 0:512], start=False, stop=True), ST)
                    prev_wqk = tk2
                    mk.w("v", tk2)
                    tC = mk.do("v", lambda E, g_=g, b_=b: E.tensor_copy(
                        QK_v[:, g_, b_ * 512:(b_ + 1) * 512], pC[:, :]), SV)
                else:
                    tB = mk.do("v", lambda E, g_=g, b_=b: E.tensor_tensor(
                        KR_v[:, g_, b_ * 512:(b_ + 1) * 512], pB[:, :],
                        scbB[:, 0:512], ALU.add), SV)
    for tc in range(NT):
        par = tc % 2
        ps = pA if par == 0 else pB
        mk.w("t", tA if par == 0 else tB)
        for j in range(NCK):
            mk.do("t", lambda E, ps_=ps, j_=j, tc_=tc: E.matmul(
                ps_[:, :], x_fm_v[:, j_, tc_ * 128:(tc_ + 1) * 128],
                weffv_v[:, j_, :], start=(j_ == 0), stop=False), ST)
        tk = mk.do("t", lambda E, ps_=ps: E.matmul(
            ps_[:, :], ones1_s[0:1, 0:128], biasv_row[0:1, :],
            start=False, stop=True), ST)
        mk.w("v", tk)
        ev = mk.do("v", lambda E, ps_=ps, tc_=tc: E.tensor_copy(
            vt_v[:, tc_, :], ps_[:, :]), SV)
        if par == 0:
            tA = ev
        else:
            tB = ev

    if _BUILD_OPTS["phases"] < 2:
        return _finish(mk, nc, out_d, outbuf_v)

    sq_tok = None
    for g in range(G):
        for side in ("q", "k"):
            src = QK_v if side == "q" else KR_v
            col = 56 if side == "q" else 57
            mk.w("v", mk.tok("v"))
            sq_tok = mk.do("v", lambda E, g_=g, s_=src: E.scalar_tensor_tensor(
                scbA[:, :], s_[:, g_, :], 1.0, s_[:, g_, :],
                ALU.mult, ALU.mult), SV)
            mk.w("t", sq_tok)
            tk = None
            for b in range(NB):
                tk = mk.do("t", lambda E, b_=b: E.matmul(
                    p4[0:4, b_ * 512:(b_ + 1) * 512], blkones4_s[:, :],
                    scbA[:, b_ * 512:(b_ + 1) * 512], start=True, stop=True), ST)
            mk.w("v", tk)
            mk.do("v", lambda E, c_=col: E.reduce_max(
                stat[0:4, c_:c_ + 1], p4[0:4, :], axis=AX), SV)
        mk.do("v", lambda E, g_=g: E.tensor_tensor(
            nb44[0:4, g_:g_ + 1], stat[0:4, 56:57], stat[0:4, 57:58], ALU.add), SV)
    mk.w("g", mk.tok("v"))
    gt = None
    for g in range(G):
        gt = mk.do("g", lambda E, g_=g: E.dma_start(
            nbrow[0:1, 4 * g_:4 * g_ + 4], nb44[0:4, g_:g_ + 1]),
            mk.dsem("NB4"), 16)
    mk.w("v", gt)
    mk.do("v", lambda E: E.reduce_max(stat[0:1, 58:59], nbrow[0:1, :], axis=AX), SV)
    mk.do("v", lambda E: E.tensor_scalar_mul(stat[0:1, 59:60], stat[0:1, 58:59],
                                             -0.5 * SCALE), SV)
    tkc = mk.do("v", lambda E: E.tensor_copy(nbrow2[0:1, 0:1], stat[0:1, 59:60]), SV)
    mk.w("t", tkc)
    tkm = mk.do("t", lambda E: E.matmul(pA[:, 0:1], ones1_s[0:1, 0:128],
                                        nbrow2[0:1, 0:1], start=True, stop=True), ST)
    mk.w("v", tkm)
    tk_nb = mk.do("v", lambda E: E.tensor_copy(nbias_pp[:, :], pA[:, 0:1]), SV)
    tA = tk_nb

    if _BUILD_OPTS["phases"] < 3:
        return _finish(mk, nc, out_d, outbuf_v)

    mk.w("s", tk_nb)
    prev_tkb = None
    exp_toks = {}
    sums_toks = {}
    PT_v4 = PT.rearrange("p (s x) -> p s x", s=4)
    att_banks = [pA, pD]
    tile_y = {-2: tA, -1: None}
    st3 = {"p": 0, "tk_rb": None, "tk_s4": None, "prev_tkb": None,
           "prev_gsm": None, "evac": None, "defer": None, "y_last": None}

    def _att_consume(pr):
        (pi, g_, ci_, hp_, off_, w_, sl_, n_, diag_, etok_, mtok_, bank_,
         first_, yt_, se_) = pr
        mk.w("t", mtok_ if diag_ else etok_)
        if first_:
            mk.w("t", yt_)
        for j in (2 * hp_, 2 * hp_ + 1):
            mk.do("t", lambda E, g2=g_, j_=j, jj_=j - 2 * hp_, c2=ci_, o2=off_,
                  w2=w_, s2=sl_, n2=n_, bk=bank_:
                  E.matmul(bk[32 * j_:32 * (j_ + 1), o2:o2 + w2],
                           vt_v[:, c2, (4 * g2 + j_) * 32:(4 * g2 + j_ + 1) * 32],
                           PT_v4[:, s2, jj_ * 512 + o2:jj_ * 512 + o2 + w2],
                           start=(c2 == 0), stop=(c2 == n2 - 1),
                           tile_position=(0, 32 * j_)), ST)
        if first_:
            mk.w("t", se_)
        pu = None
        for j in (2 * hp_, 2 * hp_ + 1):
            pu = mk.do("t", lambda E, j_=j, jj_=j - 2 * hp_, c2=ci_, o2=off_,
                       w2=w_, s2=sl_, n2=n_:
                       E.matmul(pB[32 * j_:32 * j_ + 1, o2:o2 + w2],
                                ones128_s[:, 0:1],
                                PT_v4[:, s2, jj_ * 512 + o2:jj_ * 512 + o2 + w2],
                                start=(c2 == 0), stop=(c2 == n2 - 1),
                                tile_position=(0, 32 * j_)), ST)
        sums_toks[pi] = pu

    pend = None
    ti = 0
    for g in range(G):
        for b in range(NB):
            nch = 4 * (b + 1)
            first_pidx = st3["p"]
            bank = att_banks[ti % 2]
            for ci in range(nch):
                off = (ci - 4 * b) * 128 if ci >= 4 * b else 0
                w = 512 - off
                diag = (ci >= 4 * b)
                for hp in range(2):
                    p_idx = st3["p"]
                    par2 = p_idx % 2
                    slot = p_idx % 4
                    mk.w("t", exp_toks.get(p_idx - 2))
                    stok = None
                    for j in (2 * hp, 2 * hp + 1):
                        stok = mk.do("t", lambda E, g_=g, j_=j, jj_=j - 2 * hp, ci_=ci,
                                     off_=off, w_=w, b_=b, pr_=par2:
                                     E.matmul(p4j[:, 2 * pr_ + jj_, off_:off_ + w_],
                                              KR_v[32 * j_:32 * (j_ + 1), g_,
                                                   ci_ * 128:(ci_ + 1) * 128],
                                              QK_v[32 * j_:32 * (j_ + 1), g_,
                                                   b_ * 512 + off_:b_ * 512 + off_ + w_],
                                              start=True, stop=True,
                                              tile_position=(32 * j_, 0)), ST)
                    if p_idx == first_pidx + 2 and st3["defer"] is not None:
                        st3["defer"]()
                        st3["defer"] = None
                    mk.w("s", stok)
                    mk.w("s", sums_toks.get(p_idx - 4))
                    pin = p4j[:, 2 * par2:2 * par2 + 2, off:off + w]
                    pout = PT_v4[:, slot, :].rearrange(
                        "p (j x) -> p j x", j=2)[:, :, off:off + w]
                    etok = mk.do("s", lambda E, pi=pin, po=pout: E.activation(
                        po, pi, ACT.Exp, bias=nbias_pp[:, 0:1], scale=SCALE), SS)
                    exp_toks[p_idx] = etok
                    mtok = None
                    if diag:
                        mk.w("v", etok)
                        pm = PT_v4[:, slot, :].rearrange(
                            "p (j x) -> p j x", j=2)[:, :, off:off + 128]
                        mtok = mk.do("v", lambda E, pm_=pm: E.tensor_tensor(
                            pm_, pm_, tri_s[:, None, :].broadcast_to([128, 2, 128]),
                            ALU.mult), SV)
                    if pend is not None:
                        _att_consume(pend)
                    pend = (p_idx, g, ci, hp, off, w, slot, nch, diag, etok, mtok,
                            bank, p_idx == first_pidx, tile_y.get(ti - 2),
                            st3["evac"])
                    st3["p"] = p_idx + 1
            _att_consume(pend)
            pend = None
            ptail = mk.tok("t")
            mk.w("v", ptail)
            mk.w("v", st3["prev_gsm"])
            ev = mk.do("v", lambda E: E.tensor_copy(
                scrB[0:97, 0:512], pB[0:97, :]), SV)
            st3["evac"] = ev
            mk.w("g", ev)
            mk.w("g", st3["tk_s4"])
            gsm = mk.do("g", lambda E: E.dma_start(
                sums4[:, :], bass.AP(scrB, 0, [[32 * T, 4], [1, 512]])),
                mk.dsem("SM4"), 16)
            st3["prev_gsm"] = gsm

            def _defer(g_=g, b_=b, ti_=ti, gsm_=gsm, bank_=bank):
                mk.w("s", gsm_)
                mk.w("s", st3["prev_tkb"])
                mk.do("s", lambda E: E.activation(rec4[:, :], sums4[:, :],
                                                  ACT.Ln), SS)
                tk_s4 = mk.do("s", lambda E: E.activation(
                    rec4b[:, :], rec4[:, :], ACT.Exp, scale=-1.0), SS)
                st3["tk_s4"] = tk_s4
                mk.w("t", tk_s4)
                mk.w("t", st3["tk_rb"])
                tkb = mk.do("t", lambda E: E.matmul(
                    pC[:, :], blk4T_s[:, :], rec4b[:, :],
                    start=True, stop=True), ST)
                st3["prev_tkb"] = tkb
                mk.w("v", tkb)
                st3["tk_rb"] = mk.do("v", lambda E: E.tensor_copy(
                    rbbf[:, :], pC[:, :]), SV)
                yt = mk.do("v", lambda E, g2=g_, b2=b_, bk=bank_: E.tensor_tensor(
                    ylat_v[:, g2, b2 * 512:(b2 + 1) * 512], bk[:, :], rbbf[:, :],
                    ALU.mult), SV)
                tile_y[ti_] = yt
                st3["y_last"] = yt
            st3["defer"] = _defer
            ti += 1
    if st3["defer"] is not None:
        st3["defer"]()
        st3["defer"] = None
    tk_y = st3["y_last"]

    if _BUILD_OPTS["phases"] < 4:
        return _finish(mk, nc, out_d, outbuf_v)

    mk.w("y", mk.tok("t"))
    yload("normg", normg_s[:, :], normg_in[:, :], sem="YN")
    yload("normb", normb_s[:, :], normb_in[:, :], sem="YN")
    yload("slotg", slotg_s[:, :], slotg_in[:, :], sem="YN")
    yload("slotb", slotb_s[:, :], slotb_in[:, :], sem="YN")
    yload("scl", scale_s[:, :], scale128_in[:, :], sem="YN")
    TOK_YN = mk.tok("YN")
    for _n in ("normg", "normb", "slotg", "slotb", "scl"):
        ldtoks[_n] = TOK_YN
    mk.w("t", tk_y)
    mk.w("v", ldtoks["x_tm"])

    h1s3 = [scrA, scrB, scrC]
    bases3 = [0, 8, 48]
    mm_gate = [None, None]
    A_toks = {}
    B_toks = {}
    x2_done = [None] * NT
    for step in range(NT + 2):
        if step < NT:
            i = step
            par = i % 2
            off0 = 0 if par == 0 else 1024
            mk.w("t", mm_gate[par])
            tk = None
            for g in range(G):
                for hb in range(2):
                    tk = mk.do("t", lambda E, g_=g, hb_=hb, i_=i, o=off0: E.matmul(
                        p4[:, o + hb_ * 512:o + (hb_ + 1) * 512],
                        ylat_v[:, g_, i_ * 128:(i_ + 1) * 128],
                        wpeT_v[:, g_, hb_ * 512:(hb_ + 1) * 512],
                        start=(g_ == 0), stop=False), ST)
            for hb in range(2):
                tk = mk.do("t", lambda E, hb_=hb, o=off0: E.matmul(
                    p4[:, o + hb_ * 512:o + (hb_ + 1) * 512],
                    ones1_s[0:1, 0:128], bprj_b[0:1, hb_ * 512:(hb_ + 1) * 512],
                    start=False, stop=True), ST)
            h1 = h1s3[i % 3]
            base = bases3[i % 3]
            mk.w("v", tk)
            atok = mk.do("v", lambda E, i_=i, h_=h1, b_=base, o=off0:
                         E.scalar_tensor_tensor(
                             h_[:, 0:1024], p4[:, o:o + 1024], 1.0,
                             x_tm_v[:, i_, :], ALU.mult, ALU.add,
                             accum_out=stat[:, b_:b_ + 1]), SV)
            mm_gate[par] = atok
            mk.do("v", lambda E, b_=base: E.tensor_tensor(
                stat[:, b_ + 2:b_ + 3], stat[:, b_:b_ + 1],
                stat[:, b_:b_ + 1], ALU.mult), SV)
            mk.do("v", lambda E, b_=base: E.tensor_scalar_mul(
                stat[:, b_ + 6:b_ + 7], stat[:, b_:b_ + 1], 1.0 / C), SV)
            mk.w("s", atok)
            A_toks[i] = mk.do("s", lambda E, h_=h1, b_=base: E.activation(
                h_[:, 1024:2048], h_[:, 0:1024], ACT.Square,
                accum_out=stat[:, b_ + 1:b_ + 2]), SS)
        if 0 <= step - 1 < NT:
            j = step - 1
            base = bases3[j % 3]
            mk.w("v", A_toks[j])
            vtok = mk.do("v", lambda E, b_=base: E.scalar_tensor_tensor(
                stat[:, b_ + 3:b_ + 4], stat[:, b_ + 2:b_ + 3], -1.0 / C,
                stat[:, b_ + 1:b_ + 2], ALU.mult, ALU.add), SV)
            mk.w("s", vtok)
            mk.do("s", lambda E, b_=base: E.activation(
                stat[:, b_ + 4:b_ + 5], stat[:, b_ + 3:b_ + 4], ACT.Ln,
                bias=eps_s[:, 0:1], scale=1.0 / C), SS)
            B_toks[j] = mk.do("s", lambda E, b_=base: E.activation(
                stat[:, b_ + 5:b_ + 6], stat[:, b_ + 4:b_ + 5], ACT.Exp,
                scale=-0.5), SS)
        if 0 <= step - 2 < NT:
            k = step - 2
            h1 = h1s3[k % 3]
            base = bases3[k % 3]
            mk.w("v", B_toks[k])
            x2_done[k] = mk.do("v", lambda E, k_=k, h_=h1, b_=base: E.tensor_scalar(
                x2_v[:, k_, :], h_[:, 0:1024], stat[:, b_ + 6:b_ + 7],
                stat[:, b_ + 5:b_ + 6], ALU.subtract, ALU.mult), SV)
    mk.w("g", x2_done[-1])
    mk.w("g", x2_done[-2])
    spill_tok = mk.do("g", lambda E: E.dma_start(
        x2_spill.rearrange("(i p) c -> p i c", p=128), x2_v[:, :, :]),
        mk.dsem("SPL"), 16)

    if _BUILD_OPTS["phases"] < 5:
        return _finish(mk, nc, out_d, outbuf_v)

    tr_evac = [None] * 4
    xf_free = [None, None]
    xn_f32s = [scrA, scrB]
    b5 = [16, 20, 24, 28]
    A5 = {}
    B5 = {}
    for step in range(NT + 1):
        if step < NT:
            i = step
            base = b5[i % 4]
            xf = xn_f32s[i % 2]
            mk.w("s", x2_done[i])
            mk.w("s", B5.get(i - 3))
            mk.do("s", lambda E, i_=i, x_=xf, b_=base: E.activation(
                x_[:, 1024:2048], x2_v[:, i_, :], ACT.Square,
                accum_out=stat[:, b_:b_ + 1]), SS)
            mk.do("s", lambda E, b_=base: E.activation(
                stat[:, b_ + 2:b_ + 3], stat[:, b_:b_ + 1], ACT.Ln,
                bias=eps_s[:, 0:1], scale=1.0 / C), SS)
            A5[i] = mk.do("s", lambda E, b_=base: E.activation(
                stat[:, b_ + 3:b_ + 4], stat[:, b_ + 2:b_ + 3], ACT.Exp,
                scale=-0.5), SS)
        if 0 <= step - 1 < NT:
            j = step - 1
            base = b5[j % 4]
            mk.w("v", A5[j])
            mk.w("v", ldtoks["normg"])
            mk.w("v", ldtoks["normb"])
            mk.do("v", lambda E, i_=j, b_=base: E.scalar_tensor_tensor(
                xn_tm_v[:, i_, :], x2_v[:, i_, :], stat[:, b_ + 3:b_ + 4],
                normg_s[:, :], ALU.mult, ALU.mult), SV)
            ftok = mk.do("v", lambda E, i_=j: E.tensor_tensor(
                xn_tm_v[:, i_, :], xn_tm_v[:, i_, :], normb_s[:, :], ALU.add), SV)
            B5[j] = ftok
            ttok = None
            for half in range(2):
                bank = (j * 2 + half) % 4
                mk.w("t", ftok)
                mk.w("t", tr_evac[bank])
                for q in range(4):
                    dc = half * 4 + q
                    ttok = mk.do("t", lambda E, i_=j, dc_=dc, bank_=bank, q_=q:
                                 E.matmul(p4q[:, bank_ * 4 + q_, :],
                                          xn_tm_v[:, i_, dc_ * 128:(dc_ + 1) * 128],
                                          identb_s[:, :], start=True, stop=True), ST)
                mk.w("v", ttok)
                tr_evac[bank] = mk.do("v", lambda E, i_=j, h_=half, bank_=bank:
                                      E.tensor_copy(
                    xn_fm_v[:, h_ * 4:(h_ + 1) * 4, i_ * 128:(i_ + 1) * 128],
                    p4q[:, bank_ * 4:(bank_ + 1) * 4, :]), SV)
    for ec in range(NCK):
        par = ec % 2
        sf = xn_f32s[par]
        base = 32 if par == 0 else 40
        mk.w("y", xf_free[par])
        mk.w("y", mk.tok("v"))
        mk.w("y", mk.tok("s"))
        lt = mk.do("y", lambda E, ec_=ec, s_=sf: E.dma_start(
            s_[:, 0:1024], semb_in[ec_ * 128:(ec_ + 1) * 128, :]),
            mk.dsem(f"SE{par}"), 16)
        mk.w("v", lt)
        mk.w("v", ldtoks["slotg"])
        mk.w("v", ldtoks["slotb"])
        mk.w("v", ldtoks["scl"])
        mk.do("v", lambda E, s_=sf, b_=base: E.reduce_sum(
            stat[:, b_:b_ + 1], s_[:, 0:1024], axis=AX), SV)
        mk.do("v", lambda E, s_=sf, b_=base: E.scalar_tensor_tensor(
            s_[:, 1024:2048], s_[:, 0:1024], 1.0, s_[:, 0:1024], ALU.mult, ALU.mult,
            accum_out=stat[:, b_ + 1:b_ + 2]), SV)
        mk.do("v", lambda E, b_=base: E.tensor_scalar_mul(
            stat[:, b_ + 2:b_ + 3], stat[:, b_:b_ + 1], 1.0 / C), SV)
        mk.do("v", lambda E, b_=base: E.tensor_scalar_mul(
            stat[:, b_ + 3:b_ + 4], stat[:, b_ + 1:b_ + 2], 1.0 / C), SV)
        mk.do("v", lambda E, b_=base: E.tensor_tensor(
            stat[:, b_ + 4:b_ + 5], stat[:, b_ + 2:b_ + 3],
            stat[:, b_ + 2:b_ + 3], ALU.mult), SV)
        v2 = mk.do("v", lambda E, b_=base: E.tensor_tensor(
            stat[:, b_ + 5:b_ + 6], stat[:, b_ + 3:b_ + 4],
            stat[:, b_ + 4:b_ + 5], ALU.subtract), SV)
        mk.w("s", v2)
        mk.do("s", lambda E, b_=base: E.activation(
            stat[:, b_ + 6:b_ + 7], stat[:, b_ + 5:b_ + 6], ACT.Ln,
            bias=eps_s[:, 0:1]), SS)
        stok = mk.do("s", lambda E, b_=base: E.activation(
            stat[:, b_ + 7:b_ + 8], stat[:, b_ + 6:b_ + 7], ACT.Exp, scale=-0.5), SS)
        mk.w("v", stok)
        mk.do("v", lambda E, s_=sf, b_=base: E.tensor_scalar(
            s_[:, 1024:2048], s_[:, 0:1024], stat[:, b_ + 2:b_ + 3],
            stat[:, b_ + 7:b_ + 8], ALU.subtract, ALU.mult), SV)
        mk.do("v", lambda E, s_=sf: E.tensor_tensor(
            s_[:, 1024:2048], s_[:, 1024:2048], slotg_s[:, :], ALU.mult), SV)
        mk.do("v", lambda E, s_=sf: E.tensor_tensor(
            s_[:, 1024:2048], s_[:, 1024:2048], slotb_s[:, :], ALU.add), SV)
        ftok = mk.do("v", lambda E, s_=sf: E.tensor_scalar_mul(
            s_[:, 1024:2048], s_[:, 1024:2048], scale_s[:, 0:1]), SV)
        ttok = None
        for half in range(2):
            bank = (ec * 2 + half) % 4
            mk.w("t", ftok)
            mk.w("t", tr_evac[bank])
            for q in range(4):
                dc = half * 4 + q
                ttok = mk.do("t", lambda E, s_=sf, dc_=dc, bank_=bank, q_=q:
                             E.transpose(p4q[:, bank_ * 4 + q_, :],
                                         s_[:, 1024 + dc_ * 128:1024 + (dc_ + 1) * 128],
                                         identf_s[:, :]), ST)
            mk.w("v", ttok)
            tr_evac[bank] = mk.do("v", lambda E, ec_=ec, h_=half, bank_=bank:
                                  E.tensor_copy(
                seT_v[:, h_ * 4:(h_ + 1) * 4, ec_ * 128:(ec_ + 1) * 128],
                p4q[:, bank_ * 4:(bank_ + 1) * 4, :]), SV)
        xf_free[par] = ttok

    if _BUILD_OPTS["phases"] < 6:
        return _finish(mk, nc, out_d, outbuf_v)

    for bank in range(4):
        mk.w("t", tr_evac[bank])
    lg_tok = [None, None]
    mk.w("v", spill_tok)
    mk.w("s", spill_tok)
    for i in range(NT):
        par = i % 2
        off0 = 0 if par == 0 else 1024
        mk.w("t", lg_tok[par])
        tk = None
        for hb in range(2):
            for dc in range(NCK):
                tk = mk.do("t", lambda E, i_=i, hb_=hb, dc_=dc, o=off0: E.matmul(
                    p4[:, o + hb_ * 512:o + (hb_ + 1) * 512],
                    xn_fm_v[:, dc_, i_ * 128:(i_ + 1) * 128],
                    seT_v[:, dc_, hb_ * 512:(hb_ + 1) * 512],
                    start=(dc_ == 0), stop=(dc_ == NCK - 1)), ST)
        mk.w("s", tk)
        cpt = mk.do("s", lambda E, i_=i, o=off0: E.activation(
            logits_v[:, i_, :], p4[:, o:o + 1024], ACT.Copy), SS)
        lg_tok[par] = cpt
        mk.w("v", cpt)
        mk.do("v", lambda E, i_=i: E.reduce_max(
            rowmax_tm[:, i_:i_ + 1], logits_v[:, i_, :], axis=AX), SV)

    if _BUILD_OPTS["phases"] < 7:
        return _finish(mk, nc, out_d, outbuf_v)

    nrm_tok = mk.do("v", lambda E: E.tensor_scalar_mul(
        nrm_s[:, :], rowmax_tm[:, :], -1.0), SV)
    tr_tok = None
    exp_tok3 = None
    for ec in range(NCK):
        mk.w("t", tr_tok)
        ttok = None
        for i in range(NT):
            ttok = mk.do("t", lambda E, i_=i, ec_=ec: E.transpose(
                p4q[:, i_, :], logits_v[:, i_, ec_ * 128:(ec_ + 1) * 128],
                identf_s[:, :]), ST)
        mk.w("v", ttok)
        mk.do("v", lambda E, ec_=ec: E.reduce_max(
            cmax_s[:, ec_:ec_ + 1], p4[:, :], axis=AX), SV)
        nm2 = mk.do("v", lambda E, ec_=ec: E.tensor_scalar_mul(
            ncmax_s[:, ec_:ec_ + 1], cmax_s[:, ec_:ec_ + 1], -1.0), SV)
        mk.w("s", nm2)
        tr_tok = mk.do("s", lambda E, ec_=ec: E.activation(
            dnT_v[:, ec_, :], p4[:, :], ACT.Exp, bias=ncmax_s[:, ec_:ec_ + 1],
            accum_out=cs2_s[:, ec_:ec_ + 1]), SS)
        exp_tok3 = tr_tok
    mk.do("s", lambda E: E.activation(rcl_s[:, :], cs2_s[:, :], ACT.Ln), SS)
    rck = mk.do("s", lambda E: E.activation(rcT_s[:, :], rcl_s[:, :], ACT.Exp,
                                            scale=-1.0), SS)
    mk.w("v", rck)
    nlast = None
    for ec in range(NCK):
        nlast = mk.do("v", lambda E, ec_=ec: E.tensor_scalar_mul(
            dnT_v[:, ec_, :], dnT_v[:, ec_, :], rcT_s[:, ec_:ec_ + 1]), SV)
    expc_toks = []
    for i in range(NT - 1, -1, -1):
        dst = lgb_v[:, NT + i, :] if i < 15 else scrBb[:, 0:1024]
        expc_toks.append(mk.do("s", lambda E, i_=i, d_=dst: E.activation(
            d_, logits_v[:, i_, :], ACT.Exp, bias=nrm_s[:, i_:i_ + 1],
            accum_out=rsum_tm[:, i_:i_ + 1]), SS))
    expc_tok = expc_toks[-1]
    mk.do("s", lambda E: E.activation(rsL_s[:, :], rsum_tm[:, :], ACT.Ln), SS)
    rct = mk.do("s", lambda E: E.activation(recip_tm[:, :], rsL_s[:, :], ACT.Exp,
                                            scale=-1.0), SS)
    dp_tok = [None, None]
    mk.w("t", nlast)
    mk.w("t", exp_tok3)
    for i in range(NT):
        par = i % 2
        o = par * 1024
        mk.w("t", dp_tok[par])
        tk = None
        for ec in range(NCK):
            tk = mk.do("t", lambda E, i_=i, ec_=ec, o_=o: E.matmul(
                p4[:, o_ + ec_ * 128:o_ + (ec_ + 1) * 128],
                dnT_v[:, ec_, i_ * 128:(i_ + 1) * 128], identb_s[:, :],
                start=True, stop=True), ST)
        mk.w("v", tk)
        if i == 0:
            mk.w("v", expc_tok)
        dp_tok[par] = mk.do("v", lambda E, i_=i, o_=o: E.tensor_copy(
            lgb_v[:, i_, :], p4[:, o_:o_ + 1024]), SV)
    sl_tok = [None, None]
    sl_dma = [None, None]
    mk.w("t", dp_tok[0])
    mk.w("t", dp_tok[1])
    for dc in range(NCK):
        par = dc % 2
        off0 = 0 if par == 0 else 1024
        mk.w("t", sl_tok[par])
        tk = None
        for hb in range(2):
            for i in range(NT):
                tk = mk.do("t", lambda E, dc_=dc, hb_=hb, i_=i, o=off0: E.matmul(
                    p4[:, o + hb_ * 512:o + (hb_ + 1) * 512],
                    xn_tm_v[:, i_, dc_ * 128:(dc_ + 1) * 128],
                    lgb_v[:, i_, hb_ * 512:(hb_ + 1) * 512],
                    start=(i_ == 0), stop=(i_ == NT - 1)), ST)
        mk.w("v", tk)
        mk.w("v", sl_dma[par])
        sl_tok[par] = mk.do("v", lambda E, dc_=dc, o=off0, par_=par: E.tensor_copy(
            scbA[:, par_ * 1024:(par_ + 1) * 1024], p4[:, o:o + 1024]), SV)
        mk.w("g", sl_tok[par])
        sl_dma[par] = mk.do("g", lambda E, dc_=dc, par_=par: E.dma_start(
            ag1_in.rearrange("(c p) e -> p c e", p=128)[:, dc_, :],
            scbA[:, par_ * 1024:(par_ + 1) * 1024]), mk.dsem(f"SL{par}"), 16)
    sl_ev = sl_dma[(NCK - 1) % 2]
    mk.w("v", sl_dma[0])
    mk.w("v", sl_dma[1])
    mk.w("g", sl_ev)
    mk.w("g", mk.tok("t"))
    mk.w("g", mk.tok("v"))
    mk.w("g", mk.tok("s"))
    wbufs = {0: (wgA_v, wuA_v), 1: (wgB_v, wuB_v)}
    wld = {}
    wfree = {0: None, 1: None}

    def load_wh(e, Hh, par, eng="g"):
        gv, uv = wbufs[par]
        mk.w(eng, wfree[par])
        mk.do(eng, lambda E, e_=e, H_=Hh, g_=gv: E.dma_start(
            g_[:, :, :],
            wg2_in[e_].rearrange("(c p) f -> p c f", p=128)[:, :, H_ * 1024:(H_ + 1) * 1024]),
            mk.dsem(f"WH{par}"), 16)
        wld[(e, Hh)] = mk.do(eng, lambda E, e_=e, H_=Hh, u_=uv: E.dma_start(
            u_[:, :, :],
            wu2_in[e_].rearrange("(c p) f -> p c f", p=128)[:, :, H_ * 1024:(H_ + 1) * 1024]),
            mk.dsem(f"WH{par}"), 16)

    def load_wd(e, oh, eng="g"):
        return mk.do(eng, lambda E, e_=e, o_=oh: E.dma_start(
            wdS_v[:, :, :],
            wd2_in[e_].rearrange("(c p) o -> p c o", p=128)[:, :, o_ * 512:(o_ + 1) * 512]),
            mk.dsem("WD"), 16)

    agt = mk.do("g", lambda E: E.collective_compute(
        "AllGather", ALU.bypass, replica_groups=[list(range(NCORES))],
        ins=[ag1_in.ap().opt()], outs=[ag1_out.ap().opt()]), mk.sem["cc"], 1)
    mk.w("t", sl_tok[0])
    mk.w("t", sl_tok[1])
    tr2_tok = None
    for ec in range(NCK):
        mk.w("t", tr2_tok)
        ttok = None
        for i in range(NT):
            lhs = (lgb_v[:, NT + i, :] if i < 15 else scrBb[:, 0:1024])
            ttok = mk.do("t", lambda E, i_=i, ec_=ec, l_=lhs: E.matmul(
                p4q[:, i_, :], l_[:, ec_ * 128:(ec_ + 1) * 128],
                identb_s[:, :], start=True, stop=True), ST)
        mk.w("v", ttok)
        tr2_tok = mk.do("v", lambda E, ec_=ec: E.tensor_copy(
            combineT_v[:, ec_, :], p4[:, :]), SV)

    mk.w("y", mk.tok("t"))
    mk.w("y", mk.tok("v"))
    mk.w("y", mk.tok("s"))
    brow_tok = None
    for nm, bin2, row, wdt in (("bg2", bg2_in, bg_row, FF), ("bu2", bu2_in, bu_row, FF),
                               ("bd2", bd2_in, bd_row, C)):
        mk.w("y", brow_tok)
        lt = mk.do("y", lambda E, b=bin2, w_=wdt: E.dma_start(
            scrB[0:2, 0:w_], b[:, :]), mk.dsem("YG"), 16)
        mk.w("v", lt)
        mk.w("v", brow_tok)
        ctk = mk.do("v", lambda E, w_=wdt: E.tensor_copy(
            scbB[0:2, 0:w_], scrB[0:2, 0:w_]), SV)
        mk.w("g", ctk)
        brow_tok = mk.do("g", lambda E, r=row, w_=wdt: E.dma_start(
            r[0:1, 0:2 * w_], scbB[0:2, 0:w_]), mk.dsem("BRS"), 16)
    mk.w("v", brow_tok)

    if _BUILD_OPTS["phases"] < 8:
        return _finish(mk, nc, out_d, outbuf_v)

    mk.w("g", tr2_tok)
    mk.w("g", brow_tok)
    load_wh(0, 0, 0)
    wd_tok = load_wd(0, 0)
    load_wh(0, 1, 1)
    mk.w("g", agt)

    regcell = {}

    def _dynoff(E, shift, delta):
        if "pid" not in regcell:
            regcell["pid"] = E.to_reg(E.partition_id())
            regcell["off"] = E.alloc_register("dynoff")
        if shift is not None:
            E.reg_alu(regcell["off"], regcell["pid"], shift, ALU.arith_shift_left)
        if delta:
            E.reg_add(regcell["off"], regcell["off"], delta)
        return regcell["off"]

    XeT_x = XeT.rearrange("p (dc s) -> p dc s", dc=8)
    xstg_v = xstg.rearrange("p (u dc s) -> p u dc s", u=2, dc=8)

    def _xet_dyn(E, b_, par):
        first = (b_ == 0)
        off = _dynoff(E, 7 if first else None, 0 if first else (1 << 20))
        src = bass.AP(ag1_out, off, [[EL, 128], [128 * EL, 8], [1, 128]])
        return E.dma_start(xstg_v[:, par, :, :], src)

    xet_tok = None
    xcp_tok = [None, None]
    for b_ in range(NCORES):
        par = b_ % 2
        mk.w("g", xcp_tok[par])
        ld = mk.do("g", lambda E, b__=b_, p_=par: _xet_dyn(E, b__, p_),
                   mk.dsem(f"XS{par}"), 16)
        mk.w("g", ld)
        for el_ in range(2):
            xcp_tok[par] = mk.do("g", lambda E, b__=b_, el__=el_, p_=par: E.dma_start(
                XeT_x[:, :, el__ * 512 + b__ * 64:el__ * 512 + b__ * 64 + 64],
                xstg_v[:, p_, :, el__ * 64:el__ * 64 + 64]),
                mk.dsem(f"XC{par}"), 16)
        xet_tok = xcp_tok[par]
    xet_tok2 = xcp_tok[1 - (NCORES - 1) % 2]

    mk.w("t", xet_tok)
    mk.w("t", xet_tok2)
    mk.w("t", brow_tok)
    mk.w("s", brow_tok)
    eo_dma = [None, None]
    gelu_free = None
    up_free = None
    gate_free = None
    h_cons = None
    for e in range(2):
        for Hh in range(2):
            par = Hh
            gv, uv = wbufs[par]
            mk.w("t", wld[(e, Hh)])
            for fs in range(8):
                fc = Hh * 8 + fs
                if fc == 0:
                    mk.w("t", h_cons)
                mk.w("t", gate_free)
                tkg = None
                for dc in range(NCK):
                    tkg = mk.do("t", lambda E, g_=gv, dc_=dc, fs_=fs, e_=e: E.matmul(
                        pA[:, :], g_[:, dc_, fs_ * 128:(fs_ + 1) * 128],
                        XeT_v[:, dc_, e_ * 512:(e_ + 1) * 512],
                        start=(dc_ == 0), stop=False), ST)
                tkg = mk.do("t", lambda E, e_=e, fc_=fc: E.matmul(
                    pA[:, :], bg_row[0:1, e_ * FF + fc_ * 128:e_ * FF + (fc_ + 1) * 128],
                    ones1_s[0:1, 0:512], start=False, stop=True), ST)
                mk.w("t", up_free)
                tku = None
                for dc in range(NCK):
                    tku = mk.do("t", lambda E, u_=uv, dc_=dc, fs_=fs, e_=e: E.matmul(
                        pB[:, :], u_[:, dc_, fs_ * 128:(fs_ + 1) * 128],
                        XeT_v[:, dc_, e_ * 512:(e_ + 1) * 512],
                        start=(dc_ == 0), stop=False), ST)
                tku = mk.do("t", lambda E, e_=e, fc_=fc: E.matmul(
                    pB[:, :], bu_row[0:1, e_ * FF + fc_ * 128:e_ * FF + (fc_ + 1) * 128],
                    ones65_s[32:33, 0:512], start=False, stop=True), ST)
                mk.w("s", tkg)
                mk.w("s", gelu_free)
                gt = mk.do("s", lambda E: E.activation(scbA[:, 0:512], pA[:, :],
                                                       ACT.Gelu), SS)
                gate_free = gt
                mk.w("v", tku)
                ut = mk.do("v", lambda E: E.tensor_copy(scbB[:, 0:512], pB[:, :]), SV)
                up_free = ut
                mk.w("v", gt)
                htok = mk.do("v", lambda E, fc_=fc: E.tensor_tensor(
                    hfm_v[:, fc_, :], scbA[:, 0:512], scbB[:, 0:512], ALU.mult), SV)
                gelu_free = htok
            if e == 0:
                wfree[par] = mk.tok("t")
                load_wh(1, Hh, par)
        mk.w("t", mk.tok("v"))
        for oh in range(2):
            mk.w("t", wd_tok)
            eo_par = [None, None]
            for bsl in range(4):
                par = bsl % 2
                mk.w("t", eo_par[par])
                tk = None
                for fc in range(16):
                    tk = mk.do("t", lambda E, fc_=fc, bsl_=bsl, par_=par: E.matmul(
                        p4[:, par_ * 1024:par_ * 1024 + 512],
                        hfm_v[:, fc_, bsl_ * 128:(bsl_ + 1) * 128],
                        wdS_v[:, fc_, :],
                        start=(fc_ == 0), stop=(fc_ == 15)), ST)
                tk = mk.do("t", lambda E, e_=e, oh_=oh, par_=par: E.matmul(
                    p4[:, par_ * 1024:par_ * 1024 + 512], ones65_s[64:65, 0:128],
                    bd_row[0:1, e_ * 1024 + oh_ * 512:e_ * 1024 + (oh_ + 1) * 512],
                    start=False, stop=True), ST)
                mk.w("v", tk)
                mk.w("v", eo_dma[par])
                eo_par[par] = mk.do("v", lambda E, par_=par: E.tensor_copy(
                    scbA[:, 1024 + par_ * 512:1024 + (par_ + 1) * 512],
                    p4[:, par_ * 1024:par_ * 1024 + 512]), SV)
                mk.w("g", eo_par[par])
                eo_dma[par] = mk.do("g", lambda E, e_=e, bsl_=bsl, oh_=oh, par_=par:
                                    E.dma_start(
                    ag2_in.rearrange("(c p) o -> p c o", p=128)[
                        :, e_ * 4 + bsl_, oh_ * 512:(oh_ + 1) * 512],
                    scbA[:, 1024 + par_ * 512:1024 + (par_ + 1) * 512]),
                    mk.dsem(f"EO{par}"), 16)
            mk.w("g", mk.tok("t"))
            if oh == 0:
                wd_tok = load_wd(e, 1)
            elif e == 0:
                wd_tok = load_wd(1, 0)
        h_cons = mk.tok("t")
    mk.w("g", eo_dma[0])
    mk.w("g", eo_dma[1])
    mk.w("g", mk.tok("t"))
    x2r_tok = mk.do("g", lambda E: E.dma_start(
        x2b_v[:, :, :], x2_spill.rearrange("(i p) c -> p i c", p=128)),
        mk.dsem("REL"), 16)
    agt2 = mk.do("g", lambda E: E.collective_compute(
        "AllGather", ALU.bypass, replica_groups=[list(range(NCORES))],
        ins=[ag2_in.ap().opt()], outs=[ag2_out.ap().opt()]), mk.sem["cc"], 1)
    mk.w("g", agt2)
    mk.w("g", mk.tok("t"))

    def _ef_dma(E, c_):
        first = (c_ == 0)
        off = _dynoff(E, 16 if first else None, 0 if first else (1 << 20))
        src = bass.AP(ag2_out, off, [[512 * C, 2], [C, 64], [1, C]])
        return E.dma_start(eout_f_v[:, c_, :], src)

    ef_tok = None
    for c_ in range(NCORES):
        ef_tok = mk.do("g", lambda E, c__=c_: _ef_dma(E, c__),
                       mk.dsem("EFG"), 16)

    if _BUILD_OPTS["phases"] < 10:
        return _finish(mk, nc, out_d, outbuf_v)

    mk.w("t", ef_tok)
    mk.w("t", x2r_tok)
    of_tok = [None, None]
    of_fin = [None, None]
    for i in range(NT):
        par = i % 2
        off0 = 0 if par == 0 else 1024
        base = 40 if par == 0 else 48
        mk.w("t", of_tok[par])
        tk = None
        for hb in range(2):
            for ec in range(NCK):
                tk = mk.do("t", lambda E, i_=i, hb_=hb, ec_=ec, o=off0: E.matmul(
                    p4[:, o + hb_ * 512:o + (hb_ + 1) * 512],
                    combineT_v[:, ec_, i_ * 128:(i_ + 1) * 128],
                    eout_f_v[:, ec_, hb_ * 512:(hb_ + 1) * 512],
                    start=(ec_ == 0), stop=(ec_ == NCK - 1)), ST)
        mk.w("v", tk)
        mk.w("v", x2r_tok)
        mk.w("v", rct)
        h1 = scrA if par == 0 else scrB
        mk.w("v", of_fin[par])
        atok = mk.do("v", lambda E, h=h1, i_=i, o=off0, b_=base:
                     E.scalar_tensor_tensor(
                         h[:, 0:1024], p4[:, o:o + 1024], recip_tm[:, i_:i_ + 1],
                         x2b_v[:, i_, :], ALU.mult, ALU.add,
                         accum_out=stat[:, b_:b_ + 1]), SV)
        mk.w("s", atok)
        sqt = mk.do("s", lambda E, h=h1, b_=base: E.activation(
            h[:, 1024:2048], h[:, 0:1024], ACT.Square,
            accum_out=stat[:, b_ + 1:b_ + 2]), SS)
        mk.do("v", lambda E, b_=base: E.tensor_tensor(
            stat[:, b_ + 2:b_ + 3], stat[:, b_:b_ + 1],
            stat[:, b_:b_ + 1], ALU.mult), SV)
        mk.w("v", sqt)
        vtok = mk.do("v", lambda E, b_=base: E.scalar_tensor_tensor(
            stat[:, b_ + 3:b_ + 4], stat[:, b_ + 2:b_ + 3], -1.0 / C,
            stat[:, b_ + 1:b_ + 2], ALU.mult, ALU.add), SV)
        mk.w("s", vtok)
        mk.do("s", lambda E, b_=base: E.activation(
            stat[:, b_ + 4:b_ + 5], stat[:, b_ + 3:b_ + 4], ACT.Ln,
            bias=eps_s[:, 0:1], scale=1.0 / C), SS)
        stok = mk.do("s", lambda E, b_=base: E.activation(
            stat[:, b_ + 5:b_ + 6], stat[:, b_ + 4:b_ + 5], ACT.Exp,
            scale=-0.5), SS)
        mk.do("v", lambda E, b_=base: E.tensor_scalar_mul(
            stat[:, b_ + 6:b_ + 7], stat[:, b_:b_ + 1], 1.0 / C), SV)
        mk.w("v", stok)
        of_fin[par] = mk.do("v", lambda E, h=h1, i_=i, b_=base: E.tensor_scalar(
            outbuf_v[:, i_, :], h[:, 0:1024], stat[:, b_ + 6:b_ + 7],
            stat[:, b_ + 5:b_ + 6], ALU.subtract, ALU.mult), SV)
        of_tok[par] = atok

    return _finish(mk, nc, out_d, outbuf_v)


def _finish(mk, nc, out_d, outbuf_v):
    mk.w("g", mk.tok("v"))
    mk.w("g", mk.tok("s"))
    st = mk.do("g", lambda E: E.dma_start(
        out_d.rearrange("(i p) c -> p i c", p=128), outbuf_v[:, :, :]),
        mk.dsem("OUT"), 16)
    for name in _BUILD_OPTS["taps"]:
        h = mk.sb[name]
        shp = list(h.shape)
        o = nc.dram_tensor("tap_" + name, shp, h.ap().dtype, kind="ExternalOutput")
        st = mk.do("g", lambda E, o_=o, h_=h: E.dma_start(o_[:, :], h_[:, :]),
                   mk.dsem("OUT"), 16)
    mk.w("g", st)
    nc._mk_specs = list(mk.specs)
    with nc.Block() as block:
        block.tensor(lambda E: [f(E) for f in mk.ops["t"]])
        block.vector(lambda E: [f(E) for f in mk.ops["v"]])
        block.scalar(lambda E: [f(E) for f in mk.ops["s"]])
        block.gpsimd(lambda E: [f(E) for f in mk.ops["g"]])
        block.sync(lambda E: [f(E) for f in mk.ops["y"]])
    return nc


def _solve_plan(specs, cap):
    plan = {}
    placed = []
    for name, size, p0, p1 in sorted(specs, key=lambda x: -x[1]):
        taken = sorted((plan[n], plan[n] + s) for (n, s, a0, a1) in placed
                       if not (a1 < p0 or a0 > p1))
        off = 0
        for (a, bnd) in taken:
            if off + size <= a:
                break
            off = max(off, bnd)
        assert off + size <= cap, f"SBUF plan overflow: {name} {size}B at {off}"
        plan[name] = off
        placed.append((name, size, p0, p1))
    return plan


_PLAN_CACHE = {}


def build_planned():
    key = _BUILD_OPTS["phases"]
    if key not in _PLAN_CACHE:
        nc0 = build(None)
        _PLAN_CACHE[key] = _solve_plan(nc0._mk_specs, 206336)
    return build(_PLAN_CACHE[key])


def _blkd(p2lT):
    z = np.zeros((128, 64), np.float32)
    z[0:64, 0:32] = p2lT
    z[64:128, 32:64] = p2lT
    return z


def _stage_inputs(inputs, core):
    f32 = np.float32
    x = np.asarray(inputs["L0_attn_x"][core], dtype=f32)
    caw = np.asarray(inputs["L0_attn_c_attn_w"], dtype=f32)
    cab = np.asarray(inputs["L0_attn_c_attn_b"], dtype=f32)
    q2l = np.asarray(inputs["L0_attn_q2l_w"], dtype=f32)
    q2lb = np.asarray(inputs["L0_attn_q2l_b"], dtype=f32)
    kv2l = np.asarray(inputs["L0_attn_kv2l_w"], dtype=f32)
    kv2lb = np.asarray(inputs["L0_attn_kv2l_b"], dtype=f32)
    l2o = np.asarray(inputs["L0_attn_l2o_w"], dtype=f32)
    l2ob = np.asarray(inputs["L0_attn_l2o_b"], dtype=f32)
    wqk = np.asarray(inputs["L0_attn_wqk_w"], dtype=f32)
    wqkb = np.asarray(inputs["L0_attn_wqk_b"], dtype=f32)
    cpw = np.asarray(inputs["L0_attn_cproj_w"], dtype=f32)
    cpb = np.asarray(inputs["L0_attn_cproj_b"], dtype=f32)
    d = dict(_CONSTS)
    d.update({
        "x_tm_in": np.ascontiguousarray(x),
        "x_fm_in": np.ascontiguousarray(x.T),
        "caw": caw,
        "cprojT_in": np.ascontiguousarray(cpw.T),
        "q2lT_in": np.ascontiguousarray(q2l.T),
        "kv2lT_in": np.ascontiguousarray(kv2l.T),
        "l2o_in": l2o,
        "wqkT_in": np.ascontiguousarray(wqk.T),
        "blkdQ_in": _blkd(q2l.T),
        "blkdKV_in": _blkd(kv2l.T),
        "wqkT4_in": np.tile(np.ascontiguousarray(wqk.T), (4, 1)),
        "l2o2_in": np.tile(l2o, (2, 1)),
        "bqr_in": np.ascontiguousarray(cab[0:C].reshape(H, HD).T),
        "bkr_in": np.ascontiguousarray(cab[C:2 * C].reshape(H, HD).T),
        "bvr_in": np.ascontiguousarray(cab[2 * C:3 * C].reshape(H, HD).T),
        "q2lb16_in": np.tile(q2lb[None, :], (H, 1)),
        "kvlb16_in": np.tile(kv2lb[None, :], (H, 1)),
        "wqkb512_in": np.tile(wqkb, H)[None, :].copy(),
        "l2ob_in": np.ascontiguousarray(np.tile(l2ob, H).reshape(8, 128).T),
        "cprojb_in": cpb[None, :].copy(),
        "normg128": np.tile(np.asarray(inputs["L0_ffn_norm_g"], f32)[None, :], (128, 1)),
        "normb128": np.tile(np.asarray(inputs["L0_ffn_norm_b"], f32)[None, :], (128, 1)),
        "slotg128": np.tile(np.asarray(inputs["L0_ffn_slot_g"], f32)[None, :], (128, 1)),
        "slotb128": np.tile(np.asarray(inputs["L0_ffn_slot_b"], f32)[None, :], (128, 1)),
        "scale128": np.full((128, 1), np.asarray(inputs["L0_ffn_scale"], f32)[0], f32),
        "semb": np.asarray(inputs["L0_ffn_slot_embeds"], f32).reshape(EL, C),
        "wg2": np.asarray(inputs["L0_ffn_Wg"], f32)[2 * core:2 * core + 2],
        "wu2": np.asarray(inputs["L0_ffn_Wu"], f32)[2 * core:2 * core + 2],
        "wd2": np.asarray(inputs["L0_ffn_Wd"], f32)[2 * core:2 * core + 2],
        "bg2": np.asarray(inputs["L0_ffn_bg"], f32)[2 * core:2 * core + 2],
        "bu2": np.asarray(inputs["L0_ffn_bu"], f32)[2 * core:2 * core + 2],
        "bd2": np.asarray(inputs["L0_ffn_bd"], f32)[2 * core:2 * core + 2],
    })
    return d


def kernel(**inputs):
    nc = build_planned()
    in_maps = [_stage_inputs(inputs, c) for c in range(NCORES)]
    res = run_bass_kernel_spmd(nc, in_maps, core_ids=list(range(NCORES)))
    out = np.stack([np.asarray(res.results[c]["out"]) for c in range(NCORES)], axis=0)
    if _BUILD_OPTS["taps"]:
        kernel._taps = [{k: np.asarray(v) for k, v in r.items()} for r in res.results]
    return out


# version 11
# speedup vs baseline: 1.0422x; 1.0205x over previous
import math
import numpy as np
import ml_dtypes

import concourse.bass as bass
import concourse.mybir as mybir
from concourse.bass_utils import run_bass_kernel_spmd

F32 = mybir.dt.float32
BF16 = mybir.dt.bfloat16
AX = mybir.AxisListType.X
ALU = mybir.AluOpType
ACT = mybir.ActivationFunctionType

T = 2048
C = 1024
H = 16
HD = 64
LAT = 32
G = 4
NT = 16
NB = 4
NCK = 8
E = 16
S = 64
EL = 1024
FF = 2048
NCORES = 8
SCALE = 1.0 / math.sqrt(LAT)
EPS = 1e-5

_BUILD_OPTS = {"taps": (), "phases": 99}


class _Sem:
    def __init__(self, nc, name):
        self.h = nc.alloc_semaphore(name)
        self.n = 0


class MK:
    def __init__(self, nc):
        self.nc = nc
        self.ops = {e: [] for e in "tvsgy"}
        self.sem = {k: _Sem(nc, "sem_" + k) for k in
                    ("t", "v", "s", "g", "gd", "yd", "cc")}
        self.seen = {e: {} for e in "tvsgy"}
        self.sb = {}
        self.regions = []
        self.specs = []
        self.aliases = {}
        self.plan = None
        self.arena_base = None
        self.arena_cap = None

    def on(self, e, fn):
        self.ops[e].append(fn)

    def w(self, e, tok):
        if tok is None:
            return
        sem, n = tok
        if self.seen[e].get(id(sem), -1) >= n:
            return
        self.seen[e][id(sem)] = n
        self.on(e, lambda E, s=sem, v=n: E.wait_ge(s.h, v))

    def do(self, e, fn, sem=None, inc=1):
        if e in ("v", "s") and sem is not None:
            self.w(e, (sem, sem.n))
        if sem is None:
            self.on(e, fn)
            return None
        sem.n += inc
        n = sem.n
        self.on(e, lambda E, f=fn, s=sem, i=inc: f(E).then_inc(s.h, i))
        return (sem, n)

    def tok(self, k):
        s = self.sem[k]
        return (s, s.n)

    def dsem(self, name):
        if name not in self.sem:
            self.sem[name] = _Sem(self.nc, "sem_" + name)
        return self.sem[name]

    def salloc(self, name, shape, dtype, p0, p1, alias=None):
        size = shape[1] * (2 if dtype == BF16 else 4)
        size = (size + 63) & ~63
        if alias is None:
            self.specs.append((name, size, p0, p1))
        else:
            self.aliases[name] = alias
        if self.plan is not None:
            off = self.plan[name] if alias is None else self.plan[alias]
        else:
            off = 0
        self.regions.append((off, size, p0, p1))
        h = self.nc.alloc_sbuf_tensor_at(name, list(shape), dtype,
                                         offset=self.arena_base + off)
        self.sb[name] = h
        return h


def _rope_tables():
    inv = 1.0 / (10000.0 ** (np.arange(0, LAT, 2, dtype=np.float64) / LAT))
    t = np.arange(T, dtype=np.float64)
    fr = np.outer(t, inv)
    cos = np.repeat(np.cos(fr), 2, axis=-1)[:, :LAT]
    sin = np.repeat(np.sin(fr), 2, axis=-1)[:, :LAT]
    return (np.tile(cos.T, (G, 1)).astype(np.float32),
            np.tile(sin.T, (G, 1)).astype(np.float32))


def _const_inputs():
    bf = ml_dtypes.bfloat16
    cosT, sinT = _rope_tables()
    prot = np.zeros((128, 128), dtype=np.float32)
    for blk in range(G):
        for i in range(LAT // 2):
            b = blk * 32
            prot[b + 2 * i, b + 2 * i + 1] = -1.0
            prot[b + 2 * i + 1, b + 2 * i] = 1.0
    tri = (np.arange(128)[None, :] >= np.arange(128)[:, None]).astype(np.float32)
    blk4T = np.zeros((4, 128), dtype=np.float32)
    for j in range(4):
        blk4T[j, 32 * j:32 * (j + 1)] = 1.0
    return {
        "cosT": cosT, "sinT": sinT,
        "protT": prot.T.astype(bf).copy(),
        "trimask": tri.astype(bf),
        "identf": np.eye(128, dtype=np.float32),
        "identb": np.eye(128, dtype=np.float32).astype(ml_dtypes.bfloat16),
        "ones1": np.ones((1, 512), dtype=bf),
        "ones128": np.ones((128, 1), dtype=bf),
        "blk4T": blk4T.astype(bf),
        "ones65": np.ones((65, 512), dtype=bf),
        "blkones4": blk4T.T.astype(bf).copy(),
        "eps128": np.full((128, 1), EPS, dtype=np.float32),
    }


_CONSTS = _const_inputs()


def build(plan=None):
    nc = bass.Bass("TRN2", target_bir_lowering=False)
    DI = lambda n, s, dt=F32: nc.dram_tensor(n, list(s), dt, kind="ExternalInput")

    x_tm_in = DI("x_tm_in", [T, C])
    x_fm_in = DI("x_fm_in", [C, T])
    caw_in = DI("caw", [3 * C, C])
    cprojT_in = DI("cprojT_in", [C, C])
    q2lT_in = DI("q2lT_in", [HD, LAT])
    kv2lT_in = DI("kv2lT_in", [HD, LAT])
    l2o_in = DI("l2o_in", [HD, LAT])
    wqkT_in = DI("wqkT_in", [LAT, LAT])
    bqr_in = DI("bqr_in", [HD, H])
    bkr_in = DI("bkr_in", [HD, H])
    bvr_in = DI("bvr_in", [HD, H])
    q2lb16_in = DI("q2lb16_in", [H, LAT])
    kvlb16_in = DI("kvlb16_in", [H, LAT])
    wqkb512_in = DI("wqkb512_in", [1, 512])
    l2ob_in = DI("l2ob_in", [128, 8])
    cprojb_in = DI("cprojb_in", [1, C])
    cosT_in = DI("cosT", [128, T])
    sinT_in = DI("sinT", [128, T])
    protT_in = DI("protT", [128, 128], BF16)
    tri_in = DI("trimask", [128, 128], BF16)
    identf_in = DI("identf", [128, 128], F32)
    identb_in = DI("identb", [128, 128], BF16)
    ones1_in = DI("ones1", [1, 512], BF16)
    ones128_in = DI("ones128", [128, 1], BF16)
    ones65_in = DI("ones65", [65, 512], BF16)
    blk4T_in = DI("blk4T", [4, 128], BF16)
    blkones4_in = DI("blkones4", [128, 4], BF16)
    blkdQ_in = DI("blkdQ_in", [128, 64])
    blkdKV_in = DI("blkdKV_in", [128, 64])
    wqkT4_in = DI("wqkT4_in", [128, LAT])
    l2o2_in = DI("l2o2_in", [128, LAT])
    eps128_in = DI("eps128", [128, 1], F32)
    normg_in = DI("normg128", [128, C])
    normb_in = DI("normb128", [128, C])
    slotg_in = DI("slotg128", [128, C])
    slotb_in = DI("slotb128", [128, C])
    scale128_in = DI("scale128", [128, 1])
    semb_in = DI("semb", [EL, C])
    wg2_in = DI("wg2", [2, C, FF])
    wu2_in = DI("wu2", [2, C, FF])
    wd2_in = DI("wd2", [2, FF, C])
    bg2_in = DI("bg2", [2, FF])
    bu2_in = DI("bu2", [2, FF])
    bd2_in = DI("bd2", [2, C])

    out_d = nc.dram_tensor("out", [T, C], F32, kind="ExternalOutput")

    ag1_in = nc.dram_tensor("ag1_in", [C, EL], BF16)
    ag1_out = nc.dram_tensor("ag1_out", [NCORES * C, EL], BF16, addr_space="Shared")
    ag2_in = nc.dram_tensor("ag2_in", [EL, C], BF16)
    ag2_out = nc.dram_tensor("ag2_out", [NCORES * EL, C], BF16, addr_space="Shared")
    x2_spill = nc.dram_tensor("x2_spill", [T, C], BF16)

    ARENA = 206336
    arena = nc.alloc_sbuf_tensor("arena", [128, ARENA], mybir.dt.uint8)
    mk = MK(nc)
    mk.plan = plan
    mk.arena_base = nc.lookup_mloc(arena).addr
    mk.arena_cap = ARENA

    A = mk.salloc
    scrA = A("scrA", [128, T], F32, 0, 9)
    scrB = A("scrB", [128, T], F32, 0, 9)
    scrC = A("scrC", [128, T], F32, 4, 5)
    scbA = A("scbA", [128, T], BF16, 0, 9)
    dbgA = A("dbgA", [128, 1024], F32, 0, 2)
    scbB = A("scbB", [128, T], BF16, 0, 9)
    stat = A("stat", [128, 64], F32, 0, 9)
    identf_s = A("identf_s", [128, 128], F32, 0, 9)
    identb_s = A("identb_s", [128, 128], BF16, 0, 9)
    ones1_s = A("ones1_s", [1, 512], BF16, 0, 9)
    ones128_s = A("ones128_s", [128, 1], BF16, 0, 9)
    ones65_s = A("ones65_s", [65, 512], BF16, 0, 10)
    eps_s = A("eps_s", [128, 1], F32, 0, 9)
    x_tm = A("x_tm", [128, NT * C], BF16, 0, 4)
    wpeT = A("wpeT", [128, G * C], BF16, 0, 4)
    bprj_b = A("bprj_b", [1, C], BF16, 0, 4)
    tri_s = A("tri_s", [128, 128], BF16, 0, 3)
    blk4T_s = A("blk4T_s", [4, 128], BF16, 0, 3)
    combineT = A("combineT", [128, NCK * T], BF16, 7, 10)
    recip_tm = A("recip_tm", [128, 16], F32, 7, 10)
    x_fm = A("x_fm", [128, NCK * T], BF16, 0, 2)
    weffq = A("weffq", [128, NCK * 512], BF16, 0, 2)
    weffk = A("weffk", [128, NCK * 512], BF16, 0, 2)
    weffv = A("weffv", [128, NCK * 512], BF16, 0, 2)
    cosbf = A("cosbf", [128, T], BF16, 0, 2)
    sinbf = A("sinbf", [128, T], BF16, 0, 2)
    prot_s = A("prot_s", [128, 128], BF16, 0, 2)
    blkones4_s = A("blkones4_s", [128, 4], BF16, 0, 2)
    wqkT_b = A("wqkT_b", [32, 32], BF16, 0, 2)
    wqkT4 = A("wqkT4", [128, 32], BF16, 0, 2)
    biasq_row = A("biasq_row", [1, 512], BF16, 0, 2)
    biask_row = A("biask_row", [1, 512], BF16, 0, 2)
    biasv_row = A("biasv_row", [1, 512], BF16, 0, 2)
    wqkb_b = A("wqkb_b", [1, 512], BF16, 0, 2)
    QK = A("QK", [128, G * T], BF16, 1, 3)
    KR = A("KR", [128, G * T], BF16, 1, 3)
    vt = A("vt", [128, NT * 512], BF16, 1, 3)
    xn_tm = A("xn_tm", [128, NT * C], BF16, 5, 7)
    rowsB2 = A("rowsB2", [65, 2 * T], BF16, 7, 9)
    l2o_b = A("l2o_b", [64, 32], BF16, 0, 1)
    l2o2 = A("l2o2", [128, 32], BF16, 0, 1)
    nbias_pp = A("nbias_pp", [128, 1], F32, 2, 3)
    ylat = A("ylat", [128, G * T], BF16, 3, 4)
    x2 = A("x2", [128, NT * C], BF16, 4, 5)
    normg_s = A("normg_s", [128, C], F32, 4, 5)
    normb_s = A("normb_s", [128, C], F32, 4, 5)
    slotg_s = A("slotg_s", [128, C], F32, 4, 5)
    slotb_s = A("slotb_s", [128, C], F32, 4, 5)
    scale_s = A("scale_s", [128, 1], F32, 4, 5)
    xn_fm = A("xn_fm", [128, NCK * T], BF16, 5, 6)
    seT = A("seT", [128, NCK * C], BF16, 5, 6)
    logits = A("logits", [128, NT * C], F32, 6, 7)
    rowmax_tm = A("rowmax_tm", [128, 16], F32, 6, 7)
    XeT = A("XeT", [128, NCK * C], BF16, 8, 9)
    xstg = A("xstg", [128, 2 * C], BF16, 8, 8)
    eout_f = A("eout_f", [128, NCK * C], BF16, 9, 10)
    x2b = A("x2b", [128, NT * C], BF16, 9, 10)
    cawb0 = A("cawb0", [128, C], BF16, 0, 0)
    cawb1 = A("cawb1", [128, C], BF16, 0, 0)
    cprojT2 = A("cprojT2", [128, 2 * C], BF16, 0, 0)
    q2lT_f = A("q2lT_f", [64, 32], F32, 0, 0)
    kv2lT_f = A("kv2lT_f", [64, 32], F32, 0, 0)
    q2lT_b = A("q2lT_b", [64, 32], BF16, 0, 0)
    kv2lT_b = A("kv2lT_b", [64, 32], BF16, 0, 0)
    l2o_f = A("l2o_f", [64, 32], F32, 0, 0)
    wqkT_f = A("wqkT_f", [32, 32], F32, 0, 0)
    blkdQ = A("blkdQ", [128, 64], BF16, 0, 2)
    blkdKV = A("blkdKV", [128, 64], BF16, 0, 2)
    blkdQf = A("blkdQf", [128, 64], F32, 0, 0)
    blkdKVf = A("blkdKVf", [128, 64], F32, 0, 0)
    wqkT4f = A("wqkT4f", [128, LAT], F32, 0, 0)
    l2o2f = A("l2o2f", [128, LAT], F32, 0, 0)
    bxr_f = A("bxr_f", [64, 16], F32, 0, 0)
    bxr_b = A("bxr_b", [64, 16], BF16, 0, 0)
    bx16c = A("bx16c", [16, 32], F32, 0, 0)
    bx16_f = A("bx16_f", [16, 32], F32, 0, 0)
    bx16b = A("bx16b", [16, 32], BF16, 0, 0)
    wqkb_f = A("wqkb_f", [1, 512], F32, 0, 0)
    l2ob_f = A("l2ob_f", [128, 8], F32, 0, 0)
    l2ob_b = A("l2ob_b", [128, 8], BF16, 0, 0)
    cprojb_f = A("cprojb_f", [1, C], F32, 0, 0)
    bprj_f = A("bprj_f", [1, C], F32, 0, 0)
    nb44 = A("nb44", [4, 4], F32, 2, 2)
    nbrow = A("nbrow", [1, 16], F32, 2, 2)
    nbrow2 = A("nbrow2", [1, 1], BF16, 2, 2)
    PT = A("PT", [128, 2 * T], BF16, 3, 3)
    sums4 = A("sums4", [4, 512], F32, 3, 3)
    rec4 = A("rec4", [4, 512], F32, 3, 3)
    rec4b = A("rec4b", [4, 512], BF16, 3, 3)
    rbbf = A("rbbf", [128, 512], BF16, 3, 3)
    dispatchN = A("dispatchN", [128, NT * C], BF16, 7, 7)
    lgb = A("lgb", [128, 2 * NT * C], BF16, 7, 7, alias="logits")
    scrBb = A("scrBb", [128, T], BF16, 7, 7, alias="scrB")
    nrm_s = A("nrm_s", [128, 16], F32, 7, 7)
    rsum_tm = A("rsum_tm", [128, 16], F32, 7, 7)
    rsL_s = A("rsL_s", [128, 16], F32, 7, 7)
    cmax_s = A("cmax_s", [128, 8], F32, 7, 7)
    ncmax_s = A("ncmax_s", [128, 8], F32, 7, 7)
    cs2_s = A("cs2_s", [128, 8], F32, 7, 7)
    rcl_s = A("rcl_s", [128, 8], F32, 7, 7)
    rcT_s = A("rcT_s", [128, 8], F32, 7, 7)
    wgA = A("wgA", [128, NCK * C], BF16, 8, 8)
    wgB = A("wgB", [128, NCK * C], BF16, 8, 8)
    wuA = A("wuA", [128, NCK * C], BF16, 8, 8)
    wuB = A("wuB", [128, NCK * C], BF16, 8, 8)
    wdS = A("wdS", [128, 16 * 512], BF16, 8, 8)
    hfm = A("hfm", [128, 16 * 512], BF16, 8, 8)
    outbuf = A("outbuf", [128, NT * C], F32, 10, 10)

    bg_row = rowsB2[0:1, 0:2 * FF]
    bu_row = rowsB2[32:33, 0:2 * FF]
    bd_row = rowsB2[64:65, 0:2 * C]

    p4 = nc.alloc_psum_tensor("p4", [128, 2048], F32)
    pA = nc.alloc_psum_tensor("pA", [128, 512], F32)
    pB = nc.alloc_psum_tensor("pB", [128, 512], F32)
    pC = nc.alloc_psum_tensor("pC", [128, 512], F32)
    pD = nc.alloc_psum_tensor("pD", [128, 512], F32)

    ST, SV, SS = mk.sem["t"], mk.sem["v"], mk.sem["s"]
    GD, YD = mk.sem["gd"], mk.sem["yd"]

    x_fm_v = x_fm.rearrange("p (c t) -> p c t", c=NCK)
    x_tm_v = x_tm.rearrange("p (i c) -> p i c", i=NT)
    weffq_v = weffq.rearrange("p (c x) -> p c x", c=NCK)
    weffk_v = weffk.rearrange("p (c x) -> p c x", c=NCK)
    weffv_v = weffv.rearrange("p (c x) -> p c x", c=NCK)
    wpeT_v = wpeT.rearrange("p (g o) -> p g o", g=G)
    QK_v = QK.rearrange("p (g t) -> p g t", g=G)
    KR_v = KR.rearrange("p (g t) -> p g t", g=G)
    vt_v = vt.rearrange("p (i x) -> p i x", i=NT)
    ylat_v = ylat.rearrange("p (g t) -> p g t", g=G)
    PT_v = PT.rearrange("p (u t) -> p u t", u=2)
    x2_v = x2.rearrange("p (i c) -> p i c", i=NT)
    xn_tm_v = xn_tm.rearrange("p (i c) -> p i c", i=NT)
    xn_fm_v = xn_fm.rearrange("p (c t) -> p c t", c=NCK)
    seT_v = seT.rearrange("p (c e) -> p c e", c=NCK)
    logits_v = logits.rearrange("p (i c) -> p i c", i=NT)
    combineT_v = combineT.rearrange("p (c t) -> p c t", c=NCK)
    dnT_v = dispatchN.rearrange("p (e t) -> p e t", e=NCK)
    lgb_v = lgb.rearrange("p (i c) -> p i c", i=2 * NT)
    XeT_v = XeT.rearrange("p (c e) -> p c e", c=NCK)
    wgA_v = wgA.rearrange("p (c f) -> p c f", c=NCK)
    wgB_v = wgB.rearrange("p (c f) -> p c f", c=NCK)
    wuA_v = wuA.rearrange("p (c f) -> p c f", c=NCK)
    wuB_v = wuB.rearrange("p (c f) -> p c f", c=NCK)
    wdS_v = wdS.rearrange("p (c o) -> p c o", c=16)
    hfm_v = hfm.rearrange("p (c x) -> p c x", c=16)
    eout_f_v = eout_f.rearrange("p (c o) -> p c o", c=NCK)
    x2b_v = x2b.rearrange("p (i c) -> p i c", i=NT)
    outbuf_v = outbuf.rearrange("p (i c) -> p i c", i=NT)
    p4q = p4.rearrange("p (q x) -> p q x", q=16)
    p4j = p4.rearrange("p (j x) -> p j x", j=4)

    ldtoks = {}

    def gload(name, dst_ap, src_ap, sem=None):
        s_ = mk.dsem(sem or ("g_" + name))
        ldtoks[name] = mk.do("g", lambda E, d=dst_ap, s=src_ap: E.dma_start(d, s), s_, 16)

    def yload(name, dst_ap, src_ap, sem=None):
        s_ = mk.dsem(sem or ("y_" + name))
        ldtoks[name] = mk.do("y", lambda E, d=dst_ap, s=src_ap: E.dma_start(d, s), s_, 16)
        return ldtoks[name]

    def yload_Y0(name, dst_ap, src_ap):
        return yload(name, dst_ap, src_ap, sem="Y0")

    gload("x_fm", x_fm_v[:, :, :], x_fm_in.rearrange("(c p) t -> p c t", p=128), sem="G0")
    gload("x_tm", x_tm_v[:, :, :], x_tm_in.rearrange("(i p) c -> p i c", p=128), sem="G0")
    gload("cos", cosbf[:, :], cosT_in[:, :], sem="G0")
    gload("sin", sinbf[:, :], sinT_in[:, :], sem="G0")
    yload_Y0("identf", identf_s[:, :], identf_in[:, :])
    yload_Y0("identb", identb_s[:, :], identb_in[:, :])
    yload_Y0("tri", tri_s[:, :], tri_in[:, :])
    yload_Y0("prot", prot_s[:, :], protT_in[:, :])
    yload_Y0("ones1", ones1_s[:, :], ones1_in[:, :])
    yload_Y0("ones128", ones128_s[:, :], ones128_in[:, :])
    yload_Y0("ones65", ones65_s[:, :], ones65_in[:, :])
    yload_Y0("blk4T", blk4T_s[:, :], blk4T_in[:, :])
    yload_Y0("blkones4", blkones4_s[:, :], blkones4_in[:, :])
    yload_Y0("eps", eps_s[:, :], eps128_in[:, :])
    yload_Y0("q2lT", q2lT_f[:, :], q2lT_in[:, :])
    yload_Y0("kv2lT", kv2lT_f[:, :], kv2lT_in[:, :])
    yload_Y0("wqkb", wqkb_f[:, :], wqkb512_in[:, :])
    yload_Y0("l2ob", l2ob_f[:, :], l2ob_in[:, :])
    yload_Y0("cprojb", cprojb_f[:, :], cprojb_in[:, :])
    yload_Y0("blkdQ", blkdQf[:, :], blkdQ_in[:, :])
    yload_Y0("blkdKV", blkdKVf[:, :], blkdKV_in[:, :])
    yload_Y0("wqkT4", wqkT4f[:, :], wqkT4_in[:, :])
    yload_Y0("l2o2", l2o2f[:, :], l2o2_in[:, :])

    TOK_G0 = mk.tok("G0")
    TOK_Y0 = mk.tok("Y0")
    for _n in list(ldtoks):
        if mk.sem.get("G0") and ldtoks[_n] and ldtoks[_n][0] is mk.sem["G0"]:
            ldtoks[_n] = TOK_G0
        if mk.sem.get("Y0") and ldtoks[_n] and ldtoks[_n][0] is mk.sem["Y0"]:
            ldtoks[_n] = TOK_Y0
    tok_smalls = None
    for nm, (dst, src) in {
        "q2lT": (q2lT_b, q2lT_f), "kv2lT": (kv2lT_b, kv2lT_f),
        "wqkb": (wqkb_b, wqkb_f), "l2ob": (l2ob_b, l2ob_f),
        "blkdQ": (blkdQ, blkdQf), "blkdKV": (blkdKV, blkdKVf),
        "wqkT4": (wqkT4, wqkT4f), "l2o2": (l2o2, l2o2f),
    }.items():
        mk.w("v", ldtoks[nm])
        tok_smalls = mk.do("v", lambda E, d=dst, s=src:
                           E.tensor_copy(d[:, :], s[:, :]), SV)
    tok_blkd = tok_smalls
    tok_l2o2 = tok_smalls
    tok_wqk4 = tok_smalls

    parts = [(0, blkdQ, weffq_v), (C, blkdKV, weffk_v), (2 * C, blkdKV, weffv_v)]
    caw_bufs = [cawb0, cawb1]
    caw_tok = [None, None]
    psQ = [pA, pB]
    ps_tok = [None, None]
    for pi, (roff, blkd, wv) in enumerate(parts):
        for p in range(8):
            ii = pi * 8 + p
            par = ii % 2
            buf = caw_bufs[par]
            mk.w("g", caw_tok[par])
            ct = mk.do("g", lambda E, b=buf, r=roff + p * 128:
                       E.dma_start(b[:, :], caw_in[r:r + 128, :]),
                       mk.dsem(f"CAW{par}"), 16)
            ps = psQ[par]
            mk.w("t", ct)
            mk.w("t", tok_blkd)
            mk.w("t", ps_tok[par])
            tk = None
            for j in range(NCK):
                tk = mk.do("t", lambda E, ps_=ps, b=buf, bd=blkd, j_=j:
                           E.matmul(ps_[:, j_ * 64:(j_ + 1) * 64],
                                    b[:, j_ * 128:(j_ + 1) * 128], bd[:, :],
                                    start=True, stop=True), ST)
            caw_tok[par] = tk
            mk.w("v", tk)
            dst = wv[:, :, p * 64:(p + 1) * 64]
            src = ps.rearrange("p (j x) -> p j x", j=8)
            ps_tok[par] = mk.do("v", lambda E, d=dst, s=src:
                                E.tensor_copy(d, s[:, :, :]), SV)

    prev_cast = None
    for brow, bin_, b16in, rhsw in ((biasq_row, bqr_in, q2lb16_in, q2lT_b),
                                    (biask_row, bkr_in, kvlb16_in, kv2lT_b),
                                    (biasv_row, bvr_in, kvlb16_in, kv2lT_b)):
        mk.w("y", prev_cast)
        mk.do("y", lambda E, b=bin_: E.dma_start(bxr_f[:, :], b[:, :]),
              mk.dsem("YB"), 16)
        t2 = mk.do("y", lambda E, b=b16in: E.dma_start(bx16c[:, :], b[:, :]),
                   mk.dsem("YB"), 16)
        mk.w("v", t2)
        t3 = mk.do("v", lambda E: E.tensor_copy(bxr_b[:, :], bxr_f[:, :]), SV)
        mk.w("t", t3)
        mk.w("t", ps_tok[0])
        t4 = mk.do("t", lambda E, r=rhsw: E.matmul(pA[0:16, 0:32], bxr_b[:, :],
                                                   r[:, :], start=True, stop=True), ST)
        mk.w("v", t4)
        t5 = mk.do("v", lambda E: E.tensor_tensor(bx16_f[:, :], pA[0:16, 0:32],
                                                  bx16c[:, :], ALU.add), SV)
        ps_tok[0] = t5
        t6 = mk.do("v", lambda E: E.tensor_copy(bx16b[:, :], bx16_f[:, :]), SV)
        mk.w("g", t6)
        t7 = mk.do("g", lambda E, d=brow: E.dma_start(d[0:1, :], bx16b[:, :]),
                   mk.dsem("BRW"), 16)
        mk.w("v", t7)
        prev_cast = t5
        ldtoks["brow"] = t7

    cp2 = cprojT2.rearrange("p (u o) -> p u o", u=2)
    wpe_prev_mm = None
    wpe_done = None
    for g in range(G):
        mk.w("g", wpe_prev_mm)
        ct = mk.do("g", lambda E, g_=g: E.dma_start(
            cp2[:, :, :],
            cprojT_in.rearrange("(u p) o -> p u o", p=128)[:, 2 * g_:2 * g_ + 2, :]),
            mk.dsem("CPJ"), 16)
        mk.w("t", ct)
        mk.w("t", tok_l2o2)
        mk.w("t", wpe_done)
        tk = None
        for j in range(G):
            for hb in range(2):
                tk = mk.do("t", lambda E, j_=j, hb_=hb: E.matmul(
                    p4[32 * j_:32 * (j_ + 1), hb_ * 512:(hb_ + 1) * 512],
                    l2o2[(j_ % 2) * 64:(j_ % 2) * 64 + 64, :],
                    cp2[(j_ % 2) * 64:(j_ % 2) * 64 + 64, j_ // 2,
                        hb_ * 512:(hb_ + 1) * 512],
                    start=True, stop=True,
                    tile_position=((j_ % 2) * 64, 32 * j_)), ST)
        for u in range(2):
            for hb in range(2):
                tk2 = mk.do("t", lambda E, u_=u, g_=g, hb_=hb: E.matmul(
                    p4[0:1, 1024 + hb_ * 512:1024 + (hb_ + 1) * 512],
                    l2ob_b[:, 2 * g_ + u_:2 * g_ + u_ + 1],
                    cp2[:, u_, hb_ * 512:(hb_ + 1) * 512],
                    start=(2 * g_ + u_ == 0), stop=(2 * g_ + u_ == 7)), ST)
        wpe_prev_mm = tk2
        mk.w("v", tk)
        wpe_done = mk.do("v", lambda E, g_=g: E.tensor_copy(
            wpeT_v[:, g_, :], p4[:, 0:1024]), SV)
    mk.w("v", wpe_prev_mm)
    mk.w("v", ldtoks["cprojb"])
    mk.do("v", lambda E: E.tensor_tensor(bprj_f[0:1, :], p4[0:1, 1024:2048],
                                         cprojb_f[0:1, :], ALU.add), SV)
    tok_bprj = mk.do("v", lambda E: E.tensor_copy(bprj_b[0:1, :], bprj_f[0:1, :]), SV)

    if _BUILD_OPTS["phases"] < 1:
        return _finish(mk, nc, out_d, outbuf_v)

    mk.w("t", ldtoks["x_fm"])
    mk.w("t", ldtoks["ones1"])
    mk.w("t", ldtoks["brow"])
    mk.w("t", tok_wqk4)
    mk.w("t", tok_bprj)
    mk.w("v", ldtoks["cos"])
    mk.w("v", ldtoks["sin"])

    tA = tB = tC = None
    prev_rot = None
    prev_wqk = None
    for g in range(G):
        for side in ("q", "k"):
            wv = weffq_v if side == "q" else weffk_v
            brow = biasq_row if side == "q" else biask_row
            for b in range(NB):
                mk.w("t", tA)
                for j in range(NCK):
                    mk.do("t", lambda E, w_=wv, j_=j, g_=g, b_=b: E.matmul(
                        pA[:, :], w_[:, j_, g_ * 128:(g_ + 1) * 128],
                        x_fm_v[:, j_, b_ * 512:(b_ + 1) * 512],
                        start=(j_ == 0), stop=False), ST)
                tk = mk.do("t", lambda E, br=brow, g_=g: E.matmul(
                    pA[:, :], br[0:1, g_ * 128:(g_ + 1) * 128],
                    ones1_s[0:1, 0:512], start=False, stop=True), ST)
                mk.w("v", tk)
                mk.w("v", prev_rot)
                t_qs = mk.do("v", lambda E, b_=b: E.tensor_tensor(
                    scbA[:, 0:512], pA[:, :],
                    sinbf[:, b_ * 512:(b_ + 1) * 512], ALU.mult), SV)
                mk.w("t", t_qs)
                mk.w("t", tB)
                t_rot = mk.do("t", lambda E: E.matmul(
                    pB[:, :], prot_s[:, :], scbA[:, 0:512],
                    start=True, stop=True), ST)
                prev_rot = t_rot
                mk.w("v", t_rot)
                if _BUILD_OPTS.get("dbg") and g == 0 and side == "k" and b == 0:
                    mk.do("v", lambda E: E.tensor_copy(dbgA[:, 0:512], scbA[:, 0:512]), SV)
                    mk.do("v", lambda E: E.tensor_copy(dbgA[:, 512:1024], pB[:, :]), SV)
                t_qc = mk.do("v", lambda E, b_=b: E.tensor_tensor(
                    scbB[:, 0:512], pA[:, :],
                    cosbf[:, b_ * 512:(b_ + 1) * 512], ALU.mult), SV)
                tA = t_qc
                if side == "q":
                    mk.w("v", prev_wqk)
                    t_qr = mk.do("v", lambda E: E.tensor_tensor(
                        scbA[:, 512:1024], pB[:, :], scbB[:, 0:512], ALU.add), SV)
                    tB = t_qr
                    mk.w("t", t_qr)
                    mk.w("t", tC)
                    for j in range(4):
                        mk.do("t", lambda E, j_=j: E.matmul(
                            pC[32 * j_:32 * (j_ + 1), :], wqkT4[32 * j_:32 * (j_ + 1), :],
                            scbA[32 * j_:32 * (j_ + 1), 512:1024],
                            start=True, stop=False,
                            tile_position=(32 * j_, 32 * j_)), ST)
                    tk2 = mk.do("t", lambda E, g_=g: E.matmul(
                        pC[:, :], wqkb_b[0:1, g_ * 128:(g_ + 1) * 128],
                        ones1_s[0:1, 0:512], start=False, stop=True), ST)
                    prev_wqk = tk2
                    mk.w("v", tk2)
                    tC = mk.do("v", lambda E, g_=g, b_=b: E.tensor_copy(
                        QK_v[:, g_, b_ * 512:(b_ + 1) * 512], pC[:, :]), SV)
                else:
                    tB = mk.do("v", lambda E, g_=g, b_=b: E.tensor_tensor(
                        KR_v[:, g_, b_ * 512:(b_ + 1) * 512], pB[:, :],
                        scbB[:, 0:512], ALU.add), SV)
    for tc in range(NT):
        par = tc % 2
        ps = pA if par == 0 else pB
        mk.w("t", tA if par == 0 else tB)
        for j in range(NCK):
            mk.do("t", lambda E, ps_=ps, j_=j, tc_=tc: E.matmul(
                ps_[:, :], x_fm_v[:, j_, tc_ * 128:(tc_ + 1) * 128],
                weffv_v[:, j_, :], start=(j_ == 0), stop=False), ST)
        tk = mk.do("t", lambda E, ps_=ps: E.matmul(
            ps_[:, :], ones1_s[0:1, 0:128], biasv_row[0:1, :],
            start=False, stop=True), ST)
        mk.w("v", tk)
        ev = mk.do("v", lambda E, ps_=ps, tc_=tc: E.tensor_copy(
            vt_v[:, tc_, :], ps_[:, :]), SV)
        if par == 0:
            tA = ev
        else:
            tB = ev

    if _BUILD_OPTS["phases"] < 2:
        return _finish(mk, nc, out_d, outbuf_v)

    sq_tok = None
    for g in range(G):
        for side in ("q", "k"):
            src = QK_v if side == "q" else KR_v
            col = 56 if side == "q" else 57
            mk.w("v", mk.tok("v"))
            sq_tok = mk.do("v", lambda E, g_=g, s_=src: E.scalar_tensor_tensor(
                scbA[:, :], s_[:, g_, :], 1.0, s_[:, g_, :],
                ALU.mult, ALU.mult), SV)
            mk.w("t", sq_tok)
            tk = None
            for b in range(NB):
                tk = mk.do("t", lambda E, b_=b: E.matmul(
                    p4[0:4, b_ * 512:(b_ + 1) * 512], blkones4_s[:, :],
                    scbA[:, b_ * 512:(b_ + 1) * 512], start=True, stop=True), ST)
            mk.w("v", tk)
            mk.do("v", lambda E, c_=col: E.reduce_max(
                stat[0:4, c_:c_ + 1], p4[0:4, :], axis=AX), SV)
        mk.do("v", lambda E, g_=g: E.tensor_tensor(
            nb44[0:4, g_:g_ + 1], stat[0:4, 56:57], stat[0:4, 57:58], ALU.add), SV)
    mk.w("g", mk.tok("v"))
    gt = None
    for g in range(G):
        gt = mk.do("g", lambda E, g_=g: E.dma_start(
            nbrow[0:1, 4 * g_:4 * g_ + 4], nb44[0:4, g_:g_ + 1]),
            mk.dsem("NB4"), 16)
    mk.w("v", gt)
    mk.do("v", lambda E: E.reduce_max(stat[0:1, 58:59], nbrow[0:1, :], axis=AX), SV)
    mk.do("v", lambda E: E.tensor_scalar_mul(stat[0:1, 59:60], stat[0:1, 58:59],
                                             -0.5 * SCALE), SV)
    tkc = mk.do("v", lambda E: E.tensor_copy(nbrow2[0:1, 0:1], stat[0:1, 59:60]), SV)
    mk.w("t", tkc)
    tkm = mk.do("t", lambda E: E.matmul(pA[:, 0:1], ones1_s[0:1, 0:128],
                                        nbrow2[0:1, 0:1], start=True, stop=True), ST)
    mk.w("v", tkm)
    tk_nb = mk.do("v", lambda E: E.tensor_copy(nbias_pp[:, :], pA[:, 0:1]), SV)
    tA = tk_nb

    if _BUILD_OPTS["phases"] < 3:
        return _finish(mk, nc, out_d, outbuf_v)

    mk.w("s", tk_nb)
    prev_tkb = None
    exp_toks = {}
    sums_toks = {}
    PT_v4 = PT.rearrange("p (s x) -> p s x", s=4)
    att_banks = [pA, pD]
    tile_y = {-2: tA, -1: None}
    st3 = {"p": 0, "tk_rb": None, "tk_s4": None, "prev_tkb": None,
           "prev_gsm": None, "evac": None, "defer": None, "y_last": None}

    def _att_consume(pr):
        (pi, g_, ci_, hp_, off_, w_, sl_, n_, diag_, etok_, mtok_, bank_,
         first_, yt_, se_) = pr
        mk.w("t", mtok_ if diag_ else etok_)
        if first_:
            mk.w("t", yt_)
        for j in (2 * hp_, 2 * hp_ + 1):
            mk.do("t", lambda E, g2=g_, j_=j, jj_=j - 2 * hp_, c2=ci_, o2=off_,
                  w2=w_, s2=sl_, n2=n_, bk=bank_:
                  E.matmul(bk[32 * j_:32 * (j_ + 1), o2:o2 + w2],
                           vt_v[:, c2, (4 * g2 + j_) * 32:(4 * g2 + j_ + 1) * 32],
                           PT_v4[:, s2, jj_ * 512 + o2:jj_ * 512 + o2 + w2],
                           start=(c2 == 0), stop=(c2 == n2 - 1),
                           tile_position=(0, 32 * j_)), ST)
        if first_:
            mk.w("t", se_)
        pu = None
        for j in (2 * hp_, 2 * hp_ + 1):
            pu = mk.do("t", lambda E, j_=j, jj_=j - 2 * hp_, c2=ci_, o2=off_,
                       w2=w_, s2=sl_, n2=n_:
                       E.matmul(pB[32 * j_:32 * j_ + 1, o2:o2 + w2],
                                ones128_s[:, 0:1],
                                PT_v4[:, s2, jj_ * 512 + o2:jj_ * 512 + o2 + w2],
                                start=(c2 == 0), stop=(c2 == n2 - 1),
                                tile_position=(0, 32 * j_)), ST)
        sums_toks[pi] = pu

    pend = None
    ti = 0
    for g in range(G):
        for b in range(NB):
            nch = 4 * (b + 1)
            first_pidx = st3["p"]
            bank = att_banks[ti % 2]
            for ci in range(nch):
                off = (ci - 4 * b) * 128 if ci >= 4 * b else 0
                w = 512 - off
                diag = (ci >= 4 * b)
                for hp in range(2):
                    p_idx = st3["p"]
                    par2 = p_idx % 2
                    slot = p_idx % 4
                    mk.w("t", exp_toks.get(p_idx - 2))
                    stok = None
                    for j in (2 * hp, 2 * hp + 1):
                        stok = mk.do("t", lambda E, g_=g, j_=j, jj_=j - 2 * hp, ci_=ci,
                                     off_=off, w_=w, b_=b, pr_=par2:
                                     E.matmul(p4j[:, 2 * pr_ + jj_, off_:off_ + w_],
                                              KR_v[32 * j_:32 * (j_ + 1), g_,
                                                   ci_ * 128:(ci_ + 1) * 128],
                                              QK_v[32 * j_:32 * (j_ + 1), g_,
                                                   b_ * 512 + off_:b_ * 512 + off_ + w_],
                                              start=True, stop=True,
                                              tile_position=(32 * j_, 0)), ST)
                    if p_idx == first_pidx + 2 and st3["defer"] is not None:
                        st3["defer"]()
                        st3["defer"] = None
                    mk.w("s", stok)
                    mk.w("s", sums_toks.get(p_idx - 4))
                    pin = p4j[:, 2 * par2:2 * par2 + 2, off:off + w]
                    pout = PT_v4[:, slot, :].rearrange(
                        "p (j x) -> p j x", j=2)[:, :, off:off + w]
                    etok = mk.do("s", lambda E, pi=pin, po=pout: E.activation(
                        po, pi, ACT.Exp, bias=nbias_pp[:, 0:1], scale=SCALE), SS)
                    exp_toks[p_idx] = etok
                    mtok = None
                    if diag:
                        mk.w("v", etok)
                        pm = PT_v4[:, slot, :].rearrange(
                            "p (j x) -> p j x", j=2)[:, :, off:off + 128]
                        mtok = mk.do("v", lambda E, pm_=pm: E.tensor_tensor(
                            pm_, pm_, tri_s[:, None, :].broadcast_to([128, 2, 128]),
                            ALU.mult), SV)
                    if pend is not None:
                        _att_consume(pend)
                    pend = (p_idx, g, ci, hp, off, w, slot, nch, diag, etok, mtok,
                            bank, p_idx == first_pidx, tile_y.get(ti - 2),
                            st3["evac"])
                    st3["p"] = p_idx + 1
            _att_consume(pend)
            pend = None
            ptail = mk.tok("t")
            mk.w("v", ptail)
            mk.w("v", st3["prev_gsm"])
            ev = mk.do("v", lambda E: E.tensor_copy(
                scrB[0:97, 0:512], pB[0:97, :]), SV)
            st3["evac"] = ev
            mk.w("g", ev)
            mk.w("g", st3["tk_s4"])
            gsm = mk.do("g", lambda E: E.dma_start(
                sums4[:, :], bass.AP(scrB, 0, [[32 * T, 4], [1, 512]])),
                mk.dsem("SM4"), 16)
            st3["prev_gsm"] = gsm

            def _defer(g_=g, b_=b, ti_=ti, gsm_=gsm, bank_=bank):
                mk.w("s", gsm_)
                mk.w("s", st3["prev_tkb"])
                mk.do("s", lambda E: E.activation(rec4[:, :], sums4[:, :],
                                                  ACT.Ln), SS)
                tk_s4 = mk.do("s", lambda E: E.activation(
                    rec4b[:, :], rec4[:, :], ACT.Exp, scale=-1.0), SS)
                st3["tk_s4"] = tk_s4
                mk.w("t", tk_s4)
                mk.w("t", st3["tk_rb"])
                tkb = mk.do("t", lambda E: E.matmul(
                    pC[:, :], blk4T_s[:, :], rec4b[:, :],
                    start=True, stop=True), ST)
                st3["prev_tkb"] = tkb
                mk.w("v", tkb)
                st3["tk_rb"] = mk.do("v", lambda E: E.tensor_copy(
                    rbbf[:, :], pC[:, :]), SV)
                yt = mk.do("v", lambda E, g2=g_, b2=b_, bk=bank_: E.tensor_tensor(
                    ylat_v[:, g2, b2 * 512:(b2 + 1) * 512], bk[:, :], rbbf[:, :],
                    ALU.mult), SV)
                tile_y[ti_] = yt
                st3["y_last"] = yt
            st3["defer"] = _defer
            ti += 1
    if st3["defer"] is not None:
        st3["defer"]()
        st3["defer"] = None
    tk_y = st3["y_last"]

    if _BUILD_OPTS["phases"] < 4:
        return _finish(mk, nc, out_d, outbuf_v)

    mk.w("y", mk.tok("t"))
    yload("normg", normg_s[:, :], normg_in[:, :], sem="YN")
    yload("normb", normb_s[:, :], normb_in[:, :], sem="YN")
    yload("slotg", slotg_s[:, :], slotg_in[:, :], sem="YN")
    yload("slotb", slotb_s[:, :], slotb_in[:, :], sem="YN")
    yload("scl", scale_s[:, :], scale128_in[:, :], sem="YN")
    TOK_YN = mk.tok("YN")
    for _n in ("normg", "normb", "slotg", "slotb", "scl"):
        ldtoks[_n] = TOK_YN
    mk.w("t", tk_y)
    mk.w("v", ldtoks["x_tm"])

    h1s3 = [scrA, scrB, scrC]
    bases3 = [0, 8, 48]
    mm_gate = [None, None]
    A_toks = {}
    B_toks = {}
    x2_done = [None] * NT
    for step in range(NT + 2):
        if step < NT:
            i = step
            par = i % 2
            off0 = 0 if par == 0 else 1024
            mk.w("t", mm_gate[par])
            tk = None
            for g in range(G):
                for hb in range(2):
                    tk = mk.do("t", lambda E, g_=g, hb_=hb, i_=i, o=off0: E.matmul(
                        p4[:, o + hb_ * 512:o + (hb_ + 1) * 512],
                        ylat_v[:, g_, i_ * 128:(i_ + 1) * 128],
                        wpeT_v[:, g_, hb_ * 512:(hb_ + 1) * 512],
                        start=(g_ == 0), stop=False), ST)
            for hb in range(2):
                tk = mk.do("t", lambda E, hb_=hb, o=off0: E.matmul(
                    p4[:, o + hb_ * 512:o + (hb_ + 1) * 512],
                    ones1_s[0:1, 0:128], bprj_b[0:1, hb_ * 512:(hb_ + 1) * 512],
                    start=False, stop=True), ST)
            h1 = h1s3[i % 3]
            base = bases3[i % 3]
            mk.w("v", tk)
            atok = mk.do("v", lambda E, i_=i, h_=h1, b_=base, o=off0:
                         E.scalar_tensor_tensor(
                             h_[:, 0:1024], p4[:, o:o + 1024], 1.0,
                             x_tm_v[:, i_, :], ALU.mult, ALU.add,
                             accum_out=stat[:, b_:b_ + 1]), SV)
            mm_gate[par] = atok
            mk.do("v", lambda E, b_=base: E.tensor_tensor(
                stat[:, b_ + 2:b_ + 3], stat[:, b_:b_ + 1],
                stat[:, b_:b_ + 1], ALU.mult), SV)
            mk.do("v", lambda E, b_=base: E.tensor_scalar_mul(
                stat[:, b_ + 6:b_ + 7], stat[:, b_:b_ + 1], 1.0 / C), SV)
            mk.w("s", atok)
            A_toks[i] = mk.do("s", lambda E, h_=h1, b_=base: E.activation(
                h_[:, 1024:2048], h_[:, 0:1024], ACT.Square,
                accum_out=stat[:, b_ + 1:b_ + 2]), SS)
        if 0 <= step - 1 < NT:
            j = step - 1
            base = bases3[j % 3]
            mk.w("v", A_toks[j])
            vtok = mk.do("v", lambda E, b_=base: E.scalar_tensor_tensor(
                stat[:, b_ + 3:b_ + 4], stat[:, b_ + 2:b_ + 3], -1.0 / C,
                stat[:, b_ + 1:b_ + 2], ALU.mult, ALU.add), SV)
            mk.w("s", vtok)
            mk.do("s", lambda E, b_=base: E.activation(
                stat[:, b_ + 4:b_ + 5], stat[:, b_ + 3:b_ + 4], ACT.Ln,
                bias=eps_s[:, 0:1], scale=1.0 / C), SS)
            B_toks[j] = mk.do("s", lambda E, b_=base: E.activation(
                stat[:, b_ + 5:b_ + 6], stat[:, b_ + 4:b_ + 5], ACT.Exp,
                scale=-0.5), SS)
        if 0 <= step - 2 < NT:
            k = step - 2
            h1 = h1s3[k % 3]
            base = bases3[k % 3]
            mk.w("v", B_toks[k])
            x2_done[k] = mk.do("v", lambda E, k_=k, h_=h1, b_=base: E.tensor_scalar(
                x2_v[:, k_, :], h_[:, 0:1024], stat[:, b_ + 6:b_ + 7],
                stat[:, b_ + 5:b_ + 6], ALU.subtract, ALU.mult), SV)
    mk.w("g", x2_done[-1])
    mk.w("g", x2_done[-2])
    spill_tok = mk.do("g", lambda E: E.dma_start(
        x2_spill.rearrange("(i p) c -> p i c", p=128), x2_v[:, :, :]),
        mk.dsem("SPL"), 16)

    if _BUILD_OPTS["phases"] < 5:
        return _finish(mk, nc, out_d, outbuf_v)

    tr_evac = [None] * 4
    xf_free = [None, None]
    xn_f32s = [scrA, scrB]
    b5 = [16, 20, 24, 28]
    A5 = {}
    B5 = {}
    for step in range(NT + 1):
        if step < NT:
            i = step
            base = b5[i % 4]
            xf = xn_f32s[i % 2]
            mk.w("s", x2_done[i])
            mk.w("s", B5.get(i - 3))
            mk.do("s", lambda E, i_=i, x_=xf, b_=base: E.activation(
                x_[:, 1024:2048], x2_v[:, i_, :], ACT.Square,
                accum_out=stat[:, b_:b_ + 1]), SS)
            mk.do("s", lambda E, b_=base: E.activation(
                stat[:, b_ + 2:b_ + 3], stat[:, b_:b_ + 1], ACT.Ln,
                bias=eps_s[:, 0:1], scale=1.0 / C), SS)
            A5[i] = mk.do("s", lambda E, b_=base: E.activation(
                stat[:, b_ + 3:b_ + 4], stat[:, b_ + 2:b_ + 3], ACT.Exp,
                scale=-0.5), SS)
        if 0 <= step - 1 < NT:
            j = step - 1
            base = b5[j % 4]
            mk.w("v", A5[j])
            mk.w("v", ldtoks["normg"])
            mk.w("v", ldtoks["normb"])
            mk.do("v", lambda E, i_=j, b_=base: E.scalar_tensor_tensor(
                xn_tm_v[:, i_, :], x2_v[:, i_, :], stat[:, b_ + 3:b_ + 4],
                normg_s[:, :], ALU.mult, ALU.mult), SV)
            ftok = mk.do("v", lambda E, i_=j: E.tensor_tensor(
                xn_tm_v[:, i_, :], xn_tm_v[:, i_, :], normb_s[:, :], ALU.add), SV)
            B5[j] = ftok
            ttok = None
            for half in range(2):
                bank = (j * 2 + half) % 4
                mk.w("t", ftok)
                mk.w("t", tr_evac[bank])
                for q in range(4):
                    dc = half * 4 + q
                    ttok = mk.do("t", lambda E, i_=j, dc_=dc, bank_=bank, q_=q:
                                 E.matmul(p4q[:, bank_ * 4 + q_, :],
                                          xn_tm_v[:, i_, dc_ * 128:(dc_ + 1) * 128],
                                          identb_s[:, :], start=True, stop=True), ST)
                mk.w("v", ttok)
                tr_evac[bank] = mk.do("v", lambda E, i_=j, h_=half, bank_=bank:
                                      E.tensor_copy(
                    xn_fm_v[:, h_ * 4:(h_ + 1) * 4, i_ * 128:(i_ + 1) * 128],
                    p4q[:, bank_ * 4:(bank_ + 1) * 4, :]), SV)
    se_lt = {}
    se_sq = {}
    se_t1 = {}
    se_rs = {}
    se_C = {}
    se_D = {}
    se_bufs = [scrA, scrB, scrC]
    bSE = [32, 39, 46, 53]
    for step in range(NCK + 3):
        if 0 <= step - 3 < NCK:
            ec = step - 3
            sf = se_bufs[ec % 3]
            base = bSE[ec % 4]
            mk.w("v", se_rs[ec])
            if ec == 0:
                mk.w("v", ldtoks["slotg"])
                mk.w("v", ldtoks["slotb"])
                mk.w("v", ldtoks["scl"])
            mk.do("v", lambda E, s_=sf, b_=base: E.tensor_scalar(
                s_[:, 1024:2048], s_[:, 0:1024], stat[:, b_ + 6:b_ + 7],
                stat[:, b_ + 5:b_ + 6], ALU.subtract, ALU.mult), SV)
            mk.do("v", lambda E, s_=sf: E.tensor_tensor(
                s_[:, 1024:2048], s_[:, 1024:2048], slotg_s[:, :], ALU.mult), SV)
            mk.do("v", lambda E, s_=sf: E.tensor_tensor(
                s_[:, 1024:2048], s_[:, 1024:2048], slotb_s[:, :], ALU.add), SV)
            ftok = mk.do("v", lambda E, s_=sf: E.tensor_scalar_mul(
                s_[:, 1024:2048], s_[:, 1024:2048], scale_s[:, 0:1]), SV)
            se_C[ec] = ftok
            ttok = None
            for half in range(2):
                bank = (ec * 2 + half) % 4
                mk.w("t", ftok)
                mk.w("t", tr_evac[bank])
                for q in range(4):
                    dc = half * 4 + q
                    ttok = mk.do("t", lambda E, s_=sf, dc_=dc, bank_=bank, q_=q:
                                 E.transpose(p4q[:, bank_ * 4 + q_, :],
                                             s_[:, 1024 + dc_ * 128:1024 + (dc_ + 1) * 128],
                                             identf_s[:, :]), ST)
                mk.w("v", ttok)
                tr_evac[bank] = mk.do("v", lambda E, ec_=ec, h_=half, bank_=bank:
                                      E.tensor_copy(
                    seT_v[:, h_ * 4:(h_ + 1) * 4, ec_ * 128:(ec_ + 1) * 128],
                    p4q[:, bank_ * 4:(bank_ + 1) * 4, :]), SV)
            se_D[ec] = ttok
        if step < NCK:
            ec = step
            sf = se_bufs[ec % 3]
            if ec < 3:
                mk.w("y", mk.tok("v"))
                mk.w("y", mk.tok("s"))
            mk.w("y", se_C.get(ec - 3))
            se_lt[ec] = mk.do("y", lambda E, ec_=ec, s_=sf: E.dma_start(
                s_[:, 0:1024], semb_in[ec_ * 128:(ec_ + 1) * 128, :]),
                mk.dsem(f"SE{ec % 2}"), 16)
        if 0 <= step - 1 < NCK:
            ec = step - 1
            sf = se_bufs[ec % 3]
            base = bSE[ec % 4]
            mk.w("s", se_lt[ec])
            mk.w("s", se_t1.get(ec - 4))
            mk.w("s", se_D.get(ec - 3))
            se_sq[ec] = mk.do("s", lambda E, s_=sf, b_=base: E.activation(
                s_[:, 1024:2048], s_[:, 0:1024], ACT.Square,
                accum_out=stat[:, b_ + 1:b_ + 2]), SS)
        if 0 <= step - 2 < NCK:
            ec = step - 2
            sf2 = se_bufs[ec % 3]
            base = bSE[ec % 4]
            mk.w("v", se_sq[ec])
            mk.do("v", lambda E, s_=sf2, b_=base: E.reduce_sum(
                stat[:, b_:b_ + 1], s_[:, 0:1024], axis=AX), SV)
            se_t1[ec] = mk.do("v", lambda E, b_=base: E.tensor_tensor(
                stat[:, b_ + 2:b_ + 3], stat[:, b_:b_ + 1],
                stat[:, b_:b_ + 1], ALU.mult), SV)
            mk.do("v", lambda E, b_=base: E.tensor_scalar_mul(
                stat[:, b_ + 6:b_ + 7], stat[:, b_:b_ + 1], 1.0 / C), SV)
            vtok = mk.do("v", lambda E, b_=base: E.scalar_tensor_tensor(
                stat[:, b_ + 3:b_ + 4], stat[:, b_ + 2:b_ + 3], -1.0 / C,
                stat[:, b_ + 1:b_ + 2], ALU.mult, ALU.add), SV)
            mk.w("s", vtok)
            mk.do("s", lambda E, b_=base: E.activation(
                stat[:, b_ + 4:b_ + 5], stat[:, b_ + 3:b_ + 4], ACT.Ln,
                bias=eps_s[:, 0:1], scale=1.0 / C), SS)
            se_rs[ec] = mk.do("s", lambda E, b_=base: E.activation(
                stat[:, b_ + 5:b_ + 6], stat[:, b_ + 4:b_ + 5], ACT.Exp,
                scale=-0.5), SS)

    if _BUILD_OPTS["phases"] < 6:
        return _finish(mk, nc, out_d, outbuf_v)

    for bank in range(4):
        mk.w("t", tr_evac[bank])
    lg_tok = [None, None]
    mk.w("v", spill_tok)
    mk.w("s", spill_tok)
    for i in range(NT):
        par = i % 2
        off0 = 0 if par == 0 else 1024
        mk.w("t", lg_tok[par])
        tk = None
        for hb in range(2):
            for dc in range(NCK):
                tk = mk.do("t", lambda E, i_=i, hb_=hb, dc_=dc, o=off0: E.matmul(
                    p4[:, o + hb_ * 512:o + (hb_ + 1) * 512],
                    xn_fm_v[:, dc_, i_ * 128:(i_ + 1) * 128],
                    seT_v[:, dc_, hb_ * 512:(hb_ + 1) * 512],
                    start=(dc_ == 0), stop=(dc_ == NCK - 1)), ST)
        mk.w("s", tk)
        cpt = mk.do("s", lambda E, i_=i, o=off0: E.activation(
            logits_v[:, i_, :], p4[:, o:o + 1024], ACT.Copy), SS)
        lg_tok[par] = cpt
        mk.w("v", cpt)
        mk.do("v", lambda E, i_=i: E.reduce_max(
            rowmax_tm[:, i_:i_ + 1], logits_v[:, i_, :], axis=AX), SV)

    if _BUILD_OPTS["phases"] < 7:
        return _finish(mk, nc, out_d, outbuf_v)

    nrm_tok = mk.do("v", lambda E: E.tensor_scalar_mul(
        nrm_s[:, :], rowmax_tm[:, :], -1.0), SV)
    tr_tok = None
    exp_tok3 = None
    for ec in range(NCK):
        mk.w("t", tr_tok)
        ttok = None
        for i in range(NT):
            ttok = mk.do("t", lambda E, i_=i, ec_=ec: E.transpose(
                p4q[:, i_, :], logits_v[:, i_, ec_ * 128:(ec_ + 1) * 128],
                identf_s[:, :]), ST)
        mk.w("v", ttok)
        mk.do("v", lambda E, ec_=ec: E.reduce_max(
            cmax_s[:, ec_:ec_ + 1], p4[:, :], axis=AX), SV)
        nm2 = mk.do("v", lambda E, ec_=ec: E.tensor_scalar_mul(
            ncmax_s[:, ec_:ec_ + 1], cmax_s[:, ec_:ec_ + 1], -1.0), SV)
        mk.w("s", nm2)
        tr_tok = mk.do("s", lambda E, ec_=ec: E.activation(
            dnT_v[:, ec_, :], p4[:, :], ACT.Exp, bias=ncmax_s[:, ec_:ec_ + 1],
            accum_out=cs2_s[:, ec_:ec_ + 1]), SS)
        exp_tok3 = tr_tok
    mk.do("s", lambda E: E.activation(rcl_s[:, :], cs2_s[:, :], ACT.Ln), SS)
    rck = mk.do("s", lambda E: E.activation(rcT_s[:, :], rcl_s[:, :], ACT.Exp,
                                            scale=-1.0), SS)
    mk.w("v", rck)
    nlast = None
    for ec in range(NCK):
        nlast = mk.do("v", lambda E, ec_=ec: E.tensor_scalar_mul(
            dnT_v[:, ec_, :], dnT_v[:, ec_, :], rcT_s[:, ec_:ec_ + 1]), SV)
    expc_toks = []
    for i in range(NT - 1, -1, -1):
        dst = lgb_v[:, NT + i, :] if i < 15 else scrBb[:, 0:1024]
        expc_toks.append(mk.do("s", lambda E, i_=i, d_=dst: E.activation(
            d_, logits_v[:, i_, :], ACT.Exp, bias=nrm_s[:, i_:i_ + 1],
            accum_out=rsum_tm[:, i_:i_ + 1]), SS))
    expc_tok = expc_toks[-1]
    mk.do("s", lambda E: E.activation(rsL_s[:, :], rsum_tm[:, :], ACT.Ln), SS)
    rct = mk.do("s", lambda E: E.activation(recip_tm[:, :], rsL_s[:, :], ACT.Exp,
                                            scale=-1.0), SS)
    dp_tok = [None, None]
    mk.w("t", nlast)
    mk.w("t", exp_tok3)
    for i in range(NT):
        par = i % 2
        o = par * 1024
        mk.w("t", dp_tok[par])
        tk = None
        for ec in range(NCK):
            tk = mk.do("t", lambda E, i_=i, ec_=ec, o_=o: E.matmul(
                p4[:, o_ + ec_ * 128:o_ + (ec_ + 1) * 128],
                dnT_v[:, ec_, i_ * 128:(i_ + 1) * 128], identb_s[:, :],
                start=True, stop=True), ST)
        mk.w("v", tk)
        if i == 0:
            mk.w("v", expc_tok)
        dp_tok[par] = mk.do("v", lambda E, i_=i, o_=o: E.tensor_copy(
            lgb_v[:, i_, :], p4[:, o_:o_ + 1024]), SV)
    sl_tok = [None, None]
    sl_dma = [None, None]
    mk.w("t", dp_tok[0])
    mk.w("t", dp_tok[1])
    for dc in range(NCK):
        par = dc % 2
        off0 = 0 if par == 0 else 1024
        mk.w("t", sl_tok[par])
        tk = None
        for hb in range(2):
            for i in range(NT):
                tk = mk.do("t", lambda E, dc_=dc, hb_=hb, i_=i, o=off0: E.matmul(
                    p4[:, o + hb_ * 512:o + (hb_ + 1) * 512],
                    xn_tm_v[:, i_, dc_ * 128:(dc_ + 1) * 128],
                    lgb_v[:, i_, hb_ * 512:(hb_ + 1) * 512],
                    start=(i_ == 0), stop=(i_ == NT - 1)), ST)
        mk.w("v", tk)
        mk.w("v", sl_dma[par])
        sl_tok[par] = mk.do("v", lambda E, dc_=dc, o=off0, par_=par: E.tensor_copy(
            scbA[:, par_ * 1024:(par_ + 1) * 1024], p4[:, o:o + 1024]), SV)
        mk.w("g", sl_tok[par])
        sl_dma[par] = mk.do("g", lambda E, dc_=dc, par_=par: E.dma_start(
            ag1_in.rearrange("(c p) e -> p c e", p=128)[:, dc_, :],
            scbA[:, par_ * 1024:(par_ + 1) * 1024]), mk.dsem(f"SL{par}"), 16)
    sl_ev = sl_dma[(NCK - 1) % 2]
    mk.w("v", sl_dma[0])
    mk.w("v", sl_dma[1])
    mk.w("g", sl_ev)
    mk.w("g", mk.tok("t"))
    mk.w("g", mk.tok("v"))
    mk.w("g", mk.tok("s"))
    wbufs = {0: (wgA_v, wuA_v), 1: (wgB_v, wuB_v)}
    wld = {}
    wfree = {0: None, 1: None}

    def load_wh(e, Hh, par, eng="g"):
        gv, uv = wbufs[par]
        mk.w(eng, wfree[par])
        mk.do(eng, lambda E, e_=e, H_=Hh, g_=gv: E.dma_start(
            g_[:, :, :],
            wg2_in[e_].rearrange("(c p) f -> p c f", p=128)[:, :, H_ * 1024:(H_ + 1) * 1024]),
            mk.dsem(f"WH{par}"), 16)
        wld[(e, Hh)] = mk.do(eng, lambda E, e_=e, H_=Hh, u_=uv: E.dma_start(
            u_[:, :, :],
            wu2_in[e_].rearrange("(c p) f -> p c f", p=128)[:, :, H_ * 1024:(H_ + 1) * 1024]),
            mk.dsem(f"WH{par}"), 16)

    def load_wd(e, oh, eng="g"):
        return mk.do(eng, lambda E, e_=e, o_=oh: E.dma_start(
            wdS_v[:, :, :],
            wd2_in[e_].rearrange("(c p) o -> p c o", p=128)[:, :, o_ * 512:(o_ + 1) * 512]),
            mk.dsem("WD"), 16)

    agt = mk.do("g", lambda E: E.collective_compute(
        "AllGather", ALU.bypass, replica_groups=[list(range(NCORES))],
        ins=[ag1_in.ap().opt()], outs=[ag1_out.ap().opt()]), mk.sem["cc"], 1)
    mk.w("t", sl_tok[0])
    mk.w("t", sl_tok[1])
    tr2_tok = None
    for ec in range(NCK):
        mk.w("t", tr2_tok)
        ttok = None
        for i in range(NT):
            lhs = (lgb_v[:, NT + i, :] if i < 15 else scrBb[:, 0:1024])
            ttok = mk.do("t", lambda E, i_=i, ec_=ec, l_=lhs: E.matmul(
                p4q[:, i_, :], l_[:, ec_ * 128:(ec_ + 1) * 128],
                identb_s[:, :], start=True, stop=True), ST)
        mk.w("v", ttok)
        tr2_tok = mk.do("v", lambda E, ec_=ec: E.tensor_copy(
            combineT_v[:, ec_, :], p4[:, :]), SV)

    mk.w("y", mk.tok("t"))
    mk.w("y", mk.tok("v"))
    mk.w("y", mk.tok("s"))
    brow_tok = None
    for nm, bin2, row, wdt in (("bg2", bg2_in, bg_row, FF), ("bu2", bu2_in, bu_row, FF),
                               ("bd2", bd2_in, bd_row, C)):
        mk.w("y", brow_tok)
        lt = mk.do("y", lambda E, b=bin2, w_=wdt: E.dma_start(
            scrB[0:2, 0:w_], b[:, :]), mk.dsem("YG"), 16)
        mk.w("v", lt)
        mk.w("v", brow_tok)
        ctk = mk.do("v", lambda E, w_=wdt: E.tensor_copy(
            scbB[0:2, 0:w_], scrB[0:2, 0:w_]), SV)
        mk.w("g", ctk)
        brow_tok = mk.do("g", lambda E, r=row, w_=wdt: E.dma_start(
            r[0:1, 0:2 * w_], scbB[0:2, 0:w_]), mk.dsem("BRS"), 16)
    mk.w("v", brow_tok)

    if _BUILD_OPTS["phases"] < 8:
        return _finish(mk, nc, out_d, outbuf_v)

    mk.w("g", tr2_tok)
    mk.w("g", brow_tok)
    load_wh(0, 0, 0)
    wd_tok = load_wd(0, 0)
    load_wh(0, 1, 1)
    mk.w("g", agt)

    regcell = {}

    def _dynoff(E, shift, delta):
        if "pid" not in regcell:
            regcell["pid"] = E.to_reg(E.partition_id())
            regcell["off"] = E.alloc_register("dynoff")
        if shift is not None:
            E.reg_alu(regcell["off"], regcell["pid"], shift, ALU.arith_shift_left)
        if delta:
            E.reg_add(regcell["off"], regcell["off"], delta)
        return regcell["off"]

    XeT_x = XeT.rearrange("p (dc s) -> p dc s", dc=8)
    xstg_v = xstg.rearrange("p (u dc s) -> p u dc s", u=2, dc=8)

    def _xet_dyn(E, b_, par):
        first = (b_ == 0)
        off = _dynoff(E, 7 if first else None, 0 if first else (1 << 20))
        src = bass.AP(ag1_out, off, [[EL, 128], [128 * EL, 8], [1, 128]])
        return E.dma_start(xstg_v[:, par, :, :], src)

    xet_tok = None
    xcp_tok = [None, None]
    for b_ in range(NCORES):
        par = b_ % 2
        mk.w("g", xcp_tok[par])
        ld = mk.do("g", lambda E, b__=b_, p_=par: _xet_dyn(E, b__, p_),
                   mk.dsem(f"XS{par}"), 16)
        mk.w("g", ld)
        for el_ in range(2):
            xcp_tok[par] = mk.do("g", lambda E, b__=b_, el__=el_, p_=par: E.dma_start(
                XeT_x[:, :, el__ * 512 + b__ * 64:el__ * 512 + b__ * 64 + 64],
                xstg_v[:, p_, :, el__ * 64:el__ * 64 + 64]),
                mk.dsem(f"XC{par}"), 16)
        xet_tok = xcp_tok[par]
    xet_tok2 = xcp_tok[1 - (NCORES - 1) % 2]

    mk.w("t", xet_tok)
    mk.w("t", xet_tok2)
    mk.w("t", brow_tok)
    mk.w("s", brow_tok)
    eo_dma = [None, None]
    gelu_free = None
    up_free = None
    gate_free = None
    h_cons = None
    for e in range(2):
        for Hh in range(2):
            par = Hh
            gv, uv = wbufs[par]
            mk.w("t", wld[(e, Hh)])
            for fs in range(8):
                fc = Hh * 8 + fs
                if fc == 0:
                    mk.w("t", h_cons)
                mk.w("t", gate_free)
                tkg = None
                for dc in range(NCK):
                    tkg = mk.do("t", lambda E, g_=gv, dc_=dc, fs_=fs, e_=e: E.matmul(
                        pA[:, :], g_[:, dc_, fs_ * 128:(fs_ + 1) * 128],
                        XeT_v[:, dc_, e_ * 512:(e_ + 1) * 512],
                        start=(dc_ == 0), stop=False), ST)
                tkg = mk.do("t", lambda E, e_=e, fc_=fc: E.matmul(
                    pA[:, :], bg_row[0:1, e_ * FF + fc_ * 128:e_ * FF + (fc_ + 1) * 128],
                    ones1_s[0:1, 0:512], start=False, stop=True), ST)
                mk.w("t", up_free)
                tku = None
                for dc in range(NCK):
                    tku = mk.do("t", lambda E, u_=uv, dc_=dc, fs_=fs, e_=e: E.matmul(
                        pB[:, :], u_[:, dc_, fs_ * 128:(fs_ + 1) * 128],
                        XeT_v[:, dc_, e_ * 512:(e_ + 1) * 512],
                        start=(dc_ == 0), stop=False), ST)
                tku = mk.do("t", lambda E, e_=e, fc_=fc: E.matmul(
                    pB[:, :], bu_row[0:1, e_ * FF + fc_ * 128:e_ * FF + (fc_ + 1) * 128],
                    ones65_s[32:33, 0:512], start=False, stop=True), ST)
                mk.w("s", tkg)
                mk.w("s", gelu_free)
                gt = mk.do("s", lambda E: E.activation(scbA[:, 0:512], pA[:, :],
                                                       ACT.Gelu), SS)
                gate_free = gt
                mk.w("v", tku)
                ut = mk.do("v", lambda E: E.tensor_copy(scbB[:, 0:512], pB[:, :]), SV)
                up_free = ut
                mk.w("v", gt)
                htok = mk.do("v", lambda E, fc_=fc: E.tensor_tensor(
                    hfm_v[:, fc_, :], scbA[:, 0:512], scbB[:, 0:512], ALU.mult), SV)
                gelu_free = htok
            if e == 0:
                wfree[par] = mk.tok("t")
                load_wh(1, Hh, par)
        mk.w("t", mk.tok("v"))
        for oh in range(2):
            mk.w("t", wd_tok)
            eo_par = [None, None]
            for bsl in range(4):
                par = bsl % 2
                mk.w("t", eo_par[par])
                tk = None
                for fc in range(16):
                    tk = mk.do("t", lambda E, fc_=fc, bsl_=bsl, par_=par: E.matmul(
                        p4[:, par_ * 1024:par_ * 1024 + 512],
                        hfm_v[:, fc_, bsl_ * 128:(bsl_ + 1) * 128],
                        wdS_v[:, fc_, :],
                        start=(fc_ == 0), stop=(fc_ == 15)), ST)
                tk = mk.do("t", lambda E, e_=e, oh_=oh, par_=par: E.matmul(
                    p4[:, par_ * 1024:par_ * 1024 + 512], ones65_s[64:65, 0:128],
                    bd_row[0:1, e_ * 1024 + oh_ * 512:e_ * 1024 + (oh_ + 1) * 512],
                    start=False, stop=True), ST)
                mk.w("v", tk)
                mk.w("v", eo_dma[par])
                eo_par[par] = mk.do("v", lambda E, par_=par: E.tensor_copy(
                    scbA[:, 1024 + par_ * 512:1024 + (par_ + 1) * 512],
                    p4[:, par_ * 1024:par_ * 1024 + 512]), SV)
                mk.w("g", eo_par[par])
                eo_dma[par] = mk.do("g", lambda E, e_=e, bsl_=bsl, oh_=oh, par_=par:
                                    E.dma_start(
                    ag2_in.rearrange("(c p) o -> p c o", p=128)[
                        :, e_ * 4 + bsl_, oh_ * 512:(oh_ + 1) * 512],
                    scbA[:, 1024 + par_ * 512:1024 + (par_ + 1) * 512]),
                    mk.dsem(f"EO{par}"), 16)
            mk.w("g", mk.tok("t"))
            if oh == 0:
                wd_tok = load_wd(e, 1)
            elif e == 0:
                wd_tok = load_wd(1, 0)
        h_cons = mk.tok("t")
    mk.w("g", eo_dma[0])
    mk.w("g", eo_dma[1])
    mk.w("g", mk.tok("t"))
    x2r_tok = mk.do("g", lambda E: E.dma_start(
        x2b_v[:, :, :], x2_spill.rearrange("(i p) c -> p i c", p=128)),
        mk.dsem("REL"), 16)
    agt2 = mk.do("g", lambda E: E.collective_compute(
        "AllGather", ALU.bypass, replica_groups=[list(range(NCORES))],
        ins=[ag2_in.ap().opt()], outs=[ag2_out.ap().opt()]), mk.sem["cc"], 1)
    mk.w("g", agt2)
    mk.w("g", mk.tok("t"))

    def _ef_dma(E, c_):
        first = (c_ == 0)
        off = _dynoff(E, 16 if first else None, 0 if first else (1 << 20))
        src = bass.AP(ag2_out, off, [[512 * C, 2], [C, 64], [1, C]])
        return E.dma_start(eout_f_v[:, c_, :], src)

    ef_tok = None
    for c_ in range(NCORES):
        ef_tok = mk.do("g", lambda E, c__=c_: _ef_dma(E, c__),
                       mk.dsem("EFG"), 16)

    if _BUILD_OPTS["phases"] < 10:
        return _finish(mk, nc, out_d, outbuf_v)

    mk.w("t", ef_tok)
    mk.w("t", x2r_tok)
    of_tok = [None, None]
    of_fin = [None, None]
    for i in range(NT):
        par = i % 2
        off0 = 0 if par == 0 else 1024
        base = 40 if par == 0 else 48
        mk.w("t", of_tok[par])
        tk = None
        for hb in range(2):
            for ec in range(NCK):
                tk = mk.do("t", lambda E, i_=i, hb_=hb, ec_=ec, o=off0: E.matmul(
                    p4[:, o + hb_ * 512:o + (hb_ + 1) * 512],
                    combineT_v[:, ec_, i_ * 128:(i_ + 1) * 128],
                    eout_f_v[:, ec_, hb_ * 512:(hb_ + 1) * 512],
                    start=(ec_ == 0), stop=(ec_ == NCK - 1)), ST)
        mk.w("v", tk)
        mk.w("v", x2r_tok)
        mk.w("v", rct)
        h1 = scrA if par == 0 else scrB
        mk.w("v", of_fin[par])
        atok = mk.do("v", lambda E, h=h1, i_=i, o=off0, b_=base:
                     E.scalar_tensor_tensor(
                         h[:, 0:1024], p4[:, o:o + 1024], recip_tm[:, i_:i_ + 1],
                         x2b_v[:, i_, :], ALU.mult, ALU.add,
                         accum_out=stat[:, b_:b_ + 1]), SV)
        mk.w("s", atok)
        sqt = mk.do("s", lambda E, h=h1, b_=base: E.activation(
            h[:, 1024:2048], h[:, 0:1024], ACT.Square,
            accum_out=stat[:, b_ + 1:b_ + 2]), SS)
        mk.do("v", lambda E, b_=base: E.tensor_tensor(
            stat[:, b_ + 2:b_ + 3], stat[:, b_:b_ + 1],
            stat[:, b_:b_ + 1], ALU.mult), SV)
        mk.w("v", sqt)
        vtok = mk.do("v", lambda E, b_=base: E.scalar_tensor_tensor(
            stat[:, b_ + 3:b_ + 4], stat[:, b_ + 2:b_ + 3], -1.0 / C,
            stat[:, b_ + 1:b_ + 2], ALU.mult, ALU.add), SV)
        mk.w("s", vtok)
        mk.do("s", lambda E, b_=base: E.activation(
            stat[:, b_ + 4:b_ + 5], stat[:, b_ + 3:b_ + 4], ACT.Ln,
            bias=eps_s[:, 0:1], scale=1.0 / C), SS)
        stok = mk.do("s", lambda E, b_=base: E.activation(
            stat[:, b_ + 5:b_ + 6], stat[:, b_ + 4:b_ + 5], ACT.Exp,
            scale=-0.5), SS)
        mk.do("v", lambda E, b_=base: E.tensor_scalar_mul(
            stat[:, b_ + 6:b_ + 7], stat[:, b_:b_ + 1], 1.0 / C), SV)
        mk.w("v", stok)
        of_fin[par] = mk.do("v", lambda E, h=h1, i_=i, b_=base: E.tensor_scalar(
            outbuf_v[:, i_, :], h[:, 0:1024], stat[:, b_ + 6:b_ + 7],
            stat[:, b_ + 5:b_ + 6], ALU.subtract, ALU.mult), SV)
        of_tok[par] = atok

    return _finish(mk, nc, out_d, outbuf_v)


def _finish(mk, nc, out_d, outbuf_v):
    mk.w("g", mk.tok("v"))
    mk.w("g", mk.tok("s"))
    st = mk.do("g", lambda E: E.dma_start(
        out_d.rearrange("(i p) c -> p i c", p=128), outbuf_v[:, :, :]),
        mk.dsem("OUT"), 16)
    for name in _BUILD_OPTS["taps"]:
        h = mk.sb[name]
        shp = list(h.shape)
        o = nc.dram_tensor("tap_" + name, shp, h.ap().dtype, kind="ExternalOutput")
        st = mk.do("g", lambda E, o_=o, h_=h: E.dma_start(o_[:, :], h_[:, :]),
                   mk.dsem("OUT"), 16)
    mk.w("g", st)
    nc._mk_specs = list(mk.specs)
    with nc.Block() as block:
        block.tensor(lambda E: [f(E) for f in mk.ops["t"]])
        block.vector(lambda E: [f(E) for f in mk.ops["v"]])
        block.scalar(lambda E: [f(E) for f in mk.ops["s"]])
        block.gpsimd(lambda E: [f(E) for f in mk.ops["g"]])
        block.sync(lambda E: [f(E) for f in mk.ops["y"]])
    return nc


def _solve_plan(specs, cap):
    plan = {}
    placed = []
    for name, size, p0, p1 in sorted(specs, key=lambda x: -x[1]):
        taken = sorted((plan[n], plan[n] + s) for (n, s, a0, a1) in placed
                       if not (a1 < p0 or a0 > p1))
        off = 0
        for (a, bnd) in taken:
            if off + size <= a:
                break
            off = max(off, bnd)
        assert off + size <= cap, f"SBUF plan overflow: {name} {size}B at {off}"
        plan[name] = off
        placed.append((name, size, p0, p1))
    return plan


_PLAN_CACHE = {}


def build_planned():
    key = _BUILD_OPTS["phases"]
    if key not in _PLAN_CACHE:
        nc0 = build(None)
        _PLAN_CACHE[key] = _solve_plan(nc0._mk_specs, 206336)
    return build(_PLAN_CACHE[key])


def _blkd(p2lT):
    z = np.zeros((128, 64), np.float32)
    z[0:64, 0:32] = p2lT
    z[64:128, 32:64] = p2lT
    return z


def _stage_inputs(inputs, core):
    f32 = np.float32
    x = np.asarray(inputs["L0_attn_x"][core], dtype=f32)
    caw = np.asarray(inputs["L0_attn_c_attn_w"], dtype=f32)
    cab = np.asarray(inputs["L0_attn_c_attn_b"], dtype=f32)
    q2l = np.asarray(inputs["L0_attn_q2l_w"], dtype=f32)
    q2lb = np.asarray(inputs["L0_attn_q2l_b"], dtype=f32)
    kv2l = np.asarray(inputs["L0_attn_kv2l_w"], dtype=f32)
    kv2lb = np.asarray(inputs["L0_attn_kv2l_b"], dtype=f32)
    l2o = np.asarray(inputs["L0_attn_l2o_w"], dtype=f32)
    l2ob = np.asarray(inputs["L0_attn_l2o_b"], dtype=f32)
    wqk = np.asarray(inputs["L0_attn_wqk_w"], dtype=f32)
    wqkb = np.asarray(inputs["L0_attn_wqk_b"], dtype=f32)
    cpw = np.asarray(inputs["L0_attn_cproj_w"], dtype=f32)
    cpb = np.asarray(inputs["L0_attn_cproj_b"], dtype=f32)
    d = dict(_CONSTS)
    d.update({
        "x_tm_in": np.ascontiguousarray(x),
        "x_fm_in": np.ascontiguousarray(x.T),
        "caw": caw,
        "cprojT_in": np.ascontiguousarray(cpw.T),
        "q2lT_in": np.ascontiguousarray(q2l.T),
        "kv2lT_in": np.ascontiguousarray(kv2l.T),
        "l2o_in": l2o,
        "wqkT_in": np.ascontiguousarray(wqk.T),
        "blkdQ_in": _blkd(q2l.T),
        "blkdKV_in": _blkd(kv2l.T),
        "wqkT4_in": np.tile(np.ascontiguousarray(wqk.T), (4, 1)),
        "l2o2_in": np.tile(l2o, (2, 1)),
        "bqr_in": np.ascontiguousarray(cab[0:C].reshape(H, HD).T),
        "bkr_in": np.ascontiguousarray(cab[C:2 * C].reshape(H, HD).T),
        "bvr_in": np.ascontiguousarray(cab[2 * C:3 * C].reshape(H, HD).T),
        "q2lb16_in": np.tile(q2lb[None, :], (H, 1)),
        "kvlb16_in": np.tile(kv2lb[None, :], (H, 1)),
        "wqkb512_in": np.tile(wqkb, H)[None, :].copy(),
        "l2ob_in": np.ascontiguousarray(np.tile(l2ob, H).reshape(8, 128).T),
        "cprojb_in": cpb[None, :].copy(),
        "normg128": np.tile(np.asarray(inputs["L0_ffn_norm_g"], f32)[None, :], (128, 1)),
        "normb128": np.tile(np.asarray(inputs["L0_ffn_norm_b"], f32)[None, :], (128, 1)),
        "slotg128": np.tile(np.asarray(inputs["L0_ffn_slot_g"], f32)[None, :], (128, 1)),
        "slotb128": np.tile(np.asarray(inputs["L0_ffn_slot_b"], f32)[None, :], (128, 1)),
        "scale128": np.full((128, 1), np.asarray(inputs["L0_ffn_scale"], f32)[0], f32),
        "semb": np.asarray(inputs["L0_ffn_slot_embeds"], f32).reshape(EL, C),
        "wg2": np.asarray(inputs["L0_ffn_Wg"], f32)[2 * core:2 * core + 2],
        "wu2": np.asarray(inputs["L0_ffn_Wu"], f32)[2 * core:2 * core + 2],
        "wd2": np.asarray(inputs["L0_ffn_Wd"], f32)[2 * core:2 * core + 2],
        "bg2": np.asarray(inputs["L0_ffn_bg"], f32)[2 * core:2 * core + 2],
        "bu2": np.asarray(inputs["L0_ffn_bu"], f32)[2 * core:2 * core + 2],
        "bd2": np.asarray(inputs["L0_ffn_bd"], f32)[2 * core:2 * core + 2],
    })
    return d


def kernel(**inputs):
    nc = build_planned()
    in_maps = [_stage_inputs(inputs, c) for c in range(NCORES)]
    res = run_bass_kernel_spmd(nc, in_maps, core_ids=list(range(NCORES)))
    out = np.stack([np.asarray(res.results[c]["out"]) for c in range(NCORES)], axis=0)
    if _BUILD_OPTS["taps"]:
        kernel._taps = [{k: np.asarray(v) for k, v in r.items()} for r in res.results]
    return out


# version 12
# speedup vs baseline: 1.0542x; 1.0116x over previous
import math
import numpy as np
import ml_dtypes

import concourse.bass as bass
import concourse.mybir as mybir
from concourse.bass_utils import run_bass_kernel_spmd

F32 = mybir.dt.float32
BF16 = mybir.dt.bfloat16
AX = mybir.AxisListType.X
ALU = mybir.AluOpType
ACT = mybir.ActivationFunctionType

T = 2048
C = 1024
H = 16
HD = 64
LAT = 32
G = 4
NT = 16
NB = 4
NCK = 8
E = 16
S = 64
EL = 1024
FF = 2048
NCORES = 8
SCALE = 1.0 / math.sqrt(LAT)
EPS = 1e-5

_BUILD_OPTS = {"taps": (), "phases": 99}


class _Sem:
    def __init__(self, nc, name):
        self.h = nc.alloc_semaphore(name)
        self.n = 0


class MK:
    def __init__(self, nc):
        self.nc = nc
        self.ops = {e: [] for e in "tvsgy"}
        self.sem = {k: _Sem(nc, "sem_" + k) for k in
                    ("t", "v", "s", "g", "gd", "yd", "cc")}
        self.seen = {e: {} for e in "tvsgy"}
        self.sb = {}
        self.regions = []
        self.specs = []
        self.aliases = {}
        self.plan = None
        self.arena_base = None
        self.arena_cap = None

    def on(self, e, fn):
        self.ops[e].append(fn)

    def w(self, e, tok):
        if tok is None:
            return
        sem, n = tok
        if self.seen[e].get(id(sem), -1) >= n:
            return
        self.seen[e][id(sem)] = n
        self.on(e, lambda E, s=sem, v=n: E.wait_ge(s.h, v))

    def do(self, e, fn, sem=None, inc=1):
        if e in ("v", "s") and sem is not None:
            self.w(e, (sem, sem.n))
        if sem is None:
            self.on(e, fn)
            return None
        sem.n += inc
        n = sem.n
        self.on(e, lambda E, f=fn, s=sem, i=inc: f(E).then_inc(s.h, i))
        return (sem, n)

    def tok(self, k):
        s = self.sem[k]
        return (s, s.n)

    def dsem(self, name):
        if name not in self.sem:
            self.sem[name] = _Sem(self.nc, "sem_" + name)
        return self.sem[name]

    def salloc(self, name, shape, dtype, p0, p1, alias=None):
        size = shape[1] * (2 if dtype == BF16 else 4)
        size = (size + 63) & ~63
        if alias is None:
            self.specs.append((name, size, p0, p1))
        else:
            self.aliases[name] = alias
        if self.plan is not None:
            off = self.plan[name] if alias is None else self.plan[alias]
        else:
            off = 0
        self.regions.append((off, size, p0, p1))
        h = self.nc.alloc_sbuf_tensor_at(name, list(shape), dtype,
                                         offset=self.arena_base + off)
        self.sb[name] = h
        return h


def _rope_tables():
    inv = 1.0 / (10000.0 ** (np.arange(0, LAT, 2, dtype=np.float64) / LAT))
    t = np.arange(T, dtype=np.float64)
    fr = np.outer(t, inv)
    cos = np.repeat(np.cos(fr), 2, axis=-1)[:, :LAT]
    sin = np.repeat(np.sin(fr), 2, axis=-1)[:, :LAT]
    return (np.tile(cos.T, (G, 1)).astype(np.float32),
            np.tile(sin.T, (G, 1)).astype(np.float32))


def _const_inputs():
    bf = ml_dtypes.bfloat16
    cosT, sinT = _rope_tables()
    prot = np.zeros((128, 128), dtype=np.float32)
    for blk in range(G):
        for i in range(LAT // 2):
            b = blk * 32
            prot[b + 2 * i, b + 2 * i + 1] = -1.0
            prot[b + 2 * i + 1, b + 2 * i] = 1.0
    tri = (np.arange(128)[None, :] >= np.arange(128)[:, None]).astype(np.float32)
    blk4T = np.zeros((4, 128), dtype=np.float32)
    for j in range(4):
        blk4T[j, 32 * j:32 * (j + 1)] = 1.0
    return {
        "cosT": cosT, "sinT": sinT,
        "protT": prot.T.astype(bf).copy(),
        "trimask": tri.astype(bf),
        "identf": np.eye(128, dtype=np.float32),
        "identb": np.eye(128, dtype=np.float32).astype(ml_dtypes.bfloat16),
        "ones1": np.ones((1, 512), dtype=bf),
        "ones128": np.ones((128, 1), dtype=bf),
        "blk4T": blk4T.astype(bf),
        "ones65": np.ones((65, 512), dtype=bf),
        "blkones4": blk4T.T.astype(bf).copy(),
        "eps128": np.full((128, 1), EPS, dtype=np.float32),
    }


_CONSTS = _const_inputs()


def build(plan=None):
    nc = bass.Bass("TRN2", target_bir_lowering=False)
    DI = lambda n, s, dt=F32: nc.dram_tensor(n, list(s), dt, kind="ExternalInput")

    x_tm_in = DI("x_tm_in", [T, C])
    x_fm_in = DI("x_fm_in", [C, T])
    caw_in = DI("caw", [3 * C, C])
    cprojT_in = DI("cprojT_in", [C, C])
    q2lT_in = DI("q2lT_in", [HD, LAT])
    kv2lT_in = DI("kv2lT_in", [HD, LAT])
    l2o_in = DI("l2o_in", [HD, LAT])
    wqkT_in = DI("wqkT_in", [LAT, LAT])
    bqr_in = DI("bqr_in", [HD, H])
    bkr_in = DI("bkr_in", [HD, H])
    bvr_in = DI("bvr_in", [HD, H])
    q2lb16_in = DI("q2lb16_in", [H, LAT])
    kvlb16_in = DI("kvlb16_in", [H, LAT])
    wqkb512_in = DI("wqkb512_in", [1, 512])
    l2ob_in = DI("l2ob_in", [128, 8])
    cprojb_in = DI("cprojb_in", [1, C])
    cosT_in = DI("cosT", [128, T])
    sinT_in = DI("sinT", [128, T])
    protT_in = DI("protT", [128, 128], BF16)
    tri_in = DI("trimask", [128, 128], BF16)
    identf_in = DI("identf", [128, 128], F32)
    identb_in = DI("identb", [128, 128], BF16)
    ones1_in = DI("ones1", [1, 512], BF16)
    ones128_in = DI("ones128", [128, 1], BF16)
    ones65_in = DI("ones65", [65, 512], BF16)
    blk4T_in = DI("blk4T", [4, 128], BF16)
    blkones4_in = DI("blkones4", [128, 4], BF16)
    blkdQ_in = DI("blkdQ_in", [128, 64])
    blkdKV_in = DI("blkdKV_in", [128, 64])
    wqkT4_in = DI("wqkT4_in", [128, LAT])
    l2o2_in = DI("l2o2_in", [128, LAT])
    eps128_in = DI("eps128", [128, 1], F32)
    normg_in = DI("normg128", [128, C])
    normb_in = DI("normb128", [128, C])
    slotg_in = DI("slotg128", [128, C])
    slotb_in = DI("slotb128", [128, C])
    scale128_in = DI("scale128", [128, 1])
    semb_in = DI("semb", [EL, C])
    wg2_in = DI("wg2", [2, C, FF])
    wu2_in = DI("wu2", [2, C, FF])
    wd2_in = DI("wd2", [2, FF, C])
    bg2_in = DI("bg2", [2, FF])
    bu2_in = DI("bu2", [2, FF])
    bd2_in = DI("bd2", [2, C])

    out_d = nc.dram_tensor("out", [T, C], F32, kind="ExternalOutput")

    ag1_in = nc.dram_tensor("ag1_in", [C, EL], BF16)
    ag1_out = nc.dram_tensor("ag1_out", [NCORES * C, EL], BF16, addr_space="Shared")
    ag2_in = nc.dram_tensor("ag2_in", [EL, C], BF16)
    ag2_out = nc.dram_tensor("ag2_out", [NCORES * EL, C], BF16, addr_space="Shared")
    x2_spill = nc.dram_tensor("x2_spill", [T, C], BF16)

    ARENA = 206336
    arena = nc.alloc_sbuf_tensor("arena", [128, ARENA], mybir.dt.uint8)
    mk = MK(nc)
    mk.plan = plan
    mk.arena_base = nc.lookup_mloc(arena).addr
    mk.arena_cap = ARENA

    A = mk.salloc
    scrA = A("scrA", [128, T], F32, 0, 9)
    scrB = A("scrB", [128, T], F32, 0, 9)
    scrC = A("scrC", [128, T], F32, 4, 5)
    scrD = A("scrD", [128, T], F32, 10, 10)
    scbA = A("scbA", [128, T], BF16, 0, 9)
    dbgA = A("dbgA", [128, 1024], F32, 0, 2)
    scbB = A("scbB", [128, T], BF16, 0, 9)
    stat = A("stat", [128, 64], F32, 0, 9)
    identf_s = A("identf_s", [128, 128], F32, 0, 9)
    identb_s = A("identb_s", [128, 128], BF16, 0, 9)
    ones1_s = A("ones1_s", [1, 512], BF16, 0, 9)
    ones128_s = A("ones128_s", [128, 1], BF16, 0, 9)
    ones65_s = A("ones65_s", [65, 512], BF16, 0, 10)
    eps_s = A("eps_s", [128, 1], F32, 0, 9)
    x_tm = A("x_tm", [128, NT * C], BF16, 0, 4)
    wpeT = A("wpeT", [128, G * C], BF16, 0, 4)
    bprj_b = A("bprj_b", [1, C], BF16, 0, 4)
    tri_s = A("tri_s", [128, 128], BF16, 0, 3)
    blk4T_s = A("blk4T_s", [4, 128], BF16, 0, 3)
    combineT = A("combineT", [128, NCK * T], BF16, 7, 10)
    recip_tm = A("recip_tm", [128, 16], F32, 7, 10)
    x_fm = A("x_fm", [128, NCK * T], BF16, 0, 2)
    weffq = A("weffq", [128, NCK * 512], BF16, 0, 2)
    weffk = A("weffk", [128, NCK * 512], BF16, 0, 2)
    weffv = A("weffv", [128, NCK * 512], BF16, 0, 2)
    cosbf = A("cosbf", [128, T], BF16, 0, 2)
    sinbf = A("sinbf", [128, T], BF16, 0, 2)
    prot_s = A("prot_s", [128, 128], BF16, 0, 2)
    blkones4_s = A("blkones4_s", [128, 4], BF16, 0, 2)
    wqkT_b = A("wqkT_b", [32, 32], BF16, 0, 2)
    wqkT4 = A("wqkT4", [128, 32], BF16, 0, 2)
    biasq_row = A("biasq_row", [1, 512], BF16, 0, 2)
    biask_row = A("biask_row", [1, 512], BF16, 0, 2)
    biasv_row = A("biasv_row", [1, 512], BF16, 0, 2)
    wqkb_b = A("wqkb_b", [1, 512], BF16, 0, 2)
    QK = A("QK", [128, G * T], BF16, 1, 3)
    KR = A("KR", [128, G * T], BF16, 1, 3)
    vt = A("vt", [128, NT * 512], BF16, 1, 3)
    xn_tm = A("xn_tm", [128, NT * C], BF16, 5, 7)
    rowsB2 = A("rowsB2", [65, 2 * T], BF16, 7, 9)
    l2o_b = A("l2o_b", [64, 32], BF16, 0, 1)
    l2o2 = A("l2o2", [128, 32], BF16, 0, 1)
    nbias_pp = A("nbias_pp", [128, 1], F32, 2, 3)
    ylat = A("ylat", [128, G * T], BF16, 3, 4)
    x2 = A("x2", [128, NT * C], BF16, 4, 5)
    normg_s = A("normg_s", [128, C], F32, 4, 5)
    normb_s = A("normb_s", [128, C], F32, 4, 5)
    slotg_s = A("slotg_s", [128, C], F32, 4, 5)
    slotb_s = A("slotb_s", [128, C], F32, 4, 5)
    scale_s = A("scale_s", [128, 1], F32, 4, 5)
    xn_fm = A("xn_fm", [128, NCK * T], BF16, 5, 6)
    seT = A("seT", [128, NCK * C], BF16, 5, 6)
    logits = A("logits", [128, NT * C], F32, 6, 7)
    rowmax_tm = A("rowmax_tm", [128, 16], F32, 6, 7)
    XeT = A("XeT", [128, NCK * C], BF16, 8, 9)
    xstg = A("xstg", [128, 2 * C], BF16, 8, 8)
    eout_f = A("eout_f", [128, NCK * C], BF16, 9, 10)
    x2b = A("x2b", [128, NT * C], BF16, 9, 10)
    cawb0 = A("cawb0", [128, C], BF16, 0, 0)
    cawb1 = A("cawb1", [128, C], BF16, 0, 0)
    cprojT2 = A("cprojT2", [128, 2 * C], BF16, 0, 0)
    q2lT_f = A("q2lT_f", [64, 32], F32, 0, 0)
    kv2lT_f = A("kv2lT_f", [64, 32], F32, 0, 0)
    q2lT_b = A("q2lT_b", [64, 32], BF16, 0, 0)
    kv2lT_b = A("kv2lT_b", [64, 32], BF16, 0, 0)
    l2o_f = A("l2o_f", [64, 32], F32, 0, 0)
    wqkT_f = A("wqkT_f", [32, 32], F32, 0, 0)
    blkdQ = A("blkdQ", [128, 64], BF16, 0, 2)
    blkdKV = A("blkdKV", [128, 64], BF16, 0, 2)
    blkdQf = A("blkdQf", [128, 64], F32, 0, 0)
    blkdKVf = A("blkdKVf", [128, 64], F32, 0, 0)
    wqkT4f = A("wqkT4f", [128, LAT], F32, 0, 0)
    l2o2f = A("l2o2f", [128, LAT], F32, 0, 0)
    bxr_f = A("bxr_f", [64, 16], F32, 0, 0)
    bxr_b = A("bxr_b", [64, 16], BF16, 0, 0)
    bx16c = A("bx16c", [16, 32], F32, 0, 0)
    bx16_f = A("bx16_f", [16, 32], F32, 0, 0)
    bx16b = A("bx16b", [16, 32], BF16, 0, 0)
    wqkb_f = A("wqkb_f", [1, 512], F32, 0, 0)
    l2ob_f = A("l2ob_f", [128, 8], F32, 0, 0)
    l2ob_b = A("l2ob_b", [128, 8], BF16, 0, 0)
    cprojb_f = A("cprojb_f", [1, C], F32, 0, 0)
    bprj_f = A("bprj_f", [1, C], F32, 0, 0)
    nb44 = A("nb44", [4, 4], F32, 2, 2)
    nbrow = A("nbrow", [1, 16], F32, 2, 2)
    nbrow2 = A("nbrow2", [1, 1], BF16, 2, 2)
    PT = A("PT", [128, 2 * T], BF16, 3, 3)
    sums4 = A("sums4", [4, 512], F32, 3, 3)
    rec4 = A("rec4", [4, 512], F32, 3, 3)
    rec4b = A("rec4b", [4, 512], BF16, 3, 3)
    rbbf = A("rbbf", [128, 512], BF16, 3, 3)
    dispatchN = A("dispatchN", [128, NT * C], BF16, 7, 7)
    lgb = A("lgb", [128, 2 * NT * C], BF16, 7, 7, alias="logits")
    scrBb = A("scrBb", [128, T], BF16, 7, 7, alias="scrB")
    nrm_s = A("nrm_s", [128, 16], F32, 7, 7)
    rsum_tm = A("rsum_tm", [128, 16], F32, 7, 7)
    rsL_s = A("rsL_s", [128, 16], F32, 7, 7)
    cmax_s = A("cmax_s", [128, 8], F32, 7, 7)
    ncmax_s = A("ncmax_s", [128, 8], F32, 7, 7)
    cs2_s = A("cs2_s", [128, 8], F32, 7, 7)
    rcl_s = A("rcl_s", [128, 8], F32, 7, 7)
    rcT_s = A("rcT_s", [128, 8], F32, 7, 7)
    wgA = A("wgA", [128, NCK * C], BF16, 8, 8)
    wgB = A("wgB", [128, NCK * C], BF16, 8, 8)
    wuA = A("wuA", [128, NCK * C], BF16, 8, 8)
    wuB = A("wuB", [128, NCK * C], BF16, 8, 8)
    wdS = A("wdS", [128, 16 * 512], BF16, 8, 8)
    hfm = A("hfm", [128, 16 * 512], BF16, 8, 8)
    outbuf = A("outbuf", [128, NT * C], F32, 10, 10)

    bg_row = rowsB2[0:1, 0:2 * FF]
    bu_row = rowsB2[32:33, 0:2 * FF]
    bd_row = rowsB2[64:65, 0:2 * C]

    p4 = nc.alloc_psum_tensor("p4", [128, 2048], F32)
    pA = nc.alloc_psum_tensor("pA", [128, 512], F32)
    pB = nc.alloc_psum_tensor("pB", [128, 512], F32)
    pC = nc.alloc_psum_tensor("pC", [128, 512], F32)
    pD = nc.alloc_psum_tensor("pD", [128, 512], F32)

    ST, SV, SS = mk.sem["t"], mk.sem["v"], mk.sem["s"]
    GD, YD = mk.sem["gd"], mk.sem["yd"]

    x_fm_v = x_fm.rearrange("p (c t) -> p c t", c=NCK)
    x_tm_v = x_tm.rearrange("p (i c) -> p i c", i=NT)
    weffq_v = weffq.rearrange("p (c x) -> p c x", c=NCK)
    weffk_v = weffk.rearrange("p (c x) -> p c x", c=NCK)
    weffv_v = weffv.rearrange("p (c x) -> p c x", c=NCK)
    wpeT_v = wpeT.rearrange("p (g o) -> p g o", g=G)
    QK_v = QK.rearrange("p (g t) -> p g t", g=G)
    KR_v = KR.rearrange("p (g t) -> p g t", g=G)
    vt_v = vt.rearrange("p (i x) -> p i x", i=NT)
    ylat_v = ylat.rearrange("p (g t) -> p g t", g=G)
    PT_v = PT.rearrange("p (u t) -> p u t", u=2)
    x2_v = x2.rearrange("p (i c) -> p i c", i=NT)
    xn_tm_v = xn_tm.rearrange("p (i c) -> p i c", i=NT)
    xn_fm_v = xn_fm.rearrange("p (c t) -> p c t", c=NCK)
    seT_v = seT.rearrange("p (c e) -> p c e", c=NCK)
    logits_v = logits.rearrange("p (i c) -> p i c", i=NT)
    combineT_v = combineT.rearrange("p (c t) -> p c t", c=NCK)
    dnT_v = dispatchN.rearrange("p (e t) -> p e t", e=NCK)
    lgb_v = lgb.rearrange("p (i c) -> p i c", i=2 * NT)
    XeT_v = XeT.rearrange("p (c e) -> p c e", c=NCK)
    wgA_v = wgA.rearrange("p (c f) -> p c f", c=NCK)
    wgB_v = wgB.rearrange("p (c f) -> p c f", c=NCK)
    wuA_v = wuA.rearrange("p (c f) -> p c f", c=NCK)
    wuB_v = wuB.rearrange("p (c f) -> p c f", c=NCK)
    wdS_v = wdS.rearrange("p (c o) -> p c o", c=16)
    hfm_v = hfm.rearrange("p (c x) -> p c x", c=16)
    eout_f_v = eout_f.rearrange("p (c o) -> p c o", c=NCK)
    x2b_v = x2b.rearrange("p (i c) -> p i c", i=NT)
    outbuf_v = outbuf.rearrange("p (i c) -> p i c", i=NT)
    p4q = p4.rearrange("p (q x) -> p q x", q=16)
    p4j = p4.rearrange("p (j x) -> p j x", j=4)

    ldtoks = {}

    def gload(name, dst_ap, src_ap, sem=None):
        s_ = mk.dsem(sem or ("g_" + name))
        ldtoks[name] = mk.do("g", lambda E, d=dst_ap, s=src_ap: E.dma_start(d, s), s_, 16)

    def yload(name, dst_ap, src_ap, sem=None):
        s_ = mk.dsem(sem or ("y_" + name))
        ldtoks[name] = mk.do("y", lambda E, d=dst_ap, s=src_ap: E.dma_start(d, s), s_, 16)
        return ldtoks[name]

    def yload_Y0(name, dst_ap, src_ap):
        return yload(name, dst_ap, src_ap, sem="Y0")

    gload("x_fm", x_fm_v[:, :, :], x_fm_in.rearrange("(c p) t -> p c t", p=128), sem="G0")
    gload("x_tm", x_tm_v[:, :, :], x_tm_in.rearrange("(i p) c -> p i c", p=128), sem="G0")
    gload("cos", cosbf[:, :], cosT_in[:, :], sem="G0")
    gload("sin", sinbf[:, :], sinT_in[:, :], sem="G0")
    yload_Y0("identf", identf_s[:, :], identf_in[:, :])
    yload_Y0("identb", identb_s[:, :], identb_in[:, :])
    yload_Y0("tri", tri_s[:, :], tri_in[:, :])
    yload_Y0("prot", prot_s[:, :], protT_in[:, :])
    yload_Y0("ones1", ones1_s[:, :], ones1_in[:, :])
    yload_Y0("ones128", ones128_s[:, :], ones128_in[:, :])
    yload_Y0("ones65", ones65_s[:, :], ones65_in[:, :])
    yload_Y0("blk4T", blk4T_s[:, :], blk4T_in[:, :])
    yload_Y0("blkones4", blkones4_s[:, :], blkones4_in[:, :])
    yload_Y0("eps", eps_s[:, :], eps128_in[:, :])
    yload_Y0("q2lT", q2lT_f[:, :], q2lT_in[:, :])
    yload_Y0("kv2lT", kv2lT_f[:, :], kv2lT_in[:, :])
    yload_Y0("wqkb", wqkb_f[:, :], wqkb512_in[:, :])
    yload_Y0("l2ob", l2ob_f[:, :], l2ob_in[:, :])
    yload_Y0("cprojb", cprojb_f[:, :], cprojb_in[:, :])
    yload_Y0("blkdQ", blkdQf[:, :], blkdQ_in[:, :])
    yload_Y0("blkdKV", blkdKVf[:, :], blkdKV_in[:, :])
    yload_Y0("wqkT4", wqkT4f[:, :], wqkT4_in[:, :])
    yload_Y0("l2o2", l2o2f[:, :], l2o2_in[:, :])

    TOK_G0 = mk.tok("G0")
    TOK_Y0 = mk.tok("Y0")
    for _n in list(ldtoks):
        if mk.sem.get("G0") and ldtoks[_n] and ldtoks[_n][0] is mk.sem["G0"]:
            ldtoks[_n] = TOK_G0
        if mk.sem.get("Y0") and ldtoks[_n] and ldtoks[_n][0] is mk.sem["Y0"]:
            ldtoks[_n] = TOK_Y0
    tok_smalls = None
    for nm, (dst, src) in {
        "q2lT": (q2lT_b, q2lT_f), "kv2lT": (kv2lT_b, kv2lT_f),
        "wqkb": (wqkb_b, wqkb_f), "l2ob": (l2ob_b, l2ob_f),
        "blkdQ": (blkdQ, blkdQf), "blkdKV": (blkdKV, blkdKVf),
        "wqkT4": (wqkT4, wqkT4f), "l2o2": (l2o2, l2o2f),
    }.items():
        mk.w("v", ldtoks[nm])
        tok_smalls = mk.do("v", lambda E, d=dst, s=src:
                           E.tensor_copy(d[:, :], s[:, :]), SV)
    tok_blkd = tok_smalls
    tok_l2o2 = tok_smalls
    tok_wqk4 = tok_smalls

    parts = [(0, blkdQ, weffq_v), (C, blkdKV, weffk_v), (2 * C, blkdKV, weffv_v)]
    caw_bufs = [cawb0, cawb1]
    caw_tok = [None, None]
    psQ = [pA, pB]
    ps_tok = [None, None]
    for pi, (roff, blkd, wv) in enumerate(parts):
        for p in range(8):
            ii = pi * 8 + p
            par = ii % 2
            buf = caw_bufs[par]
            mk.w("g", caw_tok[par])
            ct = mk.do("g", lambda E, b=buf, r=roff + p * 128:
                       E.dma_start(b[:, :], caw_in[r:r + 128, :]),
                       mk.dsem(f"CAW{par}"), 16)
            ps = psQ[par]
            mk.w("t", ct)
            mk.w("t", tok_blkd)
            mk.w("t", ps_tok[par])
            tk = None
            for j in range(NCK):
                tk = mk.do("t", lambda E, ps_=ps, b=buf, bd=blkd, j_=j:
                           E.matmul(ps_[:, j_ * 64:(j_ + 1) * 64],
                                    b[:, j_ * 128:(j_ + 1) * 128], bd[:, :],
                                    start=True, stop=True), ST)
            caw_tok[par] = tk
            mk.w("v", tk)
            dst = wv[:, :, p * 64:(p + 1) * 64]
            src = ps.rearrange("p (j x) -> p j x", j=8)
            ps_tok[par] = mk.do("v", lambda E, d=dst, s=src:
                                E.tensor_copy(d, s[:, :, :]), SV)

    prev_cast = None
    for brow, bin_, b16in, rhsw in ((biasq_row, bqr_in, q2lb16_in, q2lT_b),
                                    (biask_row, bkr_in, kvlb16_in, kv2lT_b),
                                    (biasv_row, bvr_in, kvlb16_in, kv2lT_b)):
        mk.w("y", prev_cast)
        mk.do("y", lambda E, b=bin_: E.dma_start(bxr_f[:, :], b[:, :]),
              mk.dsem("YB"), 16)
        t2 = mk.do("y", lambda E, b=b16in: E.dma_start(bx16c[:, :], b[:, :]),
                   mk.dsem("YB"), 16)
        mk.w("v", t2)
        t3 = mk.do("v", lambda E: E.tensor_copy(bxr_b[:, :], bxr_f[:, :]), SV)
        mk.w("t", t3)
        mk.w("t", ps_tok[0])
        t4 = mk.do("t", lambda E, r=rhsw: E.matmul(pA[0:16, 0:32], bxr_b[:, :],
                                                   r[:, :], start=True, stop=True), ST)
        mk.w("v", t4)
        t5 = mk.do("v", lambda E: E.tensor_tensor(bx16_f[:, :], pA[0:16, 0:32],
                                                  bx16c[:, :], ALU.add), SV)
        ps_tok[0] = t5
        t6 = mk.do("v", lambda E: E.tensor_copy(bx16b[:, :], bx16_f[:, :]), SV)
        mk.w("g", t6)
        t7 = mk.do("g", lambda E, d=brow: E.dma_start(d[0:1, :], bx16b[:, :]),
                   mk.dsem("BRW"), 16)
        mk.w("v", t7)
        prev_cast = t5
        ldtoks["brow"] = t7

    cp2 = cprojT2.rearrange("p (u o) -> p u o", u=2)
    wpe_prev_mm = None
    wpe_done = None
    for g in range(G):
        mk.w("g", wpe_prev_mm)
        ct = mk.do("g", lambda E, g_=g: E.dma_start(
            cp2[:, :, :],
            cprojT_in.rearrange("(u p) o -> p u o", p=128)[:, 2 * g_:2 * g_ + 2, :]),
            mk.dsem("CPJ"), 16)
        mk.w("t", ct)
        mk.w("t", tok_l2o2)
        mk.w("t", wpe_done)
        tk = None
        for j in range(G):
            for hb in range(2):
                tk = mk.do("t", lambda E, j_=j, hb_=hb: E.matmul(
                    p4[32 * j_:32 * (j_ + 1), hb_ * 512:(hb_ + 1) * 512],
                    l2o2[(j_ % 2) * 64:(j_ % 2) * 64 + 64, :],
                    cp2[(j_ % 2) * 64:(j_ % 2) * 64 + 64, j_ // 2,
                        hb_ * 512:(hb_ + 1) * 512],
                    start=True, stop=True,
                    tile_position=((j_ % 2) * 64, 32 * j_)), ST)
        for u in range(2):
            for hb in range(2):
                tk2 = mk.do("t", lambda E, u_=u, g_=g, hb_=hb: E.matmul(
                    p4[0:1, 1024 + hb_ * 512:1024 + (hb_ + 1) * 512],
                    l2ob_b[:, 2 * g_ + u_:2 * g_ + u_ + 1],
                    cp2[:, u_, hb_ * 512:(hb_ + 1) * 512],
                    start=(2 * g_ + u_ == 0), stop=(2 * g_ + u_ == 7)), ST)
        wpe_prev_mm = tk2
        mk.w("v", tk)
        wpe_done = mk.do("v", lambda E, g_=g: E.tensor_copy(
            wpeT_v[:, g_, :], p4[:, 0:1024]), SV)
    mk.w("v", wpe_prev_mm)
    mk.w("v", ldtoks["cprojb"])
    mk.do("v", lambda E: E.tensor_tensor(bprj_f[0:1, :], p4[0:1, 1024:2048],
                                         cprojb_f[0:1, :], ALU.add), SV)
    tok_bprj = mk.do("v", lambda E: E.tensor_copy(bprj_b[0:1, :], bprj_f[0:1, :]), SV)

    if _BUILD_OPTS["phases"] < 1:
        return _finish(mk, nc, out_d, outbuf_v)

    mk.w("t", ldtoks["x_fm"])
    mk.w("t", ldtoks["ones1"])
    mk.w("t", ldtoks["brow"])
    mk.w("t", tok_wqk4)
    mk.w("t", tok_bprj)
    mk.w("v", ldtoks["cos"])
    mk.w("v", ldtoks["sin"])

    tA = tB = tC = None
    prev_rot = None
    prev_wqk = None
    for g in range(G):
        for side in ("q", "k"):
            wv = weffq_v if side == "q" else weffk_v
            brow = biasq_row if side == "q" else biask_row
            for b in range(NB):
                mk.w("t", tA)
                for j in range(NCK):
                    mk.do("t", lambda E, w_=wv, j_=j, g_=g, b_=b: E.matmul(
                        pA[:, :], w_[:, j_, g_ * 128:(g_ + 1) * 128],
                        x_fm_v[:, j_, b_ * 512:(b_ + 1) * 512],
                        start=(j_ == 0), stop=False), ST)
                tk = mk.do("t", lambda E, br=brow, g_=g: E.matmul(
                    pA[:, :], br[0:1, g_ * 128:(g_ + 1) * 128],
                    ones1_s[0:1, 0:512], start=False, stop=True), ST)
                mk.w("v", tk)
                mk.w("v", prev_rot)
                t_qs = mk.do("v", lambda E, b_=b: E.tensor_tensor(
                    scbA[:, 0:512], pA[:, :],
                    sinbf[:, b_ * 512:(b_ + 1) * 512], ALU.mult), SV)
                mk.w("t", t_qs)
                mk.w("t", tB)
                t_rot = mk.do("t", lambda E: E.matmul(
                    pB[:, :], prot_s[:, :], scbA[:, 0:512],
                    start=True, stop=True), ST)
                prev_rot = t_rot
                mk.w("v", t_rot)
                if _BUILD_OPTS.get("dbg") and g == 0 and side == "k" and b == 0:
                    mk.do("v", lambda E: E.tensor_copy(dbgA[:, 0:512], scbA[:, 0:512]), SV)
                    mk.do("v", lambda E: E.tensor_copy(dbgA[:, 512:1024], pB[:, :]), SV)
                t_qc = mk.do("v", lambda E, b_=b: E.tensor_tensor(
                    scbB[:, 0:512], pA[:, :],
                    cosbf[:, b_ * 512:(b_ + 1) * 512], ALU.mult), SV)
                tA = t_qc
                if side == "q":
                    mk.w("v", prev_wqk)
                    t_qr = mk.do("v", lambda E: E.tensor_tensor(
                        scbA[:, 512:1024], pB[:, :], scbB[:, 0:512], ALU.add), SV)
                    tB = t_qr
                    mk.w("t", t_qr)
                    mk.w("t", tC)
                    for j in range(4):
                        mk.do("t", lambda E, j_=j: E.matmul(
                            pC[32 * j_:32 * (j_ + 1), :], wqkT4[32 * j_:32 * (j_ + 1), :],
                            scbA[32 * j_:32 * (j_ + 1), 512:1024],
                            start=True, stop=False,
                            tile_position=(32 * j_, 32 * j_)), ST)
                    tk2 = mk.do("t", lambda E, g_=g: E.matmul(
                        pC[:, :], wqkb_b[0:1, g_ * 128:(g_ + 1) * 128],
                        ones1_s[0:1, 0:512], start=False, stop=True), ST)
                    prev_wqk = tk2
                    mk.w("v", tk2)
                    tC = mk.do("v", lambda E, g_=g, b_=b: E.tensor_copy(
                        QK_v[:, g_, b_ * 512:(b_ + 1) * 512], pC[:, :]), SV)
                else:
                    tB = mk.do("v", lambda E, g_=g, b_=b: E.tensor_tensor(
                        KR_v[:, g_, b_ * 512:(b_ + 1) * 512], pB[:, :],
                        scbB[:, 0:512], ALU.add), SV)
    for tc in range(NT):
        par = tc % 2
        ps = pA if par == 0 else pB
        mk.w("t", tA if par == 0 else tB)
        for j in range(NCK):
            mk.do("t", lambda E, ps_=ps, j_=j, tc_=tc: E.matmul(
                ps_[:, :], x_fm_v[:, j_, tc_ * 128:(tc_ + 1) * 128],
                weffv_v[:, j_, :], start=(j_ == 0), stop=False), ST)
        tk = mk.do("t", lambda E, ps_=ps: E.matmul(
            ps_[:, :], ones1_s[0:1, 0:128], biasv_row[0:1, :],
            start=False, stop=True), ST)
        mk.w("v", tk)
        ev = mk.do("v", lambda E, ps_=ps, tc_=tc: E.tensor_copy(
            vt_v[:, tc_, :], ps_[:, :]), SV)
        if par == 0:
            tA = ev
        else:
            tB = ev

    if _BUILD_OPTS["phases"] < 2:
        return _finish(mk, nc, out_d, outbuf_v)

    sq_tok = None
    for g in range(G):
        for side in ("q", "k"):
            src = QK_v if side == "q" else KR_v
            col = 56 if side == "q" else 57
            mk.w("v", mk.tok("v"))
            sq_tok = mk.do("v", lambda E, g_=g, s_=src: E.scalar_tensor_tensor(
                scbA[:, :], s_[:, g_, :], 1.0, s_[:, g_, :],
                ALU.mult, ALU.mult), SV)
            mk.w("t", sq_tok)
            tk = None
            for b in range(NB):
                tk = mk.do("t", lambda E, b_=b: E.matmul(
                    p4[0:4, b_ * 512:(b_ + 1) * 512], blkones4_s[:, :],
                    scbA[:, b_ * 512:(b_ + 1) * 512], start=True, stop=True), ST)
            mk.w("v", tk)
            mk.do("v", lambda E, c_=col: E.reduce_max(
                stat[0:4, c_:c_ + 1], p4[0:4, :], axis=AX), SV)
        mk.do("v", lambda E, g_=g: E.tensor_tensor(
            nb44[0:4, g_:g_ + 1], stat[0:4, 56:57], stat[0:4, 57:58], ALU.add), SV)
    mk.w("g", mk.tok("v"))
    gt = None
    for g in range(G):
        gt = mk.do("g", lambda E, g_=g: E.dma_start(
            nbrow[0:1, 4 * g_:4 * g_ + 4], nb44[0:4, g_:g_ + 1]),
            mk.dsem("NB4"), 16)
    mk.w("v", gt)
    mk.do("v", lambda E: E.reduce_max(stat[0:1, 58:59], nbrow[0:1, :], axis=AX), SV)
    mk.do("v", lambda E: E.tensor_scalar_mul(stat[0:1, 59:60], stat[0:1, 58:59],
                                             -0.5 * SCALE), SV)
    tkc = mk.do("v", lambda E: E.tensor_copy(nbrow2[0:1, 0:1], stat[0:1, 59:60]), SV)
    mk.w("t", tkc)
    tkm = mk.do("t", lambda E: E.matmul(pA[:, 0:1], ones1_s[0:1, 0:128],
                                        nbrow2[0:1, 0:1], start=True, stop=True), ST)
    mk.w("v", tkm)
    tk_nb = mk.do("v", lambda E: E.tensor_copy(nbias_pp[:, :], pA[:, 0:1]), SV)
    tA = tk_nb

    if _BUILD_OPTS["phases"] < 3:
        return _finish(mk, nc, out_d, outbuf_v)

    mk.w("s", tk_nb)
    prev_tkb = None
    exp_toks = {}
    sums_toks = {}
    PT_v4 = PT.rearrange("p (s x) -> p s x", s=4)
    att_banks = [pA, pD]
    tile_y = {-2: tA, -1: None}
    st3 = {"p": 0, "tk_rb": None, "tk_s4": None, "prev_tkb": None,
           "prev_gsm": None, "evac": None, "defer": None, "y_last": None}

    def _att_consume(pr):
        (pi, g_, ci_, hp_, off_, w_, sl_, n_, diag_, etok_, mtok_, bank_,
         first_, yt_, se_) = pr
        mk.w("t", mtok_ if diag_ else etok_)
        if first_:
            mk.w("t", yt_)
        for j in (2 * hp_, 2 * hp_ + 1):
            mk.do("t", lambda E, g2=g_, j_=j, jj_=j - 2 * hp_, c2=ci_, o2=off_,
                  w2=w_, s2=sl_, n2=n_, bk=bank_:
                  E.matmul(bk[32 * j_:32 * (j_ + 1), o2:o2 + w2],
                           vt_v[:, c2, (4 * g2 + j_) * 32:(4 * g2 + j_ + 1) * 32],
                           PT_v4[:, s2, jj_ * 512 + o2:jj_ * 512 + o2 + w2],
                           start=(c2 == 0), stop=(c2 == n2 - 1),
                           tile_position=(0, 32 * j_)), ST)
        if first_:
            mk.w("t", se_)
        pu = None
        for j in (2 * hp_, 2 * hp_ + 1):
            pu = mk.do("t", lambda E, j_=j, jj_=j - 2 * hp_, c2=ci_, o2=off_,
                       w2=w_, s2=sl_, n2=n_:
                       E.matmul(pB[32 * j_:32 * j_ + 1, o2:o2 + w2],
                                ones128_s[:, 0:1],
                                PT_v4[:, s2, jj_ * 512 + o2:jj_ * 512 + o2 + w2],
                                start=(c2 == 0), stop=(c2 == n2 - 1),
                                tile_position=(0, 32 * j_)), ST)
        sums_toks[pi] = pu

    pend = None
    ti = 0
    for g in range(G):
        for b in range(NB):
            nch = 4 * (b + 1)
            first_pidx = st3["p"]
            bank = att_banks[ti % 2]
            for ci in range(nch):
                off = (ci - 4 * b) * 128 if ci >= 4 * b else 0
                w = 512 - off
                diag = (ci >= 4 * b)
                for hp in range(2):
                    p_idx = st3["p"]
                    par2 = p_idx % 2
                    slot = p_idx % 4
                    mk.w("t", exp_toks.get(p_idx - 2))
                    stok = None
                    for j in (2 * hp, 2 * hp + 1):
                        stok = mk.do("t", lambda E, g_=g, j_=j, jj_=j - 2 * hp, ci_=ci,
                                     off_=off, w_=w, b_=b, pr_=par2:
                                     E.matmul(p4j[:, 2 * pr_ + jj_, off_:off_ + w_],
                                              KR_v[32 * j_:32 * (j_ + 1), g_,
                                                   ci_ * 128:(ci_ + 1) * 128],
                                              QK_v[32 * j_:32 * (j_ + 1), g_,
                                                   b_ * 512 + off_:b_ * 512 + off_ + w_],
                                              start=True, stop=True,
                                              tile_position=(32 * j_, 0)), ST)
                    if p_idx == first_pidx + 6 and st3["defer"] is not None:
                        st3["defer"]()
                        st3["defer"] = None
                    mk.w("s", stok)
                    mk.w("s", sums_toks.get(p_idx - 4))
                    pin = p4j[:, 2 * par2:2 * par2 + 2, off:off + w]
                    pout = PT_v4[:, slot, :].rearrange(
                        "p (j x) -> p j x", j=2)[:, :, off:off + w]
                    etok = mk.do("s", lambda E, pi=pin, po=pout: E.activation(
                        po, pi, ACT.Exp, bias=nbias_pp[:, 0:1], scale=SCALE), SS)
                    exp_toks[p_idx] = etok
                    mtok = None
                    if diag:
                        mk.w("v", etok)
                        pm = PT_v4[:, slot, :].rearrange(
                            "p (j x) -> p j x", j=2)[:, :, off:off + 128]
                        mtok = mk.do("v", lambda E, pm_=pm: E.tensor_tensor(
                            pm_, pm_, tri_s[:, None, :].broadcast_to([128, 2, 128]),
                            ALU.mult), SV)
                    if pend is not None:
                        _att_consume(pend)
                    pend = (p_idx, g, ci, hp, off, w, slot, nch, diag, etok, mtok,
                            bank, p_idx == first_pidx, tile_y.get(ti - 2),
                            st3["evac"])
                    st3["p"] = p_idx + 1
            _att_consume(pend)
            pend = None
            ptail = mk.tok("t")
            mk.w("v", ptail)
            mk.w("v", st3["prev_gsm"])
            ev = mk.do("v", lambda E: E.tensor_copy(
                scrB[0:97, 0:512], pB[0:97, :]), SV)
            st3["evac"] = ev
            mk.w("g", ev)
            mk.w("g", st3["tk_s4"])
            gsm = mk.do("g", lambda E: E.dma_start(
                sums4[:, :], bass.AP(scrB, 0, [[32 * T, 4], [1, 512]])),
                mk.dsem("SM4"), 16)
            st3["prev_gsm"] = gsm

            def _defer(g_=g, b_=b, ti_=ti, gsm_=gsm, bank_=bank):
                mk.w("s", gsm_)
                mk.w("s", st3["prev_tkb"])
                mk.do("s", lambda E: E.activation(rec4[:, :], sums4[:, :],
                                                  ACT.Ln), SS)
                tk_s4 = mk.do("s", lambda E: E.activation(
                    rec4b[:, :], rec4[:, :], ACT.Exp, scale=-1.0), SS)
                st3["tk_s4"] = tk_s4
                mk.w("t", tk_s4)
                mk.w("t", st3["tk_rb"])
                tkb = mk.do("t", lambda E: E.matmul(
                    pC[:, :], blk4T_s[:, :], rec4b[:, :],
                    start=True, stop=True), ST)
                st3["prev_tkb"] = tkb
                mk.w("v", tkb)
                st3["tk_rb"] = mk.do("v", lambda E: E.tensor_copy(
                    rbbf[:, :], pC[:, :]), SV)
                yt = mk.do("v", lambda E, g2=g_, b2=b_, bk=bank_: E.tensor_tensor(
                    ylat_v[:, g2, b2 * 512:(b2 + 1) * 512], bk[:, :], rbbf[:, :],
                    ALU.mult), SV)
                tile_y[ti_] = yt
                st3["y_last"] = yt
            st3["defer"] = _defer
            ti += 1
    if st3["defer"] is not None:
        st3["defer"]()
        st3["defer"] = None
    tk_y = st3["y_last"]

    if _BUILD_OPTS["phases"] < 4:
        return _finish(mk, nc, out_d, outbuf_v)

    mk.w("y", mk.tok("t"))
    yload("normg", normg_s[:, :], normg_in[:, :], sem="YN")
    yload("normb", normb_s[:, :], normb_in[:, :], sem="YN")
    yload("slotg", slotg_s[:, :], slotg_in[:, :], sem="YN")
    yload("slotb", slotb_s[:, :], slotb_in[:, :], sem="YN")
    yload("scl", scale_s[:, :], scale128_in[:, :], sem="YN")
    TOK_YN = mk.tok("YN")
    for _n in ("normg", "normb", "slotg", "slotb", "scl"):
        ldtoks[_n] = TOK_YN
    mk.w("t", tk_y)
    mk.w("v", ldtoks["x_tm"])

    h1s3 = [scrA, scrB, scrC]
    bases3 = [0, 8, 48]
    mm_gate = [None, None]
    A_toks = {}
    B_toks = {}
    x2_done = [None] * NT
    for step in range(NT + 2):
        if step < NT:
            i = step
            par = i % 2
            off0 = 0 if par == 0 else 1024
            mk.w("t", mm_gate[par])
            tk = None
            for g in range(G):
                for hb in range(2):
                    tk = mk.do("t", lambda E, g_=g, hb_=hb, i_=i, o=off0: E.matmul(
                        p4[:, o + hb_ * 512:o + (hb_ + 1) * 512],
                        ylat_v[:, g_, i_ * 128:(i_ + 1) * 128],
                        wpeT_v[:, g_, hb_ * 512:(hb_ + 1) * 512],
                        start=(g_ == 0), stop=False), ST)
            for hb in range(2):
                tk = mk.do("t", lambda E, hb_=hb, o=off0: E.matmul(
                    p4[:, o + hb_ * 512:o + (hb_ + 1) * 512],
                    ones1_s[0:1, 0:128], bprj_b[0:1, hb_ * 512:(hb_ + 1) * 512],
                    start=False, stop=True), ST)
            h1 = h1s3[i % 3]
            base = bases3[i % 3]
            mk.w("v", tk)
            atok = mk.do("v", lambda E, i_=i, h_=h1, b_=base, o=off0:
                         E.scalar_tensor_tensor(
                             h_[:, 0:1024], p4[:, o:o + 1024], 1.0,
                             x_tm_v[:, i_, :], ALU.mult, ALU.add,
                             accum_out=stat[:, b_:b_ + 1]), SV)
            mm_gate[par] = atok
            mk.do("v", lambda E, b_=base: E.tensor_tensor(
                stat[:, b_ + 2:b_ + 3], stat[:, b_:b_ + 1],
                stat[:, b_:b_ + 1], ALU.mult), SV)
            mk.do("v", lambda E, b_=base: E.tensor_scalar_mul(
                stat[:, b_ + 6:b_ + 7], stat[:, b_:b_ + 1], 1.0 / C), SV)
            mk.w("s", atok)
            A_toks[i] = mk.do("s", lambda E, h_=h1, b_=base: E.activation(
                h_[:, 1024:2048], h_[:, 0:1024], ACT.Square,
                accum_out=stat[:, b_ + 1:b_ + 2]), SS)
        if 0 <= step - 1 < NT:
            j = step - 1
            base = bases3[j % 3]
            mk.w("v", A_toks[j])
            vtok = mk.do("v", lambda E, b_=base: E.scalar_tensor_tensor(
                stat[:, b_ + 3:b_ + 4], stat[:, b_ + 2:b_ + 3], -1.0 / C,
                stat[:, b_ + 1:b_ + 2], ALU.mult, ALU.add), SV)
            mk.w("s", vtok)
            mk.do("s", lambda E, b_=base: E.activation(
                stat[:, b_ + 4:b_ + 5], stat[:, b_ + 3:b_ + 4], ACT.Ln,
                bias=eps_s[:, 0:1], scale=1.0 / C), SS)
            B_toks[j] = mk.do("s", lambda E, b_=base: E.activation(
                stat[:, b_ + 5:b_ + 6], stat[:, b_ + 4:b_ + 5], ACT.Exp,
                scale=-0.5), SS)
        if 0 <= step - 2 < NT:
            k = step - 2
            h1 = h1s3[k % 3]
            base = bases3[k % 3]
            mk.w("v", B_toks[k])
            x2_done[k] = mk.do("v", lambda E, k_=k, h_=h1, b_=base: E.tensor_scalar(
                x2_v[:, k_, :], h_[:, 0:1024], stat[:, b_ + 6:b_ + 7],
                stat[:, b_ + 5:b_ + 6], ALU.subtract, ALU.mult), SV)
    mk.w("g", x2_done[-1])
    mk.w("g", x2_done[-2])
    spill_tok = mk.do("g", lambda E: E.dma_start(
        x2_spill.rearrange("(i p) c -> p i c", p=128), x2_v[:, :, :]),
        mk.dsem("SPL"), 16)

    if _BUILD_OPTS["phases"] < 5:
        return _finish(mk, nc, out_d, outbuf_v)

    tr_evac = [None] * 4
    xf_free = [None, None]
    xn_f32s = [scrA, scrB]
    b5 = [16, 20, 24, 28]
    A5 = {}
    B5 = {}
    for step in range(NT + 1):
        if step < NT:
            i = step
            base = b5[i % 4]
            xf = xn_f32s[i % 2]
            mk.w("s", x2_done[i])
            mk.w("s", B5.get(i - 3))
            mk.do("s", lambda E, i_=i, x_=xf, b_=base: E.activation(
                x_[:, 1024:2048], x2_v[:, i_, :], ACT.Square,
                accum_out=stat[:, b_:b_ + 1]), SS)
            mk.do("s", lambda E, b_=base: E.activation(
                stat[:, b_ + 2:b_ + 3], stat[:, b_:b_ + 1], ACT.Ln,
                bias=eps_s[:, 0:1], scale=1.0 / C), SS)
            A5[i] = mk.do("s", lambda E, b_=base: E.activation(
                stat[:, b_ + 3:b_ + 4], stat[:, b_ + 2:b_ + 3], ACT.Exp,
                scale=-0.5), SS)
        if 0 <= step - 1 < NT:
            j = step - 1
            base = b5[j % 4]
            mk.w("v", A5[j])
            mk.w("v", ldtoks["normg"])
            mk.w("v", ldtoks["normb"])
            mk.do("v", lambda E, i_=j, b_=base: E.scalar_tensor_tensor(
                xn_tm_v[:, i_, :], x2_v[:, i_, :], stat[:, b_ + 3:b_ + 4],
                normg_s[:, :], ALU.mult, ALU.mult), SV)
            ftok = mk.do("v", lambda E, i_=j: E.tensor_tensor(
                xn_tm_v[:, i_, :], xn_tm_v[:, i_, :], normb_s[:, :], ALU.add), SV)
            B5[j] = ftok
            ttok = None
            for half in range(2):
                bank = (j * 2 + half) % 4
                mk.w("t", ftok)
                mk.w("t", tr_evac[bank])
                for q in range(4):
                    dc = half * 4 + q
                    ttok = mk.do("t", lambda E, i_=j, dc_=dc, bank_=bank, q_=q:
                                 E.matmul(p4q[:, bank_ * 4 + q_, :],
                                          xn_tm_v[:, i_, dc_ * 128:(dc_ + 1) * 128],
                                          identb_s[:, :], start=True, stop=True), ST)
                mk.w("v", ttok)
                tr_evac[bank] = mk.do("v", lambda E, i_=j, h_=half, bank_=bank:
                                      E.tensor_copy(
                    xn_fm_v[:, h_ * 4:(h_ + 1) * 4, i_ * 128:(i_ + 1) * 128],
                    p4q[:, bank_ * 4:(bank_ + 1) * 4, :]), SV)
    se_lt = {}
    se_sq = {}
    se_t1 = {}
    se_rs = {}
    se_C = {}
    se_D = {}
    se_bufs = [scrA, scrB, scrC]
    bSE = [32, 39, 46, 53]
    for step in range(NCK + 3):
        if 0 <= step - 3 < NCK:
            ec = step - 3
            sf = se_bufs[ec % 3]
            base = bSE[ec % 4]
            mk.w("v", se_rs[ec])
            if ec == 0:
                mk.w("v", ldtoks["slotg"])
                mk.w("v", ldtoks["slotb"])
                mk.w("v", ldtoks["scl"])
            mk.do("v", lambda E, s_=sf, b_=base: E.tensor_scalar(
                s_[:, 1024:2048], s_[:, 0:1024], stat[:, b_ + 6:b_ + 7],
                stat[:, b_ + 5:b_ + 6], ALU.subtract, ALU.mult), SV)
            mk.do("v", lambda E, s_=sf: E.tensor_tensor(
                s_[:, 1024:2048], s_[:, 1024:2048], slotg_s[:, :], ALU.mult), SV)
            mk.do("v", lambda E, s_=sf: E.tensor_tensor(
                s_[:, 1024:2048], s_[:, 1024:2048], slotb_s[:, :], ALU.add), SV)
            ftok = mk.do("v", lambda E, s_=sf: E.tensor_scalar_mul(
                s_[:, 1024:2048], s_[:, 1024:2048], scale_s[:, 0:1]), SV)
            se_C[ec] = ftok
            ttok = None
            for half in range(2):
                bank = (ec * 2 + half) % 4
                mk.w("t", ftok)
                mk.w("t", tr_evac[bank])
                for q in range(4):
                    dc = half * 4 + q
                    ttok = mk.do("t", lambda E, s_=sf, dc_=dc, bank_=bank, q_=q:
                                 E.transpose(p4q[:, bank_ * 4 + q_, :],
                                             s_[:, 1024 + dc_ * 128:1024 + (dc_ + 1) * 128],
                                             identf_s[:, :]), ST)
                mk.w("v", ttok)
                tr_evac[bank] = mk.do("v", lambda E, ec_=ec, h_=half, bank_=bank:
                                      E.tensor_copy(
                    seT_v[:, h_ * 4:(h_ + 1) * 4, ec_ * 128:(ec_ + 1) * 128],
                    p4q[:, bank_ * 4:(bank_ + 1) * 4, :]), SV)
            se_D[ec] = ttok
        if step < NCK:
            ec = step
            sf = se_bufs[ec % 3]
            if ec < 3:
                mk.w("y", mk.tok("v"))
                mk.w("y", mk.tok("s"))
            mk.w("y", se_C.get(ec - 3))
            se_lt[ec] = mk.do("y", lambda E, ec_=ec, s_=sf: E.dma_start(
                s_[:, 0:1024], semb_in[ec_ * 128:(ec_ + 1) * 128, :]),
                mk.dsem(f"SE{ec % 2}"), 16)
        if 0 <= step - 1 < NCK:
            ec = step - 1
            sf = se_bufs[ec % 3]
            base = bSE[ec % 4]
            mk.w("s", se_lt[ec])
            mk.w("s", se_t1.get(ec - 4))
            mk.w("s", se_D.get(ec - 3))
            se_sq[ec] = mk.do("s", lambda E, s_=sf, b_=base: E.activation(
                s_[:, 1024:2048], s_[:, 0:1024], ACT.Square,
                accum_out=stat[:, b_ + 1:b_ + 2]), SS)
        if 0 <= step - 2 < NCK:
            ec = step - 2
            sf2 = se_bufs[ec % 3]
            base = bSE[ec % 4]
            mk.w("v", se_sq[ec])
            mk.do("v", lambda E, s_=sf2, b_=base: E.reduce_sum(
                stat[:, b_:b_ + 1], s_[:, 0:1024], axis=AX), SV)
            se_t1[ec] = mk.do("v", lambda E, b_=base: E.tensor_tensor(
                stat[:, b_ + 2:b_ + 3], stat[:, b_:b_ + 1],
                stat[:, b_:b_ + 1], ALU.mult), SV)
            mk.do("v", lambda E, b_=base: E.tensor_scalar_mul(
                stat[:, b_ + 6:b_ + 7], stat[:, b_:b_ + 1], 1.0 / C), SV)
            vtok = mk.do("v", lambda E, b_=base: E.scalar_tensor_tensor(
                stat[:, b_ + 3:b_ + 4], stat[:, b_ + 2:b_ + 3], -1.0 / C,
                stat[:, b_ + 1:b_ + 2], ALU.mult, ALU.add), SV)
            mk.w("s", vtok)
            mk.do("s", lambda E, b_=base: E.activation(
                stat[:, b_ + 4:b_ + 5], stat[:, b_ + 3:b_ + 4], ACT.Ln,
                bias=eps_s[:, 0:1], scale=1.0 / C), SS)
            se_rs[ec] = mk.do("s", lambda E, b_=base: E.activation(
                stat[:, b_ + 5:b_ + 6], stat[:, b_ + 4:b_ + 5], ACT.Exp,
                scale=-0.5), SS)

    if _BUILD_OPTS["phases"] < 6:
        return _finish(mk, nc, out_d, outbuf_v)

    for bank in range(4):
        mk.w("t", tr_evac[bank])
    lg_tok = [None, None]
    mk.w("v", spill_tok)
    mk.w("s", spill_tok)
    for i in range(NT):
        par = i % 2
        off0 = 0 if par == 0 else 1024
        mk.w("t", lg_tok[par])
        tk = None
        for hb in range(2):
            for dc in range(NCK):
                tk = mk.do("t", lambda E, i_=i, hb_=hb, dc_=dc, o=off0: E.matmul(
                    p4[:, o + hb_ * 512:o + (hb_ + 1) * 512],
                    xn_fm_v[:, dc_, i_ * 128:(i_ + 1) * 128],
                    seT_v[:, dc_, hb_ * 512:(hb_ + 1) * 512],
                    start=(dc_ == 0), stop=(dc_ == NCK - 1)), ST)
        mk.w("s", tk)
        cpt = mk.do("s", lambda E, i_=i, o=off0: E.activation(
            logits_v[:, i_, :], p4[:, o:o + 1024], ACT.Copy), SS)
        lg_tok[par] = cpt
        mk.w("v", cpt)
        mk.do("v", lambda E, i_=i: E.reduce_max(
            rowmax_tm[:, i_:i_ + 1], logits_v[:, i_, :], axis=AX), SV)

    if _BUILD_OPTS["phases"] < 7:
        return _finish(mk, nc, out_d, outbuf_v)

    nrm_tok = mk.do("v", lambda E: E.tensor_scalar_mul(
        nrm_s[:, :], rowmax_tm[:, :], -1.0), SV)
    tr_tok = None
    exp_tok3 = None
    for ec in range(NCK):
        mk.w("t", tr_tok)
        ttok = None
        for i in range(NT):
            ttok = mk.do("t", lambda E, i_=i, ec_=ec: E.transpose(
                p4q[:, i_, :], logits_v[:, i_, ec_ * 128:(ec_ + 1) * 128],
                identf_s[:, :]), ST)
        mk.w("v", ttok)
        mk.do("v", lambda E, ec_=ec: E.reduce_max(
            cmax_s[:, ec_:ec_ + 1], p4[:, :], axis=AX), SV)
        nm2 = mk.do("v", lambda E, ec_=ec: E.tensor_scalar_mul(
            ncmax_s[:, ec_:ec_ + 1], cmax_s[:, ec_:ec_ + 1], -1.0), SV)
        mk.w("s", nm2)
        tr_tok = mk.do("s", lambda E, ec_=ec: E.activation(
            dnT_v[:, ec_, :], p4[:, :], ACT.Exp, bias=ncmax_s[:, ec_:ec_ + 1],
            accum_out=cs2_s[:, ec_:ec_ + 1]), SS)
        exp_tok3 = tr_tok
    mk.do("s", lambda E: E.activation(rcl_s[:, :], cs2_s[:, :], ACT.Ln), SS)
    rck = mk.do("s", lambda E: E.activation(rcT_s[:, :], rcl_s[:, :], ACT.Exp,
                                            scale=-1.0), SS)
    mk.w("v", rck)
    nlast = None
    for ec in range(NCK):
        nlast = mk.do("v", lambda E, ec_=ec: E.tensor_scalar_mul(
            dnT_v[:, ec_, :], dnT_v[:, ec_, :], rcT_s[:, ec_:ec_ + 1]), SV)
    expc_toks = []
    for i in range(NT - 1, -1, -1):
        dst = lgb_v[:, NT + i, :] if i < 15 else scrBb[:, 0:1024]
        expc_toks.append(mk.do("s", lambda E, i_=i, d_=dst: E.activation(
            d_, logits_v[:, i_, :], ACT.Exp, bias=nrm_s[:, i_:i_ + 1],
            accum_out=rsum_tm[:, i_:i_ + 1]), SS))
    expc_tok = expc_toks[-1]
    mk.do("s", lambda E: E.activation(rsL_s[:, :], rsum_tm[:, :], ACT.Ln), SS)
    rct = mk.do("s", lambda E: E.activation(recip_tm[:, :], rsL_s[:, :], ACT.Exp,
                                            scale=-1.0), SS)
    dp_tok = [None, None]
    mk.w("t", nlast)
    mk.w("t", exp_tok3)
    for i in range(NT):
        par = i % 2
        o = par * 1024
        mk.w("t", dp_tok[par])
        tk = None
        for ec in range(NCK):
            tk = mk.do("t", lambda E, i_=i, ec_=ec, o_=o: E.matmul(
                p4[:, o_ + ec_ * 128:o_ + (ec_ + 1) * 128],
                dnT_v[:, ec_, i_ * 128:(i_ + 1) * 128], identb_s[:, :],
                start=True, stop=True), ST)
        mk.w("v", tk)
        if i == 0:
            mk.w("v", expc_tok)
        dp_tok[par] = mk.do("v", lambda E, i_=i, o_=o: E.tensor_copy(
            lgb_v[:, i_, :], p4[:, o_:o_ + 1024]), SV)
    sl_tok = [None, None]
    sl_dma = [None, None]
    mk.w("t", dp_tok[0])
    mk.w("t", dp_tok[1])
    for dc in range(NCK):
        par = dc % 2
        off0 = 0 if par == 0 else 1024
        mk.w("t", sl_tok[par])
        tk = None
        for hb in range(2):
            for i in range(NT):
                tk = mk.do("t", lambda E, dc_=dc, hb_=hb, i_=i, o=off0: E.matmul(
                    p4[:, o + hb_ * 512:o + (hb_ + 1) * 512],
                    xn_tm_v[:, i_, dc_ * 128:(dc_ + 1) * 128],
                    lgb_v[:, i_, hb_ * 512:(hb_ + 1) * 512],
                    start=(i_ == 0), stop=(i_ == NT - 1)), ST)
        mk.w("v", tk)
        mk.w("v", sl_dma[par])
        sl_tok[par] = mk.do("v", lambda E, dc_=dc, o=off0, par_=par: E.tensor_copy(
            scbA[:, par_ * 1024:(par_ + 1) * 1024], p4[:, o:o + 1024]), SV)
        mk.w("g", sl_tok[par])
        sl_dma[par] = mk.do("g", lambda E, dc_=dc, par_=par: E.dma_start(
            ag1_in.rearrange("(c p) e -> p c e", p=128)[:, dc_, :],
            scbA[:, par_ * 1024:(par_ + 1) * 1024]), mk.dsem(f"SL{par}"), 16)
    sl_ev = sl_dma[(NCK - 1) % 2]
    mk.w("v", sl_dma[0])
    mk.w("v", sl_dma[1])
    mk.w("g", sl_ev)
    mk.w("g", mk.tok("t"))
    mk.w("g", mk.tok("v"))
    mk.w("g", mk.tok("s"))
    wbufs = {0: (wgA_v, wuA_v), 1: (wgB_v, wuB_v)}
    wld = {}
    wfree = {0: None, 1: None}

    def load_wh(e, Hh, par, eng="g"):
        gv, uv = wbufs[par]
        mk.w(eng, wfree[par])
        mk.do(eng, lambda E, e_=e, H_=Hh, g_=gv: E.dma_start(
            g_[:, :, :],
            wg2_in[e_].rearrange("(c p) f -> p c f", p=128)[:, :, H_ * 1024:(H_ + 1) * 1024]),
            mk.dsem(f"WH{par}"), 16)
        wld[(e, Hh)] = mk.do(eng, lambda E, e_=e, H_=Hh, u_=uv: E.dma_start(
            u_[:, :, :],
            wu2_in[e_].rearrange("(c p) f -> p c f", p=128)[:, :, H_ * 1024:(H_ + 1) * 1024]),
            mk.dsem(f"WH{par}"), 16)

    def load_wd(e, oh, eng="g"):
        return mk.do(eng, lambda E, e_=e, o_=oh: E.dma_start(
            wdS_v[:, :, :],
            wd2_in[e_].rearrange("(c p) o -> p c o", p=128)[:, :, o_ * 512:(o_ + 1) * 512]),
            mk.dsem("WD"), 16)

    agt = mk.do("g", lambda E: E.collective_compute(
        "AllGather", ALU.bypass, replica_groups=[list(range(NCORES))],
        ins=[ag1_in.ap().opt()], outs=[ag1_out.ap().opt()]), mk.sem["cc"], 1)
    mk.w("t", sl_tok[0])
    mk.w("t", sl_tok[1])
    tr2_tok = None
    for ec in range(NCK):
        mk.w("t", tr2_tok)
        ttok = None
        for i in range(NT):
            lhs = (lgb_v[:, NT + i, :] if i < 15 else scrBb[:, 0:1024])
            ttok = mk.do("t", lambda E, i_=i, ec_=ec, l_=lhs: E.matmul(
                p4q[:, i_, :], l_[:, ec_ * 128:(ec_ + 1) * 128],
                identb_s[:, :], start=True, stop=True), ST)
        mk.w("v", ttok)
        tr2_tok = mk.do("v", lambda E, ec_=ec: E.tensor_copy(
            combineT_v[:, ec_, :], p4[:, :]), SV)

    mk.w("y", mk.tok("t"))
    mk.w("y", mk.tok("v"))
    mk.w("y", mk.tok("s"))
    brow_tok = None
    for nm, bin2, row, wdt in (("bg2", bg2_in, bg_row, FF), ("bu2", bu2_in, bu_row, FF),
                               ("bd2", bd2_in, bd_row, C)):
        mk.w("y", brow_tok)
        lt = mk.do("y", lambda E, b=bin2, w_=wdt: E.dma_start(
            scrB[0:2, 0:w_], b[:, :]), mk.dsem("YG"), 16)
        mk.w("v", lt)
        mk.w("v", brow_tok)
        ctk = mk.do("v", lambda E, w_=wdt: E.tensor_copy(
            scbB[0:2, 0:w_], scrB[0:2, 0:w_]), SV)
        mk.w("g", ctk)
        brow_tok = mk.do("g", lambda E, r=row, w_=wdt: E.dma_start(
            r[0:1, 0:2 * w_], scbB[0:2, 0:w_]), mk.dsem("BRS"), 16)
    mk.w("v", brow_tok)

    if _BUILD_OPTS["phases"] < 8:
        return _finish(mk, nc, out_d, outbuf_v)

    mk.w("g", tr2_tok)
    mk.w("g", brow_tok)
    load_wh(0, 0, 0)
    wd_tok = load_wd(0, 0)
    load_wh(0, 1, 1)
    mk.w("g", agt)

    regcell = {}

    def _dynoff(E, shift, delta):
        if "pid" not in regcell:
            regcell["pid"] = E.to_reg(E.partition_id())
            regcell["off"] = E.alloc_register("dynoff")
        if shift is not None:
            E.reg_alu(regcell["off"], regcell["pid"], shift, ALU.arith_shift_left)
        if delta:
            E.reg_add(regcell["off"], regcell["off"], delta)
        return regcell["off"]

    XeT_x = XeT.rearrange("p (dc s) -> p dc s", dc=8)
    xstg_v = xstg.rearrange("p (u dc s) -> p u dc s", u=2, dc=8)

    def _xet_dyn(E, b_, par):
        first = (b_ == 0)
        off = _dynoff(E, 7 if first else None, 0 if first else (1 << 20))
        src = bass.AP(ag1_out, off, [[EL, 128], [128 * EL, 8], [1, 128]])
        return E.dma_start(xstg_v[:, par, :, :], src)

    xet_tok = None
    xcp_tok = [None, None]
    for b_ in range(NCORES):
        par = b_ % 2
        mk.w("g", xcp_tok[par])
        ld = mk.do("g", lambda E, b__=b_, p_=par: _xet_dyn(E, b__, p_),
                   mk.dsem(f"XS{par}"), 16)
        mk.w("g", ld)
        for el_ in range(2):
            xcp_tok[par] = mk.do("g", lambda E, b__=b_, el__=el_, p_=par: E.dma_start(
                XeT_x[:, :, el__ * 512 + b__ * 64:el__ * 512 + b__ * 64 + 64],
                xstg_v[:, p_, :, el__ * 64:el__ * 64 + 64]),
                mk.dsem(f"XC{par}"), 16)
        xet_tok = xcp_tok[par]
    xet_tok2 = xcp_tok[1 - (NCORES - 1) % 2]

    mk.w("t", xet_tok)
    mk.w("t", xet_tok2)
    mk.w("t", brow_tok)
    mk.w("s", brow_tok)
    eo_dma = [None, None]
    gelu_free = None
    up_free = None
    gate_free = None
    h_cons = None
    for e in range(2):
        for Hh in range(2):
            par = Hh
            gv, uv = wbufs[par]
            mk.w("t", wld[(e, Hh)])
            for fs in range(8):
                fc = Hh * 8 + fs
                if fc == 0:
                    mk.w("t", h_cons)
                mk.w("t", gate_free)
                tkg = None
                for dc in range(NCK):
                    tkg = mk.do("t", lambda E, g_=gv, dc_=dc, fs_=fs, e_=e: E.matmul(
                        pA[:, :], g_[:, dc_, fs_ * 128:(fs_ + 1) * 128],
                        XeT_v[:, dc_, e_ * 512:(e_ + 1) * 512],
                        start=(dc_ == 0), stop=False), ST)
                tkg = mk.do("t", lambda E, e_=e, fc_=fc: E.matmul(
                    pA[:, :], bg_row[0:1, e_ * FF + fc_ * 128:e_ * FF + (fc_ + 1) * 128],
                    ones1_s[0:1, 0:512], start=False, stop=True), ST)
                mk.w("t", up_free)
                tku = None
                for dc in range(NCK):
                    tku = mk.do("t", lambda E, u_=uv, dc_=dc, fs_=fs, e_=e: E.matmul(
                        pB[:, :], u_[:, dc_, fs_ * 128:(fs_ + 1) * 128],
                        XeT_v[:, dc_, e_ * 512:(e_ + 1) * 512],
                        start=(dc_ == 0), stop=False), ST)
                tku = mk.do("t", lambda E, e_=e, fc_=fc: E.matmul(
                    pB[:, :], bu_row[0:1, e_ * FF + fc_ * 128:e_ * FF + (fc_ + 1) * 128],
                    ones65_s[32:33, 0:512], start=False, stop=True), ST)
                mk.w("s", tkg)
                mk.w("s", gelu_free)
                gt = mk.do("s", lambda E: E.activation(scbA[:, 0:512], pA[:, :],
                                                       ACT.Gelu), SS)
                gate_free = gt
                mk.w("v", tku)
                ut = mk.do("v", lambda E: E.tensor_copy(scbB[:, 0:512], pB[:, :]), SV)
                up_free = ut
                mk.w("v", gt)
                htok = mk.do("v", lambda E, fc_=fc: E.tensor_tensor(
                    hfm_v[:, fc_, :], scbA[:, 0:512], scbB[:, 0:512], ALU.mult), SV)
                gelu_free = htok
            if e == 0:
                wfree[par] = mk.tok("t")
                load_wh(1, Hh, par)
        mk.w("t", mk.tok("v"))
        for oh in range(2):
            mk.w("t", wd_tok)
            eo_par = [None, None]
            for bsl in range(4):
                par = bsl % 2
                mk.w("t", eo_par[par])
                tk = None
                for fc in range(16):
                    tk = mk.do("t", lambda E, fc_=fc, bsl_=bsl, par_=par: E.matmul(
                        p4[:, par_ * 1024:par_ * 1024 + 512],
                        hfm_v[:, fc_, bsl_ * 128:(bsl_ + 1) * 128],
                        wdS_v[:, fc_, :],
                        start=(fc_ == 0), stop=(fc_ == 15)), ST)
                tk = mk.do("t", lambda E, e_=e, oh_=oh, par_=par: E.matmul(
                    p4[:, par_ * 1024:par_ * 1024 + 512], ones65_s[64:65, 0:128],
                    bd_row[0:1, e_ * 1024 + oh_ * 512:e_ * 1024 + (oh_ + 1) * 512],
                    start=False, stop=True), ST)
                mk.w("v", tk)
                mk.w("v", eo_dma[par])
                eo_par[par] = mk.do("v", lambda E, par_=par: E.tensor_copy(
                    scbA[:, 1024 + par_ * 512:1024 + (par_ + 1) * 512],
                    p4[:, par_ * 1024:par_ * 1024 + 512]), SV)
                mk.w("g", eo_par[par])
                eo_dma[par] = mk.do("g", lambda E, e_=e, bsl_=bsl, oh_=oh, par_=par:
                                    E.dma_start(
                    ag2_in.rearrange("(c p) o -> p c o", p=128)[
                        :, e_ * 4 + bsl_, oh_ * 512:(oh_ + 1) * 512],
                    scbA[:, 1024 + par_ * 512:1024 + (par_ + 1) * 512]),
                    mk.dsem(f"EO{par}"), 16)
            mk.w("g", mk.tok("t"))
            if oh == 0:
                wd_tok = load_wd(e, 1)
            elif e == 0:
                wd_tok = load_wd(1, 0)
        h_cons = mk.tok("t")
    mk.w("g", eo_dma[0])
    mk.w("g", eo_dma[1])
    mk.w("g", mk.tok("t"))
    x2r_tok = mk.do("g", lambda E: E.dma_start(
        x2b_v[:, :, :], x2_spill.rearrange("(i p) c -> p i c", p=128)),
        mk.dsem("REL"), 16)
    agt2 = mk.do("g", lambda E: E.collective_compute(
        "AllGather", ALU.bypass, replica_groups=[list(range(NCORES))],
        ins=[ag2_in.ap().opt()], outs=[ag2_out.ap().opt()]), mk.sem["cc"], 1)
    mk.w("g", agt2)
    mk.w("g", mk.tok("t"))

    def _ef_dma(E, c_):
        first = (c_ == 0)
        off = _dynoff(E, 16 if first else None, 0 if first else (1 << 20))
        src = bass.AP(ag2_out, off, [[512 * C, 2], [C, 64], [1, C]])
        return E.dma_start(eout_f_v[:, c_, :], src)

    ef_tok = None
    for c_ in range(NCORES):
        ef_tok = mk.do("g", lambda E, c__=c_: _ef_dma(E, c__),
                       mk.dsem("EFG"), 16)

    if _BUILD_OPTS["phases"] < 10:
        return _finish(mk, nc, out_d, outbuf_v)

    mk.w("t", ef_tok)
    mk.w("t", x2r_tok)
    h1sX = [scrA, scrB, scrD]
    basesX = [0, 8, 48]
    mm_gateX = [None, None]
    AX_t = {}
    BX_t = {}
    for step in range(NT + 2):
        if step < NT:
            i = step
            par = i % 2
            off0 = 0 if par == 0 else 1024
            mk.w("t", mm_gateX[par])
            tk = None
            for hb in range(2):
                for ec in range(NCK):
                    tk = mk.do("t", lambda E, i_=i, hb_=hb, ec_=ec, o=off0: E.matmul(
                        p4[:, o + hb_ * 512:o + (hb_ + 1) * 512],
                        combineT_v[:, ec_, i_ * 128:(i_ + 1) * 128],
                        eout_f_v[:, ec_, hb_ * 512:(hb_ + 1) * 512],
                        start=(ec_ == 0), stop=(ec_ == NCK - 1)), ST)
            h1 = h1sX[i % 3]
            base = basesX[i % 3]
            mk.w("v", tk)
            if i == 0:
                mk.w("v", x2r_tok)
                mk.w("v", rct)
            atok = mk.do("v", lambda E, h=h1, i_=i, o=off0, b_=base:
                         E.scalar_tensor_tensor(
                             h[:, 0:1024], p4[:, o:o + 1024],
                             recip_tm[:, i_:i_ + 1], x2b_v[:, i_, :],
                             ALU.mult, ALU.add,
                             accum_out=stat[:, b_:b_ + 1]), SV)
            mm_gateX[par] = atok
            mk.do("v", lambda E, b_=base: E.tensor_tensor(
                stat[:, b_ + 2:b_ + 3], stat[:, b_:b_ + 1],
                stat[:, b_:b_ + 1], ALU.mult), SV)
            mk.do("v", lambda E, b_=base: E.tensor_scalar_mul(
                stat[:, b_ + 6:b_ + 7], stat[:, b_:b_ + 1], 1.0 / C), SV)
            mk.w("s", atok)
            AX_t[i] = mk.do("s", lambda E, h=h1, b_=base: E.activation(
                h[:, 1024:2048], h[:, 0:1024], ACT.Square,
                accum_out=stat[:, b_ + 1:b_ + 2]), SS)
        if 0 <= step - 1 < NT:
            j = step - 1
            base = basesX[j % 3]
            mk.w("v", AX_t[j])
            vtok = mk.do("v", lambda E, b_=base: E.scalar_tensor_tensor(
                stat[:, b_ + 3:b_ + 4], stat[:, b_ + 2:b_ + 3], -1.0 / C,
                stat[:, b_ + 1:b_ + 2], ALU.mult, ALU.add), SV)
            mk.w("s", vtok)
            mk.do("s", lambda E, b_=base: E.activation(
                stat[:, b_ + 4:b_ + 5], stat[:, b_ + 3:b_ + 4], ACT.Ln,
                bias=eps_s[:, 0:1], scale=1.0 / C), SS)
            BX_t[j] = mk.do("s", lambda E, b_=base: E.activation(
                stat[:, b_ + 5:b_ + 6], stat[:, b_ + 4:b_ + 5], ACT.Exp,
                scale=-0.5), SS)
        if 0 <= step - 2 < NT:
            k = step - 2
            h1 = h1sX[k % 3]
            base = basesX[k % 3]
            mk.w("v", BX_t[k])
            mk.do("v", lambda E, k_=k, h=h1, b_=base: E.tensor_scalar(
                outbuf_v[:, k_, :], h[:, 0:1024], stat[:, b_ + 6:b_ + 7],
                stat[:, b_ + 5:b_ + 6], ALU.subtract, ALU.mult), SV)

    return _finish(mk, nc, out_d, outbuf_v)


def _finish(mk, nc, out_d, outbuf_v):
    mk.w("g", mk.tok("v"))
    mk.w("g", mk.tok("s"))
    st = mk.do("g", lambda E: E.dma_start(
        out_d.rearrange("(i p) c -> p i c", p=128), outbuf_v[:, :, :]),
        mk.dsem("OUT"), 16)
    for name in _BUILD_OPTS["taps"]:
        h = mk.sb[name]
        shp = list(h.shape)
        o = nc.dram_tensor("tap_" + name, shp, h.ap().dtype, kind="ExternalOutput")
        st = mk.do("g", lambda E, o_=o, h_=h: E.dma_start(o_[:, :], h_[:, :]),
                   mk.dsem("OUT"), 16)
    mk.w("g", st)
    nc._mk_specs = list(mk.specs)
    with nc.Block() as block:
        block.tensor(lambda E: [f(E) for f in mk.ops["t"]])
        block.vector(lambda E: [f(E) for f in mk.ops["v"]])
        block.scalar(lambda E: [f(E) for f in mk.ops["s"]])
        block.gpsimd(lambda E: [f(E) for f in mk.ops["g"]])
        block.sync(lambda E: [f(E) for f in mk.ops["y"]])
    return nc


def _solve_plan(specs, cap):
    plan = {}
    placed = []
    for name, size, p0, p1 in sorted(specs, key=lambda x: -x[1]):
        taken = sorted((plan[n], plan[n] + s) for (n, s, a0, a1) in placed
                       if not (a1 < p0 or a0 > p1))
        off = 0
        for (a, bnd) in taken:
            if off + size <= a:
                break
            off = max(off, bnd)
        assert off + size <= cap, f"SBUF plan overflow: {name} {size}B at {off}"
        plan[name] = off
        placed.append((name, size, p0, p1))
    return plan


_PLAN_CACHE = {}


def build_planned():
    key = _BUILD_OPTS["phases"]
    if key not in _PLAN_CACHE:
        nc0 = build(None)
        _PLAN_CACHE[key] = _solve_plan(nc0._mk_specs, 206336)
    return build(_PLAN_CACHE[key])


def _blkd(p2lT):
    z = np.zeros((128, 64), np.float32)
    z[0:64, 0:32] = p2lT
    z[64:128, 32:64] = p2lT
    return z


def _stage_inputs(inputs, core):
    f32 = np.float32
    x = np.asarray(inputs["L0_attn_x"][core], dtype=f32)
    caw = np.asarray(inputs["L0_attn_c_attn_w"], dtype=f32)
    cab = np.asarray(inputs["L0_attn_c_attn_b"], dtype=f32)
    q2l = np.asarray(inputs["L0_attn_q2l_w"], dtype=f32)
    q2lb = np.asarray(inputs["L0_attn_q2l_b"], dtype=f32)
    kv2l = np.asarray(inputs["L0_attn_kv2l_w"], dtype=f32)
    kv2lb = np.asarray(inputs["L0_attn_kv2l_b"], dtype=f32)
    l2o = np.asarray(inputs["L0_attn_l2o_w"], dtype=f32)
    l2ob = np.asarray(inputs["L0_attn_l2o_b"], dtype=f32)
    wqk = np.asarray(inputs["L0_attn_wqk_w"], dtype=f32)
    wqkb = np.asarray(inputs["L0_attn_wqk_b"], dtype=f32)
    cpw = np.asarray(inputs["L0_attn_cproj_w"], dtype=f32)
    cpb = np.asarray(inputs["L0_attn_cproj_b"], dtype=f32)
    d = dict(_CONSTS)
    d.update({
        "x_tm_in": np.ascontiguousarray(x),
        "x_fm_in": np.ascontiguousarray(x.T),
        "caw": caw,
        "cprojT_in": np.ascontiguousarray(cpw.T),
        "q2lT_in": np.ascontiguousarray(q2l.T),
        "kv2lT_in": np.ascontiguousarray(kv2l.T),
        "l2o_in": l2o,
        "wqkT_in": np.ascontiguousarray(wqk.T),
        "blkdQ_in": _blkd(q2l.T),
        "blkdKV_in": _blkd(kv2l.T),
        "wqkT4_in": np.tile(np.ascontiguousarray(wqk.T), (4, 1)),
        "l2o2_in": np.tile(l2o, (2, 1)),
        "bqr_in": np.ascontiguousarray(cab[0:C].reshape(H, HD).T),
        "bkr_in": np.ascontiguousarray(cab[C:2 * C].reshape(H, HD).T),
        "bvr_in": np.ascontiguousarray(cab[2 * C:3 * C].reshape(H, HD).T),
        "q2lb16_in": np.tile(q2lb[None, :], (H, 1)),
        "kvlb16_in": np.tile(kv2lb[None, :], (H, 1)),
        "wqkb512_in": np.tile(wqkb, H)[None, :].copy(),
        "l2ob_in": np.ascontiguousarray(np.tile(l2ob, H).reshape(8, 128).T),
        "cprojb_in": cpb[None, :].copy(),
        "normg128": np.tile(np.asarray(inputs["L0_ffn_norm_g"], f32)[None, :], (128, 1)),
        "normb128": np.tile(np.asarray(inputs["L0_ffn_norm_b"], f32)[None, :], (128, 1)),
        "slotg128": np.tile(np.asarray(inputs["L0_ffn_slot_g"], f32)[None, :], (128, 1)),
        "slotb128": np.tile(np.asarray(inputs["L0_ffn_slot_b"], f32)[None, :], (128, 1)),
        "scale128": np.full((128, 1), np.asarray(inputs["L0_ffn_scale"], f32)[0], f32),
        "semb": np.asarray(inputs["L0_ffn_slot_embeds"], f32).reshape(EL, C),
        "wg2": np.asarray(inputs["L0_ffn_Wg"], f32)[2 * core:2 * core + 2],
        "wu2": np.asarray(inputs["L0_ffn_Wu"], f32)[2 * core:2 * core + 2],
        "wd2": np.asarray(inputs["L0_ffn_Wd"], f32)[2 * core:2 * core + 2],
        "bg2": np.asarray(inputs["L0_ffn_bg"], f32)[2 * core:2 * core + 2],
        "bu2": np.asarray(inputs["L0_ffn_bu"], f32)[2 * core:2 * core + 2],
        "bd2": np.asarray(inputs["L0_ffn_bd"], f32)[2 * core:2 * core + 2],
    })
    return d


def kernel(**inputs):
    nc = build_planned()
    in_maps = [_stage_inputs(inputs, c) for c in range(NCORES)]
    res = run_bass_kernel_spmd(nc, in_maps, core_ids=list(range(NCORES)))
    out = np.stack([np.asarray(res.results[c]["out"]) for c in range(NCORES)], axis=0)
    if _BUILD_OPTS["taps"]:
        kernel._taps = [{k: np.asarray(v) for k, v in r.items()} for r in res.results]
    return out


# version 13
# speedup vs baseline: 1.0594x; 1.0049x over previous
import math
import numpy as np
import ml_dtypes

import concourse.bass as bass
import concourse.mybir as mybir
from concourse.bass_utils import run_bass_kernel_spmd

F32 = mybir.dt.float32
BF16 = mybir.dt.bfloat16
AX = mybir.AxisListType.X
ALU = mybir.AluOpType
ACT = mybir.ActivationFunctionType

T = 2048
C = 1024
H = 16
HD = 64
LAT = 32
G = 4
NT = 16
NB = 4
NCK = 8
E = 16
S = 64
EL = 1024
FF = 2048
NCORES = 8
SCALE = 1.0 / math.sqrt(LAT)
EPS = 1e-5

_BUILD_OPTS = {"taps": (), "phases": 99}


class _Sem:
    def __init__(self, nc, name):
        self.h = nc.alloc_semaphore(name)
        self.n = 0


class MK:
    def __init__(self, nc):
        self.nc = nc
        self.ops = {e: [] for e in "tvsgy"}
        self.sem = {k: _Sem(nc, "sem_" + k) for k in
                    ("t", "v", "s", "g", "gd", "yd", "cc")}
        self.seen = {e: {} for e in "tvsgy"}
        self.sb = {}
        self.regions = []
        self.specs = []
        self.aliases = {}
        self.plan = None
        self.arena_base = None
        self.arena_cap = None

    def on(self, e, fn):
        self.ops[e].append(fn)

    def w(self, e, tok):
        if tok is None:
            return
        sem, n = tok
        if self.seen[e].get(id(sem), -1) >= n:
            return
        self.seen[e][id(sem)] = n
        self.on(e, lambda E, s=sem, v=n: E.wait_ge(s.h, v))

    def do(self, e, fn, sem=None, inc=1):
        if e in ("v", "s") and sem is not None:
            self.w(e, (sem, sem.n))
        if sem is None:
            self.on(e, fn)
            return None
        sem.n += inc
        n = sem.n
        self.on(e, lambda E, f=fn, s=sem, i=inc: f(E).then_inc(s.h, i))
        return (sem, n)

    def tok(self, k):
        s = self.sem[k]
        return (s, s.n)

    def dsem(self, name):
        if name not in self.sem:
            self.sem[name] = _Sem(self.nc, "sem_" + name)
        return self.sem[name]

    def salloc(self, name, shape, dtype, p0, p1, alias=None):
        size = shape[1] * (2 if dtype == BF16 else 4)
        size = (size + 63) & ~63
        if alias is None:
            self.specs.append((name, size, p0, p1))
        else:
            self.aliases[name] = alias
        if self.plan is not None:
            off = self.plan[name] if alias is None else self.plan[alias]
        else:
            off = 0
        self.regions.append((off, size, p0, p1))
        h = self.nc.alloc_sbuf_tensor_at(name, list(shape), dtype,
                                         offset=self.arena_base + off)
        self.sb[name] = h
        return h


def _rope_tables():
    inv = 1.0 / (10000.0 ** (np.arange(0, LAT, 2, dtype=np.float64) / LAT))
    t = np.arange(T, dtype=np.float64)
    fr = np.outer(t, inv)
    cos = np.repeat(np.cos(fr), 2, axis=-1)[:, :LAT]
    sin = np.repeat(np.sin(fr), 2, axis=-1)[:, :LAT]
    return (np.tile(cos.T, (G, 1)).astype(np.float32),
            np.tile(sin.T, (G, 1)).astype(np.float32))


def _const_inputs():
    bf = ml_dtypes.bfloat16
    cosT, sinT = _rope_tables()
    prot = np.zeros((128, 128), dtype=np.float32)
    for blk in range(G):
        for i in range(LAT // 2):
            b = blk * 32
            prot[b + 2 * i, b + 2 * i + 1] = -1.0
            prot[b + 2 * i + 1, b + 2 * i] = 1.0
    tri = (np.arange(128)[None, :] >= np.arange(128)[:, None]).astype(np.float32)
    blk4T = np.zeros((4, 128), dtype=np.float32)
    for j in range(4):
        blk4T[j, 32 * j:32 * (j + 1)] = 1.0
    return {
        "cosT": cosT, "sinT": sinT,
        "protT": prot.T.astype(bf).copy(),
        "trimask": tri.astype(bf),
        "identf": np.eye(128, dtype=np.float32),
        "identb": np.eye(128, dtype=np.float32).astype(ml_dtypes.bfloat16),
        "ones1": np.ones((1, 512), dtype=bf),
        "ones128": np.ones((128, 1), dtype=bf),
        "blk4T": blk4T.astype(bf),
        "ones65": np.ones((65, 512), dtype=bf),
        "blkones4": blk4T.T.astype(bf).copy(),
        "eps128": np.full((128, 1), EPS, dtype=np.float32),
    }


_CONSTS = _const_inputs()


def build(plan=None):
    nc = bass.Bass("TRN2", target_bir_lowering=False)
    DI = lambda n, s, dt=F32: nc.dram_tensor(n, list(s), dt, kind="ExternalInput")

    x_tm_in = DI("x_tm_in", [T, C])
    x_fm_in = DI("x_fm_in", [C, T])
    caw_in = DI("caw", [3 * C, C])
    cprojT_in = DI("cprojT_in", [C, C])
    q2lT_in = DI("q2lT_in", [HD, LAT])
    kv2lT_in = DI("kv2lT_in", [HD, LAT])
    l2o_in = DI("l2o_in", [HD, LAT])
    wqkT_in = DI("wqkT_in", [LAT, LAT])
    bqr_in = DI("bqr_in", [HD, H])
    bkr_in = DI("bkr_in", [HD, H])
    bvr_in = DI("bvr_in", [HD, H])
    q2lb16_in = DI("q2lb16_in", [H, LAT])
    kvlb16_in = DI("kvlb16_in", [H, LAT])
    wqkb512_in = DI("wqkb512_in", [1, 512])
    l2ob_in = DI("l2ob_in", [128, 8])
    cprojb_in = DI("cprojb_in", [1, C])
    cosT_in = DI("cosT", [128, T])
    sinT_in = DI("sinT", [128, T])
    protT_in = DI("protT", [128, 128], BF16)
    tri_in = DI("trimask", [128, 128], BF16)
    identf_in = DI("identf", [128, 128], F32)
    identb_in = DI("identb", [128, 128], BF16)
    ones1_in = DI("ones1", [1, 512], BF16)
    ones128_in = DI("ones128", [128, 1], BF16)
    ones65_in = DI("ones65", [65, 512], BF16)
    blk4T_in = DI("blk4T", [4, 128], BF16)
    blkones4_in = DI("blkones4", [128, 4], BF16)
    blkdQ_in = DI("blkdQ_in", [128, 64])
    blkdKV_in = DI("blkdKV_in", [128, 64])
    wqkT4_in = DI("wqkT4_in", [128, LAT])
    l2o2_in = DI("l2o2_in", [128, LAT])
    eps128_in = DI("eps128", [128, 1], F32)
    normg_in = DI("normg128", [128, C])
    normb_in = DI("normb128", [128, C])
    slotg_in = DI("slotg128", [128, C])
    slotb_in = DI("slotb128", [128, C])
    scale128_in = DI("scale128", [128, 1])
    semb_in = DI("semb", [EL, C])
    wg2_in = DI("wg2", [2, C, FF])
    wu2_in = DI("wu2", [2, C, FF])
    wd2_in = DI("wd2", [2, FF, C])
    bg2_in = DI("bg2", [2, FF])
    bu2_in = DI("bu2", [2, FF])
    bd2_in = DI("bd2", [2, C])

    out_d = nc.dram_tensor("out", [T, C], F32, kind="ExternalOutput")

    ag1_in = nc.dram_tensor("ag1_in", [C, EL], BF16)
    ag1_out = nc.dram_tensor("ag1_out", [NCORES * C, EL], BF16, addr_space="Shared")
    ag2_in = nc.dram_tensor("ag2_in", [EL, C], BF16)
    ag2_out = nc.dram_tensor("ag2_out", [NCORES * EL, C], BF16, addr_space="Shared")
    x2_spill = nc.dram_tensor("x2_spill", [T, C], BF16)

    ARENA = 206336
    arena = nc.alloc_sbuf_tensor("arena", [128, ARENA], mybir.dt.uint8)
    mk = MK(nc)
    mk.plan = plan
    mk.arena_base = nc.lookup_mloc(arena).addr
    mk.arena_cap = ARENA

    A = mk.salloc
    scrA = A("scrA", [128, T], F32, 0, 9)
    scrB = A("scrB", [128, T], F32, 0, 9)
    scrC = A("scrC", [128, T], F32, 4, 5)
    scrD = A("scrD", [128, T], F32, 10, 10)
    scbA = A("scbA", [128, T], BF16, 0, 9)
    dbgA = A("dbgA", [128, 1024], F32, 0, 2)
    scbB = A("scbB", [128, T], BF16, 0, 9)
    stat = A("stat", [128, 64], F32, 0, 9)
    identf_s = A("identf_s", [128, 128], F32, 0, 9)
    identb_s = A("identb_s", [128, 128], BF16, 0, 9)
    ones1_s = A("ones1_s", [1, 512], BF16, 0, 9)
    ones128_s = A("ones128_s", [128, 1], BF16, 0, 9)
    ones65_s = A("ones65_s", [65, 512], BF16, 0, 10)
    eps_s = A("eps_s", [128, 1], F32, 0, 9)
    x_tm = A("x_tm", [128, NT * C], BF16, 0, 4)
    wpeT = A("wpeT", [128, G * C], BF16, 0, 4)
    bprj_b = A("bprj_b", [1, C], BF16, 0, 4)
    tri_s = A("tri_s", [128, 128], BF16, 0, 3)
    blk4T_s = A("blk4T_s", [4, 128], BF16, 0, 3)
    combineT = A("combineT", [128, NCK * T], BF16, 7, 10)
    recip_tm = A("recip_tm", [128, 16], F32, 7, 10)
    x_fm = A("x_fm", [128, NCK * T], BF16, 0, 2)
    weffq = A("weffq", [128, NCK * 512], BF16, 0, 2)
    weffk = A("weffk", [128, NCK * 512], BF16, 0, 2)
    weffv = A("weffv", [128, NCK * 512], BF16, 0, 2)
    cosbf = A("cosbf", [128, T], BF16, 0, 2)
    sinbf = A("sinbf", [128, T], BF16, 0, 2)
    prot_s = A("prot_s", [128, 128], BF16, 0, 2)
    blkones4_s = A("blkones4_s", [128, 4], BF16, 0, 2)
    wqkT_b = A("wqkT_b", [32, 32], BF16, 0, 2)
    wqkT4 = A("wqkT4", [128, 32], BF16, 0, 2)
    biasq_row = A("biasq_row", [1, 512], BF16, 0, 2)
    biask_row = A("biask_row", [1, 512], BF16, 0, 2)
    biasv_row = A("biasv_row", [1, 512], BF16, 0, 2)
    wqkb_b = A("wqkb_b", [1, 512], BF16, 0, 2)
    QK = A("QK", [128, G * T], BF16, 1, 3)
    KR = A("KR", [128, G * T], BF16, 1, 3)
    vt = A("vt", [128, NT * 512], BF16, 1, 3)
    xn_tm = A("xn_tm", [128, NT * C], BF16, 5, 7)
    rowsB2 = A("rowsB2", [65, 2 * T], BF16, 7, 9)
    l2o_b = A("l2o_b", [64, 32], BF16, 0, 1)
    l2o2 = A("l2o2", [128, 32], BF16, 0, 1)
    nbias_pp = A("nbias_pp", [128, 1], F32, 2, 3)
    ylat = A("ylat", [128, G * T], BF16, 3, 4)
    x2 = A("x2", [128, NT * C], BF16, 4, 5)
    normg_s = A("normg_s", [128, C], F32, 4, 5)
    normb_s = A("normb_s", [128, C], F32, 4, 5)
    slotg_s = A("slotg_s", [128, C], F32, 4, 5)
    slotb_s = A("slotb_s", [128, C], F32, 4, 5)
    scale_s = A("scale_s", [128, 1], F32, 4, 5)
    xn_fm = A("xn_fm", [128, NCK * T], BF16, 5, 6)
    seT = A("seT", [128, NCK * C], BF16, 5, 6)
    logits = A("logits", [128, NT * C], F32, 6, 7)
    rowmax_tm = A("rowmax_tm", [128, 16], F32, 6, 7)
    XeT = A("XeT", [128, NCK * C], BF16, 8, 9)
    xstg = A("xstg", [128, 2 * C], BF16, 8, 8)
    eout_f = A("eout_f", [128, NCK * C], BF16, 9, 10)
    x2b = A("x2b", [128, NT * C], BF16, 9, 10)
    cawb0 = A("cawb0", [128, C], BF16, 0, 0)
    cawb1 = A("cawb1", [128, C], BF16, 0, 0)
    cprojT2 = A("cprojT2", [128, 2 * C], BF16, 0, 0)
    q2lT_f = A("q2lT_f", [64, 32], F32, 0, 0)
    kv2lT_f = A("kv2lT_f", [64, 32], F32, 0, 0)
    q2lT_b = A("q2lT_b", [64, 32], BF16, 0, 0)
    kv2lT_b = A("kv2lT_b", [64, 32], BF16, 0, 0)
    l2o_f = A("l2o_f", [64, 32], F32, 0, 0)
    wqkT_f = A("wqkT_f", [32, 32], F32, 0, 0)
    blkdQ = A("blkdQ", [128, 64], BF16, 0, 2)
    blkdKV = A("blkdKV", [128, 64], BF16, 0, 2)
    blkdQf = A("blkdQf", [128, 64], F32, 0, 0)
    blkdKVf = A("blkdKVf", [128, 64], F32, 0, 0)
    wqkT4f = A("wqkT4f", [128, LAT], F32, 0, 0)
    l2o2f = A("l2o2f", [128, LAT], F32, 0, 0)
    bxr_f = A("bxr_f", [64, 16], F32, 0, 0)
    bxr_b = A("bxr_b", [64, 16], BF16, 0, 0)
    bx16c = A("bx16c", [16, 32], F32, 0, 0)
    bx16_f = A("bx16_f", [16, 32], F32, 0, 0)
    bx16b = A("bx16b", [16, 32], BF16, 0, 0)
    wqkb_f = A("wqkb_f", [1, 512], F32, 0, 0)
    l2ob_f = A("l2ob_f", [128, 8], F32, 0, 0)
    l2ob_b = A("l2ob_b", [128, 8], BF16, 0, 0)
    cprojb_f = A("cprojb_f", [1, C], F32, 0, 0)
    bprj_f = A("bprj_f", [1, C], F32, 0, 0)
    nb44 = A("nb44", [4, 4], F32, 2, 2)
    nbrow = A("nbrow", [1, 16], F32, 2, 2)
    nbrow2 = A("nbrow2", [1, 1], BF16, 2, 2)
    PT = A("PT", [128, 2 * T], BF16, 3, 3)
    sums4 = A("sums4", [4, 512], F32, 3, 3)
    rec4 = A("rec4", [4, 512], F32, 3, 3)
    rec4b = A("rec4b", [4, 512], BF16, 3, 3)
    rbbf = A("rbbf", [128, 512], BF16, 3, 3)
    dispatchN = A("dispatchN", [128, NT * C], BF16, 7, 7)
    lgb = A("lgb", [128, 2 * NT * C], BF16, 7, 7, alias="logits")
    scrBb = A("scrBb", [128, T], BF16, 7, 7, alias="scrB")
    nrm_s = A("nrm_s", [128, 16], F32, 7, 7)
    rsum_tm = A("rsum_tm", [128, 16], F32, 7, 7)
    rsL_s = A("rsL_s", [128, 16], F32, 7, 7)
    cmax_s = A("cmax_s", [128, 8], F32, 7, 7)
    ncmax_s = A("ncmax_s", [128, 8], F32, 7, 7)
    cs2_s = A("cs2_s", [128, 8], F32, 7, 7)
    rcl_s = A("rcl_s", [128, 8], F32, 7, 7)
    rcT_s = A("rcT_s", [128, 8], F32, 7, 7)
    wgA = A("wgA", [128, NCK * C], BF16, 8, 8)
    wgB = A("wgB", [128, NCK * C], BF16, 8, 8)
    wuA = A("wuA", [128, NCK * C], BF16, 8, 8)
    wuB = A("wuB", [128, NCK * C], BF16, 8, 8)
    wdS = A("wdS", [128, 16 * 512], BF16, 8, 8)
    hfm = A("hfm", [128, 16 * 512], BF16, 8, 8)
    outbuf = A("outbuf", [128, NT * C], F32, 10, 10)

    bg_row = rowsB2[0:1, 0:2 * FF]
    bu_row = rowsB2[32:33, 0:2 * FF]
    bd_row = rowsB2[64:65, 0:2 * C]

    p4 = nc.alloc_psum_tensor("p4", [128, 2048], F32)
    pA = nc.alloc_psum_tensor("pA", [128, 512], F32)
    pB = nc.alloc_psum_tensor("pB", [128, 512], F32)
    pC = nc.alloc_psum_tensor("pC", [128, 512], F32)
    pD = nc.alloc_psum_tensor("pD", [128, 512], F32)

    ST, SV, SS = mk.sem["t"], mk.sem["v"], mk.sem["s"]
    GD, YD = mk.sem["gd"], mk.sem["yd"]

    x_fm_v = x_fm.rearrange("p (c t) -> p c t", c=NCK)
    x_tm_v = x_tm.rearrange("p (i c) -> p i c", i=NT)
    weffq_v = weffq.rearrange("p (c x) -> p c x", c=NCK)
    weffk_v = weffk.rearrange("p (c x) -> p c x", c=NCK)
    weffv_v = weffv.rearrange("p (c x) -> p c x", c=NCK)
    wpeT_v = wpeT.rearrange("p (g o) -> p g o", g=G)
    QK_v = QK.rearrange("p (g t) -> p g t", g=G)
    KR_v = KR.rearrange("p (g t) -> p g t", g=G)
    vt_v = vt.rearrange("p (i x) -> p i x", i=NT)
    ylat_v = ylat.rearrange("p (g t) -> p g t", g=G)
    PT_v = PT.rearrange("p (u t) -> p u t", u=2)
    x2_v = x2.rearrange("p (i c) -> p i c", i=NT)
    xn_tm_v = xn_tm.rearrange("p (i c) -> p i c", i=NT)
    xn_fm_v = xn_fm.rearrange("p (c t) -> p c t", c=NCK)
    seT_v = seT.rearrange("p (c e) -> p c e", c=NCK)
    logits_v = logits.rearrange("p (i c) -> p i c", i=NT)
    combineT_v = combineT.rearrange("p (c t) -> p c t", c=NCK)
    dnT_v = dispatchN.rearrange("p (e t) -> p e t", e=NCK)
    lgb_v = lgb.rearrange("p (i c) -> p i c", i=2 * NT)
    XeT_v = XeT.rearrange("p (c e) -> p c e", c=NCK)
    wgA_v = wgA.rearrange("p (c f) -> p c f", c=NCK)
    wgB_v = wgB.rearrange("p (c f) -> p c f", c=NCK)
    wuA_v = wuA.rearrange("p (c f) -> p c f", c=NCK)
    wuB_v = wuB.rearrange("p (c f) -> p c f", c=NCK)
    wdS_v = wdS.rearrange("p (c o) -> p c o", c=16)
    hfm_v = hfm.rearrange("p (c x) -> p c x", c=16)
    eout_f_v = eout_f.rearrange("p (c o) -> p c o", c=NCK)
    x2b_v = x2b.rearrange("p (i c) -> p i c", i=NT)
    outbuf_v = outbuf.rearrange("p (i c) -> p i c", i=NT)
    p4q = p4.rearrange("p (q x) -> p q x", q=16)
    p4j = p4.rearrange("p (j x) -> p j x", j=4)

    ldtoks = {}

    def gload(name, dst_ap, src_ap, sem=None):
        s_ = mk.dsem(sem or ("g_" + name))
        ldtoks[name] = mk.do("g", lambda E, d=dst_ap, s=src_ap: E.dma_start(d, s), s_, 16)

    def yload(name, dst_ap, src_ap, sem=None):
        s_ = mk.dsem(sem or ("y_" + name))
        ldtoks[name] = mk.do("y", lambda E, d=dst_ap, s=src_ap: E.dma_start(d, s), s_, 16)
        return ldtoks[name]

    def yload_Y0(name, dst_ap, src_ap):
        return yload(name, dst_ap, src_ap, sem="Y0")

    gload("x_fm", x_fm_v[:, :, :], x_fm_in.rearrange("(c p) t -> p c t", p=128), sem="G0")
    gload("x_tm", x_tm_v[:, :, :], x_tm_in.rearrange("(i p) c -> p i c", p=128), sem="G0")
    gload("cos", cosbf[:, :], cosT_in[:, :], sem="G0")
    gload("sin", sinbf[:, :], sinT_in[:, :], sem="G0")
    yload_Y0("identf", identf_s[:, :], identf_in[:, :])
    yload_Y0("identb", identb_s[:, :], identb_in[:, :])
    yload_Y0("tri", tri_s[:, :], tri_in[:, :])
    yload_Y0("prot", prot_s[:, :], protT_in[:, :])
    yload_Y0("ones1", ones1_s[:, :], ones1_in[:, :])
    yload_Y0("ones128", ones128_s[:, :], ones128_in[:, :])
    yload_Y0("ones65", ones65_s[:, :], ones65_in[:, :])
    yload_Y0("blk4T", blk4T_s[:, :], blk4T_in[:, :])
    yload_Y0("blkones4", blkones4_s[:, :], blkones4_in[:, :])
    yload_Y0("eps", eps_s[:, :], eps128_in[:, :])
    yload_Y0("q2lT", q2lT_f[:, :], q2lT_in[:, :])
    yload_Y0("kv2lT", kv2lT_f[:, :], kv2lT_in[:, :])
    yload_Y0("wqkb", wqkb_f[:, :], wqkb512_in[:, :])
    yload_Y0("l2ob", l2ob_f[:, :], l2ob_in[:, :])
    yload_Y0("cprojb", cprojb_f[:, :], cprojb_in[:, :])
    yload_Y0("blkdQ", blkdQf[:, :], blkdQ_in[:, :])
    yload_Y0("blkdKV", blkdKVf[:, :], blkdKV_in[:, :])
    yload_Y0("wqkT4", wqkT4f[:, :], wqkT4_in[:, :])
    yload_Y0("l2o2", l2o2f[:, :], l2o2_in[:, :])

    TOK_G0 = mk.tok("G0")
    TOK_Y0 = mk.tok("Y0")
    for _n in list(ldtoks):
        if mk.sem.get("G0") and ldtoks[_n] and ldtoks[_n][0] is mk.sem["G0"]:
            ldtoks[_n] = TOK_G0
        if mk.sem.get("Y0") and ldtoks[_n] and ldtoks[_n][0] is mk.sem["Y0"]:
            ldtoks[_n] = TOK_Y0
    tok_smalls = None
    for nm, (dst, src) in {
        "q2lT": (q2lT_b, q2lT_f), "kv2lT": (kv2lT_b, kv2lT_f),
        "wqkb": (wqkb_b, wqkb_f), "l2ob": (l2ob_b, l2ob_f),
        "blkdQ": (blkdQ, blkdQf), "blkdKV": (blkdKV, blkdKVf),
        "wqkT4": (wqkT4, wqkT4f), "l2o2": (l2o2, l2o2f),
    }.items():
        mk.w("v", ldtoks[nm])
        tok_smalls = mk.do("v", lambda E, d=dst, s=src:
                           E.tensor_copy(d[:, :], s[:, :]), SV)
    tok_blkd = tok_smalls
    tok_l2o2 = tok_smalls
    tok_wqk4 = tok_smalls

    parts = [(0, blkdQ, weffq_v), (C, blkdKV, weffk_v), (2 * C, blkdKV, weffv_v)]
    caw_bufs = [cawb0, cawb1]
    caw_tok = [None, None]
    psQ = [pA, pB]
    ps_tok = [None, None]
    for pi, (roff, blkd, wv) in enumerate(parts):
        for p in range(8):
            ii = pi * 8 + p
            par = ii % 2
            buf = caw_bufs[par]
            mk.w("g", caw_tok[par])
            ct = mk.do("g", lambda E, b=buf, r=roff + p * 128:
                       E.dma_start(b[:, :], caw_in[r:r + 128, :]),
                       mk.dsem(f"CAW{par}"), 16)
            ps = psQ[par]
            mk.w("t", ct)
            mk.w("t", tok_blkd)
            mk.w("t", ps_tok[par])
            tk = None
            for j in range(NCK):
                tk = mk.do("t", lambda E, ps_=ps, b=buf, bd=blkd, j_=j:
                           E.matmul(ps_[:, j_ * 64:(j_ + 1) * 64],
                                    b[:, j_ * 128:(j_ + 1) * 128], bd[:, :],
                                    start=True, stop=True), ST)
            caw_tok[par] = tk
            mk.w("v", tk)
            dst = wv[:, :, p * 64:(p + 1) * 64]
            src = ps.rearrange("p (j x) -> p j x", j=8)
            ps_tok[par] = mk.do("v", lambda E, d=dst, s=src:
                                E.tensor_copy(d, s[:, :, :]), SV)

    prev_cast = None
    for brow, bin_, b16in, rhsw in ((biasq_row, bqr_in, q2lb16_in, q2lT_b),
                                    (biask_row, bkr_in, kvlb16_in, kv2lT_b),
                                    (biasv_row, bvr_in, kvlb16_in, kv2lT_b)):
        mk.w("y", prev_cast)
        mk.do("y", lambda E, b=bin_: E.dma_start(bxr_f[:, :], b[:, :]),
              mk.dsem("YB"), 16)
        t2 = mk.do("y", lambda E, b=b16in: E.dma_start(bx16c[:, :], b[:, :]),
                   mk.dsem("YB"), 16)
        mk.w("v", t2)
        t3 = mk.do("v", lambda E: E.tensor_copy(bxr_b[:, :], bxr_f[:, :]), SV)
        mk.w("t", t3)
        mk.w("t", ps_tok[0])
        t4 = mk.do("t", lambda E, r=rhsw: E.matmul(pA[0:16, 0:32], bxr_b[:, :],
                                                   r[:, :], start=True, stop=True), ST)
        mk.w("v", t4)
        t5 = mk.do("v", lambda E: E.tensor_tensor(bx16_f[:, :], pA[0:16, 0:32],
                                                  bx16c[:, :], ALU.add), SV)
        ps_tok[0] = t5
        t6 = mk.do("v", lambda E: E.tensor_copy(bx16b[:, :], bx16_f[:, :]), SV)
        mk.w("g", t6)
        t7 = mk.do("g", lambda E, d=brow: E.dma_start(d[0:1, :], bx16b[:, :]),
                   mk.dsem("BRW"), 16)
        mk.w("v", t7)
        prev_cast = t5
        ldtoks["brow"] = t7

    cp2 = cprojT2.rearrange("p (u o) -> p u o", u=2)
    wpe_prev_mm = None
    wpe_done = None
    for g in range(G):
        mk.w("g", wpe_prev_mm)
        ct = mk.do("g", lambda E, g_=g: E.dma_start(
            cp2[:, :, :],
            cprojT_in.rearrange("(u p) o -> p u o", p=128)[:, 2 * g_:2 * g_ + 2, :]),
            mk.dsem("CPJ"), 16)
        mk.w("t", ct)
        mk.w("t", tok_l2o2)
        mk.w("t", wpe_done)
        tk = None
        for j in range(G):
            for hb in range(2):
                tk = mk.do("t", lambda E, j_=j, hb_=hb: E.matmul(
                    p4[32 * j_:32 * (j_ + 1), hb_ * 512:(hb_ + 1) * 512],
                    l2o2[(j_ % 2) * 64:(j_ % 2) * 64 + 64, :],
                    cp2[(j_ % 2) * 64:(j_ % 2) * 64 + 64, j_ // 2,
                        hb_ * 512:(hb_ + 1) * 512],
                    start=True, stop=True,
                    tile_position=((j_ % 2) * 64, 32 * j_)), ST)
        for u in range(2):
            for hb in range(2):
                tk2 = mk.do("t", lambda E, u_=u, g_=g, hb_=hb: E.matmul(
                    p4[0:1, 1024 + hb_ * 512:1024 + (hb_ + 1) * 512],
                    l2ob_b[:, 2 * g_ + u_:2 * g_ + u_ + 1],
                    cp2[:, u_, hb_ * 512:(hb_ + 1) * 512],
                    start=(2 * g_ + u_ == 0), stop=(2 * g_ + u_ == 7)), ST)
        wpe_prev_mm = tk2
        mk.w("v", tk)
        wpe_done = mk.do("v", lambda E, g_=g: E.tensor_copy(
            wpeT_v[:, g_, :], p4[:, 0:1024]), SV)
    mk.w("v", wpe_prev_mm)
    mk.w("v", ldtoks["cprojb"])
    mk.do("v", lambda E: E.tensor_tensor(bprj_f[0:1, :], p4[0:1, 1024:2048],
                                         cprojb_f[0:1, :], ALU.add), SV)
    tok_bprj = mk.do("v", lambda E: E.tensor_copy(bprj_b[0:1, :], bprj_f[0:1, :]), SV)

    if _BUILD_OPTS["phases"] < 1:
        return _finish(mk, nc, out_d, outbuf_v)

    mk.w("t", ldtoks["x_fm"])
    mk.w("t", ldtoks["ones1"])
    mk.w("t", ldtoks["brow"])
    mk.w("t", tok_wqk4)
    mk.w("t", tok_bprj)
    mk.w("v", ldtoks["cos"])
    mk.w("v", ldtoks["sin"])

    tA = tB = tC = None
    prev_rot = None
    prev_wqk = None
    for g in range(G):
        for side in ("q", "k"):
            wv = weffq_v if side == "q" else weffk_v
            brow = biasq_row if side == "q" else biask_row
            for b in range(NB):
                mk.w("t", tA)
                for j in range(NCK):
                    mk.do("t", lambda E, w_=wv, j_=j, g_=g, b_=b: E.matmul(
                        pA[:, :], w_[:, j_, g_ * 128:(g_ + 1) * 128],
                        x_fm_v[:, j_, b_ * 512:(b_ + 1) * 512],
                        start=(j_ == 0), stop=False), ST)
                tk = mk.do("t", lambda E, br=brow, g_=g: E.matmul(
                    pA[:, :], br[0:1, g_ * 128:(g_ + 1) * 128],
                    ones1_s[0:1, 0:512], start=False, stop=True), ST)
                mk.w("v", tk)
                mk.w("v", prev_rot)
                t_qs = mk.do("v", lambda E, b_=b: E.tensor_tensor(
                    scbA[:, 0:512], pA[:, :],
                    sinbf[:, b_ * 512:(b_ + 1) * 512], ALU.mult), SV)
                mk.w("t", t_qs)
                mk.w("t", tB)
                t_rot = mk.do("t", lambda E: E.matmul(
                    pB[:, :], prot_s[:, :], scbA[:, 0:512],
                    start=True, stop=True), ST)
                prev_rot = t_rot
                mk.w("v", t_rot)
                if _BUILD_OPTS.get("dbg") and g == 0 and side == "k" and b == 0:
                    mk.do("v", lambda E: E.tensor_copy(dbgA[:, 0:512], scbA[:, 0:512]), SV)
                    mk.do("v", lambda E: E.tensor_copy(dbgA[:, 512:1024], pB[:, :]), SV)
                t_qc = mk.do("v", lambda E, b_=b: E.tensor_tensor(
                    scbB[:, 0:512], pA[:, :],
                    cosbf[:, b_ * 512:(b_ + 1) * 512], ALU.mult), SV)
                tA = t_qc
                if side == "q":
                    mk.w("v", prev_wqk)
                    t_qr = mk.do("v", lambda E: E.tensor_tensor(
                        scbA[:, 512:1024], pB[:, :], scbB[:, 0:512], ALU.add), SV)
                    tB = t_qr
                    mk.w("t", t_qr)
                    mk.w("t", tC)
                    for j in range(4):
                        mk.do("t", lambda E, j_=j: E.matmul(
                            pC[32 * j_:32 * (j_ + 1), :], wqkT4[32 * j_:32 * (j_ + 1), :],
                            scbA[32 * j_:32 * (j_ + 1), 512:1024],
                            start=True, stop=False,
                            tile_position=(32 * j_, 32 * j_)), ST)
                    tk2 = mk.do("t", lambda E, g_=g: E.matmul(
                        pC[:, :], wqkb_b[0:1, g_ * 128:(g_ + 1) * 128],
                        ones1_s[0:1, 0:512], start=False, stop=True), ST)
                    prev_wqk = tk2
                    mk.w("v", tk2)
                    tC = mk.do("v", lambda E, g_=g, b_=b: E.tensor_copy(
                        QK_v[:, g_, b_ * 512:(b_ + 1) * 512], pC[:, :]), SV)
                else:
                    tB = mk.do("v", lambda E, g_=g, b_=b: E.tensor_tensor(
                        KR_v[:, g_, b_ * 512:(b_ + 1) * 512], pB[:, :],
                        scbB[:, 0:512], ALU.add), SV)
    for tc in range(NT):
        par = tc % 2
        ps = pA if par == 0 else pB
        mk.w("t", tA if par == 0 else tB)
        for j in range(NCK):
            mk.do("t", lambda E, ps_=ps, j_=j, tc_=tc: E.matmul(
                ps_[:, :], x_fm_v[:, j_, tc_ * 128:(tc_ + 1) * 128],
                weffv_v[:, j_, :], start=(j_ == 0), stop=False), ST)
        tk = mk.do("t", lambda E, ps_=ps: E.matmul(
            ps_[:, :], ones1_s[0:1, 0:128], biasv_row[0:1, :],
            start=False, stop=True), ST)
        mk.w("v", tk)
        ev = mk.do("v", lambda E, ps_=ps, tc_=tc: E.tensor_copy(
            vt_v[:, tc_, :], ps_[:, :]), SV)
        if par == 0:
            tA = ev
        else:
            tB = ev

    if _BUILD_OPTS["phases"] < 2:
        return _finish(mk, nc, out_d, outbuf_v)

    sq_tok = None
    for g in range(G):
        for side in ("q", "k"):
            src = QK_v if side == "q" else KR_v
            col = 56 if side == "q" else 57
            mk.w("v", mk.tok("v"))
            sq_tok = mk.do("v", lambda E, g_=g, s_=src: E.scalar_tensor_tensor(
                scbA[:, :], s_[:, g_, :], 1.0, s_[:, g_, :],
                ALU.mult, ALU.mult), SV)
            mk.w("t", sq_tok)
            tk = None
            for b in range(NB):
                tk = mk.do("t", lambda E, b_=b: E.matmul(
                    p4[0:4, b_ * 512:(b_ + 1) * 512], blkones4_s[:, :],
                    scbA[:, b_ * 512:(b_ + 1) * 512], start=True, stop=True), ST)
            mk.w("v", tk)
            mk.do("v", lambda E, c_=col: E.reduce_max(
                stat[0:4, c_:c_ + 1], p4[0:4, :], axis=AX), SV)
        mk.do("v", lambda E, g_=g: E.tensor_tensor(
            nb44[0:4, g_:g_ + 1], stat[0:4, 56:57], stat[0:4, 57:58], ALU.add), SV)
    mk.w("g", mk.tok("v"))
    gt = None
    for g in range(G):
        gt = mk.do("g", lambda E, g_=g: E.dma_start(
            nbrow[0:1, 4 * g_:4 * g_ + 4], nb44[0:4, g_:g_ + 1]),
            mk.dsem("NB4"), 16)
    mk.w("v", gt)
    mk.do("v", lambda E: E.reduce_max(stat[0:1, 58:59], nbrow[0:1, :], axis=AX), SV)
    mk.do("v", lambda E: E.tensor_scalar_mul(stat[0:1, 59:60], stat[0:1, 58:59],
                                             -0.5 * SCALE), SV)
    tkc = mk.do("v", lambda E: E.tensor_copy(nbrow2[0:1, 0:1], stat[0:1, 59:60]), SV)
    mk.w("t", tkc)
    tkm = mk.do("t", lambda E: E.matmul(pA[:, 0:1], ones1_s[0:1, 0:128],
                                        nbrow2[0:1, 0:1], start=True, stop=True), ST)
    mk.w("v", tkm)
    tk_nb = mk.do("v", lambda E: E.tensor_copy(nbias_pp[:, :], pA[:, 0:1]), SV)
    tA = tk_nb

    if _BUILD_OPTS["phases"] < 3:
        return _finish(mk, nc, out_d, outbuf_v)

    mk.w("s", tk_nb)
    prev_tkb = None
    exp_toks = {}
    sums_toks = {}
    PT_v4 = PT.rearrange("p (s x) -> p s x", s=4)
    att_banks = [pA, pD]
    tile_y = {-2: tA, -1: None}
    st3 = {"p": 0, "tk_rb": None, "tk_s4": None, "prev_tkb": None,
           "prev_gsm": None, "evac": None, "defer": None, "y_last": None}

    def _att_consume(pr):
        (pi, g_, ci_, hp_, off_, w_, sl_, n_, diag_, etok_, mtok_, bank_,
         first_, yt_, se_) = pr
        mk.w("t", mtok_ if diag_ else etok_)
        if first_:
            mk.w("t", yt_)
        for j in (2 * hp_, 2 * hp_ + 1):
            mk.do("t", lambda E, g2=g_, j_=j, jj_=j - 2 * hp_, c2=ci_, o2=off_,
                  w2=w_, s2=sl_, n2=n_, bk=bank_:
                  E.matmul(bk[32 * j_:32 * (j_ + 1), o2:o2 + w2],
                           vt_v[:, c2, (4 * g2 + j_) * 32:(4 * g2 + j_ + 1) * 32],
                           PT_v4[:, s2, jj_ * 512 + o2:jj_ * 512 + o2 + w2],
                           start=(c2 == 0), stop=(c2 == n2 - 1),
                           tile_position=(0, 32 * j_)), ST)
        if first_:
            mk.w("t", se_)
        pu = None
        for j in (2 * hp_, 2 * hp_ + 1):
            pu = mk.do("t", lambda E, j_=j, jj_=j - 2 * hp_, c2=ci_, o2=off_,
                       w2=w_, s2=sl_, n2=n_:
                       E.matmul(pB[32 * j_:32 * j_ + 1, o2:o2 + w2],
                                ones128_s[:, 0:1],
                                PT_v4[:, s2, jj_ * 512 + o2:jj_ * 512 + o2 + w2],
                                start=(c2 == 0), stop=(c2 == n2 - 1),
                                tile_position=(0, 32 * j_)), ST)
        sums_toks[pi] = pu

    pend = None
    ti = 0
    for g in range(G):
        for b in range(NB):
            nch = 4 * (b + 1)
            first_pidx = st3["p"]
            bank = att_banks[ti % 2]
            for ci in range(nch):
                off = (ci - 4 * b) * 128 if ci >= 4 * b else 0
                w = 512 - off
                diag = (ci >= 4 * b)
                for hp in range(2):
                    p_idx = st3["p"]
                    par2 = p_idx % 2
                    slot = p_idx % 4
                    mk.w("t", exp_toks.get(p_idx - 2))
                    stok = None
                    for j in (2 * hp, 2 * hp + 1):
                        stok = mk.do("t", lambda E, g_=g, j_=j, jj_=j - 2 * hp, ci_=ci,
                                     off_=off, w_=w, b_=b, pr_=par2:
                                     E.matmul(p4j[:, 2 * pr_ + jj_, off_:off_ + w_],
                                              KR_v[32 * j_:32 * (j_ + 1), g_,
                                                   ci_ * 128:(ci_ + 1) * 128],
                                              QK_v[32 * j_:32 * (j_ + 1), g_,
                                                   b_ * 512 + off_:b_ * 512 + off_ + w_],
                                              start=True, stop=True,
                                              tile_position=(32 * j_, 0)), ST)
                    if p_idx == first_pidx + 2 and st3["defer"] is not None:
                        st3["defer"]()
                        st3["defer"] = None
                    mk.w("s", stok)
                    mk.w("s", sums_toks.get(p_idx - 4))
                    pin = p4j[:, 2 * par2:2 * par2 + 2, off:off + w]
                    pout = PT_v4[:, slot, :].rearrange(
                        "p (j x) -> p j x", j=2)[:, :, off:off + w]
                    etok = mk.do("s", lambda E, pi=pin, po=pout: E.activation(
                        po, pi, ACT.Exp, bias=nbias_pp[:, 0:1], scale=SCALE), SS)
                    exp_toks[p_idx] = etok
                    mtok = None
                    if diag:
                        mk.w("v", etok)
                        pm = PT_v4[:, slot, :].rearrange(
                            "p (j x) -> p j x", j=2)[:, :, off:off + 128]
                        mtok = mk.do("v", lambda E, pm_=pm: E.tensor_tensor(
                            pm_, pm_, tri_s[:, None, :].broadcast_to([128, 2, 128]),
                            ALU.mult), SV)
                    if pend is not None:
                        _att_consume(pend)
                    pend = (p_idx, g, ci, hp, off, w, slot, nch, diag, etok, mtok,
                            bank, p_idx == first_pidx, tile_y.get(ti - 2),
                            st3["evac"])
                    st3["p"] = p_idx + 1
            _att_consume(pend)
            pend = None
            ptail = mk.tok("t")
            mk.w("v", ptail)
            mk.w("v", st3["prev_gsm"])
            ev = mk.do("v", lambda E: E.tensor_copy(
                scrB[0:97, 0:512], pB[0:97, :]), SV)
            st3["evac"] = ev
            mk.w("g", ev)
            mk.w("g", st3["tk_s4"])
            gsm = mk.do("g", lambda E: E.dma_start(
                sums4[:, :], bass.AP(scrB, 0, [[32 * T, 4], [1, 512]])),
                mk.dsem("SM4"), 16)
            st3["prev_gsm"] = gsm

            def _defer(g_=g, b_=b, ti_=ti, gsm_=gsm, bank_=bank):
                mk.w("s", gsm_)
                mk.w("s", st3["prev_tkb"])
                mk.do("s", lambda E: E.activation(rec4[:, :], sums4[:, :],
                                                  ACT.Ln), SS)
                tk_s4 = mk.do("s", lambda E: E.activation(
                    rec4b[:, :], rec4[:, :], ACT.Exp, scale=-1.0), SS)
                st3["tk_s4"] = tk_s4
                mk.w("t", tk_s4)
                mk.w("t", st3["tk_rb"])
                tkb = mk.do("t", lambda E: E.matmul(
                    pC[:, :], blk4T_s[:, :], rec4b[:, :],
                    start=True, stop=True), ST)
                st3["prev_tkb"] = tkb
                mk.w("v", tkb)
                st3["tk_rb"] = mk.do("v", lambda E: E.tensor_copy(
                    rbbf[:, :], pC[:, :]), SV)
                yt = mk.do("v", lambda E, g2=g_, b2=b_, bk=bank_: E.tensor_tensor(
                    ylat_v[:, g2, b2 * 512:(b2 + 1) * 512], bk[:, :], rbbf[:, :],
                    ALU.mult), SV)
                tile_y[ti_] = yt
                st3["y_last"] = yt
            st3["defer"] = _defer
            ti += 1
    if st3["defer"] is not None:
        st3["defer"]()
        st3["defer"] = None
    tk_y = st3["y_last"]

    if _BUILD_OPTS["phases"] < 4:
        return _finish(mk, nc, out_d, outbuf_v)

    mk.w("y", mk.tok("t"))
    yload("normg", normg_s[:, :], normg_in[:, :], sem="YN")
    yload("normb", normb_s[:, :], normb_in[:, :], sem="YN")
    yload("slotg", slotg_s[:, :], slotg_in[:, :], sem="YN")
    yload("slotb", slotb_s[:, :], slotb_in[:, :], sem="YN")
    yload("scl", scale_s[:, :], scale128_in[:, :], sem="YN")
    TOK_YN = mk.tok("YN")
    for _n in ("normg", "normb", "slotg", "slotb", "scl"):
        ldtoks[_n] = TOK_YN
    mk.w("t", tk_y)
    mk.w("v", ldtoks["x_tm"])

    h1s3 = [scrA, scrB, scrC]
    bases3 = [0, 8, 48]
    mm_gate = [None, None]
    A_toks = {}
    B_toks = {}
    x2_done = [None] * NT
    for step in range(NT + 2):
        if step < NT:
            i = step
            par = i % 2
            off0 = 0 if par == 0 else 1024
            mk.w("t", mm_gate[par])
            tk = None
            for g in range(G):
                for hb in range(2):
                    tk = mk.do("t", lambda E, g_=g, hb_=hb, i_=i, o=off0: E.matmul(
                        p4[:, o + hb_ * 512:o + (hb_ + 1) * 512],
                        ylat_v[:, g_, i_ * 128:(i_ + 1) * 128],
                        wpeT_v[:, g_, hb_ * 512:(hb_ + 1) * 512],
                        start=(g_ == 0), stop=False), ST)
            for hb in range(2):
                tk = mk.do("t", lambda E, hb_=hb, o=off0: E.matmul(
                    p4[:, o + hb_ * 512:o + (hb_ + 1) * 512],
                    ones1_s[0:1, 0:128], bprj_b[0:1, hb_ * 512:(hb_ + 1) * 512],
                    start=False, stop=True), ST)
            h1 = h1s3[i % 3]
            base = bases3[i % 3]
            mk.w("v", tk)
            atok = mk.do("v", lambda E, i_=i, h_=h1, b_=base, o=off0:
                         E.scalar_tensor_tensor(
                             h_[:, 0:1024], p4[:, o:o + 1024], 1.0,
                             x_tm_v[:, i_, :], ALU.mult, ALU.add,
                             accum_out=stat[:, b_:b_ + 1]), SV)
            mm_gate[par] = atok
            mk.do("v", lambda E, b_=base: E.tensor_tensor(
                stat[:, b_ + 2:b_ + 3], stat[:, b_:b_ + 1],
                stat[:, b_:b_ + 1], ALU.mult), SV)
            mk.do("v", lambda E, b_=base: E.tensor_scalar_mul(
                stat[:, b_ + 6:b_ + 7], stat[:, b_:b_ + 1], 1.0 / C), SV)
            mk.w("s", atok)
            A_toks[i] = mk.do("s", lambda E, h_=h1, b_=base: E.activation(
                h_[:, 1024:2048], h_[:, 0:1024], ACT.Square,
                accum_out=stat[:, b_ + 1:b_ + 2]), SS)
        if 0 <= step - 1 < NT:
            j = step - 1
            base = bases3[j % 3]
            mk.w("v", A_toks[j])
            vtok = mk.do("v", lambda E, b_=base: E.scalar_tensor_tensor(
                stat[:, b_ + 3:b_ + 4], stat[:, b_ + 2:b_ + 3], -1.0 / C,
                stat[:, b_ + 1:b_ + 2], ALU.mult, ALU.add), SV)
            mk.w("s", vtok)
            mk.do("s", lambda E, b_=base: E.activation(
                stat[:, b_ + 4:b_ + 5], stat[:, b_ + 3:b_ + 4], ACT.Ln,
                bias=eps_s[:, 0:1], scale=1.0 / C), SS)
            B_toks[j] = mk.do("s", lambda E, b_=base: E.activation(
                stat[:, b_ + 5:b_ + 6], stat[:, b_ + 4:b_ + 5], ACT.Exp,
                scale=-0.5), SS)
        if 0 <= step - 2 < NT:
            k = step - 2
            h1 = h1s3[k % 3]
            base = bases3[k % 3]
            mk.w("v", B_toks[k])
            x2_done[k] = mk.do("v", lambda E, k_=k, h_=h1, b_=base: E.tensor_scalar(
                x2_v[:, k_, :], h_[:, 0:1024], stat[:, b_ + 6:b_ + 7],
                stat[:, b_ + 5:b_ + 6], ALU.subtract, ALU.mult), SV)
    mk.w("g", x2_done[-1])
    mk.w("g", x2_done[-2])
    spill_tok = mk.do("g", lambda E: E.dma_start(
        x2_spill.rearrange("(i p) c -> p i c", p=128), x2_v[:, :, :]),
        mk.dsem("SPL"), 16)

    if _BUILD_OPTS["phases"] < 5:
        return _finish(mk, nc, out_d, outbuf_v)

    tr_evac = [None] * 4
    xf_free = [None, None]
    xn_f32s = [scrA, scrB]
    b5 = [16, 20, 24, 28]
    A5 = {}
    B5 = {}
    for step in range(NT + 1):
        if step < NT:
            i = step
            base = b5[i % 4]
            xf = xn_f32s[i % 2]
            mk.w("s", x2_done[i])
            mk.w("s", B5.get(i - 3))
            mk.do("s", lambda E, i_=i, x_=xf, b_=base: E.activation(
                x_[:, 1024:2048], x2_v[:, i_, :], ACT.Square,
                accum_out=stat[:, b_:b_ + 1]), SS)
            mk.do("s", lambda E, b_=base: E.activation(
                stat[:, b_ + 2:b_ + 3], stat[:, b_:b_ + 1], ACT.Ln,
                bias=eps_s[:, 0:1], scale=1.0 / C), SS)
            A5[i] = mk.do("s", lambda E, b_=base: E.activation(
                stat[:, b_ + 3:b_ + 4], stat[:, b_ + 2:b_ + 3], ACT.Exp,
                scale=-0.5), SS)
        if 0 <= step - 1 < NT:
            j = step - 1
            base = b5[j % 4]
            mk.w("v", A5[j])
            mk.w("v", ldtoks["normg"])
            mk.w("v", ldtoks["normb"])
            mk.do("v", lambda E, i_=j, b_=base: E.scalar_tensor_tensor(
                xn_tm_v[:, i_, :], x2_v[:, i_, :], stat[:, b_ + 3:b_ + 4],
                normg_s[:, :], ALU.mult, ALU.mult), SV)
            ftok = mk.do("v", lambda E, i_=j: E.tensor_tensor(
                xn_tm_v[:, i_, :], xn_tm_v[:, i_, :], normb_s[:, :], ALU.add), SV)
            B5[j] = ftok
            ttok = None
            for half in range(2):
                bank = (j * 2 + half) % 4
                mk.w("t", ftok)
                mk.w("t", tr_evac[bank])
                for q in range(4):
                    dc = half * 4 + q
                    ttok = mk.do("t", lambda E, i_=j, dc_=dc, bank_=bank, q_=q:
                                 E.matmul(p4q[:, bank_ * 4 + q_, :],
                                          xn_tm_v[:, i_, dc_ * 128:(dc_ + 1) * 128],
                                          identb_s[:, :], start=True, stop=True), ST)
                mk.w("v", ttok)
                tr_evac[bank] = mk.do("v", lambda E, i_=j, h_=half, bank_=bank:
                                      E.tensor_copy(
                    xn_fm_v[:, h_ * 4:(h_ + 1) * 4, i_ * 128:(i_ + 1) * 128],
                    p4q[:, bank_ * 4:(bank_ + 1) * 4, :]), SV)
    se_lt = {}
    se_sq = {}
    se_t1 = {}
    se_rs = {}
    se_C = {}
    se_D = {}
    se_bufs = [scrA, scrB, scrC]
    bSE = [32, 39, 46, 53]
    for step in range(NCK + 3):
        if 0 <= step - 3 < NCK:
            ec = step - 3
            sf = se_bufs[ec % 3]
            base = bSE[ec % 4]
            mk.w("v", se_rs[ec])
            if ec == 0:
                mk.w("v", ldtoks["slotg"])
                mk.w("v", ldtoks["slotb"])
                mk.w("v", ldtoks["scl"])
            mk.do("v", lambda E, s_=sf, b_=base: E.tensor_scalar(
                s_[:, 1024:2048], s_[:, 0:1024], stat[:, b_ + 6:b_ + 7],
                stat[:, b_ + 5:b_ + 6], ALU.subtract, ALU.mult), SV)
            mk.do("v", lambda E, s_=sf: E.tensor_tensor(
                s_[:, 1024:2048], s_[:, 1024:2048], slotg_s[:, :], ALU.mult), SV)
            mk.do("v", lambda E, s_=sf: E.tensor_tensor(
                s_[:, 1024:2048], s_[:, 1024:2048], slotb_s[:, :], ALU.add), SV)
            ftok = mk.do("v", lambda E, s_=sf: E.tensor_scalar_mul(
                s_[:, 1024:2048], s_[:, 1024:2048], scale_s[:, 0:1]), SV)
            se_C[ec] = ftok
            ttok = None
            for half in range(2):
                bank = (ec * 2 + half) % 4
                mk.w("t", ftok)
                mk.w("t", tr_evac[bank])
                for q in range(4):
                    dc = half * 4 + q
                    ttok = mk.do("t", lambda E, s_=sf, dc_=dc, bank_=bank, q_=q:
                                 E.transpose(p4q[:, bank_ * 4 + q_, :],
                                             s_[:, 1024 + dc_ * 128:1024 + (dc_ + 1) * 128],
                                             identf_s[:, :]), ST)
                mk.w("v", ttok)
                tr_evac[bank] = mk.do("v", lambda E, ec_=ec, h_=half, bank_=bank:
                                      E.tensor_copy(
                    seT_v[:, h_ * 4:(h_ + 1) * 4, ec_ * 128:(ec_ + 1) * 128],
                    p4q[:, bank_ * 4:(bank_ + 1) * 4, :]), SV)
            se_D[ec] = ttok
        if step < NCK:
            ec = step
            sf = se_bufs[ec % 3]
            if ec < 3:
                mk.w("y", mk.tok("v"))
                mk.w("y", mk.tok("s"))
            mk.w("y", se_C.get(ec - 3))
            se_lt[ec] = mk.do("y", lambda E, ec_=ec, s_=sf: E.dma_start(
                s_[:, 0:1024], semb_in[ec_ * 128:(ec_ + 1) * 128, :]),
                mk.dsem(f"SE{ec % 2}"), 16)
        if 0 <= step - 1 < NCK:
            ec = step - 1
            sf = se_bufs[ec % 3]
            base = bSE[ec % 4]
            mk.w("s", se_lt[ec])
            mk.w("s", se_t1.get(ec - 4))
            mk.w("s", se_D.get(ec - 3))
            se_sq[ec] = mk.do("s", lambda E, s_=sf, b_=base: E.activation(
                s_[:, 1024:2048], s_[:, 0:1024], ACT.Square,
                accum_out=stat[:, b_ + 1:b_ + 2]), SS)
        if 0 <= step - 2 < NCK:
            ec = step - 2
            sf2 = se_bufs[ec % 3]
            base = bSE[ec % 4]
            mk.w("v", se_sq[ec])
            mk.do("v", lambda E, s_=sf2, b_=base: E.reduce_sum(
                stat[:, b_:b_ + 1], s_[:, 0:1024], axis=AX), SV)
            se_t1[ec] = mk.do("v", lambda E, b_=base: E.tensor_tensor(
                stat[:, b_ + 2:b_ + 3], stat[:, b_:b_ + 1],
                stat[:, b_:b_ + 1], ALU.mult), SV)
            mk.do("v", lambda E, b_=base: E.tensor_scalar_mul(
                stat[:, b_ + 6:b_ + 7], stat[:, b_:b_ + 1], 1.0 / C), SV)
            vtok = mk.do("v", lambda E, b_=base: E.scalar_tensor_tensor(
                stat[:, b_ + 3:b_ + 4], stat[:, b_ + 2:b_ + 3], -1.0 / C,
                stat[:, b_ + 1:b_ + 2], ALU.mult, ALU.add), SV)
            mk.w("s", vtok)
            mk.do("s", lambda E, b_=base: E.activation(
                stat[:, b_ + 4:b_ + 5], stat[:, b_ + 3:b_ + 4], ACT.Ln,
                bias=eps_s[:, 0:1], scale=1.0 / C), SS)
            se_rs[ec] = mk.do("s", lambda E, b_=base: E.activation(
                stat[:, b_ + 5:b_ + 6], stat[:, b_ + 4:b_ + 5], ACT.Exp,
                scale=-0.5), SS)

    if _BUILD_OPTS["phases"] < 6:
        return _finish(mk, nc, out_d, outbuf_v)

    for bank in range(4):
        mk.w("t", tr_evac[bank])
    lg_tok = [None, None]
    mk.w("v", spill_tok)
    mk.w("s", spill_tok)
    for i in range(NT):
        par = i % 2
        off0 = 0 if par == 0 else 1024
        mk.w("t", lg_tok[par])
        tk = None
        for hb in range(2):
            for dc in range(NCK):
                tk = mk.do("t", lambda E, i_=i, hb_=hb, dc_=dc, o=off0: E.matmul(
                    p4[:, o + hb_ * 512:o + (hb_ + 1) * 512],
                    xn_fm_v[:, dc_, i_ * 128:(i_ + 1) * 128],
                    seT_v[:, dc_, hb_ * 512:(hb_ + 1) * 512],
                    start=(dc_ == 0), stop=(dc_ == NCK - 1)), ST)
        mk.w("s", tk)
        cpt = mk.do("s", lambda E, i_=i, o=off0: E.activation(
            logits_v[:, i_, :], p4[:, o:o + 1024], ACT.Copy), SS)
        lg_tok[par] = cpt
        mk.w("v", cpt)
        mk.do("v", lambda E, i_=i: E.reduce_max(
            rowmax_tm[:, i_:i_ + 1], logits_v[:, i_, :], axis=AX), SV)

    if _BUILD_OPTS["phases"] < 7:
        return _finish(mk, nc, out_d, outbuf_v)

    nrm_tok = mk.do("v", lambda E: E.tensor_scalar_mul(
        nrm_s[:, :], rowmax_tm[:, :], -1.0), SV)
    tr_tok = None
    exp_tok3 = None
    for ec in range(NCK):
        mk.w("t", tr_tok)
        ttok = None
        for i in range(NT):
            ttok = mk.do("t", lambda E, i_=i, ec_=ec: E.transpose(
                p4q[:, i_, :], logits_v[:, i_, ec_ * 128:(ec_ + 1) * 128],
                identf_s[:, :]), ST)
        mk.w("v", ttok)
        mk.do("v", lambda E, ec_=ec: E.reduce_max(
            cmax_s[:, ec_:ec_ + 1], p4[:, :], axis=AX), SV)
        nm2 = mk.do("v", lambda E, ec_=ec: E.tensor_scalar_mul(
            ncmax_s[:, ec_:ec_ + 1], cmax_s[:, ec_:ec_ + 1], -1.0), SV)
        mk.w("s", nm2)
        tr_tok = mk.do("s", lambda E, ec_=ec: E.activation(
            dnT_v[:, ec_, :], p4[:, :], ACT.Exp, bias=ncmax_s[:, ec_:ec_ + 1],
            accum_out=cs2_s[:, ec_:ec_ + 1]), SS)
        exp_tok3 = tr_tok
    mk.do("s", lambda E: E.activation(rcl_s[:, :], cs2_s[:, :], ACT.Ln), SS)
    rck = mk.do("s", lambda E: E.activation(rcT_s[:, :], rcl_s[:, :], ACT.Exp,
                                            scale=-1.0), SS)
    mk.w("v", rck)
    nlast = None
    for ec in range(NCK):
        nlast = mk.do("v", lambda E, ec_=ec: E.tensor_scalar_mul(
            dnT_v[:, ec_, :], dnT_v[:, ec_, :], rcT_s[:, ec_:ec_ + 1]), SV)
    expc_toks = []
    for i in range(NT - 1, -1, -1):
        dst = lgb_v[:, NT + i, :] if i < 15 else scrBb[:, 0:1024]
        expc_toks.append(mk.do("s", lambda E, i_=i, d_=dst: E.activation(
            d_, logits_v[:, i_, :], ACT.Exp, bias=nrm_s[:, i_:i_ + 1],
            accum_out=rsum_tm[:, i_:i_ + 1]), SS))
    expc_tok = expc_toks[-1]
    mk.do("s", lambda E: E.activation(rsL_s[:, :], rsum_tm[:, :], ACT.Ln), SS)
    rct = mk.do("s", lambda E: E.activation(recip_tm[:, :], rsL_s[:, :], ACT.Exp,
                                            scale=-1.0), SS)
    dp_tok = [None, None]
    mk.w("t", nlast)
    mk.w("t", exp_tok3)
    for i in range(NT):
        par = i % 2
        o = par * 1024
        mk.w("t", dp_tok[par])
        tk = None
        for ec in range(NCK):
            tk = mk.do("t", lambda E, i_=i, ec_=ec, o_=o: E.matmul(
                p4[:, o_ + ec_ * 128:o_ + (ec_ + 1) * 128],
                dnT_v[:, ec_, i_ * 128:(i_ + 1) * 128], identb_s[:, :],
                start=True, stop=True), ST)
        mk.w("v", tk)
        if i == 0:
            mk.w("v", expc_tok)
        dp_tok[par] = mk.do("v", lambda E, i_=i, o_=o: E.tensor_copy(
            lgb_v[:, i_, :], p4[:, o_:o_ + 1024]), SV)
    sl_tok = [None, None]
    sl_dma = [None, None]
    mk.w("t", dp_tok[0])
    mk.w("t", dp_tok[1])
    for dc in range(NCK):
        par = dc % 2
        off0 = 0 if par == 0 else 1024
        mk.w("t", sl_tok[par])
        tk = None
        for hb in range(2):
            for i in range(NT):
                tk = mk.do("t", lambda E, dc_=dc, hb_=hb, i_=i, o=off0: E.matmul(
                    p4[:, o + hb_ * 512:o + (hb_ + 1) * 512],
                    xn_tm_v[:, i_, dc_ * 128:(dc_ + 1) * 128],
                    lgb_v[:, i_, hb_ * 512:(hb_ + 1) * 512],
                    start=(i_ == 0), stop=(i_ == NT - 1)), ST)
        mk.w("v", tk)
        mk.w("v", sl_dma[par])
        sl_tok[par] = mk.do("v", lambda E, dc_=dc, o=off0, par_=par: E.tensor_copy(
            scbA[:, par_ * 1024:(par_ + 1) * 1024], p4[:, o:o + 1024]), SV)
        mk.w("g", sl_tok[par])
        sl_dma[par] = mk.do("g", lambda E, dc_=dc, par_=par: E.dma_start(
            ag1_in.rearrange("(c p) e -> p c e", p=128)[:, dc_, :],
            scbA[:, par_ * 1024:(par_ + 1) * 1024]), mk.dsem(f"SL{par}"), 16)
    sl_ev = sl_dma[(NCK - 1) % 2]
    mk.w("v", sl_dma[0])
    mk.w("v", sl_dma[1])
    mk.w("g", sl_ev)
    mk.w("g", mk.tok("t"))
    mk.w("g", mk.tok("v"))
    mk.w("g", mk.tok("s"))
    wbufs = {0: (wgA_v, wuA_v), 1: (wgB_v, wuB_v)}
    wld = {}
    wfree = {0: None, 1: None}

    def load_wh(e, Hh, par, eng="g"):
        gv, uv = wbufs[par]
        mk.w(eng, wfree[par])
        mk.do(eng, lambda E, e_=e, H_=Hh, g_=gv: E.dma_start(
            g_[:, :, :],
            wg2_in[e_].rearrange("(c p) f -> p c f", p=128)[:, :, H_ * 1024:(H_ + 1) * 1024]),
            mk.dsem(f"WH{par}"), 16)
        wld[(e, Hh)] = mk.do(eng, lambda E, e_=e, H_=Hh, u_=uv: E.dma_start(
            u_[:, :, :],
            wu2_in[e_].rearrange("(c p) f -> p c f", p=128)[:, :, H_ * 1024:(H_ + 1) * 1024]),
            mk.dsem(f"WH{par}"), 16)

    def load_wd(e, oh, eng="g"):
        return mk.do(eng, lambda E, e_=e, o_=oh: E.dma_start(
            wdS_v[:, :, :],
            wd2_in[e_].rearrange("(c p) o -> p c o", p=128)[:, :, o_ * 512:(o_ + 1) * 512]),
            mk.dsem("WD"), 16)

    agt = mk.do("g", lambda E: E.collective_compute(
        "AllGather", ALU.bypass, replica_groups=[list(range(NCORES))],
        ins=[ag1_in.ap().opt()], outs=[ag1_out.ap().opt()]), mk.sem["cc"], 1)
    mk.w("t", sl_tok[0])
    mk.w("t", sl_tok[1])
    tr2_tok = None
    for ec in range(NCK):
        mk.w("t", tr2_tok)
        ttok = None
        for i in range(NT):
            lhs = (lgb_v[:, NT + i, :] if i < 15 else scrBb[:, 0:1024])
            ttok = mk.do("t", lambda E, i_=i, ec_=ec, l_=lhs: E.matmul(
                p4q[:, i_, :], l_[:, ec_ * 128:(ec_ + 1) * 128],
                identb_s[:, :], start=True, stop=True), ST)
        mk.w("v", ttok)
        tr2_tok = mk.do("v", lambda E, ec_=ec: E.tensor_copy(
            combineT_v[:, ec_, :], p4[:, :]), SV)

    mk.w("y", mk.tok("t"))
    mk.w("y", mk.tok("v"))
    mk.w("y", mk.tok("s"))
    brow_tok = None
    for nm, bin2, row, wdt in (("bg2", bg2_in, bg_row, FF), ("bu2", bu2_in, bu_row, FF),
                               ("bd2", bd2_in, bd_row, C)):
        mk.w("y", brow_tok)
        lt = mk.do("y", lambda E, b=bin2, w_=wdt: E.dma_start(
            scrB[0:2, 0:w_], b[:, :]), mk.dsem("YG"), 16)
        mk.w("v", lt)
        mk.w("v", brow_tok)
        ctk = mk.do("v", lambda E, w_=wdt: E.tensor_copy(
            scbB[0:2, 0:w_], scrB[0:2, 0:w_]), SV)
        mk.w("g", ctk)
        brow_tok = mk.do("g", lambda E, r=row, w_=wdt: E.dma_start(
            r[0:1, 0:2 * w_], scbB[0:2, 0:w_]), mk.dsem("BRS"), 16)
    mk.w("v", brow_tok)

    if _BUILD_OPTS["phases"] < 8:
        return _finish(mk, nc, out_d, outbuf_v)

    mk.w("g", tr2_tok)
    mk.w("g", brow_tok)
    load_wh(0, 0, 0)
    wd_tok = load_wd(0, 0)
    load_wh(0, 1, 1)
    mk.w("g", agt)

    regcell = {}

    def _dynoff(E, shift, delta):
        if "pid" not in regcell:
            regcell["pid"] = E.to_reg(E.partition_id())
            regcell["off"] = E.alloc_register("dynoff")
        if shift is not None:
            E.reg_alu(regcell["off"], regcell["pid"], shift, ALU.arith_shift_left)
        if delta:
            E.reg_add(regcell["off"], regcell["off"], delta)
        return regcell["off"]

    XeT_x = XeT.rearrange("p (dc s) -> p dc s", dc=8)
    xstg_v = xstg.rearrange("p (u dc s) -> p u dc s", u=2, dc=8)

    def _xet_dyn(E, b_, par):
        first = (b_ == 0)
        off = _dynoff(E, 7 if first else None, 0 if first else (1 << 20))
        src = bass.AP(ag1_out, off, [[EL, 128], [128 * EL, 8], [1, 128]])
        return E.dma_start(xstg_v[:, par, :, :], src)

    xet_tok = None
    xcp_tok = [None, None]
    for b_ in range(NCORES):
        par = b_ % 2
        mk.w("g", xcp_tok[par])
        ld = mk.do("g", lambda E, b__=b_, p_=par: _xet_dyn(E, b__, p_),
                   mk.dsem(f"XS{par}"), 16)
        mk.w("g", ld)
        for el_ in range(2):
            xcp_tok[par] = mk.do("g", lambda E, b__=b_, el__=el_, p_=par: E.dma_start(
                XeT_x[:, :, el__ * 512 + b__ * 64:el__ * 512 + b__ * 64 + 64],
                xstg_v[:, p_, :, el__ * 64:el__ * 64 + 64]),
                mk.dsem(f"XC{par}"), 16)
        xet_tok = xcp_tok[par]
    xet_tok2 = xcp_tok[1 - (NCORES - 1) % 2]

    mk.w("t", xet_tok)
    mk.w("t", xet_tok2)
    mk.w("t", brow_tok)
    mk.w("s", brow_tok)
    eo_dma = [None, None]
    gelu_free = None
    up_free = None
    gate_free = None
    h_cons = None
    for e in range(2):
        for Hh in range(2):
            par = Hh
            gv, uv = wbufs[par]
            mk.w("t", wld[(e, Hh)])
            for fs in range(8):
                fc = Hh * 8 + fs
                if fc == 0:
                    mk.w("t", h_cons)
                mk.w("t", gate_free)
                tkg = None
                for dc in range(NCK):
                    tkg = mk.do("t", lambda E, g_=gv, dc_=dc, fs_=fs, e_=e: E.matmul(
                        pA[:, :], g_[:, dc_, fs_ * 128:(fs_ + 1) * 128],
                        XeT_v[:, dc_, e_ * 512:(e_ + 1) * 512],
                        start=(dc_ == 0), stop=False), ST)
                tkg = mk.do("t", lambda E, e_=e, fc_=fc: E.matmul(
                    pA[:, :], bg_row[0:1, e_ * FF + fc_ * 128:e_ * FF + (fc_ + 1) * 128],
                    ones1_s[0:1, 0:512], start=False, stop=True), ST)
                mk.w("t", up_free)
                tku = None
                for dc in range(NCK):
                    tku = mk.do("t", lambda E, u_=uv, dc_=dc, fs_=fs, e_=e: E.matmul(
                        pB[:, :], u_[:, dc_, fs_ * 128:(fs_ + 1) * 128],
                        XeT_v[:, dc_, e_ * 512:(e_ + 1) * 512],
                        start=(dc_ == 0), stop=False), ST)
                tku = mk.do("t", lambda E, e_=e, fc_=fc: E.matmul(
                    pB[:, :], bu_row[0:1, e_ * FF + fc_ * 128:e_ * FF + (fc_ + 1) * 128],
                    ones65_s[32:33, 0:512], start=False, stop=True), ST)
                mk.w("s", tkg)
                mk.w("s", gelu_free)
                gt = mk.do("s", lambda E: E.activation(scbA[:, 0:512], pA[:, :],
                                                       ACT.Gelu), SS)
                gate_free = gt
                mk.w("v", tku)
                ut = mk.do("v", lambda E: E.tensor_copy(scbB[:, 0:512], pB[:, :]), SV)
                up_free = ut
                mk.w("v", gt)
                htok = mk.do("v", lambda E, fc_=fc: E.tensor_tensor(
                    hfm_v[:, fc_, :], scbA[:, 0:512], scbB[:, 0:512], ALU.mult), SV)
                gelu_free = htok
            if e == 0:
                wfree[par] = mk.tok("t")
                load_wh(1, Hh, par)
        mk.w("t", mk.tok("v"))
        for oh in range(2):
            mk.w("t", wd_tok)
            eo_par = [None, None]
            for bsl in range(4):
                par = bsl % 2
                mk.w("t", eo_par[par])
                tk = None
                for fc in range(16):
                    tk = mk.do("t", lambda E, fc_=fc, bsl_=bsl, par_=par: E.matmul(
                        p4[:, par_ * 1024:par_ * 1024 + 512],
                        hfm_v[:, fc_, bsl_ * 128:(bsl_ + 1) * 128],
                        wdS_v[:, fc_, :],
                        start=(fc_ == 0), stop=(fc_ == 15)), ST)
                tk = mk.do("t", lambda E, e_=e, oh_=oh, par_=par: E.matmul(
                    p4[:, par_ * 1024:par_ * 1024 + 512], ones65_s[64:65, 0:128],
                    bd_row[0:1, e_ * 1024 + oh_ * 512:e_ * 1024 + (oh_ + 1) * 512],
                    start=False, stop=True), ST)
                mk.w("v", tk)
                mk.w("v", eo_dma[par])
                eo_par[par] = mk.do("v", lambda E, par_=par: E.tensor_copy(
                    scbA[:, 1024 + par_ * 512:1024 + (par_ + 1) * 512],
                    p4[:, par_ * 1024:par_ * 1024 + 512]), SV)
                mk.w("g", eo_par[par])
                eo_dma[par] = mk.do("g", lambda E, e_=e, bsl_=bsl, oh_=oh, par_=par:
                                    E.dma_start(
                    ag2_in.rearrange("(c p) o -> p c o", p=128)[
                        :, e_ * 4 + bsl_, oh_ * 512:(oh_ + 1) * 512],
                    scbA[:, 1024 + par_ * 512:1024 + (par_ + 1) * 512]),
                    mk.dsem(f"EO{par}"), 16)
            mk.w("g", mk.tok("t"))
            if oh == 0:
                wd_tok = load_wd(e, 1)
            elif e == 0:
                wd_tok = load_wd(1, 0)
        h_cons = mk.tok("t")
    mk.w("g", eo_dma[0])
    mk.w("g", eo_dma[1])
    mk.w("g", mk.tok("t"))
    x2r_tok = mk.do("g", lambda E: E.dma_start(
        x2b_v[:, :, :], x2_spill.rearrange("(i p) c -> p i c", p=128)),
        mk.dsem("REL"), 16)
    agt2 = mk.do("g", lambda E: E.collective_compute(
        "AllGather", ALU.bypass, replica_groups=[list(range(NCORES))],
        ins=[ag2_in.ap().opt()], outs=[ag2_out.ap().opt()]), mk.sem["cc"], 1)
    mk.w("g", agt2)
    mk.w("g", mk.tok("t"))

    def _ef_dma(E, c_):
        first = (c_ == 0)
        off = _dynoff(E, 16 if first else None, 0 if first else (1 << 20))
        src = bass.AP(ag2_out, off, [[512 * C, 2], [C, 64], [1, C]])
        return E.dma_start(eout_f_v[:, c_, :], src)

    ef_tok = None
    for c_ in range(NCORES):
        ef_tok = mk.do("g", lambda E, c__=c_: _ef_dma(E, c__),
                       mk.dsem("EFG"), 16)

    if _BUILD_OPTS["phases"] < 10:
        return _finish(mk, nc, out_d, outbuf_v)

    mk.w("t", ef_tok)
    mk.w("t", x2r_tok)
    h1sX = [scrA, scrB, scrD]
    basesX = [0, 8, 48]
    mm_gateX = [None, None]
    AX_t = {}
    BX_t = {}
    for step in range(NT + 2):
        if step < NT:
            i = step
            par = i % 2
            off0 = 0 if par == 0 else 1024
            mk.w("t", mm_gateX[par])
            tk = None
            for hb in range(2):
                for ec in range(NCK):
                    tk = mk.do("t", lambda E, i_=i, hb_=hb, ec_=ec, o=off0: E.matmul(
                        p4[:, o + hb_ * 512:o + (hb_ + 1) * 512],
                        combineT_v[:, ec_, i_ * 128:(i_ + 1) * 128],
                        eout_f_v[:, ec_, hb_ * 512:(hb_ + 1) * 512],
                        start=(ec_ == 0), stop=(ec_ == NCK - 1)), ST)
            h1 = h1sX[i % 3]
            base = basesX[i % 3]
            mk.w("v", tk)
            if i == 0:
                mk.w("v", x2r_tok)
                mk.w("v", rct)
            atok = mk.do("v", lambda E, h=h1, i_=i, o=off0, b_=base:
                         E.scalar_tensor_tensor(
                             h[:, 0:1024], p4[:, o:o + 1024],
                             recip_tm[:, i_:i_ + 1], x2b_v[:, i_, :],
                             ALU.mult, ALU.add,
                             accum_out=stat[:, b_:b_ + 1]), SV)
            mm_gateX[par] = atok
            mk.do("v", lambda E, b_=base: E.tensor_tensor(
                stat[:, b_ + 2:b_ + 3], stat[:, b_:b_ + 1],
                stat[:, b_:b_ + 1], ALU.mult), SV)
            mk.do("v", lambda E, b_=base: E.tensor_scalar_mul(
                stat[:, b_ + 6:b_ + 7], stat[:, b_:b_ + 1], 1.0 / C), SV)
            mk.w("s", atok)
            AX_t[i] = mk.do("s", lambda E, h=h1, b_=base: E.activation(
                h[:, 1024:2048], h[:, 0:1024], ACT.Square,
                accum_out=stat[:, b_ + 1:b_ + 2]), SS)
        if 0 <= step - 1 < NT:
            j = step - 1
            base = basesX[j % 3]
            mk.w("v", AX_t[j])
            vtok = mk.do("v", lambda E, b_=base: E.scalar_tensor_tensor(
                stat[:, b_ + 3:b_ + 4], stat[:, b_ + 2:b_ + 3], -1.0 / C,
                stat[:, b_ + 1:b_ + 2], ALU.mult, ALU.add), SV)
            mk.w("s", vtok)
            mk.do("s", lambda E, b_=base: E.activation(
                stat[:, b_ + 4:b_ + 5], stat[:, b_ + 3:b_ + 4], ACT.Ln,
                bias=eps_s[:, 0:1], scale=1.0 / C), SS)
            BX_t[j] = mk.do("s", lambda E, b_=base: E.activation(
                stat[:, b_ + 5:b_ + 6], stat[:, b_ + 4:b_ + 5], ACT.Exp,
                scale=-0.5), SS)
        if 0 <= step - 2 < NT:
            k = step - 2
            h1 = h1sX[k % 3]
            base = basesX[k % 3]
            mk.w("v", BX_t[k])
            mk.do("v", lambda E, k_=k, h=h1, b_=base: E.tensor_scalar(
                outbuf_v[:, k_, :], h[:, 0:1024], stat[:, b_ + 6:b_ + 7],
                stat[:, b_ + 5:b_ + 6], ALU.subtract, ALU.mult), SV)

    return _finish(mk, nc, out_d, outbuf_v)


def _finish(mk, nc, out_d, outbuf_v):
    mk.w("g", mk.tok("v"))
    mk.w("g", mk.tok("s"))
    st = mk.do("g", lambda E: E.dma_start(
        out_d.rearrange("(i p) c -> p i c", p=128), outbuf_v[:, :, :]),
        mk.dsem("OUT"), 16)
    for name in _BUILD_OPTS["taps"]:
        h = mk.sb[name]
        shp = list(h.shape)
        o = nc.dram_tensor("tap_" + name, shp, h.ap().dtype, kind="ExternalOutput")
        st = mk.do("g", lambda E, o_=o, h_=h: E.dma_start(o_[:, :], h_[:, :]),
                   mk.dsem("OUT"), 16)
    mk.w("g", st)
    nc._mk_specs = list(mk.specs)
    with nc.Block() as block:
        block.tensor(lambda E: [f(E) for f in mk.ops["t"]])
        block.vector(lambda E: [f(E) for f in mk.ops["v"]])
        block.scalar(lambda E: [f(E) for f in mk.ops["s"]])
        block.gpsimd(lambda E: [f(E) for f in mk.ops["g"]])
        block.sync(lambda E: [f(E) for f in mk.ops["y"]])
    return nc


def _solve_plan(specs, cap):
    plan = {}
    placed = []
    for name, size, p0, p1 in sorted(specs, key=lambda x: -x[1]):
        taken = sorted((plan[n], plan[n] + s) for (n, s, a0, a1) in placed
                       if not (a1 < p0 or a0 > p1))
        off = 0
        for (a, bnd) in taken:
            if off + size <= a:
                break
            off = max(off, bnd)
        assert off + size <= cap, f"SBUF plan overflow: {name} {size}B at {off}"
        plan[name] = off
        placed.append((name, size, p0, p1))
    return plan


_PLAN_CACHE = {}


def build_planned():
    key = _BUILD_OPTS["phases"]
    if key not in _PLAN_CACHE:
        nc0 = build(None)
        _PLAN_CACHE[key] = _solve_plan(nc0._mk_specs, 206336)
    return build(_PLAN_CACHE[key])


def _blkd(p2lT):
    z = np.zeros((128, 64), np.float32)
    z[0:64, 0:32] = p2lT
    z[64:128, 32:64] = p2lT
    return z


def _stage_inputs(inputs, core):
    f32 = np.float32
    x = np.asarray(inputs["L0_attn_x"][core], dtype=f32)
    caw = np.asarray(inputs["L0_attn_c_attn_w"], dtype=f32)
    cab = np.asarray(inputs["L0_attn_c_attn_b"], dtype=f32)
    q2l = np.asarray(inputs["L0_attn_q2l_w"], dtype=f32)
    q2lb = np.asarray(inputs["L0_attn_q2l_b"], dtype=f32)
    kv2l = np.asarray(inputs["L0_attn_kv2l_w"], dtype=f32)
    kv2lb = np.asarray(inputs["L0_attn_kv2l_b"], dtype=f32)
    l2o = np.asarray(inputs["L0_attn_l2o_w"], dtype=f32)
    l2ob = np.asarray(inputs["L0_attn_l2o_b"], dtype=f32)
    wqk = np.asarray(inputs["L0_attn_wqk_w"], dtype=f32)
    wqkb = np.asarray(inputs["L0_attn_wqk_b"], dtype=f32)
    cpw = np.asarray(inputs["L0_attn_cproj_w"], dtype=f32)
    cpb = np.asarray(inputs["L0_attn_cproj_b"], dtype=f32)
    d = dict(_CONSTS)
    d.update({
        "x_tm_in": np.ascontiguousarray(x),
        "x_fm_in": np.ascontiguousarray(x.T),
        "caw": caw,
        "cprojT_in": np.ascontiguousarray(cpw.T),
        "q2lT_in": np.ascontiguousarray(q2l.T),
        "kv2lT_in": np.ascontiguousarray(kv2l.T),
        "l2o_in": l2o,
        "wqkT_in": np.ascontiguousarray(wqk.T),
        "blkdQ_in": _blkd(q2l.T),
        "blkdKV_in": _blkd(kv2l.T),
        "wqkT4_in": np.tile(np.ascontiguousarray(wqk.T), (4, 1)),
        "l2o2_in": np.tile(l2o, (2, 1)),
        "bqr_in": np.ascontiguousarray(cab[0:C].reshape(H, HD).T),
        "bkr_in": np.ascontiguousarray(cab[C:2 * C].reshape(H, HD).T),
        "bvr_in": np.ascontiguousarray(cab[2 * C:3 * C].reshape(H, HD).T),
        "q2lb16_in": np.tile(q2lb[None, :], (H, 1)),
        "kvlb16_in": np.tile(kv2lb[None, :], (H, 1)),
        "wqkb512_in": np.tile(wqkb, H)[None, :].copy(),
        "l2ob_in": np.ascontiguousarray(np.tile(l2ob, H).reshape(8, 128).T),
        "cprojb_in": cpb[None, :].copy(),
        "normg128": np.tile(np.asarray(inputs["L0_ffn_norm_g"], f32)[None, :], (128, 1)),
        "normb128": np.tile(np.asarray(inputs["L0_ffn_norm_b"], f32)[None, :], (128, 1)),
        "slotg128": np.tile(np.asarray(inputs["L0_ffn_slot_g"], f32)[None, :], (128, 1)),
        "slotb128": np.tile(np.asarray(inputs["L0_ffn_slot_b"], f32)[None, :], (128, 1)),
        "scale128": np.full((128, 1), np.asarray(inputs["L0_ffn_scale"], f32)[0], f32),
        "semb": np.asarray(inputs["L0_ffn_slot_embeds"], f32).reshape(EL, C),
        "wg2": np.asarray(inputs["L0_ffn_Wg"], f32)[2 * core:2 * core + 2],
        "wu2": np.asarray(inputs["L0_ffn_Wu"], f32)[2 * core:2 * core + 2],
        "wd2": np.asarray(inputs["L0_ffn_Wd"], f32)[2 * core:2 * core + 2],
        "bg2": np.asarray(inputs["L0_ffn_bg"], f32)[2 * core:2 * core + 2],
        "bu2": np.asarray(inputs["L0_ffn_bu"], f32)[2 * core:2 * core + 2],
        "bd2": np.asarray(inputs["L0_ffn_bd"], f32)[2 * core:2 * core + 2],
    })
    return d


def kernel(**inputs):
    nc = build_planned()
    in_maps = [_stage_inputs(inputs, c) for c in range(NCORES)]
    res = run_bass_kernel_spmd(nc, in_maps, core_ids=list(range(NCORES)))
    out = np.stack([np.asarray(res.results[c]["out"]) for c in range(NCORES)], axis=0)
    if _BUILD_OPTS["taps"]:
        kernel._taps = [{k: np.asarray(v) for k, v in r.items()} for r in res.results]
    return out
